# version 01
# baseline (speedup 1.0000x reference)
import numpy as np
import ml_dtypes
from contextlib import ExitStack
import concourse.bass as bass
import concourse.mybir as mybir
from concourse.bass_utils import run_bass_kernel_spmd

F32 = mybir.dt.float32
BF16 = mybir.dt.bfloat16
AF = mybir.ActivationFunctionType
ALU = mybir.AluOpType
AX = mybir.AxisListType
BF = ml_dtypes.bfloat16

P = 128
T = 2048
NB = 2
TT = NB * T
NEG = -1e30
D = 1024
DT = D // P

ENG_SEM = {"tensor": "pe", "vector": "dv", "scalar": "ac",
           "gpsimd": "pl", "sync": "qs"}
QUEUE_ENG = {"qs": "sync", "qg": "gpsimd", "qa": "scalar"}


def cdiv(a, b):
    return (a + b - 1) // b


class Ker:
    def __init__(self, nc, block, ctx):
        self.nc = nc
        self.block = block
        self.ctx = ctx
        E = ctx.enter_context
        self.sems = {}
        self.cnt = {}
        for name in ["pe", "dv", "ac", "pl", "qs", "qg", "qa"]:
            self.sems[name] = E(nc.semaphore(name))
            self.cnt[name] = 0
        self.ps = [E(nc.psum_tensor(f"ps{i}", [P, 512], F32)) for i in range(8)]
        self.sb_off = 16512
        self.sb_peak = self.sb_off
        self._phase_stack = []
        self._names = 0
        self.scratch = self.sb([P, 512], F32, "kscratch")
        self.scratch2 = self.sb([P, 512], F32, "kscratch2")

    def sb(self, shape, dtype, name=None):
        per_part = int(np.prod(shape[1:])) * mybir.dt.size(dtype)
        off = (self.sb_off + 63) & ~63
        self.sb_off = off + per_part
        self.sb_peak = max(self.sb_peak, self.sb_off)
        assert self.sb_off <= 229370, f"SBUF overflow: {self.sb_off} B/partition"
        self._names += 1
        nm = name or f"sb{self._names}"
        return self.nc.alloc_sbuf_tensor_at(nm, list(shape), dtype, offset=off)

    def sb_tiles(self, n, cols, dtype):
        return [self.sb([P, cols], dtype) for _ in range(n)]

    def phase(self):
        self._phase_stack.append(self.sb_off)

    def unphase(self):
        self.sb_off = self._phase_stack.pop()

    def inc(self, inst, sem, n=1):
        inst.then_inc(self.sems[sem], n)
        self.cnt[sem] += n
        return self.cnt[sem]

    def op(self, fn_map, snap=None):
        if snap is None:
            snap = dict(self.cnt)
        for en in ["sync", "gpsimd", "tensor", "vector", "scalar"]:
            if en not in fn_map:
                continue
            fn = fn_map[en]

            def body(eng, fn=fn, en=en):
                own = ENG_SEM.get(en)
                for s, v in snap.items():
                    if v > 0 and s != own:
                        eng.wait_ge(self.sems[s], v)
                fn(eng)
            getattr(self.block, en)(body)

    def finish(self, out_queues=("qs", "qg", "qa")):
        snap = dict(self.cnt)

        def body(eng):
            for s, v in snap.items():
                if v > 0:
                    eng.wait_ge(self.sems[s], v)
        self.block.sync(body)



def op_load(k, pairs, queue="qs"):
    def body(eng):
        for d, s in pairs:
            k.inc(eng.dma_start(out=d, in_=s), queue, 16)
    k.op({QUEUE_ENG[queue]: body})


def op_store(k, pairs, queue="qs"):
    def body(eng):
        for d, s in pairs:
            k.inc(eng.dma_start(out=d, in_=s), queue, 16)
    k.op({QUEUE_ENG[queue]: body})



def op_linearB(k, xs, w_hbm, din, dout, tt, epi, epi_eng="vector",
               w_chunk=512, epi_incs=1):
    ndi = din // P
    ndo_ch = cdiv(dout, w_chunk)
    ntc = tt // 512
    esem = ENG_SEM[epi_eng]
    k.phase()
    wbuf = [k.sb([P, ndi * w_chunk], BF16), k.sb([P, ndi * w_chunk], BF16)]
    pe0, ev0, qg0 = k.cnt["pe"], k.cnt[esem], k.cnt["qg"]

    rounds_per_chunk = []
    for c in range(ndo_ch):
        nds = min(w_chunk, dout - c * w_chunk)
        rounds_per_chunk.append(ntc * cdiv(nds, P))
    pe_after_chunk = np.cumsum(rounds_per_chunk)

    def dma_body(eng):
        for c in range(ndo_ch):
            if c >= 2:
                eng.wait_ge(k.sems["pe"], pe0 + int(pe_after_chunk[c - 2]))
            nds = min(w_chunk, dout - c * w_chunk)
            for di in range(ndi):
                inst = eng.dma_start(
                    out=wbuf[c % 2][:, di * w_chunk: di * w_chunk + nds],
                    in_=w_hbm[di * P:(di + 1) * P,
                              c * w_chunk: c * w_chunk + nds])
                k.inc(inst, "qg", 16)

    def pe_body(eng):
        rd = 0
        for c in range(ndo_ch):
            eng.wait_ge(k.sems["qg"], qg0 + 16 * ndi * (c + 1))
            nds = min(w_chunk, dout - c * w_chunk)
            for tc in range(ntc):
                for s in range(cdiv(nds, P)):
                    m = min(P, nds - s * P)
                    psb = k.ps[rd % 8]
                    if rd >= 8:
                        eng.wait_ge(k.sems[esem], ev0 + (rd - 7) * epi_incs)
                    for di in range(ndi):
                        inst = eng.matmul(
                            psb[0:m, :],
                            wbuf[c % 2][:, di * w_chunk + s * P:
                                        di * w_chunk + s * P + m],
                            xs[di][:, tc * 512:(tc + 1) * 512],
                            start=(di == 0), stop=(di == ndi - 1))
                    k.inc(inst, "pe", 1)
                    rd += 1

    def ev_body(eng):
        rd = 0
        for c in range(ndo_ch):
            nds = min(w_chunk, dout - c * w_chunk)
            for tc in range(ntc):
                for s in range(cdiv(nds, P)):
                    m = min(P, nds - s * P)
                    eng.wait_ge(k.sems["pe"], pe0 + rd + 1)
                    epi(eng, k.ps[rd % 8][0:m, :], c * (w_chunk // P) + s,
                        tc, m)
                    rd += 1

    k.op({"gpsimd": dma_body, "tensor": pe_body, epi_eng: ev_body})
    k.unphase()


def op_linearA(k, xs, w_sb, din, dout, tt, epi, epi_eng="vector",
               epi_incs=1):
    ndi = din // P
    ntl = tt // P
    ndc = cdiv(dout, 512)
    esem = ENG_SEM[epi_eng]
    pe0, ev0 = k.cnt["pe"], k.cnt[esem]

    def pe_body(eng):
        rd = 0
        for ti in range(ntl):
            for dc in range(ndc):
                n = min(512, dout - dc * 512)
                psb = k.ps[rd % 8]
                if rd >= 8:
                    eng.wait_ge(k.sems[esem], ev0 + (rd - 7) * epi_incs)
                for di in range(ndi):
                    inst = eng.matmul(
                        psb[:, 0:n],
                        xs[di][:, ti * P:(ti + 1) * P],
                        w_sb[di][:, dc * 512: dc * 512 + n],
                        start=(di == 0), stop=(di == ndi - 1))
                k.inc(inst, "pe", 1)
                rd += 1

    def ev_body(eng):
        rd = 0
        for ti in range(ntl):
            for dc in range(ndc):
                n = min(512, dout - dc * 512)
                eng.wait_ge(k.sems["pe"], pe0 + rd + 1)
                epi(eng, k.ps[rd % 8][:, 0:n], ti, dc, n)
                rd += 1

    k.op({"tensor": pe_body, epi_eng: ev_body})



def op_norm(k, x_tiles, nr, tt, ones_col, ones_row, apply_fn,
            rms=False, eps=1e-5, chunk=2048):
    DIM = nr * P
    ncH = tt // chunk
    n512 = chunk // 512
    k.phase()
    x2 = k.sb_tiles(nr, chunk, F32)
    srow = k.sb([1, chunk], F32)
    s2row = k.sb([1, chunk], F32)
    arow = k.sb([1, chunk], F32)
    brow = k.sb([1, chunk], F32)
    trow = k.sb([1, chunk], F32)

    for ch in range(ncH):
        c0 = ch * chunk

        def ac1(eng, c0=c0):
            for di in range(nr):
                inst = eng.activation(out=x2[di][:],
                                      in_=x_tiles[di][:, c0:c0 + chunk],
                                      func=AF.Square)
            k.inc(inst, "ac", 1)
        k.op({"scalar": ac1})

        def pe1(eng, c0=c0):
            for j in range(n512):
                if not rms:
                    for di in range(nr):
                        inst = eng.matmul(
                            k.ps[2 * j][0:1, :], ones_col[:, 0:1],
                            x_tiles[di][:, c0 + j * 512: c0 + (j + 1) * 512],
                            start=(di == 0), stop=(di == nr - 1))
                for di in range(nr):
                    inst = eng.matmul(
                        k.ps[2 * j + 1][0:1, :], ones_col[:, 0:1],
                        x2[di][:, j * 512:(j + 1) * 512],
                        start=(di == 0), stop=(di == nr - 1))
            k.inc(inst, "pe", 1)
        k.op({"tensor": pe1})

        def dv1(eng):
            for j in range(n512):
                cs = slice(j * 512, (j + 1) * 512)
                if not rms:
                    eng.tensor_scalar(out=srow[0:1, cs], in0=k.ps[2 * j][0:1, :],
                                      scalar1=1.0 / DIM, scalar2=None,
                                      op0=ALU.mult)
                eng.tensor_scalar(out=s2row[0:1, cs], in0=k.ps[2 * j + 1][0:1, :],
                                  scalar1=1.0 / DIM, scalar2=eps,
                                  op0=ALU.mult, op1=ALU.add)
            if not rms:
                inst = eng.tensor_mul(out=trow[0:1, :], in0=srow[0:1, :],
                                      in1=srow[0:1, :])
                eng.wait_ge(k.sems["dv"], k.inc(inst, "dv", 1))
                inst = eng.tensor_sub(out=trow[0:1, :], in0=s2row[0:1, :],
                                      in1=trow[0:1, :])
            else:
                inst = eng.tensor_copy(out=trow[0:1, :], in_=s2row[0:1, :])
            k.inc(inst, "dv", 1)
        k.op({"vector": dv1})

        def ac2(eng):
            inst = eng.activation(out=arow[0:1, :], in_=trow[0:1, :],
                                  func=AF.Ln)
            eng.wait_ge(k.sems["ac"], k.inc(inst, "ac", 1))
            inst = eng.activation(out=arow[0:1, :], in_=arow[0:1, :],
                                  func=AF.Exp, scale=-0.5)
            k.inc(inst, "ac", 1)
        k.op({"scalar": ac2})

        if not rms:
            def dv2(eng):
                inst = eng.scalar_tensor_tensor(
                    out=brow[0:1, :], in0=srow[0:1, :], scalar=-1.0,
                    in1=arow[0:1, :], op0=ALU.mult, op1=ALU.mult)
                k.inc(inst, "dv", 1)
            k.op({"vector": dv2})

        def pe2(eng):
            for j in range(n512):
                inst = eng.matmul(k.ps[j][:, :], ones_row[0:1, :],
                                  arow[0:1, j * 512:(j + 1) * 512],
                                  start=True, stop=True)
                if not rms:
                    inst = eng.matmul(k.ps[4 + j][:, :], ones_row[0:1, :],
                                      brow[0:1, j * 512:(j + 1) * 512],
                                      start=True, stop=True)
            k.inc(inst, "pe", 1)
        k.op({"tensor": pe2})

        a_ps = [k.ps[j] for j in range(n512)]
        b_ps = None if rms else [k.ps[4 + j] for j in range(n512)]

        def dv3(eng, c0=c0, a_ps=a_ps, b_ps=b_ps):
            for di in range(nr):
                apply_fn(eng, di, a_ps, b_ps, c0, chunk)
        k.op({"vector": dv3})
    k.unphase()



class Seq:

    def __init__(self, k):
        self.k = k
        self.records = []
        self.snap0 = dict(k.cnt)
        self.last_ms = {}

    def do(self, engine, emit_fn, waits=(), inc=None, incn=1):
        k = self.k
        waits = list(waits)
        if engine != "tensor" and engine in self.last_ms:
            waits.append(self.last_ms[engine])
        val = None
        if inc is not None:
            k.cnt[inc] += incn
            val = k.cnt[inc]
            self.last_ms[engine] = (inc, val)
        self.records.append((engine, waits, emit_fn, inc, incn))
        return val

    def cur(self, sem):
        return self.k.cnt[sem]

    def run(self):
        k = self.k
        engines = {r[0] for r in self.records}
        fn_map = {}
        for en in engines:
            recs = [r for r in self.records if r[0] == en]

            def body(eng, recs=recs):
                last_wait = {}
                for _, waits, emit_fn, inc, incn in recs:
                    for sem, val in waits:
                        if val > 0 and last_wait.get(sem, -1) < val:
                            eng.wait_ge(k.sems[sem], val)
                            last_wait[sem] = val
                    inst = emit_fn(eng)
                    if inc is not None:
                        inst.then_inc(k.sems[inc], incn)
            fn_map[en] = body
        k.op(fn_map, snap=self.snap0)


def _rows(tiles_row0, dh):
    out = []
    r0 = tiles_row0
    left = dh
    while left:
        ti, ro = divmod(r0, P)
        n = min(left, P - ro)
        out.append((ti, ro, n))
        r0 += n
        left -= n
    return out


def op_attention(k, q_tiles, k_tiles, v_tiles, o_tiles, nh, dh, nkv, Tq,
                 mode, scale, cst, qcol0=0):
    QB = 256 if mode == "win256" else 512
    nqb = Tq // QB
    nqt = QB // P
    ndv = cdiv(dh, P)
    k.phase()
    S_sb = k.sb([P, 2048], F32)
    Pdump = k.sb([P, 2048], BF16)
    mcol = k.sb([P, 16], F32)
    den = k.sb([P, 1], F32)
    brow = k.sb([1, 512], F32)
    b2bc = k.sb([P, 512], F32)
    stf = [k.sb([P, 512], F32) for _ in range(2)]
    Pt = [k.sb([P, 512], BF16) for _ in range(4)]

    seq = Seq(k)
    stats_rd = 0
    stats_evac_ms = []
    stt_ms = []
    exp_ms = []
    pv_ms = []
    st_rd = 0
    last_X1 = 0
    last_B1 = 0
    last_B2 = 0
    last_X3 = 0
    last_X4 = 0
    dv_X2 = 0
    last_Oev = 0

    for h in range(nh):
        kvh = h * nkv // nh
        qrows = _rows(h * dh, dh)
        krows = _rows(kvh * dh, dh)
        for qb in range(nqb):
            q0 = qb * QB

            if mode == "full":
                st_subs = [(k0, None) for k0 in range(0, Tq, P)]
                def stats_chunks(qt):
                    return [[(c * 512 + j * P, None) for j in range(4)]
                            for c in range(Tq // 512)]
            elif mode == "causal":
                st_subs = []
                for k0 in range(0, (qb + 1) * 512, P):
                    if q0 >= k0 + P - 1:
                        st_subs.append((k0, None))
                    else:
                        st_subs.append((k0, [("ge", q0 - k0)]))
                def stats_chunks(qt, q0=q0):
                    qa = q0 + qt * P
                    ch = []
                    nfull = qa // 512
                    for c in range(nfull):
                        ch.append([(c * 512 + j * P, None) for j in range(4)])
                    off = qa - nfull * 512
                    ch.append([(nfull * 512 + j * P, ("c", off - j * P))
                               for j in range(4)])
                    return ch
            else:
                lo = max(0, q0 - 256)
                st_subs = []
                for k0 in range(lo, q0 + QB, P):
                    conds = []
                    if q0 - k0 - (P - 1) < 0:
                        conds.append(("ge", q0 - k0))
                    if 256 - ((q0 + QB - 1) - k0) < 0:
                        conds.append(("le256", 256 - (q0 - k0)))
                    st_subs.append((k0, conds or None))
                def stats_chunks(qt, q0=q0):
                    qa = q0 + qt * P
                    segs = []
                    for kk in range(max(0, qa - 256), qa + P, P):
                        off = qa - kk
                        m = ("w", off)
                        segs.append((kk, m))
                    return [segs]

            X1_this = []
            for qt in range(nqt):
                chunks = stats_chunks(qt)
                vis = 0
                evac_of_chunk = []
                for ci, segs in enumerate(chunks):
                    rd = stats_rd
                    bank = k.ps[rd % 2]
                    waits = [("dv", stats_evac_ms[rd - 2])] if rd >= 2 else []

                    def A(eng, segs=segs, bank=bank, qt=qt, qrows=qrows,
                          krows=krows, q0=q0):
                        first = True
                        col = 0
                        for (k0, _m) in segs:
                            for di in range(len(qrows)):
                                ti, ro, n = qrows[di]
                                kti, kro, kn = krows[di]
                                inst = eng.matmul(
                                    bank[:, col:col + P],
                                    q_tiles[ti][ro:ro + n,
                                                qcol0 + q0 + qt * P:
                                                qcol0 + q0 + (qt + 1) * P],
                                    k_tiles[kti][kro:kro + kn,
                                                 qcol0 + k0:qcol0 + k0 + P],
                                    start=(di == 0),
                                    stop=(di == len(qrows) - 1))
                            first = False
                            col += P
                        return inst
                    pe_ms = seq.do("tensor", A, waits=waits, inc="pe")

                    def E1a(eng, segs=segs, bank=bank, vis0=vis):
                        col = 0
                        for (k0, msk) in segs:
                            dst = S_sb[:, vis0 + col: vis0 + col + P]
                            if msk is None:
                                inst = eng.tensor_copy(
                                    out=dst, in_=bank[:, col:col + P])
                            elif msk[0] == "c":
                                off = msk[1]
                                if off >= P:
                                    inst = eng.tensor_copy(
                                        out=dst, in_=bank[:, col:col + P])
                                elif off < 0:
                                    inst = eng.memset(dst, NEG)
                                else:
                                    inst = eng.tensor_tensor(
                                        out=dst, in0=bank[:, col:col + P],
                                        in1=cst["wC"][:, 0:P], op=ALU.add)
                            else:
                                off = msk[1]
                                if off == 256:
                                    inst = eng.tensor_tensor(
                                        out=dst, in0=bank[:, col:col + P],
                                        in1=cst["wA"][:, 0:P], op=ALU.add)
                                elif off == 0:
                                    inst = eng.tensor_tensor(
                                        out=dst, in0=bank[:, col:col + P],
                                        in1=cst["wC"][:, 0:P], op=ALU.add)
                                else:
                                    inst = eng.tensor_copy(
                                        out=dst, in_=bank[:, col:col + P])
                            col += P
                        return inst
                    wts = [("pe", pe_ms)]
                    if ci == 0 and last_X1 > 0:
                        wts.append(("ac", last_X1))
                    ms = seq.do("vector", E1a, waits=wts, inc="dv")
                    stats_evac_ms.append(ms)
                    evac_of_chunk.append(ms)
                    stats_rd += 1
                    vis += len(segs) * P

                def E1b0(eng, qt=qt, vis=vis):
                    return eng.reduce_max(out=mcol[:, 4 * qt:4 * qt + 1],
                                          in_=S_sb[:, 0:vis], axis=AX.X)
                seq.do("vector", E1b0, inc="dv")

                def E1b1(eng, qt=qt):
                    return eng.tensor_scalar_mul(
                        out=mcol[:, 4 * qt + 1:4 * qt + 2],
                        in0=mcol[:, 4 * qt:4 * qt + 1], scalar1=-scale)
                dv_E1b = seq.do("vector", E1b1, inc="dv")

                def X1a(eng, qt=qt, vis=vis):
                    return eng.activation(out=Pdump[:, 0:vis],
                                          in_=S_sb[:, 0:vis],
                                          func=AF.Exp, scale=scale,
                                          bias=mcol[:, 4 * qt + 1:4 * qt + 2],
                                          accum_out=den[:, 0:1])
                seq.do("scalar", X1a, waits=[("dv", dv_E1b)], inc="ac")

                def X1b(eng, qt=qt):
                    return eng.activation(out=mcol[:, 4 * qt + 2:4 * qt + 3],
                                          in_=den[:, 0:1], func=AF.Ln)
                last_X1 = seq.do("scalar", X1b, inc="ac")
                X1_this.append(last_X1)

                def X2(eng, qt=qt):
                    return eng.tensor_sub(
                        out=mcol[:, 4 * qt + 3:4 * qt + 4],
                        in0=mcol[:, 4 * qt + 1:4 * qt + 2],
                        in1=mcol[:, 4 * qt + 2:4 * qt + 3])
                wts = [("ac", last_X1)]
                if last_B1 > 0:
                    wts.append(("pe", last_B1))
                dv_X2 = seq.do("vector", X2, waits=wts, inc="dv")

            def B1(eng, nqt=nqt):
                for qt in range(nqt):
                    inst = eng.transpose(
                        k.ps[6][0:1, qt * P:(qt + 1) * P],
                        mcol[:, 4 * qt + 3:4 * qt + 4], cst["identity"][:])
                return inst
            wts = [("dv", dv_X2)]
            if last_X4 > 0:
                wts.append(("dv", last_X4))
            last_B1 = seq.do("tensor", B1, waits=wts, inc="pe")

            def X3(eng, QB=QB):
                return eng.tensor_copy(out=brow[0:1, 0:QB],
                                       in_=k.ps[6][0:1, 0:QB])
            wts = [("pe", last_B1)]
            if last_B2 > 0:
                wts.append(("pe", last_B2))
            last_X3 = seq.do("vector", X3, waits=wts, inc="dv")

            def B2(eng, QB=QB):
                return eng.matmul(k.ps[6][:, 0:QB], cst["ones_row"][0:1, :],
                                  brow[0:1, 0:QB], start=True, stop=True)
            last_B2 = seq.do("tensor", B2, waits=[("dv", last_X3)], inc="pe")

            def X4(eng, QB=QB):
                return eng.tensor_copy(out=b2bc[:, 0:QB], in_=k.ps[6][:, 0:QB])
            last_X4 = seq.do("vector", X4, waits=[("pe", last_B2)], inc="dv")

            nsub = len(st_subs)
            first_pv = True
            for si, (k0, conds) in enumerate(st_subs):
                g = st_rd
                bank = k.ps[2 + g % 2]

                def ST(eng, k0=k0, bank=bank, QB=QB, q0=q0, qrows=qrows,
                       krows=krows):
                    for di in range(len(qrows)):
                        ti, ro, n = qrows[di]
                        kti, kro, kn = krows[di]
                        inst = eng.matmul(
                            bank[:, 0:QB],
                            k_tiles[kti][kro:kro + kn, qcol0 + k0:qcol0 + k0 + P],
                            q_tiles[ti][ro:ro + n, qcol0 + q0:qcol0 + q0 + QB],
                            start=(di == 0), stop=(di == len(qrows) - 1))
                    return inst
                wts = []
                if g >= 2:
                    wts.append(("dv", stt_ms[g - 2]))
                pe_st = seq.do("tensor", ST, waits=wts, inc="pe")

                def STT(eng, bank=bank, g=g, QB=QB):
                    return eng.scalar_tensor_tensor(
                        out=stf[g % 2][:, 0:QB], in0=bank[:, 0:QB],
                        scalar=scale, in1=b2bc[:, 0:QB],
                        op0=ALU.mult, op1=ALU.add)
                wts = [("pe", pe_st), ("dv", last_X4)]
                if g >= 2:
                    wts.append(("ac", exp_ms[g - 2]))
                dv_stt = seq.do("vector", STT, waits=wts, inc="dv")
                stt_ms.append(dv_stt)

                def EXP(eng, g=g, QB=QB):
                    return eng.activation(out=Pt[g % 4][:, 0:QB],
                                          in_=stf[g % 2][:, 0:QB], func=AF.Exp)
                wts = [("dv", dv_stt)]
                if g >= 4:
                    wts.append(("pe", pv_ms[g - 4]))
                ac_exp = seq.do("scalar", EXP, waits=wts, inc="ac")
                exp_ms.append(ac_exp)

                pv_wait = [("ac", ac_exp)]
                if conds:
                    def MSK(eng, conds=conds, g=g, QB=QB, q0=q0, k0=k0):
                        for c in conds:
                            if c[0] == "ge":
                                inst = eng.affine_select(
                                    out=Pt[g % 4][:, 0:QB],
                                    in_=Pt[g % 4][:, 0:QB],
                                    pattern=[[1, QB]], compare_op=ALU.is_ge,
                                    fill=0.0, base=c[1], channel_multiplier=-1)
                            else:
                                inst = eng.affine_select(
                                    out=Pt[g % 4][:, 0:QB],
                                    in_=Pt[g % 4][:, 0:QB],
                                    pattern=[[-1, QB]], compare_op=ALU.is_ge,
                                    fill=0.0, base=c[1], channel_multiplier=1)
                        return inst
                    pl_ms = seq.do("gpsimd", MSK, waits=[("ac", ac_exp)],
                                   inc="pl")
                    pv_wait = [("pl", pl_ms)]

                def PV(eng, k0=k0, g=g, QB=QB, kvh=kvh, first=first_pv,
                       last=(si == nsub - 1)):
                    kt = (qcol0 + k0) // P
                    for dvt in range(ndv):
                        n = min(P, dh - dvt * P)
                        inst = eng.matmul(
                            k.ps[4 + dvt][0:n, 0:QB],
                            v_tiles[kt][:, kvh * dh + dvt * P:
                                        kvh * dh + dvt * P + n],
                            Pt[g % 4][:, 0:QB],
                            start=first, stop=last)
                    return inst
                wts = list(pv_wait)
                if first_pv and last_Oev > 0:
                    wts.append(("dv", last_Oev))
                pe_pv = seq.do("tensor", PV, waits=wts, inc="pe")
                pv_ms.append(pe_pv)
                first_pv = False
                st_rd += 1

            def OEV(eng, h=h, q0=q0, QB=QB):
                for dvt in range(ndv):
                    n = min(P, dh - dvt * P)
                    for (ti, ro, nn) in _rows(h * dh + dvt * P, n):
                        sub = ro % P
                        inst = eng.tensor_copy(
                            out=o_tiles[ti][ro:ro + nn,
                                            qcol0 + q0:qcol0 + q0 + QB],
                            in_=k.ps[4 + dvt][0:nn, 0:QB])
                return inst
            last_Oev = seq.do("vector", OEV, waits=[("pe", pv_ms[-1])],
                              inc="dv")

    seq.run()
    k.unphase()


def op_qk_prep(k, tiles, nh, dh, tt, cst, l2eps=None, scale_col=None,
               rope_tabs=None, Tb=T):
    nt = len(tiles)
    ntc = tt // 512
    k.phase()
    q2 = [k.sb([P, 512], F32) for _ in range(2)]
    rrow = k.sb([min(nh, P), 512], F32)
    tmp = k.sb([P, 512], F32)
    t2 = k.sb([P, 512], F32)
    seq = Seq(k)
    mm_ms = []
    bc_ms = {}
    rot_ms = {}
    stt_ms = {}
    fin_ms = {}

    for tc in range(ntc):
        cs = slice(tc * 512, (tc + 1) * 512)
        tb0 = (tc * 512) % Tb
        tbc = slice(tb0, tb0 + 512)

        if l2eps is not None:
            for j in range(nt):
                g = tc * nt + j

                def SQ(eng, j=j, cs=cs, g=g):
                    return eng.activation(out=q2[g % 2][:],
                                          in_=tiles[j][:, cs], func=AF.Square)
                wts = [("pe", mm_ms[g - 2])] if g >= 2 else []
                ac_sq = seq.do("scalar", SQ, waits=wts, inc="ac")

                def MM(eng, j=j, g=g, first=(j == 0), last=(j == nt - 1)):
                    return eng.matmul(k.ps[0][0:nh, :],
                                      cst["sel_ssq"][:, j * nh:(j + 1) * nh],
                                      q2[g % 2][:], start=first, stop=last)
                mm_ms.append(seq.do("tensor", MM, waits=[("ac", ac_sq)],
                                    inc="pe"))

            def RN1(eng, cs=cs):
                return eng.tensor_copy(out=rrow[:], in_=k.ps[0][0:nh, :])
            dv_rn1 = seq.do("vector", RN1, waits=[("pe", mm_ms[-1])], inc="dv")

            def RN2(eng):
                return eng.activation(out=rrow[:], in_=rrow[:], func=AF.Sqrt)
            ac_rn2 = seq.do("scalar", RN2, waits=[("dv", dv_rn1)], inc="ac")

            def RN3(eng, l2eps=l2eps):
                return eng.tensor_scalar_max(out=rrow[:], in0=rrow[:],
                                             scalar1=l2eps)
            dv_rn3 = seq.do("vector", RN3, waits=[("ac", ac_rn2)], inc="dv")

            def RN4(eng):
                return eng.reciprocal(out=rrow[:], in_=rrow[:])
            dv_rn4 = seq.do("vector", RN4, inc="dv")

        for j in range(nt):
            g = tc * nt + j
            bank_bc = k.ps[1 + g % 2]
            if l2eps is not None:
                def BC(eng, j=j, bank=bank_bc):
                    return eng.matmul(bank[:, :],
                                      cst["sel_bc"][0:nh, j * P:(j + 1) * P],
                                      rrow[:], start=True, stop=True)
                wts = [("dv", dv_rn4)]
                prev = stt_ms.get((tc, j - 2)) or (stt_ms.get((tc - 1, nt - 2 + j))
                                                   if tc else None)
                if (tc * nt + j) >= 2:
                    pg = g - 2
                    pv = stt_ms.get((pg // nt, pg % nt))
                    if pv:
                        wts.append(("dv", pv))
                bc_ms[(tc, j)] = seq.do("tensor", BC, waits=wts, inc="pe")

                def APPLY(eng, j=j, cs=cs, bank=bank_bc):
                    if scale_col is not None:
                        return eng.scalar_tensor_tensor(
                            out=tiles[j][:, cs], in0=tiles[j][:, cs],
                            scalar=scale_col[j][:, 0:1], in1=bank[:, :],
                            op0=ALU.mult, op1=ALU.mult)
                    return eng.tensor_tensor(
                        out=tiles[j][:, cs], in0=tiles[j][:, cs],
                        in1=bank[:, :], op=ALU.mult)
                wts = [("pe", bc_ms[(tc, j)])]
                rp = rot_ms.get((tc - 1, j)) if tc else None
                stt_ms[(tc, j)] = seq.do("vector", APPLY, waits=wts, inc="dv")

            if rope_tabs is not None:
                A_t, B_t = rope_tabs
                bank_rot = k.ps[3 + g % 2]

                def ROT(eng, j=j, cs=cs, bank=bank_rot):
                    return eng.matmul(bank[:, :], cst["pswap"][:],
                                      tiles[j][:, cs], start=True, stop=True)
                wts = []
                if l2eps is not None:
                    wts.append(("dv", stt_ms[(tc, j)]))
                pg = g - 2
                if pg >= 0:
                    pv = fin_ms.get((pg // nt, pg % nt))
                    if pv:
                        wts.append(("dv", pv))
                rot_ms[(tc, j)] = seq.do("tensor", ROT, waits=wts, inc="pe")

                def RP1(eng, j=j, cs=cs, tbc=tbc):
                    return eng.tensor_tensor(out=tmp[:], in0=tiles[j][:, cs],
                                             in1=A_t[:, tbc], op=ALU.mult)
                wts = [("pe", rot_ms[(tc, j)])]
                dv_rp1 = seq.do("vector", RP1, waits=wts, inc="dv")

                def RP2(eng, j=j, tbc=tbc, bank=bank_rot):
                    return eng.tensor_tensor(out=t2[:], in0=bank[:, :],
                                             in1=B_t[:, tbc], op=ALU.mult)
                dv_rp2 = seq.do("vector", RP2, inc="dv")

                def RP3(eng, j=j, cs=cs):
                    return eng.tensor_add(out=tiles[j][:, cs], in0=tmp[:],
                                          in1=t2[:])
                fin_ms[(tc, j)] = seq.do("vector", RP3, inc="dv")

    seq.run()
    k.unphase()



def op_mamba_conv(k, xr_tiles, xs_t, cst, Tb=T):
    nd = 16
    ntc = Tb // 512
    k.phase()
    acc = k.sb([P, 512], F32)

    def conv_body(eng):
        last_copy = None
        for j in range(nd):
            for tc in range(ntc):
                c0 = 3 + tc * 512
                if last_copy is not None:
                    eng.wait_ge(k.sems["dv"], last_copy)
                inst = eng.tensor_scalar(
                    out=acc[:], in0=xr_tiles[j][:, c0 - 3:c0 - 3 + 512],
                    scalar1=cst["conv_w"][:, 4 * j:4 * j + 1],
                    scalar2=cst["conv_b"][:, j:j + 1],
                    op0=ALU.mult, op1=ALU.add)
                for i in (1, 2, 3):
                    eng.wait_ge(k.sems["dv"], k.inc(inst, "dv", 1))
                    inst = eng.scalar_tensor_tensor(
                        out=acc[:], in0=xr_tiles[j][:, c0 - 3 + i:c0 - 3 + i + 512],
                        scalar=cst["conv_w"][:, 4 * j + i:4 * j + i + 1],
                        in1=acc[:], op0=ALU.mult, op1=ALU.add)
                eng.wait_ge(k.sems["dv"], k.inc(inst, "dv", 1))
                inst = eng.tensor_copy(out=xs_t[j][:, tc * 512:(tc + 1) * 512],
                                       in_=acc[:])
                last_copy = k.inc(inst, "dv", 1)
    k.op({"vector": conv_body})

    def silu_body(eng):
        for j in range(nd):
            inst = eng.activation(out=xs_t[j][:], in_=xs_t[j][:], func=AF.Silu)
            k.inc(inst, "ac", 1)
    k.op({"scalar": silu_body})
    k.unphase()


def op_mamba_scan(k, xs_t, z_hbm, w_xproj_hbm, cst, Tb=T, zcol0=0):
    nd = 16
    ntc = Tb // 512
    y_tiles = xs_t
    k.phase()
    ssm = k.sb([80, Tb], F32)
    delta = k.sb([P, Tb], F32)
    u = k.sb([P, Tb], F32)
    dA = k.sb([P, Tb], F32)
    Pin = k.sb([P, Tb], F32)
    hbuf = k.sb([P, Tb], F32)
    yacc = k.sb([P, Tb], F32)
    tmp = k.sb([P, 512], F32)
    zbuf = [k.sb([P, Tb], BF16) for _ in range(2)]

    def epi(eng, ps_ap, ds, tc, m):
        inst = eng.tensor_copy(out=ssm[0:m, tc * 512:(tc + 1) * 512],
                               in_=ps_ap)
        k.inc(inst, "dv", 1)
    op_linearB(k, xs_t, w_xproj_hbm, nd * P, 80, Tb, epi, w_chunk=80)

    seq = Seq(k)
    dep_ms = {}
    zld_ms = {}
    for j in range(nd):
        def ZLD(eng, j=j):
            return eng.dma_start(out=zbuf[j % 2][:],
                                 in_=z_hbm[j * P:(j + 1) * P,
                                           zcol0:zcol0 + Tb])
        wz = []
        if (j - 2, "zgate") in dep_ms:
            wz.append(dep_ms[(j - 2, "zgate")])
        zld_ms[j] = seq.do("sync", ZLD, waits=wz, inc="qs", incn=16)
        for tc in range(ntc):
            def OUT(eng, j=j, tc=tc):
                return eng.matmul(
                    k.ps[tc][:, :], cst["dtw_row"][0:1, j * P:(j + 1) * P],
                    ssm[0:1, tc * 512:(tc + 1) * 512],
                    start=True, stop=True)
            wts = []
            if (j - 1, tc, "dpre_free") in dep_ms:
                wts.append(dep_ms[(j - 1, tc, "dpre_free")])
            pe_out = seq.do("tensor", OUT, waits=wts, inc="pe")

            def SP1(eng, tc=tc):
                return eng.activation(out=delta[:, tc * 512:(tc + 1) * 512],
                                      in_=k.ps[tc][:, :], func=AF.Abs,
                                      bias=cst["dtb"][:, j:j + 1] if False
                                      else 0.0)
            def SP1b(eng, tc=tc, j=j):
                return eng.activation(out=delta[:, tc * 512:(tc + 1) * 512],
                                      in_=k.ps[tc][:, :], func=AF.Abs,
                                      bias=cst["dtb"][:, j:j + 1])
            w1 = [("pe", pe_out)]
            if (j - 1, "delta_last_reader") in dep_ms and tc == 0:
                w1.append(dep_ms[(j - 1, "delta_last_reader")])
            ac1 = seq.do("scalar", SP1b, waits=w1, inc="ac")

            def SP2(eng, tc=tc):
                cs = slice(tc * 512, (tc + 1) * 512)
                return eng.activation(out=delta[:, cs], in_=delta[:, cs],
                                      func=AF.Exp, scale=-1.0)
            ac2 = seq.do("scalar", SP2, inc="ac")

            def SP3(eng, tc=tc):
                cs = slice(tc * 512, (tc + 1) * 512)
                return eng.activation(out=delta[:, cs], in_=delta[:, cs],
                                      func=AF.Ln, bias=1.0)
            ac3 = seq.do("scalar", SP3, inc="ac")

            def RL1(eng, tc=tc, j=j):
                return eng.tensor_scalar(
                    out=tmp[:], in0=k.ps[tc][:, :],
                    scalar1=cst["dtb"][:, j:j + 1], scalar2=0.0,
                    op0=ALU.add, op1=ALU.max)
            dv1 = seq.do("vector", RL1, waits=[("ac", ac1)], inc="dv")

            def RL2(eng, tc=tc):
                cs = slice(tc * 512, (tc + 1) * 512)
                return eng.tensor_add(out=delta[:, cs], in0=delta[:, cs],
                                      in1=tmp[:])
            dv2 = seq.do("vector", RL2, waits=[("ac", ac3)], inc="dv")
            dep_ms[(j, tc, "dpre_free")] = ("dv", dv1)

        def U(eng, j=j):
            return eng.tensor_mul(out=u[:], in0=delta[:], in1=xs_t[j][:])
        dv_u = seq.do("vector", U, inc="dv")

        def YZ(eng):
            return eng.memset(yacc[:], 0.0)
        dv_yz = seq.do("vector", YZ, inc="dv")

        prev_ac_da = None
        for n in range(16):
            def DA1(eng, j=j, n=n):
                return eng.tensor_scalar_mul(
                    out=dA[:], in0=delta[:],
                    scalar1=cst["A_cols"][:, j * 16 + n:j * 16 + n + 1])
            wda = [("ac", prev_ac_da)] if prev_ac_da else []
            dv_da1 = seq.do("vector", DA1, waits=wda, inc="dv")
            if n == 15:
                dep_ms[(j, "delta_last_reader")] = ("dv", dv_da1)

            def DA2(eng):
                return eng.activation(out=dA[:], in_=dA[:], func=AF.Exp)
            ac_da = seq.do("scalar", DA2, waits=[("dv", dv_da1)], inc="ac")
            prev_ac_da = ac_da

            bc_ms = []
            for tc in range(ntc):
                g2 = (j * 16 + n) * ntc + tc

                def BB(eng, n=n, tc=tc, g2=g2):
                    return eng.matmul(
                        k.ps[4 + g2 % 2][:, :],
                        cst["selB"][0:80, n * P:(n + 1) * P],
                        ssm[0:80, tc * 512:(tc + 1) * 512],
                        start=True, stop=True)
                wts = []
                if ("pin", g2 - 2) in dep_ms:
                    wts.append(dep_ms[("pin", g2 - 2)])
                pe_b = seq.do("tensor", BB, waits=wts, inc="pe")

                def CC(eng, n=n, tc=tc, g2=g2):
                    return eng.matmul(
                        k.ps[6 + g2 % 2][:, :],
                        cst["selC"][0:80, n * P:(n + 1) * P],
                        ssm[0:80, tc * 512:(tc + 1) * 512],
                        start=True, stop=True)
                wts = []
                if ("ym", g2 - 2) in dep_ms:
                    wts.append(dep_ms[("ym", g2 - 2)])
                pe_c = seq.do("tensor", CC, waits=wts, inc="pe")
                bc_ms.append((pe_b, pe_c, g2))

                def PIN(eng, tc=tc, g2=g2):
                    cs = slice(tc * 512, (tc + 1) * 512)
                    return eng.tensor_tensor(out=Pin[:, cs], in0=u[:, cs],
                                             in1=k.ps[4 + g2 % 2][:, :],
                                             op=ALU.mult)
                dv_pin = seq.do("vector", PIN, waits=[("pe", pe_b)], inc="dv")
                dep_ms[("pin", g2)] = ("dv", dv_pin)

            def SCAN(eng):
                return eng.tensor_tensor_scan(
                    out=hbuf[:], data0=dA[:], data1=Pin[:], initial=0.0,
                    op0=ALU.mult, op1=ALU.add)
            dv_scan = seq.do("vector", SCAN, waits=[("ac", ac_da)], inc="dv")

            for tc in range(ntc):
                g2 = bc_ms[tc][2]

                def YM(eng, tc=tc, g2=g2):
                    cs = slice(tc * 512, (tc + 1) * 512)
                    return eng.tensor_tensor(out=hbuf[:, cs], in0=hbuf[:, cs],
                                             in1=k.ps[6 + g2 % 2][:, :],
                                             op=ALU.mult)
                dv_ym = seq.do("vector", YM, waits=[("pe", bc_ms[tc][1])],
                               inc="dv")
                dep_ms[("ym", g2)] = ("dv", dv_ym)

                def YA(eng, tc=tc):
                    cs = slice(tc * 512, (tc + 1) * 512)
                    return eng.tensor_add(out=yacc[:, cs], in0=yacc[:, cs],
                                          in1=hbuf[:, cs])
                seq.do("vector", YA, inc="dv")

        def FD(eng, j=j):
            return eng.scalar_tensor_tensor(
                out=yacc[:], in0=xs_t[j][:], scalar=cst["D_col"][:, j:j + 1],
                in1=yacc[:], op0=ALU.mult, op1=ALU.add)
        dv_fd = seq.do("vector", FD, inc="dv")

        def SZ(eng, j=j):
            return eng.activation(out=y_tiles[j][:], in_=zbuf[j % 2][:],
                                  func=AF.Silu)
        ac_sz = seq.do("scalar", SZ, waits=[("qs", zld_ms[j]),
                                            ("dv", dv_fd)], inc="ac")

        def YG(eng, j=j):
            return eng.tensor_mul(out=y_tiles[j][:], in0=yacc[:],
                                  in1=y_tiles[j][:])
        dv_yg = seq.do("vector", YG, waits=[("ac", ac_sz), ("dv", dv_fd)],
                       inc="dv")
        dep_ms[(j, "zgate")] = ("ac", ac_sz)

    seq.run()
    k.unphase()


"""Orchestration: host prep + per-sublayer program builders + kernel().

Multi-launch version: one launch per top-level sublayer application.
R state [1024, 4096] fp32 per core flows between launches host-side.
"""
import numpy as np
import ml_dtypes
from contextlib import ExitStack
import concourse.bass as bass
import concourse.mybir as mybir
from concourse.bass_utils import run_bass_kernel_spmd


BFd = ml_dtypes.bfloat16
NCORE = 8



def host_consts():
    c = {}
    c["identity"] = np.eye(P, dtype=np.float32)
    qi, kj = np.arange(P)[:, None], np.arange(P)[None, :]
    c["wC"] = np.where(qi >= kj, 0.0, NEG).astype(np.float32)
    c["wA"] = np.where(kj >= qi, 0.0, NEG).astype(np.float32)
    ssq = np.zeros((P, 8 * 16), np.float32)
    sbc = np.zeros((16, 8 * P), np.float32)
    for j in range(8):
        for hf in range(2):
            h = 2 * j + hf
            ssq[hf * 64:(hf + 1) * 64, j * 16 + h] = 1.0
            sbc[h, j * P + hf * 64: j * P + (hf + 1) * 64] = 1.0
    c["sel_ssq"] = ssq
    c["sel_bc"] = sbc
    ssq4 = np.zeros((P, 8 * 4), np.float32)
    sbc4 = np.zeros((4, 8 * P), np.float32)
    for j in range(8):
        h = j // 2
        ssq4[:, j * 4 + h] = 1.0
        sbc4[h, j * P:(j + 1) * P] = 1.0
    c["sel_ssq4"] = ssq4
    c["sel_bc4"] = sbc4
    psw = np.zeros((P, P), np.float32)
    for blk in range(2):
        for r in range(64):
            psw[blk * 64 + (r + 32) % 64, blk * 64 + r] = 1.0
    c["pswap"] = psw
    DIM_HEAD, WINDOW, n = 64, 256, T
    half = np.arange(0, DIM_HEAD, 2, dtype=np.float32)
    inv_freq = 1.0 / 10000.0 ** (half / DIM_HEAD)
    t = np.arange(n, dtype=np.float32)
    freqs = np.concatenate([t[:, None] * inv_freq[None, :]] * 2, -1)
    base_scale = (half + 0.4 * DIM_HEAD) / (1.4 * DIM_HEAD)
    power = (t - n // 2) / (WINDOW // 2)
    scale = np.concatenate([base_scale[None, :] ** power[:, None]] * 2, -1)
    sgn = np.where(np.arange(DIM_HEAD) < 32, -1.0, 1.0)[None, :]

    def tilep(tab):
        return np.concatenate([tab.T, tab.T], 0).astype(np.float32)
    c["xpos_qA"] = tilep(np.cos(freqs) * scale)
    c["xpos_qB"] = tilep(np.sin(freqs) * scale * sgn)
    c["xpos_kA"] = tilep(np.cos(freqs) / scale)
    c["xpos_kB"] = tilep(np.sin(freqs) / scale * sgn)
    inv2 = 1.0 / 100000.0 ** (np.arange(0, 64, 2, dtype=np.float32) / 64)
    f2 = t[:, None] * inv2[None, :]
    A2 = np.concatenate([np.cos(f2), np.cos(f2)], 1)
    B2 = np.concatenate([-np.sin(f2), np.sin(f2)], 1)
    c["rope1_A"] = tilep(A2 * 1.0)
    c["rope1_B"] = tilep(B2 * 1.0)
    selB = np.zeros((80, 16 * P), np.float32)
    selC = np.zeros((80, 16 * P), np.float32)
    for nn in range(16):
        selB[32 + nn, nn * P:(nn + 1) * P] = 1.0
        selC[64 + nn, nn * P:(nn + 1) * P] = 1.0
    c["selB"] = selB
    c["selC"] = selC
    es = np.zeros((32, 9 * P), np.float32)
    for e in range(9):
        es[e, e * P:(e + 1) * P] = 1.0
    c["exp_sel"] = es
    return c


def col_layout(v):
    n = v.shape[0] // P
    return v.reshape(n, P).T.copy().astype(np.float32)


def pad_cols(w, to):
    out = np.zeros((w.shape[0], to), w.dtype)
    out[:, :w.shape[1]] = w
    return out


def pad_rows(w, to):
    out = np.zeros((to, w.shape[1]), w.dtype)
    out[:w.shape[0]] = w
    return out


def host_weights(inp):
    w = {}
    for i in range(6):
        w[f"f{i}_wqkv"] = np.ascontiguousarray(inp["L0_ffn_Wqkv"][i])
        w[f"f{i}_wo"] = np.ascontiguousarray(inp["L0_ffn_Wo"][i])
        W1 = inp["L0_ffn_W1"][i]
        w[f"f{i}_w1a"] = pad_cols(np.ascontiguousarray(W1[:, :2730]), 2816)
        w[f"f{i}_w1g"] = pad_cols(np.ascontiguousarray(W1[:, 2730:]), 2816)
        b1 = inp["L0_ffn_b1"][i]
        w[f"f{i}_b1a"] = col_layout(pad_rows(b1[:2730].reshape(-1, 1), 2816)[:, 0])
        w[f"f{i}_b1g"] = col_layout(pad_rows(b1[2730:].reshape(-1, 1), 2816)[:, 0])
        w[f"f{i}_w2"] = pad_rows(np.ascontiguousarray(inp["L0_ffn_W2"][i]), 2816)
        w[f"f{i}_b2"] = col_layout(inp["L0_ffn_b2"][i])
        w[f"f{i}_lng"] = col_layout(inp["L0_ffn_ln_g"][i])
        w[f"f{i}_lnb"] = col_layout(inp["L0_ffn_ln_b"][i])
        w[f"f{i}_qs"] = np.concatenate([inp["L0_ffn_q_scale"][i]] * 2).reshape(P, 1).astype(np.float32)
        w[f"f{i}_ks"] = np.concatenate([inp["L0_ffn_k_scale"][i]] * 2).reshape(P, 1).astype(np.float32)
    w["m_norm_m"] = col_layout(inp["L0_attn_norm_m_w"])
    w["m_inproj"] = np.ascontiguousarray(inp["L0_attn_in_proj_w"].T)
    cw = inp["L0_attn_conv_w"][:, 0, :]
    w["m_cw"] = cw.reshape(16, P, 4).transpose(1, 0, 2).reshape(P, 64).copy()
    w["m_cb"] = col_layout(inp["L0_attn_conv_b"])
    xpw = inp["L0_attn_x_proj_w"]
    WXP = np.zeros((2048, 80), np.float32)
    WXP[:, 0] = xpw[32]
    WXP[:, 32:48] = xpw[0:16].T
    WXP[:, 64:80] = xpw[16:32].T
    w["m_wxp"] = WXP
    w["m_dtw"] = inp["L0_attn_dt_w"].reshape(1, 2048).astype(np.float32)
    w["m_dtb"] = col_layout(inp["L0_attn_dt_b"])
    w["m_alog"] = inp["L0_attn_A_log"].reshape(16, P, 16).transpose(1, 0, 2).reshape(P, 256).copy()
    w["m_dc"] = col_layout(inp["L0_attn_D_skip"])
    w["m_outproj"] = np.ascontiguousarray(inp["L0_attn_out_proj_w"].T)
    w["m_norm_attn"] = col_layout(inp["L0_attn_norm_attn_w"])
    w["m_qw"] = np.ascontiguousarray(inp["L0_attn_q_w"].T)
    w["m_kw"] = np.ascontiguousarray(inp["L0_attn_k_w"].T)
    w["m_vw"] = np.ascontiguousarray(inp["L0_attn_v_w"].T)
    w["m_ow"] = np.ascontiguousarray(inp["L0_attn_o_w"].T)
    w["m_norm_ffn"] = col_layout(inp["L0_attn_norm_ffn_w"])
    w["m_gate"] = pad_cols(np.ascontiguousarray(inp["L0_attn_gate_w"].T), 2816)
    w["m_up"] = pad_cols(np.ascontiguousarray(inp["L0_attn_up_w"].T), 2816)
    w["m_down"] = pad_rows(np.ascontiguousarray(inp["L0_attn_down_w"].T), 2816)
    w["a_wq"] = np.ascontiguousarray(inp["L1_attn_Wq"].T)
    w["a_wk"] = np.ascontiguousarray(inp["L1_attn_Wk"].T)
    w["a_wv"] = np.ascontiguousarray(inp["L1_attn_Wv"].T)
    w["a_wo"] = np.ascontiguousarray(inp["L1_attn_Wo"].T)
    w["a_bo"] = col_layout(inp["L1_attn_bo"])
    w["a_qs"] = inp["L1_attn_q_scale"].reshape(1024)
    w["a_qs"] = w["a_qs"].reshape(8, P, 1).transpose(1, 0, 2).reshape(P, 8).copy()
    w["a_ks"] = inp["L1_attn_k_scale"].reshape(1024)
    w["a_ks"] = w["a_ks"].reshape(8, P, 1).transpose(1, 0, 2).reshape(P, 8).copy()
    w["g_ln1"] = col_layout(inp["L1_ffn_ln1_w"])
    w["g_ln2"] = col_layout(inp["L1_ffn_ln2_w"])
    w["g_wq"] = np.ascontiguousarray(inp["L1_ffn_Wq"])
    w["g_wkv"] = np.ascontiguousarray(inp["L1_ffn_Wkv"])
    w["g_wkup"] = np.ascontiguousarray(
        inp["L1_ffn_Wk_up"].transpose(1, 0, 2).reshape(256, 1024))
    w["g_wvup"] = np.ascontiguousarray(
        inp["L1_ffn_Wv_up"].transpose(1, 0, 2).reshape(256, 1024))
    w["g_wo"] = np.ascontiguousarray(inp["L1_ffn_Wo"])
    w["g_wr"] = np.ascontiguousarray(inp["L1_ffn_Wr"])
    w["g_rb"] = inp["L1_ffn_r_bias"].reshape(1, 8).astype(np.float32)
    w["g_sw1"] = np.ascontiguousarray(inp["L1_ffn_sw1"])
    w["g_sw2"] = np.ascontiguousarray(inp["L1_ffn_sw2"])
    w["g_sw3"] = np.ascontiguousarray(inp["L1_ffn_sw3"])
    for e in range(8):
        w[f"g_ew1_{e}"] = np.ascontiguousarray(inp["L1_ffn_ew1"][e])
        w[f"g_ew2_{e}"] = np.ascontiguousarray(inp["L1_ffn_ew2"][e])
        w[f"g_ew3_{e}"] = np.ascontiguousarray(inp["L1_ffn_ew3"][e])
    return w



CONST_SPECS = {
    "identity": ([P, P], F32), "wC": ([P, P], F32), "wA": ([P, P], F32),
    "sel_ssq": ([P, 8 * 16], F32), "sel_bc": ([16, 8 * P], F32),
    "sel_ssq4": ([P, 8 * 4], F32), "sel_bc4": ([4, 8 * P], F32),
    "pswap": ([P, P], BF16),
    "xpos_qA": ([P, T], BF16), "xpos_qB": ([P, T], BF16),
    "xpos_kA": ([P, T], BF16), "xpos_kB": ([P, T], BF16),
    "rope1_A": ([P, T], BF16), "rope1_B": ([P, T], BF16),
    "selB": ([80, 16 * P], F32), "selC": ([80, 16 * P], F32),
    "exp_sel": ([32, 9 * P], BF16),
}


def setup(nc, ctx, const_names, hostc):
    block = ctx.enter_context(nc.Block())
    k = Ker(nc, block, ctx)
    cst = {}
    pairs_qs, pairs_qg = [], []
    im = {}
    for nm in const_names:
        shape, dt = CONST_SPECS[nm]
        ap = nc.dram_tensor("c_" + nm, list(shape), F32,
                            kind="ExternalInput").ap()
        sb = k.sb(shape, dt, "cst_" + nm)
        cst[nm] = sb
        (pairs_qg if dt == BF16 else pairs_qs).append((sb[:], ap))
        im["c_" + nm] = np.ascontiguousarray(hostc[nm][:shape[0], :shape[1]]
                                             if hostc[nm].ndim == 2
                                             else hostc[nm]).astype(np.float32)
    cst["ones_col"] = k.sb([P, 1], F32, "ones_col")
    cst["ones_row"] = k.sb([1, P], F32, "ones_row")

    def init(eng):
        eng.memset(cst["ones_col"][:], 1.0)
        inst = eng.memset(cst["ones_row"][:], 1.0)
        k.inc(inst, "pl", 1)
    k.op({"gpsimd": init})
    if pairs_qs:
        op_load(k, pairs_qs, queue="qs")
    if pairs_qg:
        op_load(k, pairs_qg, queue="qg")
    return k, cst, im


def win(nc, name, shape, dtype=F32):
    return nc.dram_tensor(name, list(shape), dtype, kind="ExternalInput").ap()


def wcol_sb(k, nc, name, arr_shape):
    ap = nc.dram_tensor(name, list(arr_shape), F32, kind="ExternalInput").ap()
    sb = k.sb(list(arr_shape), F32, name + "_sb")
    return sb, ap


def ld_shifted_bf16(k, dst_tiles, X_ap, b, upper_from=4):
    c0 = b * T

    def body(eng):
        for j in range(upper_from, 8):
            inst = eng.memset(dst_tiles[j][:, 0:1], 0.0)
        k.inc(inst, "pl", 1)
        for j in range(8):
            if j < upper_from:
                inst = eng.dma_start(out=dst_tiles[j][:],
                                     in_=X_ap[j * P:(j + 1) * P, c0:c0 + T])
            else:
                inst = eng.dma_start(
                    out=dst_tiles[j][:, 1:T],
                    in_=X_ap[j * P:(j + 1) * P, c0:c0 + T - 1])
            k.inc(inst, "qg", 16)
    k.op({"gpsimd": body})


def ld_tiles(k, dst_tiles, X_ap, b, n=8, queue="qg", Tb=T):
    c0 = b * Tb

    def body(eng):
        for j in range(n):
            inst = eng.dma_start(out=dst_tiles[j][:],
                                 in_=X_ap[j * P:(j + 1) * P, c0:c0 + Tb])
            k.inc(inst, queue, 16)
    k.op({QUEUE_ENG[queue]: body})


def st_tiles(k, src_tiles, X_ap, b, n=8, queue="qg", Tb=T):
    c0 = b * Tb

    def body(eng):
        for j in range(n):
            inst = eng.dma_start(out=X_ap[j * P:(j + 1) * P, c0:c0 + Tb],
                                 in_=src_tiles[j][:])
            k.inc(inst, queue, 16)
    k.op({QUEUE_ENG[queue]: body})




def norm_to_bf16(k, x_tiles, out_tiles, cst, rms=False, eps=1e-5,
                 gain_col=None, bias_col=None, chunk=512, nr=8):
    k.phase()
    tmp = k.sb([P, 512], F32)
    tmp2 = k.sb([P, 512], F32)

    def apply_fn(eng, di, a_ps, b_ps, c0, w):
        for j in range(w // 512):
            cs = slice(c0 + j * 512, c0 + (j + 1) * 512)
            if rms:
                if gain_col is not None:
                    inst = eng.scalar_tensor_tensor(
                        out=out_tiles[di][:, cs], in0=x_tiles[di][:, cs],
                        scalar=gain_col[:, di:di + 1], in1=a_ps[j][:],
                        op0=ALU.mult, op1=ALU.mult)
                else:
                    inst = eng.tensor_tensor(
                        out=out_tiles[di][:, cs], in0=x_tiles[di][:, cs],
                        in1=a_ps[j][:], op=ALU.mult)
                k.inc(inst, "dv", 1)
            else:
                inst = eng.tensor_tensor(out=tmp[:], in0=x_tiles[di][:, cs],
                                         in1=a_ps[j][:], op=ALU.mult)
                eng.wait_ge(k.sems["dv"], k.inc(inst, "dv", 1))
                if gain_col is None:
                    inst = eng.tensor_tensor(out=out_tiles[di][:, cs],
                                             in0=tmp[:], in1=b_ps[j][:],
                                             op=ALU.add)
                    k.inc(inst, "dv", 1)
                else:
                    inst = eng.tensor_tensor(out=tmp2[:], in0=tmp[:],
                                             in1=b_ps[j][:], op=ALU.add)
                    eng.wait_ge(k.sems["dv"], k.inc(inst, "dv", 1))
                    inst = eng.tensor_scalar(
                        out=out_tiles[di][:, cs], in0=tmp2[:],
                        scalar1=gain_col[:, di:di + 1],
                        scalar2=bias_col[:, di:di + 1],
                        op0=ALU.mult, op1=ALU.add)
                    k.inc(inst, "dv", 1)
    op_norm(k, x_tiles, nr, x_tiles[0].shape[1] if hasattr(x_tiles[0], "shape")
            else T, cst["ones_col"], cst["ones_row"], apply_fn, rms=rms,
            eps=eps, chunk=chunk)
    k.unphase()


def op_hmul(k, a_tiles, b_tiles, n):
    def body(eng):
        for i in range(n):
            inst = eng.tensor_mul(out=a_tiles[i][:], in0=a_tiles[i][:],
                                  in1=b_tiles[i][:])
            k.inc(inst, "dv", 1)
    k.op({"vector": body})


def mk_epi_copy(k, out_tiles, row_off=0, col_off=0, eng_sem="dv"):
    def epi(eng, ps_ap, ds, tc, m):
        ti, ro = divmod(row_off + ds * P, P)
        inst = eng.tensor_copy(
            out=out_tiles[ti][ro:ro + m, col_off + tc * 512:
                              col_off + (tc + 1) * 512], in_=ps_ap)
        k.inc(inst, eng_sem, 1)
    return epi


def mk_epiA_copy(k, out_tiles, tile_off=0, col_off=0):
    def epi(eng, ps_ap, ti, dc, n):
        inst = eng.tensor_copy(
            out=out_tiles[tile_off + ti][:, col_off + dc * 512:
                                         col_off + dc * 512 + n], in_=ps_ap)
        k.inc(inst, "dv", 1)
    return epi


def mk_epi_scale(k, out_tiles, mul, col_off=0):
    def epi(eng, ps_ap, ds, tc, m):
        inst = eng.tensor_scalar_mul(
            out=out_tiles[ds][0:m, col_off + tc * 512:col_off + (tc + 1) * 512],
            in0=ps_ap, scalar1=mul)
        k.inc(inst, "dv", 1)
    return epi


def mk_epi_bias(k, out_tiles, bias_col, col_off=0):
    def epi(eng, ps_ap, ds, tc, m):
        inst = eng.tensor_scalar(
            out=out_tiles[ds][0:m, col_off + tc * 512:col_off + (tc + 1) * 512],
            in0=ps_ap, scalar1=bias_col[:, ds:ds + 1], scalar2=None,
            op0=ALU.add)
        k.inc(inst, "dv", 1)
    return epi


def mk_epi_act(k, out_tiles, func, bias_col=None, col_off=0):
    def epi(eng, ps_ap, ds, tc, m):
        kw = {}
        if bias_col is not None:
            kw["bias"] = bias_col[:, ds:ds + 1]
        inst = eng.activation(
            out=out_tiles[ds][0:m, col_off + tc * 512:col_off + (tc + 1) * 512],
            in_=ps_ap, func=func, **kw)
        k.inc(inst, "ac", 1)
    return epi


def mk_epi_residual(k, r_tiles, alpha=1.0, bias_col=None, col_off=0):
    tmp = k.scratch

    def epi(eng, ps_ap, ds, tc, m):
        cs = slice(col_off + tc * 512, col_off + (tc + 1) * 512)
        if bias_col is None:
            inst = eng.scalar_tensor_tensor(
                out=r_tiles[ds][0:m, cs], in0=ps_ap, scalar=alpha,
                in1=r_tiles[ds][0:m, cs], op0=ALU.mult, op1=ALU.add)
            k.inc(inst, "dv", 1)
        else:
            inst = eng.tensor_scalar(out=tmp[0:m, :], in0=ps_ap,
                                     scalar1=bias_col[:, ds:ds + 1],
                                     scalar2=None, op0=ALU.add)
            eng.wait_ge(k.sems["dv"], k.inc(inst, "dv", 1))
            inst = eng.scalar_tensor_tensor(
                out=r_tiles[ds][0:m, cs], in0=tmp[0:m, :], scalar=alpha,
                in1=r_tiles[ds][0:m, cs], op0=ALU.mult, op1=ALU.add)
            k.inc(inst, "dv", 1)
    return epi



def build_l0ffn(hostc, hostw, nlayers=6):
    nc = bass.Bass("TRN2", target_bir_lowering=False)
    ctx = ExitStack()
    names = ["identity", "wC", "wA", "sel_ssq", "sel_bc", "pswap",
             "xpos_qA", "xpos_qB", "xpos_kA", "xpos_kB"]
    with ctx:
        k, cst, im = setup(nc, ctx, names, hostc)
        R = win(nc, "R", [D, TT])
        Rout = nc.dram_tensor("Rout", [D, TT], F32, kind="ExternalOutput").ap()
        X = nc.dram_tensor("Xscr", [D, TT], F32, kind="Internal").ap()
        wq_aps, cols = {}, {}
        for i in range(6):
            for nm, shp in [("wqkv", [D, 3072]), ("wo", [D, D]),
                            ("w1a", [D, 2816]), ("w1g", [D, 2816]),
                            ("w2", [2816, D])]:
                wq_aps[(i, nm)] = win(nc, f"f{i}_{nm}", shp)
                im[f"f{i}_{nm}"] = hostw[f"f{i}_{nm}"]
            for nm, shp in [("b1a", [P, 22]), ("b1g", [P, 22]),
                            ("b2", [P, 8]), ("lng", [P, 8]), ("lnb", [P, 8]),
                            ("qs", [P, 1]), ("ks", [P, 1])]:
                sb, ap = wcol_sb(k, nc, f"f{i}_{nm}", shp)
                cols[(i, nm)] = sb
                im[f"f{i}_{nm}"] = hostw[f"f{i}_{nm}"]
                op_load(k, [(sb[:], ap)], queue="qs")

        for b in range(NB):
            k.phase()
            rb = k.sb_tiles(8, T, F32)
            xf = k.sb_tiles(8, T, F32)
            ld_tiles(k, rb, R, b, queue="qs")
            k.phase()
            tmp = k.sb([P, 512], F32)

            def apply_fn(eng, di, a_ps, b_ps, c0, w):
                for j in range(w // 512):
                    cs = slice(c0 + j * 512, c0 + (j + 1) * 512)
                    inst = eng.tensor_tensor(out=tmp[:], in0=rb[di][:, cs],
                                             in1=a_ps[j][:], op=ALU.mult)
                    eng.wait_ge(k.sems["dv"], k.inc(inst, "dv", 1))
                    inst = eng.tensor_tensor(out=xf[di][:, cs], in0=tmp[:],
                                             in1=b_ps[j][:], op=ALU.add)
                    k.inc(inst, "dv", 1)
            op_norm(k, rb, 8, T, cst["ones_col"], cst["ones_row"], apply_fn,
                    rms=False, chunk=512)
            k.unphase()
            st_tiles(k, xf, X, b, queue="qs")
            k.unphase()

        for i in range(nlayers):
            for b in range(NB):
                k.phase()
                ot = k.sb_tiles(8, T, BF16)
                k.phase()
                vtok = k.sb_tiles(16, 1024, BF16)
                qk = k.sb_tiles(16, T, BF16)
                k.phase()
                xs = k.sb_tiles(8, T, BF16)
                wv_sb = k.sb_tiles(8, 512, BF16)
                ld_shifted_bf16(k, xs, X, b)
                for dc2 in range(2):
                    op_load(k, [(wv_sb[j][:],
                                 wq_aps[(i, "wqkv")][j * P:(j + 1) * P,
                                                     2048 + dc2 * 512:
                                                     2560 + dc2 * 512])
                                for j in range(8)], queue="qg")
                    op_linearA(k, xs, wv_sb, D, 512, T,
                               mk_epiA_copy(k, vtok, col_off=dc2 * 512))
                op_linearB(k, xs, wq_aps[(i, "wqkv")][:, 0:2048], D, 2048, T,
                           mk_epi_copy(k, qk), w_chunk=256)
                k.unphase()
                op_qk_prep(k, qk[0:8], 16, 64, T, cst, l2eps=1e-12,
                           scale_col=[cols[(i, "qs")]] * 8,
                           rope_tabs=(cst["xpos_qA"], cst["xpos_qB"]))
                op_qk_prep(k, qk[8:16], 16, 64, T, cst, l2eps=1e-12,
                           scale_col=[cols[(i, "ks")]] * 8,
                           rope_tabs=(cst["xpos_kA"], cst["xpos_kB"]))
                op_attention(k, qk[0:8], qk[8:16], vtok, ot, 16, 64, 16, T,
                             "win256", 8.0, cst)
                k.unphase()
                k.phase()
                xb = k.sb_tiles(8, T, F32)
                ld_tiles(k, xb, X, b, queue="qs")
                op_linearB(k, ot, wq_aps[(i, "wo")], D, D, T,
                           mk_epi_residual(k, xb))
                st_tiles(k, xb, X, b, queue="qs")
                k.unphase()
                k.unphase()

            for b in range(NB):
                k.phase()
                yout = k.sb_tiles(8, T, BF16)
                k.phase()
                xn = k.sb_tiles(8, T, BF16)
                k.phase()
                xsf = k.sb_tiles(8, T, F32)
                ld_shifted_bf16(k, xsf, X, b)
                norm_to_bf16(k, xsf, xn, cst, rms=False, eps=1e-5,
                             gain_col=cols[(i, "lng")],
                             bias_col=cols[(i, "lnb")], chunk=512)
                k.unphase()
                k.phase()
                gt = [k.sb([P, T], BF16) for _ in range(22)]
                op_linearB(k, xn, wq_aps[(i, "w1g")], D, 2816, T,
                           mk_epi_act(k, gt, AF.Gelu, cols[(i, "b1g")]),
                           epi_eng="scalar")

                def epi_amul(eng, ps_ap, ds, tc, m, i=i):
                    inst = eng.tensor_scalar(
                        out=k.scratch[0:m, :], in0=ps_ap,
                        scalar1=cols[(i, "b1a")][:, ds:ds + 1], scalar2=None,
                        op0=ALU.add)
                    eng.wait_ge(k.sems["dv"], k.inc(inst, "dv", 1))
                    inst = eng.tensor_mul(
                        out=gt[ds][0:m, tc * 512:(tc + 1) * 512],
                        in0=gt[ds][0:m, tc * 512:(tc + 1) * 512],
                        in1=k.scratch[0:m, :])
                    k.inc(inst, "dv", 1)
                op_linearB(k, xn, wq_aps[(i, "w1a")], D, 2816, T, epi_amul,
                           epi_incs=2)
                op_linearB(k, gt, wq_aps[(i, "w2")], 2816, D, T,
                           mk_epi_bias(k, yout, cols[(i, "b2")]),
                           w_chunk=256)
                k.unphase()
                k.unphase()
                k.phase()
                xb = k.sb_tiles(8, T, F32)
                ld_tiles(k, xb, X, b, queue="qs")

                def addres(eng):
                    for j in range(8):
                        inst = eng.tensor_add(out=xb[j][:], in0=xb[j][:],
                                              in1=yout[j][:])
                        k.inc(inst, "dv", 1)
                k.op({"vector": addres})
                st_tiles(k, xb, X, b, queue="qs")
                k.unphase()
                k.unphase()

        for b in range(NB):
            k.phase()
            rb = k.sb_tiles(8, T, F32)
            xb = k.sb_tiles(8, T, F32)
            ld_tiles(k, rb, R, b, queue="qs")
            ld_tiles(k, xb, X, b, queue="qg")

            def addf(eng):
                for j in range(8):
                    inst = eng.scalar_tensor_tensor(
                        out=rb[j][:], in0=xb[j][:], scalar=0.5, in1=rb[j][:],
                        op0=ALU.mult, op1=ALU.add)
                    k.inc(inst, "dv", 1)
            k.op({"vector": addf})
            st_tiles(k, rb, Rout, b, queue="qs")
            k.unphase()
        k.finish()
    return nc, im



def build_l0attn(hostc, hostw):
    nc = bass.Bass("TRN2", target_bir_lowering=False)
    ctx = ExitStack()
    names = ["identity", "wC", "wA", "selB", "selC"]
    with ctx:
        k, cst, im = setup(nc, ctx, names, hostc)
        R = win(nc, "R", [D, TT])
        Rout = nc.dram_tensor("Rout", [D, TT], F32, kind="ExternalOutput").ap()
        X = nc.dram_tensor("Xscr", [D, TT], F32, kind="Internal").ap()
        Z = nc.dram_tensor("Zscr", [2048, TT], BF16, kind="Internal").ap()
        XN = nc.dram_tensor("XNscr", [D, TT], BF16, kind="Internal").ap()
        waps = {}
        for nm, shp in [("inproj", [D, 4096]), ("wxp", [2048, 80]),
                        ("outproj", [2048, D]), ("qw", [D, D]),
                        ("kw", [D, 512]), ("vw", [D, 512]), ("ow", [D, D]),
                        ("gate", [D, 2816]), ("up", [D, 2816]),
                        ("down", [2816, D])]:
            waps[nm] = win(nc, "m_" + nm, shp)
            im["m_" + nm] = hostw["m_" + nm]
        mcols = {}
        for nm, shp in [("norm_m", [P, 8]), ("cw", [P, 64]), ("cb", [P, 16]),
                        ("dtw", [1, 2048]), ("dtb", [P, 16]),
                        ("alog", [P, 256]), ("dc", [P, 16]),
                        ("norm_attn", [P, 8]), ("norm_ffn", [P, 8])]:
            sb, ap = wcol_sb(k, nc, "m_" + nm, shp)
            mcols[nm] = sb
            im["m_" + nm] = hostw["m_" + nm]
            op_load(k, [(sb[:], ap)], queue="qs")
        A_cols = k.sb([P, 256], F32, "A_cols")

        def acomp(eng):
            inst = eng.activation(out=A_cols[:], in_=mcols["alog"][:],
                                  func=AF.Exp)
            k.inc(inst, "ac", 1)
        k.op({"scalar": acomp})

        def aneg(eng):
            inst = eng.tensor_scalar_mul(out=A_cols[:], in0=A_cols[:],
                                         scalar1=-1.0)
            k.inc(inst, "dv", 1)
        k.op({"vector": aneg})
        mcst = {"conv_w": mcols["cw"], "conv_b": mcols["cb"],
                "dtw_row": mcols["dtw"], "dtb": mcols["dtb"],
                "selB": cst["selB"], "selC": cst["selC"],
                "A_cols": A_cols, "D_col": mcols["dc"]}

        for b in range(NB):
            k.phase()
            rb = k.sb_tiles(8, T, F32)
            xf = k.sb_tiles(8, T, F32)
            ld_tiles(k, rb, R, b, queue="qs")
            k.phase()
            tmp = k.sb([P, 512], F32)

            def apply_fn(eng, di, a_ps, b_ps, c0, w, rb=rb, xf=xf, tmp=tmp):
                for j in range(w // 512):
                    cs = slice(c0 + j * 512, c0 + (j + 1) * 512)
                    inst = eng.tensor_tensor(out=tmp[:], in0=rb[di][:, cs],
                                             in1=a_ps[j][:], op=ALU.mult)
                    eng.wait_ge(k.sems["dv"], k.inc(inst, "dv", 1))
                    inst = eng.tensor_tensor(out=xf[di][:, cs], in0=tmp[:],
                                             in1=b_ps[j][:], op=ALU.add)
                    k.inc(inst, "dv", 1)
            op_norm(k, rb, 8, T, cst["ones_col"], cst["ones_row"], apply_fn,
                    rms=False, chunk=512)
            k.unphase()
            st_tiles(k, xf, X, b, queue="qs")
            k.unphase()

        for b in range(NB):
            k.phase()
            xnm = k.sb_tiles(8, T, BF16)
            k.phase()
            xb = k.sb_tiles(8, T, F32)
            ld_tiles(k, xb, X, b, queue="qs")
            norm_to_bf16(k, xb, xnm, cst, rms=True, eps=1e-6,
                         gain_col=mcols["norm_m"], chunk=512)
            k.unphase()
            st_tiles(k, xnm, XN, b, queue="qg")
            k.unphase()
            k.phase()
            xs_t = k.sb_tiles(16, T, BF16)
            k.phase()
            xr = k.sb_tiles(16, 3 + T, BF16)
            k.phase()
            xnm = k.sb_tiles(8, T, BF16)
            ld_tiles(k, xnm, XN, b, queue="qg")
            op_linearB(k, xnm, waps["inproj"][:, 2048:4096], D, 2048, T,
                       mk_epi_copy(k, xs_t), w_chunk=256)
            st_tiles(k, xs_t, Z, b, n=16, queue="qg", Tb=T)

            def zeropad(eng, xr=xr):
                for j in range(16):
                    inst = eng.memset(xr[j][:, 0:3], 0.0)
                k.inc(inst, "pl", 1)
            k.op({"gpsimd": zeropad})

            def epi_inx(eng, ps_ap, ds, tc, m, xr=xr):
                inst = eng.tensor_copy(
                    out=xr[ds][0:m, 3 + tc * 512:3 + (tc + 1) * 512],
                    in_=ps_ap)
                k.inc(inst, "dv", 1)
            op_linearB(k, xnm, waps["inproj"][:, 0:2048], D, 2048, T,
                       epi_inx, w_chunk=256)
            k.unphase()
            op_mamba_conv(k, xr, xs_t, mcst, Tb=T)
            k.unphase()
            op_mamba_scan(k, xs_t, Z, waps["wxp"], mcst, Tb=T, zcol0=b * T)
            k.phase()
            xb = k.sb_tiles(8, T, F32)
            ld_tiles(k, xb, X, b, queue="qs")
            op_linearB(k, xs_t, waps["outproj"], 2048, D, T,
                       mk_epi_residual(k, xb), w_chunk=256)
            st_tiles(k, xb, X, b, queue="qs")
            k.unphase()
            k.unphase()

        for b in range(NB):
            k.phase()
            xnm = k.sb_tiles(8, T, BF16)
            k.phase()
            xb = k.sb_tiles(8, T, F32)
            ld_tiles(k, xb, X, b, queue="qs")
            norm_to_bf16(k, xb, xnm, cst, rms=True, eps=1e-6,
                         gain_col=mcols["norm_attn"], chunk=512)
            k.unphase()
            k.phase()
            ot = k.sb_tiles(8, T, BF16)
            k.phase()
            qt = k.sb_tiles(8, T, BF16)
            kt = k.sb_tiles(4, T, BF16)
            vtok = k.sb_tiles(16, 512, BF16)
            k.phase()
            wv_sb = k.sb_tiles(8, 512, BF16)
            op_load(k, [(wv_sb[j][:], waps["vw"][j * P:(j + 1) * P, :])
                        for j in range(8)], queue="qg")
            op_linearA(k, xnm, wv_sb, D, 512, T, mk_epiA_copy(k, vtok))
            k.unphase()
            op_linearB(k, xnm, waps["qw"], D, D, T, mk_epi_copy(k, qt))
            op_linearB(k, xnm, waps["kw"], D, 512, T, mk_epi_copy(k, kt))
            op_attention(k, qt, kt, vtok, ot, 4, 256, 2, T, "causal",
                         1.0 / 16.0, cst)
            k.unphase()
            k.phase()
            xb = k.sb_tiles(8, T, F32)
            ld_tiles(k, xb, X, b, queue="qs")
            op_linearB(k, ot, waps["ow"], D, D, T, mk_epi_residual(k, xb))
            st_tiles(k, xb, X, b, queue="qs")
            k.unphase()
            k.unphase()
            k.unphase()

        for b in range(NB):
            k.phase()
            yout = k.sb_tiles(8, T, BF16)
            k.phase()
            xn = k.sb_tiles(8, T, BF16)
            k.phase()
            xb = k.sb_tiles(8, T, F32)
            ld_tiles(k, xb, X, b, queue="qs")
            norm_to_bf16(k, xb, xn, cst, rms=True, eps=1e-6,
                         gain_col=mcols["norm_ffn"], chunk=512)
            k.unphase()
            k.phase()
            gt = [k.sb([P, T], BF16) for _ in range(22)]
            op_linearB(k, xn, waps["gate"], D, 2816, T,
                       mk_epi_act(k, gt, AF.Silu), epi_eng="scalar")

            def epi_umul(eng, ps_ap, ds, tc, m):
                inst = eng.tensor_copy(out=k.scratch[0:m, :], in_=ps_ap)
                eng.wait_ge(k.sems["dv"], k.inc(inst, "dv", 1))
                inst = eng.tensor_mul(
                    out=gt[ds][0:m, tc * 512:(tc + 1) * 512],
                    in0=gt[ds][0:m, tc * 512:(tc + 1) * 512],
                    in1=k.scratch[0:m, :])
                k.inc(inst, "dv", 1)
            op_linearB(k, xn, waps["up"], D, 2816, T, epi_umul, epi_incs=2)
            op_linearB(k, gt, waps["down"], 2816, D, T,
                       mk_epi_copy(k, yout), w_chunk=256)
            k.unphase()
            k.unphase()
            k.phase()
            xb = k.sb_tiles(8, T, F32)
            ld_tiles(k, xb, X, b, queue="qs")

            def addres(eng, xb=xb, yout=yout):
                for j in range(8):
                    inst = eng.tensor_add(out=xb[j][:], in0=xb[j][:],
                                          in1=yout[j][:])
                    k.inc(inst, "dv", 1)
            k.op({"vector": addres})
            st_tiles(k, xb, X, b, queue="qs")
            k.unphase()
            k.unphase()

        for b in range(NB):
            k.phase()
            rb = k.sb_tiles(8, T, F32)
            xb = k.sb_tiles(8, T, F32)
            ld_tiles(k, rb, R, b, queue="qs")
            ld_tiles(k, xb, X, b, queue="qg")

            def addf(eng, rb=rb, xb=xb):
                for j in range(8):
                    inst = eng.tensor_add(out=rb[j][:], in0=rb[j][:],
                                          in1=xb[j][:])
                    k.inc(inst, "dv", 1)
            k.op({"vector": addf})
            st_tiles(k, rb, Rout, b, queue="qs")
            k.unphase()
        k.finish()
    return nc, im



def build_l1attn(hostc, hostw):
    nc = bass.Bass("TRN2", target_bir_lowering=False)
    ctx = ExitStack()
    names = ["identity", "wC", "wA", "sel_ssq", "sel_bc"]
    with ctx:
        k, cst, im = setup(nc, ctx, names, hostc)
        R = win(nc, "R", [D, TT])
        Rout = nc.dram_tensor("Rout", [D, TT], F32, kind="ExternalOutput").ap()
        waps = {}
        for nm in ["wq", "wk", "wv", "wo"]:
            waps[nm] = win(nc, "a_" + nm, [D, D])
            im["a_" + nm] = hostw["a_" + nm]
        acols = {}
        for nm, shp in [("bo", [P, 8]), ("qs", [P, 8]), ("ks", [P, 8])]:
            sb, ap = wcol_sb(k, nc, "a_" + nm, shp)
            acols[nm] = sb
            im["a_" + nm] = hostw["a_" + nm]
            op_load(k, [(sb[:], ap)], queue="qs")

        for b in range(NB):
            k.phase()
            xn = k.sb_tiles(8, T, BF16)
            k.phase()
            rb = k.sb_tiles(8, T, F32)
            ld_tiles(k, rb, R, b, queue="qs")
            k.phase()
            tmp = k.sb([P, 512], F32)

            def apply_fn(eng, di, a_ps, b_ps, c0, w, rb=rb, xn=xn, tmp=tmp):
                for j in range(w // 512):
                    cs = slice(c0 + j * 512, c0 + (j + 1) * 512)
                    inst = eng.tensor_tensor(out=tmp[:], in0=rb[di][:, cs],
                                             in1=a_ps[j][:], op=ALU.mult)
                    eng.wait_ge(k.sems["dv"], k.inc(inst, "dv", 1))
                    inst = eng.tensor_tensor(out=xn[di][:, cs], in0=tmp[:],
                                             in1=b_ps[j][:], op=ALU.add)
                    k.inc(inst, "dv", 1)
            op_norm(k, rb, 8, T, cst["ones_col"], cst["ones_row"], apply_fn,
                    rms=False, chunk=512)
            k.unphase()
            k.unphase()
            k.phase()
            ot = k.sb_tiles(8, T, BF16)
            k.phase()
            qt = k.sb_tiles(8, T, BF16)
            kt = k.sb_tiles(8, T, BF16)
            vtok = k.sb_tiles(16, 1024, BF16)
            k.phase()
            wv_sb = k.sb_tiles(8, 512, BF16)
            for dc2 in range(2):
                op_load(k, [(wv_sb[j][:],
                             waps["wv"][j * P:(j + 1) * P,
                                        dc2 * 512:(dc2 + 1) * 512])
                            for j in range(8)], queue="qg")
                op_linearA(k, xn, wv_sb, D, 512, T,
                           mk_epiA_copy(k, vtok, col_off=dc2 * 512))
            k.unphase()
            op_linearB(k, xn, waps["wq"], D, D, T, mk_epi_copy(k, qt))
            op_linearB(k, xn, waps["wk"], D, D, T, mk_epi_copy(k, kt))
            op_qk_prep(k, qt, 16, 64, T, cst, l2eps=1e-12,
                       scale_col=[acols["qs"][:, j:j + 1] for j in range(8)])
            op_qk_prep(k, kt, 16, 64, T, cst, l2eps=1e-12,
                       scale_col=[acols["ks"][:, j:j + 1] for j in range(8)])
            op_attention(k, qt, kt, vtok, ot, 16, 64, 16, T, "full",
                         0.125, cst)
            k.unphase()
            k.phase()
            rb = k.sb_tiles(8, T, F32)
            ld_tiles(k, rb, R, b, queue="qs")
            op_linearB(k, ot, waps["wo"], D, D, T,
                       mk_epi_residual(k, rb, bias_col=acols["bo"]),
                       epi_incs=2)
            st_tiles(k, rb, Rout, b, queue="qs")
            k.unphase()
            k.unphase()
            k.unphase()
        k.finish()
    return nc, im



def build_l1ffn(hostc, hostw, upto="all"):
    nc = bass.Bass("TRN2", target_bir_lowering=False)
    ctx = ExitStack()
    names = ["identity", "wC", "wA", "rope1_A", "rope1_B", "pswap", "exp_sel"]
    with ctx:
        k, cst, im = setup(nc, ctx, names, hostc)
        R = win(nc, "R", [D, TT])
        Rout = nc.dram_tensor("Rout", [D, TT], F32, kind="ExternalOutput").ap()
        X = nc.dram_tensor("Xscr", [D, TT], F32, kind="Internal").ap()
        waps = {}
        for nm, shp in [("wq", [D, D]), ("wkv", [D, 256]), ("wkup", [256, D]),
                        ("wvup", [256, D]), ("wo", [D, D]), ("wr", [D, 8]),
                        ("sw1", [D, 2048]), ("sw2", [2048, D]),
                        ("sw3", [D, 2048])]:
            waps[nm] = win(nc, "g_" + nm, shp)
            im["g_" + nm] = hostw["g_" + nm]
        for e in range(8):
            for nm, shp in [("ew1", [D, 2048]), ("ew2", [2048, D]),
                            ("ew3", [D, 2048])]:
                waps[f"{nm}_{e}"] = win(nc, f"g_{nm}_{e}", shp)
                im[f"g_{nm}_{e}"] = hostw[f"g_{nm}_{e}"]
        gcols = {}
        for nm, shp in [("ln1", [P, 8]), ("ln2", [P, 8]), ("rb", [1, 8])]:
            sb, ap = wcol_sb(k, nc, "g_" + nm, shp)
            gcols[nm] = sb
            im["g_" + nm] = hostw["g_" + nm]
            op_load(k, [(sb[:], ap)], queue="qs")

        for b in range(NB):
            k.phase()
            rb = k.sb_tiles(8, T, F32)
            xf = k.sb_tiles(8, T, F32)
            ld_tiles(k, rb, R, b, queue="qs")
            k.phase()
            tmp = k.sb([P, 512], F32)

            def apply_fn(eng, di, a_ps, b_ps, c0, w, rb=rb, xf=xf, tmp=tmp):
                for j in range(w // 512):
                    cs = slice(c0 + j * 512, c0 + (j + 1) * 512)
                    inst = eng.tensor_tensor(out=tmp[:], in0=rb[di][:, cs],
                                             in1=a_ps[j][:], op=ALU.mult)
                    eng.wait_ge(k.sems["dv"], k.inc(inst, "dv", 1))
                    inst = eng.tensor_tensor(out=xf[di][:, cs], in0=tmp[:],
                                             in1=b_ps[j][:], op=ALU.add)
                    k.inc(inst, "dv", 1)
            op_norm(k, rb, 8, T, cst["ones_col"], cst["ones_row"], apply_fn,
                    rms=False, chunk=512)
            k.unphase()
            st_tiles(k, xf, X, b, queue="qs")
            k.unphase()

        for b in range(NB):
            k.phase()
            xn = k.sb_tiles(8, T, BF16)
            k.phase()
            xb = k.sb_tiles(8, T, F32)
            ld_tiles(k, xb, X, b, queue="qs")
            norm_to_bf16(k, xb, xn, cst, rms=True, eps=1e-6,
                         gain_col=gcols["ln1"], chunk=512)
            k.unphase()
            k.phase()
            ot = k.sb_tiles(8, T, BF16)
            k.phase()
            qt = k.sb_tiles(8, T, BF16)
            kt = k.sb_tiles(8, T, BF16)
            vtok = k.sb_tiles(16, 1024, BF16)
            k.phase()
            kv = k.sb_tiles(2, T, BF16)
            op_linearB(k, xn, waps["wkv"], D, 256, T,
                       mk_epi_scale(k, kv, 0.25))
            op_linearB(k, kv, waps["wkup"], 256, D, T, mk_epi_copy(k, kt))
            k.phase()
            wv_sb = k.sb_tiles(2, 1024, BF16)
            op_load(k, [(wv_sb[j][:], waps["wvup"][j * P:(j + 1) * P, :])
                        for j in range(2)], queue="qg")
            op_linearA(k, kv, wv_sb, 256, 1024, T, mk_epiA_copy(k, vtok))
            k.unphase()
            k.unphase()
            op_linearB(k, xn, waps["wq"], D, D, T, mk_epi_copy(k, qt))
            op_qk_prep(k, qt, 16, 64, T, cst,
                       rope_tabs=(cst["rope1_A"], cst["rope1_B"]))
            op_qk_prep(k, kt, 16, 64, T, cst,
                       rope_tabs=(cst["rope1_A"], cst["rope1_B"]))
            op_attention(k, qt, kt, vtok, ot, 16, 64, 16, T, "causal",
                         0.125, cst)
            k.unphase()
            k.phase()
            xb = k.sb_tiles(8, T, F32)
            ld_tiles(k, xb, X, b, queue="qs")
            op_linearB(k, ot, waps["wo"], D, D, T, mk_epi_residual(k, xb))
            st_tiles(k, xb, X, b, queue="qs")
            k.unphase()
            k.unphase()
            k.unphase()

        for b in range(NB if upto == "all" else 0):
            k.phase()
            xn = k.sb_tiles(8, T, BF16)
            k.phase()
            xb = k.sb_tiles(8, T, F32)
            ld_tiles(k, xb, X, b, queue="qs")
            norm_to_bf16(k, xb, xn, cst, rms=True, eps=1e-6,
                         gain_col=gcols["ln2"], chunk=512)
            k.unphase()
            k.phase()
            acc = k.sb_tiles(8, T, F32)
            ST = k.sb([32, T], BF16)
            abc = k.sb([P, T], BF16)

            def zacc(eng, acc=acc, ST=ST):
                for j in range(8):
                    inst = eng.memset(acc[j][:], 0.0)
                inst = eng.memset(ST[:], 0.0)
                k.inc(inst, "pl", 1)
            k.op({"gpsimd": zacc})

            k.phase()
            wr_sb = k.sb_tiles(8, 8, BF16)
            rbc = k.sb([P, 8], F32)
            probs = k.sb([P, 16 * 8], F32)
            sca = k.sb([P, 16], F32)
            eqb = k.sb([P, 8], F32)
            cumb = k.sb([P, 8], F32)
            zeros8 = k.sb([P, 8], F32)
            op_load(k, [(wr_sb[j][:], waps["wr"][j * P:(j + 1) * P, :])
                        for j in range(8)], queue="qg")

            def pre(eng):
                inst = eng.memset(zeros8[:], 0.0)
                k.inc(inst, "pl", 1)
            k.op({"gpsimd": pre})

            def rbcast(eng):
                inst = eng.matmul(k.ps[7][:, 0:8], cst["ones_row"][0:1, :],
                                  gcols["rb"][0:1, 0:8], start=True, stop=True)
                k.inc(inst, "pe", 1)
            k.op({"tensor": rbcast})

            def rbc_cp(eng):
                inst = eng.tensor_copy(out=rbc[:], in_=k.ps[7][:, 0:8])
                k.inc(inst, "dv", 1)
            k.op({"vector": rbc_cp})

            def epi_router(eng, ps_ap, ti, dc, n):
                inst = eng.tensor_tensor(out=probs[:, ti * 8:ti * 8 + 8],
                                         in0=ps_ap, in1=rbc[:], op=ALU.add)
                k.inc(inst, "dv", 1)
            op_linearA(k, xn, wr_sb, D, 8, T, epi_router)

            seq = Seq(k)
            for ti in range(16):
                pr = lambda ti=ti: probs[:, ti * 8:ti * 8 + 8]

                def C1(eng, pr=pr):
                    return eng.tensor_scalar_max(out=pr(), in0=pr(),
                                                 scalar1=-50.0)
                seq.do("vector", C1, inc="dv")

                def C2(eng, pr=pr):
                    return eng.tensor_scalar_min(out=pr(), in0=pr(),
                                                 scalar1=50.0)
                dvc = seq.do("vector", C2, inc="dv")

                def SG(eng, pr=pr):
                    return eng.activation(out=pr(), in_=pr(), func=AF.Sigmoid)
                acs = seq.do("scalar", SG, waits=[("dv", dvc)], inc="ac")

                def PM(eng, pr=pr, ti=ti):
                    return eng.reduce_max(out=sca[:, 15:16], in_=pr(),
                                          axis=AX.X)
                seq.do("vector", PM, waits=[("ac", acs)], inc="dv")

                def EQ(eng, pr=pr):
                    return eng.tensor_tensor(
                        out=eqb[:], in0=pr(),
                        in1=sca[:, 15:16].broadcast_to([P, 8]),
                        op=ALU.is_equal)
                seq.do("vector", EQ, inc="dv")

                def CU(eng):
                    return eng.tensor_tensor_scan(
                        out=cumb[:], data0=eqb[:], data1=zeros8[:],
                        initial=0.0, op0=ALU.add, op1=ALU.add)
                seq.do("vector", CU, inc="dv")

                def FE(eng):
                    return eng.tensor_single_scalar(out=cumb[:], in_=cumb[:],
                                                    scalar=1.0,
                                                    op=ALU.is_equal)
                seq.do("vector", FE, inc="dv")

                def OH(eng):
                    return eng.tensor_mul(out=eqb[:], in0=eqb[:], in1=cumb[:])
                seq.do("vector", OH, inc="dv")

                def PC1(eng):
                    return eng.tensor_scalar_max(out=sca[:, 15:16],
                                                 in0=sca[:, 15:16],
                                                 scalar1=1e-8)
                seq.do("vector", PC1, inc="dv")

                def PC2(eng):
                    return eng.tensor_scalar_min(out=sca[:, 15:16],
                                                 in0=sca[:, 15:16],
                                                 scalar1=1.0 - 1e-8)
                seq.do("vector", PC2, inc="dv")

                def TW(eng):
                    return eng.tensor_scalar(out=sca[:, 14:15],
                                             in0=sca[:, 15:16],
                                             scalar1=0.5 + 1e-8, scalar2=2.0,
                                             op0=ALU.add, op1=ALU.min)
                seq.do("vector", TW, inc="dv")

                def TW2(eng):
                    return eng.tensor_scalar_max(out=sca[:, 14:15],
                                                 in0=sca[:, 14:15],
                                                 scalar1=0.5)
                seq.do("vector", TW2, inc="dv")

                def IT(eng):
                    return eng.reciprocal(out=sca[:, 14:15],
                                          in_=sca[:, 14:15])
                seq.do("vector", IT, inc="dv")

                def SA(eng):
                    return eng.tensor_scalar_mul(out=sca[:, 0:1],
                                                 in0=sca[:, 14:15],
                                                 scalar1=0.5)
                seq.do("vector", SA, inc="dv")

                def PI(eng):
                    return eng.tensor_mul(out=sca[:, 15:16],
                                          in0=sca[:, 15:16],
                                          in1=sca[:, 14:15])
                seq.do("vector", PI, inc="dv")

                def BC(eng):
                    return eng.tensor_tensor(
                        out=sca[:, 1:9], in0=eqb[:],
                        in1=sca[:, 15:16].broadcast_to([P, 8]), op=ALU.mult)
                dvb = seq.do("vector", BC, inc="dv")

                def TR(eng, ti=ti):
                    return eng.transpose(k.ps[6][0:9, 0:P], sca[:, 0:9],
                                         cst["identity"][:])
                pet = seq.do("tensor", TR, waits=[("dv", dvb)], inc="pe")

                def CP(eng, ti=ti):
                    return eng.tensor_copy(out=ST[0:9, ti * P:(ti + 1) * P],
                                           in_=k.ps[6][0:9, 0:P])
                seq.do("vector", CP, waits=[("pe", pet)], inc="dv")
            seq.run()
            k.unphase()

            k.phase()
            ht = [k.sb([P, T], BF16) for _ in range(16)]
            for e in range(-1, 8):
                w1 = waps["sw1"] if e < 0 else waps[f"ew1_{e}"]
                w3 = waps["sw3"] if e < 0 else waps[f"ew3_{e}"]
                w2 = waps["sw2"] if e < 0 else waps[f"ew2_{e}"]
                op_linearB(k, xn, w1, D, 2048, T,
                           mk_epi_act(k, ht, AF.Silu), epi_eng="scalar",
                           w_chunk=256)

                def epi_mul(eng, ps_ap, ds, tc, m):
                    inst = eng.tensor_copy(out=k.scratch[0:m, :], in_=ps_ap)
                    eng.wait_ge(k.sems["dv"], k.inc(inst, "dv", 1))
                    inst = eng.tensor_mul(
                        out=ht[ds][0:m, tc * 512:(tc + 1) * 512],
                        in0=ht[ds][0:m, tc * 512:(tc + 1) * 512],
                        in1=k.scratch[0:m, :])
                    k.inc(inst, "dv", 1)
                op_linearB(k, xn, w3, D, 2048, T, epi_mul, epi_incs=2,
                           w_chunk=256)

                sl = 0 if e < 0 else 1 + e

                def bca(eng, sl=sl):
                    for tc in range(4):
                        inst = eng.matmul(
                            k.ps[4 + tc][:, :],
                            cst["exp_sel"][0:32, sl * P:(sl + 1) * P],
                            ST[0:32, tc * 512:(tc + 1) * 512],
                            start=True, stop=True)
                        k.inc(inst, "pe", 1)
                k.op({"tensor": bca})

                def bcc(eng):
                    for tc in range(4):
                        inst = eng.tensor_copy(
                            out=abc[:, tc * 512:(tc + 1) * 512],
                            in_=k.ps[4 + tc][:, :])
                        k.inc(inst, "dv", 1)
                k.op({"vector": bcc})

                def epi_acc(eng, ps_ap, ds, tc, m):
                    inst = eng.tensor_tensor(
                        out=k.scratch[0:m, :], in0=ps_ap,
                        in1=abc[0:m, tc * 512:(tc + 1) * 512], op=ALU.mult)
                    eng.wait_ge(k.sems["dv"], k.inc(inst, "dv", 1))
                    inst = eng.tensor_add(
                        out=acc[ds][0:m, tc * 512:(tc + 1) * 512],
                        in0=acc[ds][0:m, tc * 512:(tc + 1) * 512],
                        in1=k.scratch[0:m, :])
                    k.inc(inst, "dv", 1)
                op_linearB(k, ht, w2, 2048, D, T, epi_acc, epi_incs=2,
                           w_chunk=256)
            k.unphase()

            k.phase()
            xb = k.sb_tiles(8, T, F32)
            ld_tiles(k, xb, X, b, queue="qs")

            def addres(eng, xb=xb, acc=acc):
                for j in range(8):
                    inst = eng.tensor_add(out=xb[j][:], in0=xb[j][:],
                                          in1=acc[j][:])
                    k.inc(inst, "dv", 1)
            k.op({"vector": addres})
            st_tiles(k, xb, X, b, queue="qs")
            k.unphase()
            k.unphase()
            k.unphase()

        for b in range(NB):
            k.phase()
            rb = k.sb_tiles(8, T, F32)
            xb = k.sb_tiles(8, T, F32)
            ld_tiles(k, rb, R, b, queue="qs")
            ld_tiles(k, xb, X, b, queue="qg")

            def addf(eng, rb=rb, xb=xb):
                for j in range(8):
                    inst = eng.scalar_tensor_tensor(
                        out=rb[j][:], in0=xb[j][:], scalar=0.5, in1=rb[j][:],
                        op0=ALU.mult, op1=ALU.add)
                    k.inc(inst, "dv", 1)
            k.op({"vector": addf})
            st_tiles(k, rb, Rout, b, queue="qs")
            k.unphase()
        k.finish()
    return nc, im



def kernel(**inputs):
    hostc = host_consts()
    hostw = host_weights(inputs)
    x = np.ascontiguousarray(np.asarray(inputs["L0_attn_x"], dtype=np.float32))
    Rs = []
    for c in range(NCORE):
        Rs.append(np.ascontiguousarray(
            np.concatenate([x[2 * c].T, x[2 * c + 1].T], axis=1)))
    ncF, imF = build_l0ffn(hostc, hostw)
    ncM, imM = build_l0attn(hostc, hostw)
    ncG, imG = build_l1ffn(hostc, hostw)
    ncA, imA = build_l1attn(hostc, hostw)
    for ncx, imx in [(ncF, imF), (ncM, imM), (ncF, imF),
                     (ncG, imG), (ncA, imA), (ncG, imG)]:
        in_maps = [dict(imx, R=Rs[c]) for c in range(NCORE)]
        res = run_bass_kernel_spmd(ncx, in_maps, core_ids=list(range(NCORE)))
        Rs = [np.ascontiguousarray(np.asarray(res.results[c]["Rout"]))
              for c in range(NCORE)]
    out = np.empty((16, 2048, 1024), np.float32)
    for c in range(NCORE):
        out[2 * c] = Rs[c][:, :T].T
        out[2 * c + 1] = Rs[c][:, T:].T
    return out


# version 02
# speedup vs baseline: 1.0058x; 1.0058x over previous
import numpy as np
import ml_dtypes
from contextlib import ExitStack
import concourse.bass as bass
import concourse.mybir as mybir
from concourse.bass_utils import run_bass_kernel_spmd

F32 = mybir.dt.float32
BF16 = mybir.dt.bfloat16
AF = mybir.ActivationFunctionType
ALU = mybir.AluOpType
AX = mybir.AxisListType
BF = ml_dtypes.bfloat16

P = 128
T = 2048
NB = 2
TT = NB * T
NEG = -1e30
D = 1024
DT = D // P

ENG_SEM = {"tensor": "pe", "vector": "dv", "scalar": "ac",
           "gpsimd": "pl", "sync": "qs"}
QUEUE_ENG = {"qs": "sync", "qg": "gpsimd", "qa": "scalar"}


def cdiv(a, b):
    return (a + b - 1) // b


class Ker:
    def __init__(self, nc, block, ctx):
        self.nc = nc
        self.block = block
        self.ctx = ctx
        E = ctx.enter_context
        self.sems = {}
        self.cnt = {}
        for name in ["pe", "dv", "ac", "pl", "qs", "qg", "qa"]:
            self.sems[name] = E(nc.semaphore(name))
            self.cnt[name] = 0
        self.ps = [E(nc.psum_tensor(f"ps{i}", [P, 512], F32)) for i in range(8)]
        self.sb_off = 16512
        self.sb_peak = self.sb_off
        self._phase_stack = []
        self._names = 0
        self.scratch = self.sb([P, 512], F32, "kscratch")
        self.scratch2 = self.sb([P, 512], F32, "kscratch2")

    def sb(self, shape, dtype, name=None):
        per_part = int(np.prod(shape[1:])) * mybir.dt.size(dtype)
        off = (self.sb_off + 63) & ~63
        self.sb_off = off + per_part
        self.sb_peak = max(self.sb_peak, self.sb_off)
        assert self.sb_off <= 229370, f"SBUF overflow: {self.sb_off} B/partition"
        self._names += 1
        nm = name or f"sb{self._names}"
        return self.nc.alloc_sbuf_tensor_at(nm, list(shape), dtype, offset=off)

    def sb_tiles(self, n, cols, dtype):
        return [self.sb([P, cols], dtype) for _ in range(n)]

    def phase(self):
        self._phase_stack.append(self.sb_off)

    def unphase(self):
        self.sb_off = self._phase_stack.pop()

    def inc(self, inst, sem, n=1):
        inst.then_inc(self.sems[sem], n)
        self.cnt[sem] += n
        return self.cnt[sem]

    def op(self, fn_map, snap=None):
        if snap is None:
            snap = dict(self.cnt)
        for en in ["sync", "gpsimd", "tensor", "vector", "scalar"]:
            if en not in fn_map:
                continue
            fn = fn_map[en]

            def body(eng, fn=fn, en=en):
                own = ENG_SEM.get(en)
                for s, v in snap.items():
                    if v > 0 and s != own:
                        eng.wait_ge(self.sems[s], v)
                fn(eng)
            getattr(self.block, en)(body)

    def finish(self, out_queues=("qs", "qg", "qa")):
        snap = dict(self.cnt)

        def body(eng):
            for s, v in snap.items():
                if v > 0:
                    eng.wait_ge(self.sems[s], v)
        self.block.sync(body)



def op_load(k, pairs, queue="qs"):
    def body(eng):
        for d, s in pairs:
            k.inc(eng.dma_start(out=d, in_=s), queue, 16)
    k.op({QUEUE_ENG[queue]: body})


def op_store(k, pairs, queue="qs"):
    def body(eng):
        for d, s in pairs:
            k.inc(eng.dma_start(out=d, in_=s), queue, 16)
    k.op({QUEUE_ENG[queue]: body})



def op_linearB(k, xs, w_hbm, din, dout, tt, epi, epi_eng="vector",
               w_chunk=512, epi_incs=1):
    ndi = din // P
    ndo_ch = cdiv(dout, w_chunk)
    ntc = tt // 512
    esem = ENG_SEM[epi_eng]
    k.phase()
    wbuf = [k.sb([P, ndi * w_chunk], BF16), k.sb([P, ndi * w_chunk], BF16)]
    pe0, ev0, qg0 = k.cnt["pe"], k.cnt[esem], k.cnt["qg"]

    rounds_per_chunk = []
    for c in range(ndo_ch):
        nds = min(w_chunk, dout - c * w_chunk)
        rounds_per_chunk.append(ntc * cdiv(nds, P))
    pe_after_chunk = np.cumsum(rounds_per_chunk)

    def dma_body(eng):
        for c in range(ndo_ch):
            if c >= 2:
                eng.wait_ge(k.sems["pe"], pe0 + int(pe_after_chunk[c - 2]))
            nds = min(w_chunk, dout - c * w_chunk)
            for di in range(ndi):
                inst = eng.dma_start(
                    out=wbuf[c % 2][:, di * w_chunk: di * w_chunk + nds],
                    in_=w_hbm[di * P:(di + 1) * P,
                              c * w_chunk: c * w_chunk + nds])
                k.inc(inst, "qg", 16)

    def pe_body(eng):
        rd = 0
        for c in range(ndo_ch):
            eng.wait_ge(k.sems["qg"], qg0 + 16 * ndi * (c + 1))
            nds = min(w_chunk, dout - c * w_chunk)
            for tc in range(ntc):
                for s in range(cdiv(nds, P)):
                    m = min(P, nds - s * P)
                    psb = k.ps[rd % 8]
                    if rd >= 8:
                        eng.wait_ge(k.sems[esem], ev0 + (rd - 7) * epi_incs)
                    for di in range(ndi):
                        inst = eng.matmul(
                            psb[0:m, :],
                            wbuf[c % 2][:, di * w_chunk + s * P:
                                        di * w_chunk + s * P + m],
                            xs[di][:, tc * 512:(tc + 1) * 512],
                            start=(di == 0), stop=(di == ndi - 1))
                    k.inc(inst, "pe", 1)
                    rd += 1

    def ev_body(eng):
        rd = 0
        for c in range(ndo_ch):
            nds = min(w_chunk, dout - c * w_chunk)
            for tc in range(ntc):
                for s in range(cdiv(nds, P)):
                    m = min(P, nds - s * P)
                    eng.wait_ge(k.sems["pe"], pe0 + rd + 1)
                    epi(eng, k.ps[rd % 8][0:m, :], c * (w_chunk // P) + s,
                        tc, m)
                    rd += 1

    k.op({"gpsimd": dma_body, "tensor": pe_body, epi_eng: ev_body})
    k.unphase()


def op_linearA(k, xs, w_sb, din, dout, tt, epi, epi_eng="vector",
               epi_incs=1):
    ndi = din // P
    ntl = tt // P
    ndc = cdiv(dout, 512)
    esem = ENG_SEM[epi_eng]
    pe0, ev0 = k.cnt["pe"], k.cnt[esem]

    def pe_body(eng):
        rd = 0
        for ti in range(ntl):
            for dc in range(ndc):
                n = min(512, dout - dc * 512)
                psb = k.ps[rd % 8]
                if rd >= 8:
                    eng.wait_ge(k.sems[esem], ev0 + (rd - 7) * epi_incs)
                for di in range(ndi):
                    inst = eng.matmul(
                        psb[:, 0:n],
                        xs[di][:, ti * P:(ti + 1) * P],
                        w_sb[di][:, dc * 512: dc * 512 + n],
                        start=(di == 0), stop=(di == ndi - 1))
                k.inc(inst, "pe", 1)
                rd += 1

    def ev_body(eng):
        rd = 0
        for ti in range(ntl):
            for dc in range(ndc):
                n = min(512, dout - dc * 512)
                eng.wait_ge(k.sems["pe"], pe0 + rd + 1)
                epi(eng, k.ps[rd % 8][:, 0:n], ti, dc, n)
                rd += 1

    k.op({"tensor": pe_body, epi_eng: ev_body})



def op_norm(k, x_tiles, nr, tt, ones_col, ones_row, apply_fn,
            rms=False, eps=1e-5, chunk=2048):
    DIM = nr * P
    ncH = tt // chunk
    n512 = chunk // 512
    k.phase()
    x2 = k.sb_tiles(nr, chunk, F32)
    srow = k.sb([1, chunk], F32)
    s2row = k.sb([1, chunk], F32)
    arow = k.sb([1, chunk], F32)
    brow = k.sb([1, chunk], F32)
    trow = k.sb([1, chunk], F32)

    for ch in range(ncH):
        c0 = ch * chunk

        def ac1(eng, c0=c0):
            for di in range(nr):
                inst = eng.activation(out=x2[di][:],
                                      in_=x_tiles[di][:, c0:c0 + chunk],
                                      func=AF.Square)
            k.inc(inst, "ac", 1)
        k.op({"scalar": ac1})

        def pe1(eng, c0=c0):
            for j in range(n512):
                if not rms:
                    for di in range(nr):
                        inst = eng.matmul(
                            k.ps[2 * j][0:1, :], ones_col[:, 0:1],
                            x_tiles[di][:, c0 + j * 512: c0 + (j + 1) * 512],
                            start=(di == 0), stop=(di == nr - 1))
                for di in range(nr):
                    inst = eng.matmul(
                        k.ps[2 * j + 1][0:1, :], ones_col[:, 0:1],
                        x2[di][:, j * 512:(j + 1) * 512],
                        start=(di == 0), stop=(di == nr - 1))
            k.inc(inst, "pe", 1)
        k.op({"tensor": pe1})

        def dv1(eng):
            for j in range(n512):
                cs = slice(j * 512, (j + 1) * 512)
                if not rms:
                    eng.tensor_scalar(out=srow[0:1, cs], in0=k.ps[2 * j][0:1, :],
                                      scalar1=1.0 / DIM, scalar2=None,
                                      op0=ALU.mult)
                eng.tensor_scalar(out=s2row[0:1, cs], in0=k.ps[2 * j + 1][0:1, :],
                                  scalar1=1.0 / DIM, scalar2=eps,
                                  op0=ALU.mult, op1=ALU.add)
            if not rms:
                inst = eng.tensor_mul(out=trow[0:1, :], in0=srow[0:1, :],
                                      in1=srow[0:1, :])
                eng.wait_ge(k.sems["dv"], k.inc(inst, "dv", 1))
                inst = eng.tensor_sub(out=trow[0:1, :], in0=s2row[0:1, :],
                                      in1=trow[0:1, :])
            else:
                inst = eng.tensor_copy(out=trow[0:1, :], in_=s2row[0:1, :])
            k.inc(inst, "dv", 1)
        k.op({"vector": dv1})

        def ac2(eng):
            inst = eng.activation(out=arow[0:1, :], in_=trow[0:1, :],
                                  func=AF.Ln)
            eng.wait_ge(k.sems["ac"], k.inc(inst, "ac", 1))
            inst = eng.activation(out=arow[0:1, :], in_=arow[0:1, :],
                                  func=AF.Exp, scale=-0.5)
            k.inc(inst, "ac", 1)
        k.op({"scalar": ac2})

        if not rms:
            def dv2(eng):
                inst = eng.scalar_tensor_tensor(
                    out=brow[0:1, :], in0=srow[0:1, :], scalar=-1.0,
                    in1=arow[0:1, :], op0=ALU.mult, op1=ALU.mult)
                k.inc(inst, "dv", 1)
            k.op({"vector": dv2})

        def pe2(eng):
            for j in range(n512):
                inst = eng.matmul(k.ps[j][:, :], ones_row[0:1, :],
                                  arow[0:1, j * 512:(j + 1) * 512],
                                  start=True, stop=True)
                if not rms:
                    inst = eng.matmul(k.ps[4 + j][:, :], ones_row[0:1, :],
                                      brow[0:1, j * 512:(j + 1) * 512],
                                      start=True, stop=True)
            k.inc(inst, "pe", 1)
        k.op({"tensor": pe2})

        a_ps = [k.ps[j] for j in range(n512)]
        b_ps = None if rms else [k.ps[4 + j] for j in range(n512)]

        def dv3(eng, c0=c0, a_ps=a_ps, b_ps=b_ps):
            for di in range(nr):
                apply_fn(eng, di, a_ps, b_ps, c0, chunk)
        k.op({"vector": dv3})
    k.unphase()



class Seq:

    def __init__(self, k):
        self.k = k
        self.records = []
        self.snap0 = dict(k.cnt)
        self.last_ms = {}

    def do(self, engine, emit_fn, waits=(), inc=None, incn=1):
        k = self.k
        waits = list(waits)
        if engine != "tensor" and engine in self.last_ms:
            waits.append(self.last_ms[engine])
        val = None
        if inc is not None:
            k.cnt[inc] += incn
            val = k.cnt[inc]
            self.last_ms[engine] = (inc, val)
        self.records.append((engine, waits, emit_fn, inc, incn))
        return val

    def cur(self, sem):
        return self.k.cnt[sem]

    def run(self):
        k = self.k
        engines = {r[0] for r in self.records}
        fn_map = {}
        for en in engines:
            recs = [r for r in self.records if r[0] == en]

            def body(eng, recs=recs):
                last_wait = {}
                for _, waits, emit_fn, inc, incn in recs:
                    for sem, val in waits:
                        if val > 0 and last_wait.get(sem, -1) < val:
                            eng.wait_ge(k.sems[sem], val)
                            last_wait[sem] = val
                    inst = emit_fn(eng)
                    if inc is not None:
                        inst.then_inc(k.sems[inc], incn)
            fn_map[en] = body
        k.op(fn_map, snap=self.snap0)


def _rows(tiles_row0, dh):
    out = []
    r0 = tiles_row0
    left = dh
    while left:
        ti, ro = divmod(r0, P)
        n = min(left, P - ro)
        out.append((ti, ro, n))
        r0 += n
        left -= n
    return out


def op_attention(k, q_tiles, k_tiles, v_tiles, o_tiles, nh, dh, nkv, Tq,
                 mode, scale, cst, qcol0=0):
    QB = 256 if mode == "win256" else 512
    nqb = Tq // QB
    nqt = QB // P
    ndv = cdiv(dh, P)
    k.phase()
    S_sb = k.sb([P, 2048], F32)
    Pdump = k.sb([P, 2048], BF16)
    mcol = k.sb([P, 16], F32)
    den = k.sb([P, 1], F32)
    brow = k.sb([1, 512], F32)
    b2bc = k.sb([P, 512], F32)
    stf = [k.sb([P, 512], F32) for _ in range(2)]
    Pt = [k.sb([P, 512], BF16) for _ in range(4)]

    seq = Seq(k)
    stats_rd = 0
    stats_evac_ms = []
    stt_ms = []
    exp_ms = []
    pv_ms = []
    st_rd = 0
    last_X1 = 0
    last_B1 = 0
    last_B2 = 0
    last_X3 = 0
    last_X4 = 0
    dv_X2 = 0
    last_Oev = 0

    for h in range(nh):
        kvh = h * nkv // nh
        qrows = _rows(h * dh, dh)
        krows = _rows(kvh * dh, dh)
        for qb in range(nqb):
            q0 = qb * QB

            if mode == "full":
                st_subs = [(k0, None) for k0 in range(0, Tq, P)]
                def stats_chunks(qt):
                    return [[(c * 512 + j * P, None) for j in range(4)]
                            for c in range(Tq // 512)]
            elif mode == "causal":
                st_subs = []
                for k0 in range(0, (qb + 1) * 512, P):
                    if q0 >= k0 + P - 1:
                        st_subs.append((k0, None))
                    else:
                        st_subs.append((k0, [("ge", q0 - k0)]))
                def stats_chunks(qt, q0=q0):
                    qa = q0 + qt * P
                    ch = []
                    nfull = qa // 512
                    for c in range(nfull):
                        ch.append([(c * 512 + j * P, None) for j in range(4)])
                    off = qa - nfull * 512
                    ch.append([(nfull * 512 + j * P, ("c", off - j * P))
                               for j in range(4)])
                    return ch
            else:
                lo = max(0, q0 - 256)
                st_subs = []
                for k0 in range(lo, q0 + QB, P):
                    conds = []
                    if q0 - k0 - (P - 1) < 0:
                        conds.append(("ge", q0 - k0))
                    if 256 - ((q0 + QB - 1) - k0) < 0:
                        conds.append(("le256", 256 - (q0 - k0)))
                    st_subs.append((k0, conds or None))
                def stats_chunks(qt, q0=q0):
                    qa = q0 + qt * P
                    segs = []
                    for kk in range(max(0, qa - 256), qa + P, P):
                        off = qa - kk
                        m = ("w", off)
                        segs.append((kk, m))
                    return [segs]

            X1_this = []
            for qt in range(nqt):
                chunks = stats_chunks(qt)
                vis = 0
                evac_of_chunk = []
                for ci, segs in enumerate(chunks):
                    rd = stats_rd
                    bank = k.ps[rd % 2]
                    waits = [("dv", stats_evac_ms[rd - 2])] if rd >= 2 else []

                    def A(eng, segs=segs, bank=bank, qt=qt, qrows=qrows,
                          krows=krows, q0=q0):
                        first = True
                        col = 0
                        for (k0, _m) in segs:
                            for di in range(len(qrows)):
                                ti, ro, n = qrows[di]
                                kti, kro, kn = krows[di]
                                inst = eng.matmul(
                                    bank[:, col:col + P],
                                    q_tiles[ti][ro:ro + n,
                                                qcol0 + q0 + qt * P:
                                                qcol0 + q0 + (qt + 1) * P],
                                    k_tiles[kti][kro:kro + kn,
                                                 qcol0 + k0:qcol0 + k0 + P],
                                    start=(di == 0),
                                    stop=(di == len(qrows) - 1))
                            first = False
                            col += P
                        return inst
                    pe_ms = seq.do("tensor", A, waits=waits, inc="pe")

                    def E1a(eng, segs=segs, bank=bank, vis0=vis):
                        col = 0
                        for (k0, msk) in segs:
                            dst = S_sb[:, vis0 + col: vis0 + col + P]
                            if msk is None:
                                inst = eng.tensor_copy(
                                    out=dst, in_=bank[:, col:col + P])
                            elif msk[0] == "c":
                                off = msk[1]
                                if off >= P:
                                    inst = eng.tensor_copy(
                                        out=dst, in_=bank[:, col:col + P])
                                elif off < 0:
                                    inst = eng.memset(dst, NEG)
                                else:
                                    inst = eng.tensor_tensor(
                                        out=dst, in0=bank[:, col:col + P],
                                        in1=cst["wC"][:, 0:P], op=ALU.add)
                            else:
                                off = msk[1]
                                if off == 256:
                                    inst = eng.tensor_tensor(
                                        out=dst, in0=bank[:, col:col + P],
                                        in1=cst["wA"][:, 0:P], op=ALU.add)
                                elif off == 0:
                                    inst = eng.tensor_tensor(
                                        out=dst, in0=bank[:, col:col + P],
                                        in1=cst["wC"][:, 0:P], op=ALU.add)
                                else:
                                    inst = eng.tensor_copy(
                                        out=dst, in_=bank[:, col:col + P])
                            col += P
                        return inst
                    wts = [("pe", pe_ms)]
                    if ci == 0 and last_X1 > 0:
                        wts.append(("ac", last_X1))
                    ms = seq.do("vector", E1a, waits=wts, inc="dv")
                    stats_evac_ms.append(ms)
                    evac_of_chunk.append(ms)
                    stats_rd += 1
                    vis += len(segs) * P

                def E1b0(eng, qt=qt, vis=vis):
                    return eng.reduce_max(out=mcol[:, 4 * qt:4 * qt + 1],
                                          in_=S_sb[:, 0:vis], axis=AX.X)
                seq.do("vector", E1b0, inc="dv")

                def E1b1(eng, qt=qt):
                    return eng.tensor_scalar_mul(
                        out=mcol[:, 4 * qt + 1:4 * qt + 2],
                        in0=mcol[:, 4 * qt:4 * qt + 1], scalar1=-scale)
                dv_E1b = seq.do("vector", E1b1, inc="dv")

                def X1a(eng, qt=qt, vis=vis):
                    return eng.activation(out=Pdump[:, 0:vis],
                                          in_=S_sb[:, 0:vis],
                                          func=AF.Exp, scale=scale,
                                          bias=mcol[:, 4 * qt + 1:4 * qt + 2],
                                          accum_out=den[:, 0:1])
                seq.do("scalar", X1a, waits=[("dv", dv_E1b)], inc="ac")

                def X1b(eng, qt=qt):
                    return eng.activation(out=mcol[:, 4 * qt + 2:4 * qt + 3],
                                          in_=den[:, 0:1], func=AF.Ln)
                last_X1 = seq.do("scalar", X1b, inc="ac")
                X1_this.append(last_X1)

                def X2(eng, qt=qt):
                    return eng.tensor_sub(
                        out=mcol[:, 4 * qt + 3:4 * qt + 4],
                        in0=mcol[:, 4 * qt + 1:4 * qt + 2],
                        in1=mcol[:, 4 * qt + 2:4 * qt + 3])
                wts = [("ac", last_X1)]
                if last_B1 > 0:
                    wts.append(("pe", last_B1))
                dv_X2 = seq.do("vector", X2, waits=wts, inc="dv")

            def B1(eng, nqt=nqt):
                for qt in range(nqt):
                    inst = eng.transpose(
                        k.ps[6][0:1, qt * P:(qt + 1) * P],
                        mcol[:, 4 * qt + 3:4 * qt + 4], cst["identity"][:])
                return inst
            wts = [("dv", dv_X2)]
            if last_X4 > 0:
                wts.append(("dv", last_X4))
            last_B1 = seq.do("tensor", B1, waits=wts, inc="pe")

            def X3(eng, QB=QB):
                return eng.tensor_copy(out=brow[0:1, 0:QB],
                                       in_=k.ps[6][0:1, 0:QB])
            wts = [("pe", last_B1)]
            if last_B2 > 0:
                wts.append(("pe", last_B2))
            last_X3 = seq.do("vector", X3, waits=wts, inc="dv")

            def B2(eng, QB=QB):
                return eng.matmul(k.ps[6][:, 0:QB], cst["ones_row"][0:1, :],
                                  brow[0:1, 0:QB], start=True, stop=True)
            last_B2 = seq.do("tensor", B2, waits=[("dv", last_X3)], inc="pe")

            def X4(eng, QB=QB):
                return eng.tensor_copy(out=b2bc[:, 0:QB], in_=k.ps[6][:, 0:QB])
            last_X4 = seq.do("vector", X4, waits=[("pe", last_B2)], inc="dv")

            nsub = len(st_subs)
            first_pv = True
            for si, (k0, conds) in enumerate(st_subs):
                g = st_rd
                bank = k.ps[2 + g % 2]

                def ST(eng, k0=k0, bank=bank, QB=QB, q0=q0, qrows=qrows,
                       krows=krows):
                    for di in range(len(qrows)):
                        ti, ro, n = qrows[di]
                        kti, kro, kn = krows[di]
                        inst = eng.matmul(
                            bank[:, 0:QB],
                            k_tiles[kti][kro:kro + kn, qcol0 + k0:qcol0 + k0 + P],
                            q_tiles[ti][ro:ro + n, qcol0 + q0:qcol0 + q0 + QB],
                            start=(di == 0), stop=(di == len(qrows) - 1))
                    return inst
                wts = []
                if g >= 2:
                    wts.append(("dv", stt_ms[g - 2]))
                pe_st = seq.do("tensor", ST, waits=wts, inc="pe")

                def STT(eng, bank=bank, g=g, QB=QB):
                    return eng.scalar_tensor_tensor(
                        out=stf[g % 2][:, 0:QB], in0=bank[:, 0:QB],
                        scalar=scale, in1=b2bc[:, 0:QB],
                        op0=ALU.mult, op1=ALU.add)
                wts = [("pe", pe_st), ("dv", last_X4)]
                if g >= 2:
                    wts.append(("ac", exp_ms[g - 2]))
                dv_stt = seq.do("vector", STT, waits=wts, inc="dv")
                stt_ms.append(dv_stt)

                def EXP(eng, g=g, QB=QB):
                    return eng.activation(out=Pt[g % 4][:, 0:QB],
                                          in_=stf[g % 2][:, 0:QB], func=AF.Exp)
                wts = [("dv", dv_stt)]
                if g >= 4:
                    wts.append(("pe", pv_ms[g - 4]))
                ac_exp = seq.do("scalar", EXP, waits=wts, inc="ac")
                exp_ms.append(ac_exp)

                pv_wait = [("ac", ac_exp)]
                if conds:
                    def MSK(eng, conds=conds, g=g, QB=QB, q0=q0, k0=k0):
                        for c in conds:
                            if c[0] == "ge":
                                inst = eng.affine_select(
                                    out=Pt[g % 4][:, 0:QB],
                                    in_=Pt[g % 4][:, 0:QB],
                                    pattern=[[1, QB]], compare_op=ALU.is_ge,
                                    fill=0.0, base=c[1], channel_multiplier=-1)
                            else:
                                inst = eng.affine_select(
                                    out=Pt[g % 4][:, 0:QB],
                                    in_=Pt[g % 4][:, 0:QB],
                                    pattern=[[-1, QB]], compare_op=ALU.is_ge,
                                    fill=0.0, base=c[1], channel_multiplier=1)
                        return inst
                    pl_ms = seq.do("gpsimd", MSK, waits=[("ac", ac_exp)],
                                   inc="pl")
                    pv_wait = [("pl", pl_ms)]

                def PV(eng, k0=k0, g=g, QB=QB, kvh=kvh, first=first_pv,
                       last=(si == nsub - 1)):
                    kt = (qcol0 + k0) // P
                    for dvt in range(ndv):
                        n = min(P, dh - dvt * P)
                        inst = eng.matmul(
                            k.ps[4 + dvt][0:n, 0:QB],
                            v_tiles[kt][:, kvh * dh + dvt * P:
                                        kvh * dh + dvt * P + n],
                            Pt[g % 4][:, 0:QB],
                            start=first, stop=last)
                    return inst
                wts = list(pv_wait)
                if first_pv and last_Oev > 0:
                    wts.append(("dv", last_Oev))
                pe_pv = seq.do("tensor", PV, waits=wts, inc="pe")
                pv_ms.append(pe_pv)
                first_pv = False
                st_rd += 1

            def OEV(eng, h=h, q0=q0, QB=QB):
                for dvt in range(ndv):
                    n = min(P, dh - dvt * P)
                    for (ti, ro, nn) in _rows(h * dh + dvt * P, n):
                        sub = ro % P
                        inst = eng.tensor_copy(
                            out=o_tiles[ti][ro:ro + nn,
                                            qcol0 + q0:qcol0 + q0 + QB],
                            in_=k.ps[4 + dvt][0:nn, 0:QB])
                return inst
            last_Oev = seq.do("vector", OEV, waits=[("pe", pv_ms[-1])],
                              inc="dv")

    seq.run()
    k.unphase()


def op_qk_prep(k, tiles, nh, dh, tt, cst, l2eps=None, scale_col=None,
               rope_tabs=None, Tb=T):
    nt = len(tiles)
    ntc = tt // 512
    k.phase()
    q2 = [k.sb([P, 512], F32) for _ in range(2)]
    rrow = k.sb([min(nh, P), 512], F32)
    tmp = k.sb([P, 512], F32)
    t2 = k.sb([P, 512], F32)
    seq = Seq(k)
    mm_ms = []
    bc_ms = {}
    rot_ms = {}
    stt_ms = {}
    fin_ms = {}

    for tc in range(ntc):
        cs = slice(tc * 512, (tc + 1) * 512)
        tb0 = (tc * 512) % Tb
        tbc = slice(tb0, tb0 + 512)

        if l2eps is not None:
            for j in range(nt):
                g = tc * nt + j

                def SQ(eng, j=j, cs=cs, g=g):
                    return eng.activation(out=q2[g % 2][:],
                                          in_=tiles[j][:, cs], func=AF.Square)
                wts = [("pe", mm_ms[g - 2])] if g >= 2 else []
                ac_sq = seq.do("scalar", SQ, waits=wts, inc="ac")

                def MM(eng, j=j, g=g, first=(j == 0), last=(j == nt - 1)):
                    return eng.matmul(k.ps[0][0:nh, :],
                                      cst["sel_ssq"][:, j * nh:(j + 1) * nh],
                                      q2[g % 2][:], start=first, stop=last)
                mm_ms.append(seq.do("tensor", MM, waits=[("ac", ac_sq)],
                                    inc="pe"))

            def RN1(eng, cs=cs):
                return eng.tensor_copy(out=rrow[:], in_=k.ps[0][0:nh, :])
            dv_rn1 = seq.do("vector", RN1, waits=[("pe", mm_ms[-1])], inc="dv")

            def RN2(eng):
                return eng.activation(out=rrow[:], in_=rrow[:], func=AF.Sqrt)
            ac_rn2 = seq.do("scalar", RN2, waits=[("dv", dv_rn1)], inc="ac")

            def RN3(eng, l2eps=l2eps):
                return eng.tensor_scalar_max(out=rrow[:], in0=rrow[:],
                                             scalar1=l2eps)
            dv_rn3 = seq.do("vector", RN3, waits=[("ac", ac_rn2)], inc="dv")

            def RN4(eng):
                return eng.reciprocal(out=rrow[:], in_=rrow[:])
            dv_rn4 = seq.do("vector", RN4, inc="dv")

        for j in range(nt):
            g = tc * nt + j
            bank_bc = k.ps[1 + g % 2]
            if l2eps is not None:
                def BC(eng, j=j, bank=bank_bc):
                    return eng.matmul(bank[:, :],
                                      cst["sel_bc"][0:nh, j * P:(j + 1) * P],
                                      rrow[:], start=True, stop=True)
                wts = [("dv", dv_rn4)]
                prev = stt_ms.get((tc, j - 2)) or (stt_ms.get((tc - 1, nt - 2 + j))
                                                   if tc else None)
                if (tc * nt + j) >= 2:
                    pg = g - 2
                    pv = stt_ms.get((pg // nt, pg % nt))
                    if pv:
                        wts.append(("dv", pv))
                bc_ms[(tc, j)] = seq.do("tensor", BC, waits=wts, inc="pe")

                def APPLY(eng, j=j, cs=cs, bank=bank_bc):
                    if scale_col is not None:
                        return eng.scalar_tensor_tensor(
                            out=tiles[j][:, cs], in0=tiles[j][:, cs],
                            scalar=scale_col[j][:, 0:1], in1=bank[:, :],
                            op0=ALU.mult, op1=ALU.mult)
                    return eng.tensor_tensor(
                        out=tiles[j][:, cs], in0=tiles[j][:, cs],
                        in1=bank[:, :], op=ALU.mult)
                wts = [("pe", bc_ms[(tc, j)])]
                rp = rot_ms.get((tc - 1, j)) if tc else None
                stt_ms[(tc, j)] = seq.do("vector", APPLY, waits=wts, inc="dv")

            if rope_tabs is not None:
                A_t, B_t = rope_tabs
                bank_rot = k.ps[3 + g % 2]

                def ROT(eng, j=j, cs=cs, bank=bank_rot):
                    return eng.matmul(bank[:, :], cst["pswap"][:],
                                      tiles[j][:, cs], start=True, stop=True)
                wts = []
                if l2eps is not None:
                    wts.append(("dv", stt_ms[(tc, j)]))
                pg = g - 2
                if pg >= 0:
                    pv = fin_ms.get((pg // nt, pg % nt))
                    if pv:
                        wts.append(("dv", pv))
                rot_ms[(tc, j)] = seq.do("tensor", ROT, waits=wts, inc="pe")

                def RP1(eng, j=j, cs=cs, tbc=tbc):
                    return eng.tensor_tensor(out=tmp[:], in0=tiles[j][:, cs],
                                             in1=A_t[:, tbc], op=ALU.mult)
                wts = [("pe", rot_ms[(tc, j)])]
                dv_rp1 = seq.do("vector", RP1, waits=wts, inc="dv")

                def RP2(eng, j=j, tbc=tbc, bank=bank_rot):
                    return eng.tensor_tensor(out=t2[:], in0=bank[:, :],
                                             in1=B_t[:, tbc], op=ALU.mult)
                dv_rp2 = seq.do("vector", RP2, inc="dv")

                def RP3(eng, j=j, cs=cs):
                    return eng.tensor_add(out=tiles[j][:, cs], in0=tmp[:],
                                          in1=t2[:])
                fin_ms[(tc, j)] = seq.do("vector", RP3, inc="dv")

    seq.run()
    k.unphase()



def op_mamba_conv(k, xr_tiles, xs_t, cst, Tb=T):
    nd = 16
    ntc = Tb // 512
    k.phase()
    acc = k.sb([P, 512], F32)

    def conv_body(eng):
        last_copy = None
        for j in range(nd):
            for tc in range(ntc):
                c0 = 3 + tc * 512
                if last_copy is not None:
                    eng.wait_ge(k.sems["dv"], last_copy)
                inst = eng.tensor_scalar(
                    out=acc[:], in0=xr_tiles[j][:, c0 - 3:c0 - 3 + 512],
                    scalar1=cst["conv_w"][:, 4 * j:4 * j + 1],
                    scalar2=cst["conv_b"][:, j:j + 1],
                    op0=ALU.mult, op1=ALU.add)
                for i in (1, 2, 3):
                    eng.wait_ge(k.sems["dv"], k.inc(inst, "dv", 1))
                    inst = eng.scalar_tensor_tensor(
                        out=acc[:], in0=xr_tiles[j][:, c0 - 3 + i:c0 - 3 + i + 512],
                        scalar=cst["conv_w"][:, 4 * j + i:4 * j + i + 1],
                        in1=acc[:], op0=ALU.mult, op1=ALU.add)
                eng.wait_ge(k.sems["dv"], k.inc(inst, "dv", 1))
                inst = eng.tensor_copy(out=xs_t[j][:, tc * 512:(tc + 1) * 512],
                                       in_=acc[:])
                last_copy = k.inc(inst, "dv", 1)
    k.op({"vector": conv_body})

    def silu_body(eng):
        for j in range(nd):
            inst = eng.activation(out=xs_t[j][:], in_=xs_t[j][:], func=AF.Silu)
            k.inc(inst, "ac", 1)
    k.op({"scalar": silu_body})
    k.unphase()


def op_mamba_scan(k, xs_t, z_hbm, w_xproj_hbm, cst, Tb=T, zcol0=0):
    nd = 16
    ntc = Tb // 512
    y_tiles = xs_t
    k.phase()
    ssm = k.sb([80, Tb], F32)
    delta = k.sb([P, Tb], F32)
    u = k.sb([P, Tb], F32)
    dA = k.sb([P, Tb], F32)
    Pin = k.sb([P, Tb], F32)
    hbuf = k.sb([P, Tb], F32)
    yacc = k.sb([P, Tb], F32)
    tmp = k.sb([P, 512], F32)
    zbuf = [k.sb([P, Tb], BF16) for _ in range(2)]

    def epi(eng, ps_ap, ds, tc, m):
        inst = eng.tensor_copy(out=ssm[0:m, tc * 512:(tc + 1) * 512],
                               in_=ps_ap)
        k.inc(inst, "dv", 1)
    op_linearB(k, xs_t, w_xproj_hbm, nd * P, 80, Tb, epi, w_chunk=80)

    seq = Seq(k)
    dep_ms = {}
    zld_ms = {}
    for j in range(nd):
        def ZLD(eng, j=j):
            return eng.dma_start(out=zbuf[j % 2][:],
                                 in_=z_hbm[j * P:(j + 1) * P,
                                           zcol0:zcol0 + Tb])
        wz = []
        if (j - 2, "zgate") in dep_ms:
            wz.append(dep_ms[(j - 2, "zgate")])
        zld_ms[j] = seq.do("sync", ZLD, waits=wz, inc="qs", incn=16)
        for tc in range(ntc):
            def OUT(eng, j=j, tc=tc):
                return eng.matmul(
                    k.ps[tc][:, :], cst["dtw_row"][0:1, j * P:(j + 1) * P],
                    ssm[0:1, tc * 512:(tc + 1) * 512],
                    start=True, stop=True)
            wts = []
            if (j - 1, tc, "dpre_free") in dep_ms:
                wts.append(dep_ms[(j - 1, tc, "dpre_free")])
            pe_out = seq.do("tensor", OUT, waits=wts, inc="pe")

            def SP1(eng, tc=tc):
                return eng.activation(out=delta[:, tc * 512:(tc + 1) * 512],
                                      in_=k.ps[tc][:, :], func=AF.Abs,
                                      bias=cst["dtb"][:, j:j + 1] if False
                                      else 0.0)
            def SP1b(eng, tc=tc, j=j):
                return eng.activation(out=delta[:, tc * 512:(tc + 1) * 512],
                                      in_=k.ps[tc][:, :], func=AF.Abs,
                                      bias=cst["dtb"][:, j:j + 1])
            w1 = [("pe", pe_out)]
            if (j - 1, "delta_last_reader") in dep_ms and tc == 0:
                w1.append(dep_ms[(j - 1, "delta_last_reader")])
            ac1 = seq.do("scalar", SP1b, waits=w1, inc="ac")

            def SP2(eng, tc=tc):
                cs = slice(tc * 512, (tc + 1) * 512)
                return eng.activation(out=delta[:, cs], in_=delta[:, cs],
                                      func=AF.Exp, scale=-1.0)
            ac2 = seq.do("scalar", SP2, inc="ac")

            def SP3(eng, tc=tc):
                cs = slice(tc * 512, (tc + 1) * 512)
                return eng.activation(out=delta[:, cs], in_=delta[:, cs],
                                      func=AF.Ln, bias=1.0)
            ac3 = seq.do("scalar", SP3, inc="ac")

            def RL1(eng, tc=tc, j=j):
                return eng.tensor_scalar(
                    out=tmp[:], in0=k.ps[tc][:, :],
                    scalar1=cst["dtb"][:, j:j + 1], scalar2=0.0,
                    op0=ALU.add, op1=ALU.max)
            dv1 = seq.do("vector", RL1, waits=[("ac", ac1)], inc="dv")

            def RL2(eng, tc=tc):
                cs = slice(tc * 512, (tc + 1) * 512)
                return eng.tensor_add(out=delta[:, cs], in0=delta[:, cs],
                                      in1=tmp[:])
            dv2 = seq.do("vector", RL2, waits=[("ac", ac3)], inc="dv")
            dep_ms[(j, tc, "dpre_free")] = ("dv", dv1)

        def U(eng, j=j):
            return eng.tensor_mul(out=u[:], in0=delta[:], in1=xs_t[j][:])
        dv_u = seq.do("vector", U, inc="dv")

        def YZ(eng):
            return eng.memset(yacc[:], 0.0)
        dv_yz = seq.do("vector", YZ, inc="dv")

        prev_ac_da = None
        for n in range(16):
            def DA1(eng, j=j, n=n):
                return eng.tensor_scalar_mul(
                    out=dA[:], in0=delta[:],
                    scalar1=cst["A_cols"][:, j * 16 + n:j * 16 + n + 1])
            wda = [("ac", prev_ac_da)] if prev_ac_da else []
            dv_da1 = seq.do("vector", DA1, waits=wda, inc="dv")
            if n == 15:
                dep_ms[(j, "delta_last_reader")] = ("dv", dv_da1)

            def DA2(eng):
                return eng.activation(out=dA[:], in_=dA[:], func=AF.Exp)
            ac_da = seq.do("scalar", DA2, waits=[("dv", dv_da1)], inc="ac")
            prev_ac_da = ac_da

            bc_ms = []
            for tc in range(ntc):
                g2 = (j * 16 + n) * ntc + tc

                def BB(eng, n=n, tc=tc, g2=g2):
                    return eng.matmul(
                        k.ps[4 + g2 % 2][:, :],
                        cst["selB"][0:80, n * P:(n + 1) * P],
                        ssm[0:80, tc * 512:(tc + 1) * 512],
                        start=True, stop=True)
                wts = []
                if ("pin", g2 - 2) in dep_ms:
                    wts.append(dep_ms[("pin", g2 - 2)])
                pe_b = seq.do("tensor", BB, waits=wts, inc="pe")

                def CC(eng, n=n, tc=tc, g2=g2):
                    return eng.matmul(
                        k.ps[6 + g2 % 2][:, :],
                        cst["selC"][0:80, n * P:(n + 1) * P],
                        ssm[0:80, tc * 512:(tc + 1) * 512],
                        start=True, stop=True)
                wts = []
                if ("ym", g2 - 2) in dep_ms:
                    wts.append(dep_ms[("ym", g2 - 2)])
                pe_c = seq.do("tensor", CC, waits=wts, inc="pe")
                bc_ms.append((pe_b, pe_c, g2))

                def PIN(eng, tc=tc, g2=g2):
                    cs = slice(tc * 512, (tc + 1) * 512)
                    return eng.tensor_tensor(out=Pin[:, cs], in0=u[:, cs],
                                             in1=k.ps[4 + g2 % 2][:, :],
                                             op=ALU.mult)
                dv_pin = seq.do("vector", PIN, waits=[("pe", pe_b)], inc="dv")
                dep_ms[("pin", g2)] = ("dv", dv_pin)

            def SCAN(eng):
                return eng.tensor_tensor_scan(
                    out=hbuf[:], data0=dA[:], data1=Pin[:], initial=0.0,
                    op0=ALU.mult, op1=ALU.add)
            dv_scan = seq.do("vector", SCAN, waits=[("ac", ac_da)], inc="dv")

            for tc in range(ntc):
                g2 = bc_ms[tc][2]

                def YM(eng, tc=tc, g2=g2):
                    cs = slice(tc * 512, (tc + 1) * 512)
                    return eng.tensor_tensor(out=hbuf[:, cs], in0=hbuf[:, cs],
                                             in1=k.ps[6 + g2 % 2][:, :],
                                             op=ALU.mult)
                dv_ym = seq.do("vector", YM, waits=[("pe", bc_ms[tc][1])],
                               inc="dv")
                dep_ms[("ym", g2)] = ("dv", dv_ym)

                def YA(eng, tc=tc):
                    cs = slice(tc * 512, (tc + 1) * 512)
                    return eng.tensor_add(out=yacc[:, cs], in0=yacc[:, cs],
                                          in1=hbuf[:, cs])
                seq.do("vector", YA, inc="dv")

        def FD(eng, j=j):
            return eng.scalar_tensor_tensor(
                out=yacc[:], in0=xs_t[j][:], scalar=cst["D_col"][:, j:j + 1],
                in1=yacc[:], op0=ALU.mult, op1=ALU.add)
        dv_fd = seq.do("vector", FD, inc="dv")

        def SZ(eng, j=j):
            return eng.activation(out=y_tiles[j][:], in_=zbuf[j % 2][:],
                                  func=AF.Silu)
        ac_sz = seq.do("scalar", SZ, waits=[("qs", zld_ms[j]),
                                            ("dv", dv_fd)], inc="ac")

        def YG(eng, j=j):
            return eng.tensor_mul(out=y_tiles[j][:], in0=yacc[:],
                                  in1=y_tiles[j][:])
        dv_yg = seq.do("vector", YG, waits=[("ac", ac_sz), ("dv", dv_fd)],
                       inc="dv")
        dep_ms[(j, "zgate")] = ("ac", ac_sz)

    seq.run()
    k.unphase()


"""Orchestration: host prep + per-sublayer program builders + kernel().

Multi-launch version: one launch per top-level sublayer application.
R state [1024, 4096] fp32 per core flows between launches host-side.
"""
import numpy as np
import ml_dtypes
from contextlib import ExitStack
import concourse.bass as bass
import concourse.mybir as mybir
from concourse.bass_utils import run_bass_kernel_spmd


BFd = ml_dtypes.bfloat16
NCORE = 8



def host_consts():
    c = {}
    c["identity"] = np.eye(P, dtype=np.float32)
    qi, kj = np.arange(P)[:, None], np.arange(P)[None, :]
    c["wC"] = np.where(qi >= kj, 0.0, NEG).astype(np.float32)
    c["wA"] = np.where(kj >= qi, 0.0, NEG).astype(np.float32)
    ssq = np.zeros((P, 8 * 16), np.float32)
    sbc = np.zeros((16, 8 * P), np.float32)
    for j in range(8):
        for hf in range(2):
            h = 2 * j + hf
            ssq[hf * 64:(hf + 1) * 64, j * 16 + h] = 1.0
            sbc[h, j * P + hf * 64: j * P + (hf + 1) * 64] = 1.0
    c["sel_ssq"] = ssq
    c["sel_bc"] = sbc
    ssq4 = np.zeros((P, 8 * 4), np.float32)
    sbc4 = np.zeros((4, 8 * P), np.float32)
    for j in range(8):
        h = j // 2
        ssq4[:, j * 4 + h] = 1.0
        sbc4[h, j * P:(j + 1) * P] = 1.0
    c["sel_ssq4"] = ssq4
    c["sel_bc4"] = sbc4
    psw = np.zeros((P, P), np.float32)
    for blk in range(2):
        for r in range(64):
            psw[blk * 64 + (r + 32) % 64, blk * 64 + r] = 1.0
    c["pswap"] = psw
    DIM_HEAD, WINDOW, n = 64, 256, T
    half = np.arange(0, DIM_HEAD, 2, dtype=np.float32)
    inv_freq = 1.0 / 10000.0 ** (half / DIM_HEAD)
    t = np.arange(n, dtype=np.float32)
    freqs = np.concatenate([t[:, None] * inv_freq[None, :]] * 2, -1)
    base_scale = (half + 0.4 * DIM_HEAD) / (1.4 * DIM_HEAD)
    power = (t - n // 2) / (WINDOW // 2)
    scale = np.concatenate([base_scale[None, :] ** power[:, None]] * 2, -1)
    sgn = np.where(np.arange(DIM_HEAD) < 32, -1.0, 1.0)[None, :]

    def tilep(tab):
        return np.concatenate([tab.T, tab.T], 0).astype(np.float32)
    c["xpos_qA"] = tilep(np.cos(freqs) * scale)
    c["xpos_qB"] = tilep(np.sin(freqs) * scale * sgn)
    c["xpos_kA"] = tilep(np.cos(freqs) / scale)
    c["xpos_kB"] = tilep(np.sin(freqs) / scale * sgn)
    inv2 = 1.0 / 100000.0 ** (np.arange(0, 64, 2, dtype=np.float32) / 64)
    f2 = t[:, None] * inv2[None, :]
    A2 = np.concatenate([np.cos(f2), np.cos(f2)], 1)
    B2 = np.concatenate([-np.sin(f2), np.sin(f2)], 1)
    c["rope1_A"] = tilep(A2 * 1.0)
    c["rope1_B"] = tilep(B2 * 1.0)
    selB = np.zeros((80, 16 * P), np.float32)
    selC = np.zeros((80, 16 * P), np.float32)
    for nn in range(16):
        selB[32 + nn, nn * P:(nn + 1) * P] = 1.0
        selC[64 + nn, nn * P:(nn + 1) * P] = 1.0
    c["selB"] = selB
    c["selC"] = selC
    es = np.zeros((32, 9 * P), np.float32)
    for e in range(9):
        es[e, e * P:(e + 1) * P] = 1.0
    c["exp_sel"] = es
    return c


def col_layout(v):
    n = v.shape[0] // P
    return v.reshape(n, P).T.copy().astype(np.float32)


def pad_cols(w, to):
    out = np.zeros((w.shape[0], to), w.dtype)
    out[:, :w.shape[1]] = w
    return out


def pad_rows(w, to):
    out = np.zeros((to, w.shape[1]), w.dtype)
    out[:w.shape[0]] = w
    return out


def host_weights(inp):
    w = {}
    for i in range(6):
        w[f"f{i}_wqkv"] = np.ascontiguousarray(inp["L0_ffn_Wqkv"][i])
        w[f"f{i}_wo"] = np.ascontiguousarray(inp["L0_ffn_Wo"][i])
        W1 = inp["L0_ffn_W1"][i]
        w[f"f{i}_w1a"] = pad_cols(np.ascontiguousarray(W1[:, :2730]), 2816)
        w[f"f{i}_w1g"] = pad_cols(np.ascontiguousarray(W1[:, 2730:]), 2816)
        b1 = inp["L0_ffn_b1"][i]
        w[f"f{i}_b1a"] = col_layout(pad_rows(b1[:2730].reshape(-1, 1), 2816)[:, 0])
        w[f"f{i}_b1g"] = col_layout(pad_rows(b1[2730:].reshape(-1, 1), 2816)[:, 0])
        w[f"f{i}_w2"] = pad_rows(np.ascontiguousarray(inp["L0_ffn_W2"][i]), 2816)
        w[f"f{i}_b2"] = col_layout(inp["L0_ffn_b2"][i])
        w[f"f{i}_lng"] = col_layout(inp["L0_ffn_ln_g"][i])
        w[f"f{i}_lnb"] = col_layout(inp["L0_ffn_ln_b"][i])
        w[f"f{i}_qs"] = np.concatenate([inp["L0_ffn_q_scale"][i]] * 2).reshape(P, 1).astype(np.float32)
        w[f"f{i}_ks"] = np.concatenate([inp["L0_ffn_k_scale"][i]] * 2).reshape(P, 1).astype(np.float32)
    w["m_norm_m"] = col_layout(inp["L0_attn_norm_m_w"])
    w["m_inproj"] = np.ascontiguousarray(inp["L0_attn_in_proj_w"].T)
    cw = inp["L0_attn_conv_w"][:, 0, :]
    w["m_cw"] = cw.reshape(16, P, 4).transpose(1, 0, 2).reshape(P, 64).copy()
    w["m_cb"] = col_layout(inp["L0_attn_conv_b"])
    xpw = inp["L0_attn_x_proj_w"]
    WXP = np.zeros((2048, 80), np.float32)
    WXP[:, 0] = xpw[32]
    WXP[:, 32:48] = xpw[0:16].T
    WXP[:, 64:80] = xpw[16:32].T
    w["m_wxp"] = WXP
    w["m_dtw"] = inp["L0_attn_dt_w"].reshape(1, 2048).astype(np.float32)
    w["m_dtb"] = col_layout(inp["L0_attn_dt_b"])
    w["m_alog"] = inp["L0_attn_A_log"].reshape(16, P, 16).transpose(1, 0, 2).reshape(P, 256).copy()
    w["m_dc"] = col_layout(inp["L0_attn_D_skip"])
    w["m_outproj"] = np.ascontiguousarray(inp["L0_attn_out_proj_w"].T)
    w["m_norm_attn"] = col_layout(inp["L0_attn_norm_attn_w"])
    w["m_qw"] = np.ascontiguousarray(inp["L0_attn_q_w"].T)
    w["m_kw"] = np.ascontiguousarray(inp["L0_attn_k_w"].T)
    w["m_vw"] = np.ascontiguousarray(inp["L0_attn_v_w"].T)
    w["m_ow"] = np.ascontiguousarray(inp["L0_attn_o_w"].T)
    w["m_norm_ffn"] = col_layout(inp["L0_attn_norm_ffn_w"])
    w["m_gate"] = pad_cols(np.ascontiguousarray(inp["L0_attn_gate_w"].T), 2816)
    w["m_up"] = pad_cols(np.ascontiguousarray(inp["L0_attn_up_w"].T), 2816)
    w["m_down"] = pad_rows(np.ascontiguousarray(inp["L0_attn_down_w"].T), 2816)
    w["a_wq"] = np.ascontiguousarray(inp["L1_attn_Wq"].T)
    w["a_wk"] = np.ascontiguousarray(inp["L1_attn_Wk"].T)
    w["a_wv"] = np.ascontiguousarray(inp["L1_attn_Wv"].T)
    w["a_wo"] = np.ascontiguousarray(inp["L1_attn_Wo"].T)
    w["a_bo"] = col_layout(inp["L1_attn_bo"])
    w["a_qs"] = inp["L1_attn_q_scale"].reshape(1024)
    w["a_qs"] = w["a_qs"].reshape(8, P, 1).transpose(1, 0, 2).reshape(P, 8).copy()
    w["a_ks"] = inp["L1_attn_k_scale"].reshape(1024)
    w["a_ks"] = w["a_ks"].reshape(8, P, 1).transpose(1, 0, 2).reshape(P, 8).copy()
    w["g_ln1"] = col_layout(inp["L1_ffn_ln1_w"])
    w["g_ln2"] = col_layout(inp["L1_ffn_ln2_w"])
    w["g_wq"] = np.ascontiguousarray(inp["L1_ffn_Wq"])
    w["g_wkv"] = np.ascontiguousarray(inp["L1_ffn_Wkv"])
    w["g_wkup"] = np.ascontiguousarray(
        inp["L1_ffn_Wk_up"].transpose(1, 0, 2).reshape(256, 1024))
    w["g_wvup"] = np.ascontiguousarray(
        inp["L1_ffn_Wv_up"].transpose(1, 0, 2).reshape(256, 1024))
    w["g_wo"] = np.ascontiguousarray(inp["L1_ffn_Wo"])
    w["g_wr"] = np.ascontiguousarray(inp["L1_ffn_Wr"])
    w["g_rb"] = inp["L1_ffn_r_bias"].reshape(1, 8).astype(np.float32)
    w["g_sw1"] = np.ascontiguousarray(inp["L1_ffn_sw1"])
    w["g_sw2"] = np.ascontiguousarray(inp["L1_ffn_sw2"])
    w["g_sw3"] = np.ascontiguousarray(inp["L1_ffn_sw3"])
    for e in range(8):
        w[f"g_ew1_{e}"] = np.ascontiguousarray(inp["L1_ffn_ew1"][e])
        w[f"g_ew2_{e}"] = np.ascontiguousarray(inp["L1_ffn_ew2"][e])
        w[f"g_ew3_{e}"] = np.ascontiguousarray(inp["L1_ffn_ew3"][e])
    return w



CONST_SPECS = {
    "identity": ([P, P], F32), "wC": ([P, P], F32), "wA": ([P, P], F32),
    "sel_ssq": ([P, 8 * 16], F32), "sel_bc": ([16, 8 * P], F32),
    "sel_ssq4": ([P, 8 * 4], F32), "sel_bc4": ([4, 8 * P], F32),
    "pswap": ([P, P], BF16),
    "xpos_qA": ([P, T], BF16), "xpos_qB": ([P, T], BF16),
    "xpos_kA": ([P, T], BF16), "xpos_kB": ([P, T], BF16),
    "rope1_A": ([P, T], BF16), "rope1_B": ([P, T], BF16),
    "selB": ([80, 16 * P], F32), "selC": ([80, 16 * P], F32),
    "exp_sel": ([32, 9 * P], BF16),
}


BASE_CONSTS = ["identity", "wC", "wA"]


def mk_ker(nc, ctx, im, hostc):
    block = ctx.enter_context(nc.Block())
    k = Ker(nc, block, ctx)
    k._const_aps = {}
    k._w_aps = {}
    base = {}
    pairs = []
    for nm in BASE_CONSTS:
        shape, dt = CONST_SPECS[nm]
        ap = const_ap(k, nc, im, hostc, nm)
        sb = k.sb(shape, dt, "cst_" + nm)
        base[nm] = sb
        pairs.append((sb[:], ap))
    base["ones_col"] = k.sb([P, 1], F32, "ones_col")
    base["ones_row"] = k.sb([1, P], F32, "ones_row")

    def init(eng):
        eng.memset(base["ones_col"][:], 1.0)
        inst = eng.memset(base["ones_row"][:], 1.0)
        k.inc(inst, "pl", 1)
    k.op({"gpsimd": init})
    op_load(k, pairs, queue="qs")
    k._base_cst = base
    return k


def const_ap(k, nc, im, hostc, nm):
    if nm in k._const_aps:
        return k._const_aps[nm]
    shape, dt = CONST_SPECS[nm]
    ap = nc.dram_tensor("c_" + nm, list(shape), F32,
                        kind="ExternalInput").ap()
    k._const_aps[nm] = ap
    im["c_" + nm] = np.ascontiguousarray(hostc[nm]).astype(np.float32)
    return ap


def sec_consts(k, nc, im, hostc, names):
    cst = dict(k._base_cst)
    pairs_qs, pairs_qg = [], []
    for nm in names:
        if nm in BASE_CONSTS:
            continue
        shape, dt = CONST_SPECS[nm]
        ap = const_ap(k, nc, im, hostc, nm)
        sb = k.sb(shape, dt)
        cst[nm] = sb
        (pairs_qg if dt == BF16 else pairs_qs).append((sb[:], ap))
    if pairs_qs:
        op_load(k, pairs_qs, queue="qs")
    if pairs_qg:
        op_load(k, pairs_qg, queue="qg")
    return cst


def scratch_ap(k, nc, name, shape, dtype):
    if not hasattr(k, "_s_aps"):
        k._s_aps = {}
    if name in k._s_aps:
        return k._s_aps[name]
    ap = nc.dram_tensor(name, list(shape), dtype, kind="Internal").ap()
    k._s_aps[name] = ap
    return ap


def weight_ap(k, nc, im, hostw, name, shape):
    if name in k._w_aps:
        return k._w_aps[name]
    ap = nc.dram_tensor(name, list(shape), F32, kind="ExternalInput").ap()
    k._w_aps[name] = ap
    im[name] = hostw[name]
    return ap


def win(nc, name, shape, dtype=F32):
    return nc.dram_tensor(name, list(shape), dtype, kind="ExternalInput").ap()


def wcol_sb(k, nc, name, arr_shape):
    ap = nc.dram_tensor(name, list(arr_shape), F32, kind="ExternalInput").ap()
    sb = k.sb(list(arr_shape), F32, name + "_sb")
    return sb, ap


def ld_shifted_bf16(k, dst_tiles, X_ap, b, upper_from=4):
    c0 = b * T

    def body(eng):
        for j in range(upper_from, 8):
            inst = eng.memset(dst_tiles[j][:, 0:1], 0.0)
        k.inc(inst, "pl", 1)
        for j in range(8):
            if j < upper_from:
                inst = eng.dma_start(out=dst_tiles[j][:],
                                     in_=X_ap[j * P:(j + 1) * P, c0:c0 + T])
            else:
                inst = eng.dma_start(
                    out=dst_tiles[j][:, 1:T],
                    in_=X_ap[j * P:(j + 1) * P, c0:c0 + T - 1])
            k.inc(inst, "qg", 16)
    k.op({"gpsimd": body})


def ld_tiles(k, dst_tiles, X_ap, b, n=8, queue="qg", Tb=T):
    c0 = b * Tb

    def body(eng):
        for j in range(n):
            inst = eng.dma_start(out=dst_tiles[j][:],
                                 in_=X_ap[j * P:(j + 1) * P, c0:c0 + Tb])
            k.inc(inst, queue, 16)
    k.op({QUEUE_ENG[queue]: body})


def st_tiles(k, src_tiles, X_ap, b, n=8, queue="qg", Tb=T):
    c0 = b * Tb

    def body(eng):
        for j in range(n):
            inst = eng.dma_start(out=X_ap[j * P:(j + 1) * P, c0:c0 + Tb],
                                 in_=src_tiles[j][:])
            k.inc(inst, queue, 16)
    k.op({QUEUE_ENG[queue]: body})




def norm_to_bf16(k, x_tiles, out_tiles, cst, rms=False, eps=1e-5,
                 gain_col=None, bias_col=None, chunk=512, nr=8):
    k.phase()
    tmp = k.sb([P, 512], F32)
    tmp2 = k.sb([P, 512], F32)

    def apply_fn(eng, di, a_ps, b_ps, c0, w):
        for j in range(w // 512):
            cs = slice(c0 + j * 512, c0 + (j + 1) * 512)
            if rms:
                if gain_col is not None:
                    inst = eng.scalar_tensor_tensor(
                        out=out_tiles[di][:, cs], in0=x_tiles[di][:, cs],
                        scalar=gain_col[:, di:di + 1], in1=a_ps[j][:],
                        op0=ALU.mult, op1=ALU.mult)
                else:
                    inst = eng.tensor_tensor(
                        out=out_tiles[di][:, cs], in0=x_tiles[di][:, cs],
                        in1=a_ps[j][:], op=ALU.mult)
                k.inc(inst, "dv", 1)
            else:
                inst = eng.tensor_tensor(out=tmp[:], in0=x_tiles[di][:, cs],
                                         in1=a_ps[j][:], op=ALU.mult)
                eng.wait_ge(k.sems["dv"], k.inc(inst, "dv", 1))
                if gain_col is None:
                    inst = eng.tensor_tensor(out=out_tiles[di][:, cs],
                                             in0=tmp[:], in1=b_ps[j][:],
                                             op=ALU.add)
                    k.inc(inst, "dv", 1)
                else:
                    inst = eng.tensor_tensor(out=tmp2[:], in0=tmp[:],
                                             in1=b_ps[j][:], op=ALU.add)
                    eng.wait_ge(k.sems["dv"], k.inc(inst, "dv", 1))
                    inst = eng.tensor_scalar(
                        out=out_tiles[di][:, cs], in0=tmp2[:],
                        scalar1=gain_col[:, di:di + 1],
                        scalar2=bias_col[:, di:di + 1],
                        op0=ALU.mult, op1=ALU.add)
                    k.inc(inst, "dv", 1)
    op_norm(k, x_tiles, nr, x_tiles[0].shape[1] if hasattr(x_tiles[0], "shape")
            else T, cst["ones_col"], cst["ones_row"], apply_fn, rms=rms,
            eps=eps, chunk=chunk)
    k.unphase()


def op_hmul(k, a_tiles, b_tiles, n):
    def body(eng):
        for i in range(n):
            inst = eng.tensor_mul(out=a_tiles[i][:], in0=a_tiles[i][:],
                                  in1=b_tiles[i][:])
            k.inc(inst, "dv", 1)
    k.op({"vector": body})


def mk_epi_copy(k, out_tiles, row_off=0, col_off=0, eng_sem="dv"):
    def epi(eng, ps_ap, ds, tc, m):
        ti, ro = divmod(row_off + ds * P, P)
        inst = eng.tensor_copy(
            out=out_tiles[ti][ro:ro + m, col_off + tc * 512:
                              col_off + (tc + 1) * 512], in_=ps_ap)
        k.inc(inst, eng_sem, 1)
    return epi


def mk_epiA_copy(k, out_tiles, tile_off=0, col_off=0):
    def epi(eng, ps_ap, ti, dc, n):
        inst = eng.tensor_copy(
            out=out_tiles[tile_off + ti][:, col_off + dc * 512:
                                         col_off + dc * 512 + n], in_=ps_ap)
        k.inc(inst, "dv", 1)
    return epi


def mk_epi_scale(k, out_tiles, mul, col_off=0):
    def epi(eng, ps_ap, ds, tc, m):
        inst = eng.tensor_scalar_mul(
            out=out_tiles[ds][0:m, col_off + tc * 512:col_off + (tc + 1) * 512],
            in0=ps_ap, scalar1=mul)
        k.inc(inst, "dv", 1)
    return epi


def mk_epi_bias(k, out_tiles, bias_col, col_off=0):
    def epi(eng, ps_ap, ds, tc, m):
        inst = eng.tensor_scalar(
            out=out_tiles[ds][0:m, col_off + tc * 512:col_off + (tc + 1) * 512],
            in0=ps_ap, scalar1=bias_col[:, ds:ds + 1], scalar2=None,
            op0=ALU.add)
        k.inc(inst, "dv", 1)
    return epi


def mk_epi_act(k, out_tiles, func, bias_col=None, col_off=0):
    def epi(eng, ps_ap, ds, tc, m):
        kw = {}
        if bias_col is not None:
            kw["bias"] = bias_col[:, ds:ds + 1]
        inst = eng.activation(
            out=out_tiles[ds][0:m, col_off + tc * 512:col_off + (tc + 1) * 512],
            in_=ps_ap, func=func, **kw)
        k.inc(inst, "ac", 1)
    return epi


def mk_epi_residual(k, r_tiles, alpha=1.0, bias_col=None, col_off=0):
    tmp = k.scratch

    def epi(eng, ps_ap, ds, tc, m):
        cs = slice(col_off + tc * 512, col_off + (tc + 1) * 512)
        if bias_col is None:
            inst = eng.scalar_tensor_tensor(
                out=r_tiles[ds][0:m, cs], in0=ps_ap, scalar=alpha,
                in1=r_tiles[ds][0:m, cs], op0=ALU.mult, op1=ALU.add)
            k.inc(inst, "dv", 1)
        else:
            inst = eng.tensor_scalar(out=tmp[0:m, :], in0=ps_ap,
                                     scalar1=bias_col[:, ds:ds + 1],
                                     scalar2=None, op0=ALU.add)
            eng.wait_ge(k.sems["dv"], k.inc(inst, "dv", 1))
            inst = eng.scalar_tensor_tensor(
                out=r_tiles[ds][0:m, cs], in0=tmp[0:m, :], scalar=alpha,
                in1=r_tiles[ds][0:m, cs], op0=ALU.mult, op1=ALU.add)
            k.inc(inst, "dv", 1)
    return epi



def emit_l0ffn(k, nc, im, hostc, hostw, R, Rout, nlayers=6):
    names = ["sel_ssq", "sel_bc", "pswap",
             "xpos_qA", "xpos_qB", "xpos_kA", "xpos_kB"]
    k.phase()
    if True:
        cst = sec_consts(k, nc, im, hostc, names)
        X = scratch_ap(k, nc, "Xscr", [D, TT], F32)
        wq_aps, cols = {}, {}
        colpairs = []
        for i in range(6):
            for nm, shp in [("wqkv", [D, 3072]), ("wo", [D, D]),
                            ("w1a", [D, 2816]), ("w1g", [D, 2816]),
                            ("w2", [2816, D])]:
                wq_aps[(i, nm)] = weight_ap(k, nc, im, hostw,
                                            f"f{i}_{nm}", shp)
            for nm, shp in [("b1a", [P, 22]), ("b1g", [P, 22]),
                            ("b2", [P, 8]), ("lng", [P, 8]), ("lnb", [P, 8]),
                            ("qs", [P, 1]), ("ks", [P, 1])]:
                ap = weight_ap(k, nc, im, hostw, f"f{i}_{nm}", shp)
                sb = k.sb(list(shp), F32)
                cols[(i, nm)] = sb
                colpairs.append((sb[:], ap))
        op_load(k, colpairs, queue="qs")

        for b in range(NB):
            k.phase()
            rb = k.sb_tiles(8, T, F32)
            xf = k.sb_tiles(8, T, F32)
            ld_tiles(k, rb, R, b, queue="qs")
            k.phase()
            tmp = k.sb([P, 512], F32)

            def apply_fn(eng, di, a_ps, b_ps, c0, w):
                for j in range(w // 512):
                    cs = slice(c0 + j * 512, c0 + (j + 1) * 512)
                    inst = eng.tensor_tensor(out=tmp[:], in0=rb[di][:, cs],
                                             in1=a_ps[j][:], op=ALU.mult)
                    eng.wait_ge(k.sems["dv"], k.inc(inst, "dv", 1))
                    inst = eng.tensor_tensor(out=xf[di][:, cs], in0=tmp[:],
                                             in1=b_ps[j][:], op=ALU.add)
                    k.inc(inst, "dv", 1)
            op_norm(k, rb, 8, T, cst["ones_col"], cst["ones_row"], apply_fn,
                    rms=False, chunk=512)
            k.unphase()
            st_tiles(k, xf, X, b, queue="qs")
            k.unphase()

        for i in range(nlayers):
            for b in range(NB):
                k.phase()
                ot = k.sb_tiles(8, T, BF16)
                k.phase()
                vtok = k.sb_tiles(16, 1024, BF16)
                qk = k.sb_tiles(16, T, BF16)
                k.phase()
                xs = k.sb_tiles(8, T, BF16)
                wv_sb = k.sb_tiles(8, 512, BF16)
                ld_shifted_bf16(k, xs, X, b)
                for dc2 in range(2):
                    op_load(k, [(wv_sb[j][:],
                                 wq_aps[(i, "wqkv")][j * P:(j + 1) * P,
                                                     2048 + dc2 * 512:
                                                     2560 + dc2 * 512])
                                for j in range(8)], queue="qg")
                    op_linearA(k, xs, wv_sb, D, 512, T,
                               mk_epiA_copy(k, vtok, col_off=dc2 * 512))
                op_linearB(k, xs, wq_aps[(i, "wqkv")][:, 0:2048], D, 2048, T,
                           mk_epi_copy(k, qk), w_chunk=256)
                k.unphase()
                op_qk_prep(k, qk[0:8], 16, 64, T, cst, l2eps=1e-12,
                           scale_col=[cols[(i, "qs")]] * 8,
                           rope_tabs=(cst["xpos_qA"], cst["xpos_qB"]))
                op_qk_prep(k, qk[8:16], 16, 64, T, cst, l2eps=1e-12,
                           scale_col=[cols[(i, "ks")]] * 8,
                           rope_tabs=(cst["xpos_kA"], cst["xpos_kB"]))
                op_attention(k, qk[0:8], qk[8:16], vtok, ot, 16, 64, 16, T,
                             "win256", 8.0, cst)
                k.unphase()
                k.phase()
                xb = k.sb_tiles(8, T, F32)
                ld_tiles(k, xb, X, b, queue="qs")
                op_linearB(k, ot, wq_aps[(i, "wo")], D, D, T,
                           mk_epi_residual(k, xb))
                st_tiles(k, xb, X, b, queue="qs")
                k.unphase()
                k.unphase()

            for b in range(NB):
                k.phase()
                yout = k.sb_tiles(8, T, BF16)
                k.phase()
                xn = k.sb_tiles(8, T, BF16)
                k.phase()
                xsf = k.sb_tiles(8, T, F32)
                ld_shifted_bf16(k, xsf, X, b)
                norm_to_bf16(k, xsf, xn, cst, rms=False, eps=1e-5,
                             gain_col=cols[(i, "lng")],
                             bias_col=cols[(i, "lnb")], chunk=512)
                k.unphase()
                k.phase()
                gt = [k.sb([P, T], BF16) for _ in range(22)]
                op_linearB(k, xn, wq_aps[(i, "w1g")], D, 2816, T,
                           mk_epi_act(k, gt, AF.Gelu, cols[(i, "b1g")]),
                           epi_eng="scalar")

                def epi_amul(eng, ps_ap, ds, tc, m, i=i):
                    inst = eng.tensor_scalar(
                        out=k.scratch[0:m, :], in0=ps_ap,
                        scalar1=cols[(i, "b1a")][:, ds:ds + 1], scalar2=None,
                        op0=ALU.add)
                    eng.wait_ge(k.sems["dv"], k.inc(inst, "dv", 1))
                    inst = eng.tensor_mul(
                        out=gt[ds][0:m, tc * 512:(tc + 1) * 512],
                        in0=gt[ds][0:m, tc * 512:(tc + 1) * 512],
                        in1=k.scratch[0:m, :])
                    k.inc(inst, "dv", 1)
                op_linearB(k, xn, wq_aps[(i, "w1a")], D, 2816, T, epi_amul,
                           epi_incs=2)
                op_linearB(k, gt, wq_aps[(i, "w2")], 2816, D, T,
                           mk_epi_bias(k, yout, cols[(i, "b2")]),
                           w_chunk=256)
                k.unphase()
                k.unphase()
                k.phase()
                xb = k.sb_tiles(8, T, F32)
                ld_tiles(k, xb, X, b, queue="qs")

                def addres(eng):
                    for j in range(8):
                        inst = eng.tensor_add(out=xb[j][:], in0=xb[j][:],
                                              in1=yout[j][:])
                        k.inc(inst, "dv", 1)
                k.op({"vector": addres})
                st_tiles(k, xb, X, b, queue="qs")
                k.unphase()
                k.unphase()

        for b in range(NB):
            k.phase()
            rb = k.sb_tiles(8, T, F32)
            xb = k.sb_tiles(8, T, F32)
            ld_tiles(k, rb, R, b, queue="qs")
            ld_tiles(k, xb, X, b, queue="qg")

            def addf(eng):
                for j in range(8):
                    inst = eng.scalar_tensor_tensor(
                        out=rb[j][:], in0=xb[j][:], scalar=0.5, in1=rb[j][:],
                        op0=ALU.mult, op1=ALU.add)
                    k.inc(inst, "dv", 1)
            k.op({"vector": addf})
            st_tiles(k, rb, Rout, b, queue="qs")
            k.unphase()
    k.unphase()


def build_l0ffn(hostc, hostw, nlayers=6):
    nc = bass.Bass("TRN2", target_bir_lowering=False)
    ctx = ExitStack()
    im = {}
    with ctx:
        k = mk_ker(nc, ctx, im, hostc)
        R = win(nc, "R", [D, TT])
        Rout = nc.dram_tensor("Rout", [D, TT], F32, kind="ExternalOutput").ap()
        emit_l0ffn(k, nc, im, hostc, hostw, R, Rout, nlayers)
        k.finish()
    return nc, im



def emit_l0attn(k, nc, im, hostc, hostw, R, Rout):
    names = ["selB", "selC"]
    k.phase()
    if True:
        cst = sec_consts(k, nc, im, hostc, names)
        X = scratch_ap(k, nc, "Xscr", [D, TT], F32)
        Z = scratch_ap(k, nc, "Zscr", [2048, TT], BF16)
        XN = scratch_ap(k, nc, "XNscr", [D, TT], BF16)
        waps = {}
        for nm, shp in [("inproj", [D, 4096]), ("wxp", [2048, 80]),
                        ("outproj", [2048, D]), ("qw", [D, D]),
                        ("kw", [D, 512]), ("vw", [D, 512]), ("ow", [D, D]),
                        ("gate", [D, 2816]), ("up", [D, 2816]),
                        ("down", [2816, D])]:
            waps[nm] = weight_ap(k, nc, im, hostw, "m_" + nm, shp)
        mcols = {}
        colpairs = []
        for nm, shp in [("norm_m", [P, 8]), ("cw", [P, 64]), ("cb", [P, 16]),
                        ("dtw", [1, 2048]), ("dtb", [P, 16]),
                        ("alog", [P, 256]), ("dc", [P, 16]),
                        ("norm_attn", [P, 8]), ("norm_ffn", [P, 8])]:
            ap = weight_ap(k, nc, im, hostw, "m_" + nm, shp)
            sb = k.sb(list(shp), F32)
            mcols[nm] = sb
            colpairs.append((sb[:], ap))
        op_load(k, colpairs, queue="qs")
        A_cols = k.sb([P, 256], F32, "A_cols")

        def acomp(eng):
            inst = eng.activation(out=A_cols[:], in_=mcols["alog"][:],
                                  func=AF.Exp)
            k.inc(inst, "ac", 1)
        k.op({"scalar": acomp})

        def aneg(eng):
            inst = eng.tensor_scalar_mul(out=A_cols[:], in0=A_cols[:],
                                         scalar1=-1.0)
            k.inc(inst, "dv", 1)
        k.op({"vector": aneg})
        mcst = {"conv_w": mcols["cw"], "conv_b": mcols["cb"],
                "dtw_row": mcols["dtw"], "dtb": mcols["dtb"],
                "selB": cst["selB"], "selC": cst["selC"],
                "A_cols": A_cols, "D_col": mcols["dc"]}

        for b in range(NB):
            k.phase()
            rb = k.sb_tiles(8, T, F32)
            xf = k.sb_tiles(8, T, F32)
            ld_tiles(k, rb, R, b, queue="qs")
            k.phase()
            tmp = k.sb([P, 512], F32)

            def apply_fn(eng, di, a_ps, b_ps, c0, w, rb=rb, xf=xf, tmp=tmp):
                for j in range(w // 512):
                    cs = slice(c0 + j * 512, c0 + (j + 1) * 512)
                    inst = eng.tensor_tensor(out=tmp[:], in0=rb[di][:, cs],
                                             in1=a_ps[j][:], op=ALU.mult)
                    eng.wait_ge(k.sems["dv"], k.inc(inst, "dv", 1))
                    inst = eng.tensor_tensor(out=xf[di][:, cs], in0=tmp[:],
                                             in1=b_ps[j][:], op=ALU.add)
                    k.inc(inst, "dv", 1)
            op_norm(k, rb, 8, T, cst["ones_col"], cst["ones_row"], apply_fn,
                    rms=False, chunk=512)
            k.unphase()
            st_tiles(k, xf, X, b, queue="qs")
            k.unphase()

        for b in range(NB):
            k.phase()
            xnm = k.sb_tiles(8, T, BF16)
            k.phase()
            xb = k.sb_tiles(8, T, F32)
            ld_tiles(k, xb, X, b, queue="qs")
            norm_to_bf16(k, xb, xnm, cst, rms=True, eps=1e-6,
                         gain_col=mcols["norm_m"], chunk=512)
            k.unphase()
            st_tiles(k, xnm, XN, b, queue="qg")
            k.unphase()
            k.phase()
            xs_t = k.sb_tiles(16, T, BF16)
            k.phase()
            xr = k.sb_tiles(16, 3 + T, BF16)
            k.phase()
            xnm = k.sb_tiles(8, T, BF16)
            ld_tiles(k, xnm, XN, b, queue="qg")
            op_linearB(k, xnm, waps["inproj"][:, 2048:4096], D, 2048, T,
                       mk_epi_copy(k, xs_t), w_chunk=256)
            st_tiles(k, xs_t, Z, b, n=16, queue="qg", Tb=T)

            def zeropad(eng, xr=xr):
                for j in range(16):
                    inst = eng.memset(xr[j][:, 0:3], 0.0)
                k.inc(inst, "pl", 1)
            k.op({"gpsimd": zeropad})

            def epi_inx(eng, ps_ap, ds, tc, m, xr=xr):
                inst = eng.tensor_copy(
                    out=xr[ds][0:m, 3 + tc * 512:3 + (tc + 1) * 512],
                    in_=ps_ap)
                k.inc(inst, "dv", 1)
            op_linearB(k, xnm, waps["inproj"][:, 0:2048], D, 2048, T,
                       epi_inx, w_chunk=256)
            k.unphase()
            op_mamba_conv(k, xr, xs_t, mcst, Tb=T)
            k.unphase()
            op_mamba_scan(k, xs_t, Z, waps["wxp"], mcst, Tb=T, zcol0=b * T)
            k.phase()
            xb = k.sb_tiles(8, T, F32)
            ld_tiles(k, xb, X, b, queue="qs")
            op_linearB(k, xs_t, waps["outproj"], 2048, D, T,
                       mk_epi_residual(k, xb), w_chunk=256)
            st_tiles(k, xb, X, b, queue="qs")
            k.unphase()
            k.unphase()

        for b in range(NB):
            k.phase()
            xnm = k.sb_tiles(8, T, BF16)
            k.phase()
            xb = k.sb_tiles(8, T, F32)
            ld_tiles(k, xb, X, b, queue="qs")
            norm_to_bf16(k, xb, xnm, cst, rms=True, eps=1e-6,
                         gain_col=mcols["norm_attn"], chunk=512)
            k.unphase()
            k.phase()
            ot = k.sb_tiles(8, T, BF16)
            k.phase()
            qt = k.sb_tiles(8, T, BF16)
            kt = k.sb_tiles(4, T, BF16)
            vtok = k.sb_tiles(16, 512, BF16)
            k.phase()
            wv_sb = k.sb_tiles(8, 512, BF16)
            op_load(k, [(wv_sb[j][:], waps["vw"][j * P:(j + 1) * P, :])
                        for j in range(8)], queue="qg")
            op_linearA(k, xnm, wv_sb, D, 512, T, mk_epiA_copy(k, vtok))
            k.unphase()
            op_linearB(k, xnm, waps["qw"], D, D, T, mk_epi_copy(k, qt))
            op_linearB(k, xnm, waps["kw"], D, 512, T, mk_epi_copy(k, kt))
            op_attention(k, qt, kt, vtok, ot, 4, 256, 2, T, "causal",
                         1.0 / 16.0, cst)
            k.unphase()
            k.phase()
            xb = k.sb_tiles(8, T, F32)
            ld_tiles(k, xb, X, b, queue="qs")
            op_linearB(k, ot, waps["ow"], D, D, T, mk_epi_residual(k, xb))
            st_tiles(k, xb, X, b, queue="qs")
            k.unphase()
            k.unphase()
            k.unphase()

        for b in range(NB):
            k.phase()
            yout = k.sb_tiles(8, T, BF16)
            k.phase()
            xn = k.sb_tiles(8, T, BF16)
            k.phase()
            xb = k.sb_tiles(8, T, F32)
            ld_tiles(k, xb, X, b, queue="qs")
            norm_to_bf16(k, xb, xn, cst, rms=True, eps=1e-6,
                         gain_col=mcols["norm_ffn"], chunk=512)
            k.unphase()
            k.phase()
            gt = [k.sb([P, T], BF16) for _ in range(22)]
            op_linearB(k, xn, waps["gate"], D, 2816, T,
                       mk_epi_act(k, gt, AF.Silu), epi_eng="scalar")

            def epi_umul(eng, ps_ap, ds, tc, m):
                inst = eng.tensor_copy(out=k.scratch[0:m, :], in_=ps_ap)
                eng.wait_ge(k.sems["dv"], k.inc(inst, "dv", 1))
                inst = eng.tensor_mul(
                    out=gt[ds][0:m, tc * 512:(tc + 1) * 512],
                    in0=gt[ds][0:m, tc * 512:(tc + 1) * 512],
                    in1=k.scratch[0:m, :])
                k.inc(inst, "dv", 1)
            op_linearB(k, xn, waps["up"], D, 2816, T, epi_umul, epi_incs=2)
            op_linearB(k, gt, waps["down"], 2816, D, T,
                       mk_epi_copy(k, yout), w_chunk=256)
            k.unphase()
            k.unphase()
            k.phase()
            xb = k.sb_tiles(8, T, F32)
            ld_tiles(k, xb, X, b, queue="qs")

            def addres(eng, xb=xb, yout=yout):
                for j in range(8):
                    inst = eng.tensor_add(out=xb[j][:], in0=xb[j][:],
                                          in1=yout[j][:])
                    k.inc(inst, "dv", 1)
            k.op({"vector": addres})
            st_tiles(k, xb, X, b, queue="qs")
            k.unphase()
            k.unphase()

        for b in range(NB):
            k.phase()
            rb = k.sb_tiles(8, T, F32)
            xb = k.sb_tiles(8, T, F32)
            ld_tiles(k, rb, R, b, queue="qs")
            ld_tiles(k, xb, X, b, queue="qg")

            def addf(eng, rb=rb, xb=xb):
                for j in range(8):
                    inst = eng.tensor_add(out=rb[j][:], in0=rb[j][:],
                                          in1=xb[j][:])
                    k.inc(inst, "dv", 1)
            k.op({"vector": addf})
            st_tiles(k, rb, Rout, b, queue="qs")
            k.unphase()
    k.unphase()


def build_l0attn(hostc, hostw):
    nc = bass.Bass("TRN2", target_bir_lowering=False)
    ctx = ExitStack()
    im = {}
    with ctx:
        k = mk_ker(nc, ctx, im, hostc)
        R = win(nc, "R", [D, TT])
        Rout = nc.dram_tensor("Rout", [D, TT], F32, kind="ExternalOutput").ap()
        emit_l0attn(k, nc, im, hostc, hostw, R, Rout)
        k.finish()
    return nc, im



def emit_l1attn(k, nc, im, hostc, hostw, R, Rout):
    names = ["sel_ssq", "sel_bc"]
    k.phase()
    if True:
        cst = sec_consts(k, nc, im, hostc, names)
        waps = {}
        for nm in ["wq", "wk", "wv", "wo"]:
            waps[nm] = weight_ap(k, nc, im, hostw, "a_" + nm, [D, D])
        acols = {}
        colpairs = []
        for nm, shp in [("bo", [P, 8]), ("qs", [P, 8]), ("ks", [P, 8])]:
            ap = weight_ap(k, nc, im, hostw, "a_" + nm, shp)
            sb = k.sb(list(shp), F32)
            acols[nm] = sb
            colpairs.append((sb[:], ap))
        op_load(k, colpairs, queue="qs")

        for b in range(NB):
            k.phase()
            xn = k.sb_tiles(8, T, BF16)
            k.phase()
            rb = k.sb_tiles(8, T, F32)
            ld_tiles(k, rb, R, b, queue="qs")
            k.phase()
            tmp = k.sb([P, 512], F32)

            def apply_fn(eng, di, a_ps, b_ps, c0, w, rb=rb, xn=xn, tmp=tmp):
                for j in range(w // 512):
                    cs = slice(c0 + j * 512, c0 + (j + 1) * 512)
                    inst = eng.tensor_tensor(out=tmp[:], in0=rb[di][:, cs],
                                             in1=a_ps[j][:], op=ALU.mult)
                    eng.wait_ge(k.sems["dv"], k.inc(inst, "dv", 1))
                    inst = eng.tensor_tensor(out=xn[di][:, cs], in0=tmp[:],
                                             in1=b_ps[j][:], op=ALU.add)
                    k.inc(inst, "dv", 1)
            op_norm(k, rb, 8, T, cst["ones_col"], cst["ones_row"], apply_fn,
                    rms=False, chunk=512)
            k.unphase()
            k.unphase()
            k.phase()
            ot = k.sb_tiles(8, T, BF16)
            k.phase()
            qt = k.sb_tiles(8, T, BF16)
            kt = k.sb_tiles(8, T, BF16)
            vtok = k.sb_tiles(16, 1024, BF16)
            k.phase()
            wv_sb = k.sb_tiles(8, 512, BF16)
            for dc2 in range(2):
                op_load(k, [(wv_sb[j][:],
                             waps["wv"][j * P:(j + 1) * P,
                                        dc2 * 512:(dc2 + 1) * 512])
                            for j in range(8)], queue="qg")
                op_linearA(k, xn, wv_sb, D, 512, T,
                           mk_epiA_copy(k, vtok, col_off=dc2 * 512))
            k.unphase()
            op_linearB(k, xn, waps["wq"], D, D, T, mk_epi_copy(k, qt))
            op_linearB(k, xn, waps["wk"], D, D, T, mk_epi_copy(k, kt))
            op_qk_prep(k, qt, 16, 64, T, cst, l2eps=1e-12,
                       scale_col=[acols["qs"][:, j:j + 1] for j in range(8)])
            op_qk_prep(k, kt, 16, 64, T, cst, l2eps=1e-12,
                       scale_col=[acols["ks"][:, j:j + 1] for j in range(8)])
            op_attention(k, qt, kt, vtok, ot, 16, 64, 16, T, "full",
                         0.125, cst)
            k.unphase()
            k.phase()
            rb = k.sb_tiles(8, T, F32)
            ld_tiles(k, rb, R, b, queue="qs")
            op_linearB(k, ot, waps["wo"], D, D, T,
                       mk_epi_residual(k, rb, bias_col=acols["bo"]),
                       epi_incs=2)
            st_tiles(k, rb, Rout, b, queue="qs")
            k.unphase()
            k.unphase()
            k.unphase()
    k.unphase()


def build_l1attn(hostc, hostw):
    nc = bass.Bass("TRN2", target_bir_lowering=False)
    ctx = ExitStack()
    im = {}
    with ctx:
        k = mk_ker(nc, ctx, im, hostc)
        R = win(nc, "R", [D, TT])
        Rout = nc.dram_tensor("Rout", [D, TT], F32, kind="ExternalOutput").ap()
        emit_l1attn(k, nc, im, hostc, hostw, R, Rout)
        k.finish()
    return nc, im



def emit_l1ffn(k, nc, im, hostc, hostw, R, Rout, upto="all"):
    names = ["rope1_A", "rope1_B", "pswap", "exp_sel"]
    k.phase()
    if True:
        cst = sec_consts(k, nc, im, hostc, names)
        X = scratch_ap(k, nc, "Xscr", [D, TT], F32)
        waps = {}
        for nm, shp in [("wq", [D, D]), ("wkv", [D, 256]), ("wkup", [256, D]),
                        ("wvup", [256, D]), ("wo", [D, D]), ("wr", [D, 8]),
                        ("sw1", [D, 2048]), ("sw2", [2048, D]),
                        ("sw3", [D, 2048])]:
            waps[nm] = weight_ap(k, nc, im, hostw, "g_" + nm, shp)
        for e in range(8):
            for nm, shp in [("ew1", [D, 2048]), ("ew2", [2048, D]),
                            ("ew3", [D, 2048])]:
                waps[f"{nm}_{e}"] = weight_ap(k, nc, im, hostw,
                                              f"g_{nm}_{e}", shp)
        gcols = {}
        colpairs = []
        for nm, shp in [("ln1", [P, 8]), ("ln2", [P, 8]), ("rb", [1, 8])]:
            ap = weight_ap(k, nc, im, hostw, "g_" + nm, shp)
            sb = k.sb(list(shp), F32)
            gcols[nm] = sb
            colpairs.append((sb[:], ap))
        op_load(k, colpairs, queue="qs")

        for b in range(NB):
            k.phase()
            rb = k.sb_tiles(8, T, F32)
            xf = k.sb_tiles(8, T, F32)
            ld_tiles(k, rb, R, b, queue="qs")
            k.phase()
            tmp = k.sb([P, 512], F32)

            def apply_fn(eng, di, a_ps, b_ps, c0, w, rb=rb, xf=xf, tmp=tmp):
                for j in range(w // 512):
                    cs = slice(c0 + j * 512, c0 + (j + 1) * 512)
                    inst = eng.tensor_tensor(out=tmp[:], in0=rb[di][:, cs],
                                             in1=a_ps[j][:], op=ALU.mult)
                    eng.wait_ge(k.sems["dv"], k.inc(inst, "dv", 1))
                    inst = eng.tensor_tensor(out=xf[di][:, cs], in0=tmp[:],
                                             in1=b_ps[j][:], op=ALU.add)
                    k.inc(inst, "dv", 1)
            op_norm(k, rb, 8, T, cst["ones_col"], cst["ones_row"], apply_fn,
                    rms=False, chunk=512)
            k.unphase()
            st_tiles(k, xf, X, b, queue="qs")
            k.unphase()

        for b in range(NB):
            k.phase()
            xn = k.sb_tiles(8, T, BF16)
            k.phase()
            xb = k.sb_tiles(8, T, F32)
            ld_tiles(k, xb, X, b, queue="qs")
            norm_to_bf16(k, xb, xn, cst, rms=True, eps=1e-6,
                         gain_col=gcols["ln1"], chunk=512)
            k.unphase()
            k.phase()
            ot = k.sb_tiles(8, T, BF16)
            k.phase()
            qt = k.sb_tiles(8, T, BF16)
            kt = k.sb_tiles(8, T, BF16)
            vtok = k.sb_tiles(16, 1024, BF16)
            k.phase()
            kv = k.sb_tiles(2, T, BF16)
            op_linearB(k, xn, waps["wkv"], D, 256, T,
                       mk_epi_scale(k, kv, 0.25))
            op_linearB(k, kv, waps["wkup"], 256, D, T, mk_epi_copy(k, kt))
            k.phase()
            wv_sb = k.sb_tiles(2, 1024, BF16)
            op_load(k, [(wv_sb[j][:], waps["wvup"][j * P:(j + 1) * P, :])
                        for j in range(2)], queue="qg")
            op_linearA(k, kv, wv_sb, 256, 1024, T, mk_epiA_copy(k, vtok))
            k.unphase()
            k.unphase()
            op_linearB(k, xn, waps["wq"], D, D, T, mk_epi_copy(k, qt))
            op_qk_prep(k, qt, 16, 64, T, cst,
                       rope_tabs=(cst["rope1_A"], cst["rope1_B"]))
            op_qk_prep(k, kt, 16, 64, T, cst,
                       rope_tabs=(cst["rope1_A"], cst["rope1_B"]))
            op_attention(k, qt, kt, vtok, ot, 16, 64, 16, T, "causal",
                         0.125, cst)
            k.unphase()
            k.phase()
            xb = k.sb_tiles(8, T, F32)
            ld_tiles(k, xb, X, b, queue="qs")
            op_linearB(k, ot, waps["wo"], D, D, T, mk_epi_residual(k, xb))
            st_tiles(k, xb, X, b, queue="qs")
            k.unphase()
            k.unphase()
            k.unphase()

        for b in range(NB if upto == "all" else 0):
            k.phase()
            xn = k.sb_tiles(8, T, BF16)
            k.phase()
            xb = k.sb_tiles(8, T, F32)
            ld_tiles(k, xb, X, b, queue="qs")
            norm_to_bf16(k, xb, xn, cst, rms=True, eps=1e-6,
                         gain_col=gcols["ln2"], chunk=512)
            k.unphase()
            k.phase()
            acc = k.sb_tiles(8, T, F32)
            ST = k.sb([32, T], BF16)
            abc = k.sb([P, T], BF16)

            def zacc(eng, acc=acc, ST=ST):
                for j in range(8):
                    inst = eng.memset(acc[j][:], 0.0)
                inst = eng.memset(ST[:], 0.0)
                k.inc(inst, "pl", 1)
            k.op({"gpsimd": zacc})

            k.phase()
            wr_sb = k.sb_tiles(8, 8, BF16)
            rbc = k.sb([P, 8], F32)
            probs = k.sb([P, 16 * 8], F32)
            sca = k.sb([P, 16], F32)
            eqb = k.sb([P, 8], F32)
            cumb = k.sb([P, 8], F32)
            zeros8 = k.sb([P, 8], F32)
            op_load(k, [(wr_sb[j][:], waps["wr"][j * P:(j + 1) * P, :])
                        for j in range(8)], queue="qg")

            def pre(eng):
                inst = eng.memset(zeros8[:], 0.0)
                k.inc(inst, "pl", 1)
            k.op({"gpsimd": pre})

            def rbcast(eng):
                inst = eng.matmul(k.ps[7][:, 0:8], cst["ones_row"][0:1, :],
                                  gcols["rb"][0:1, 0:8], start=True, stop=True)
                k.inc(inst, "pe", 1)
            k.op({"tensor": rbcast})

            def rbc_cp(eng):
                inst = eng.tensor_copy(out=rbc[:], in_=k.ps[7][:, 0:8])
                k.inc(inst, "dv", 1)
            k.op({"vector": rbc_cp})

            def epi_router(eng, ps_ap, ti, dc, n):
                inst = eng.tensor_tensor(out=probs[:, ti * 8:ti * 8 + 8],
                                         in0=ps_ap, in1=rbc[:], op=ALU.add)
                k.inc(inst, "dv", 1)
            op_linearA(k, xn, wr_sb, D, 8, T, epi_router)

            seq = Seq(k)
            for ti in range(16):
                pr = lambda ti=ti: probs[:, ti * 8:ti * 8 + 8]

                def C1(eng, pr=pr):
                    return eng.tensor_scalar_max(out=pr(), in0=pr(),
                                                 scalar1=-50.0)
                seq.do("vector", C1, inc="dv")

                def C2(eng, pr=pr):
                    return eng.tensor_scalar_min(out=pr(), in0=pr(),
                                                 scalar1=50.0)
                dvc = seq.do("vector", C2, inc="dv")

                def SG(eng, pr=pr):
                    return eng.activation(out=pr(), in_=pr(), func=AF.Sigmoid)
                acs = seq.do("scalar", SG, waits=[("dv", dvc)], inc="ac")

                def PM(eng, pr=pr, ti=ti):
                    return eng.reduce_max(out=sca[:, 15:16], in_=pr(),
                                          axis=AX.X)
                seq.do("vector", PM, waits=[("ac", acs)], inc="dv")

                def EQ(eng, pr=pr):
                    return eng.tensor_tensor(
                        out=eqb[:], in0=pr(),
                        in1=sca[:, 15:16].broadcast_to([P, 8]),
                        op=ALU.is_equal)
                seq.do("vector", EQ, inc="dv")

                def CU(eng):
                    return eng.tensor_tensor_scan(
                        out=cumb[:], data0=eqb[:], data1=zeros8[:],
                        initial=0.0, op0=ALU.add, op1=ALU.add)
                seq.do("vector", CU, inc="dv")

                def FE(eng):
                    return eng.tensor_single_scalar(out=cumb[:], in_=cumb[:],
                                                    scalar=1.0,
                                                    op=ALU.is_equal)
                seq.do("vector", FE, inc="dv")

                def OH(eng):
                    return eng.tensor_mul(out=eqb[:], in0=eqb[:], in1=cumb[:])
                seq.do("vector", OH, inc="dv")

                def PC1(eng):
                    return eng.tensor_scalar_max(out=sca[:, 15:16],
                                                 in0=sca[:, 15:16],
                                                 scalar1=1e-8)
                seq.do("vector", PC1, inc="dv")

                def PC2(eng):
                    return eng.tensor_scalar_min(out=sca[:, 15:16],
                                                 in0=sca[:, 15:16],
                                                 scalar1=1.0 - 1e-8)
                seq.do("vector", PC2, inc="dv")

                def TW(eng):
                    return eng.tensor_scalar(out=sca[:, 14:15],
                                             in0=sca[:, 15:16],
                                             scalar1=0.5 + 1e-8, scalar2=2.0,
                                             op0=ALU.add, op1=ALU.min)
                seq.do("vector", TW, inc="dv")

                def TW2(eng):
                    return eng.tensor_scalar_max(out=sca[:, 14:15],
                                                 in0=sca[:, 14:15],
                                                 scalar1=0.5)
                seq.do("vector", TW2, inc="dv")

                def IT(eng):
                    return eng.reciprocal(out=sca[:, 14:15],
                                          in_=sca[:, 14:15])
                seq.do("vector", IT, inc="dv")

                def SA(eng):
                    return eng.tensor_scalar_mul(out=sca[:, 0:1],
                                                 in0=sca[:, 14:15],
                                                 scalar1=0.5)
                seq.do("vector", SA, inc="dv")

                def PI(eng):
                    return eng.tensor_mul(out=sca[:, 15:16],
                                          in0=sca[:, 15:16],
                                          in1=sca[:, 14:15])
                seq.do("vector", PI, inc="dv")

                def BC(eng):
                    return eng.tensor_tensor(
                        out=sca[:, 1:9], in0=eqb[:],
                        in1=sca[:, 15:16].broadcast_to([P, 8]), op=ALU.mult)
                dvb = seq.do("vector", BC, inc="dv")

                def TR(eng, ti=ti):
                    return eng.transpose(k.ps[6][0:9, 0:P], sca[:, 0:9],
                                         cst["identity"][:])
                pet = seq.do("tensor", TR, waits=[("dv", dvb)], inc="pe")

                def CP(eng, ti=ti):
                    return eng.tensor_copy(out=ST[0:9, ti * P:(ti + 1) * P],
                                           in_=k.ps[6][0:9, 0:P])
                seq.do("vector", CP, waits=[("pe", pet)], inc="dv")
            seq.run()
            k.unphase()

            k.phase()
            ht = [k.sb([P, T], BF16) for _ in range(16)]
            for e in range(-1, 8):
                w1 = waps["sw1"] if e < 0 else waps[f"ew1_{e}"]
                w3 = waps["sw3"] if e < 0 else waps[f"ew3_{e}"]
                w2 = waps["sw2"] if e < 0 else waps[f"ew2_{e}"]
                op_linearB(k, xn, w1, D, 2048, T,
                           mk_epi_act(k, ht, AF.Silu), epi_eng="scalar",
                           w_chunk=256)

                def epi_mul(eng, ps_ap, ds, tc, m):
                    inst = eng.tensor_copy(out=k.scratch[0:m, :], in_=ps_ap)
                    eng.wait_ge(k.sems["dv"], k.inc(inst, "dv", 1))
                    inst = eng.tensor_mul(
                        out=ht[ds][0:m, tc * 512:(tc + 1) * 512],
                        in0=ht[ds][0:m, tc * 512:(tc + 1) * 512],
                        in1=k.scratch[0:m, :])
                    k.inc(inst, "dv", 1)
                op_linearB(k, xn, w3, D, 2048, T, epi_mul, epi_incs=2,
                           w_chunk=256)

                sl = 0 if e < 0 else 1 + e

                def bca(eng, sl=sl):
                    for tc in range(4):
                        inst = eng.matmul(
                            k.ps[4 + tc][:, :],
                            cst["exp_sel"][0:32, sl * P:(sl + 1) * P],
                            ST[0:32, tc * 512:(tc + 1) * 512],
                            start=True, stop=True)
                        k.inc(inst, "pe", 1)
                k.op({"tensor": bca})

                def bcc(eng):
                    for tc in range(4):
                        inst = eng.tensor_copy(
                            out=abc[:, tc * 512:(tc + 1) * 512],
                            in_=k.ps[4 + tc][:, :])
                        k.inc(inst, "dv", 1)
                k.op({"vector": bcc})

                def epi_acc(eng, ps_ap, ds, tc, m):
                    inst = eng.tensor_tensor(
                        out=k.scratch[0:m, :], in0=ps_ap,
                        in1=abc[0:m, tc * 512:(tc + 1) * 512], op=ALU.mult)
                    eng.wait_ge(k.sems["dv"], k.inc(inst, "dv", 1))
                    inst = eng.tensor_add(
                        out=acc[ds][0:m, tc * 512:(tc + 1) * 512],
                        in0=acc[ds][0:m, tc * 512:(tc + 1) * 512],
                        in1=k.scratch[0:m, :])
                    k.inc(inst, "dv", 1)
                op_linearB(k, ht, w2, 2048, D, T, epi_acc, epi_incs=2,
                           w_chunk=256)
            k.unphase()

            k.phase()
            xb = k.sb_tiles(8, T, F32)
            ld_tiles(k, xb, X, b, queue="qs")

            def addres(eng, xb=xb, acc=acc):
                for j in range(8):
                    inst = eng.tensor_add(out=xb[j][:], in0=xb[j][:],
                                          in1=acc[j][:])
                    k.inc(inst, "dv", 1)
            k.op({"vector": addres})
            st_tiles(k, xb, X, b, queue="qs")
            k.unphase()
            k.unphase()
            k.unphase()

        for b in range(NB):
            k.phase()
            rb = k.sb_tiles(8, T, F32)
            xb = k.sb_tiles(8, T, F32)
            ld_tiles(k, rb, R, b, queue="qs")
            ld_tiles(k, xb, X, b, queue="qg")

            def addf(eng, rb=rb, xb=xb):
                for j in range(8):
                    inst = eng.scalar_tensor_tensor(
                        out=rb[j][:], in0=xb[j][:], scalar=0.5, in1=rb[j][:],
                        op0=ALU.mult, op1=ALU.add)
                    k.inc(inst, "dv", 1)
            k.op({"vector": addf})
            st_tiles(k, rb, Rout, b, queue="qs")
            k.unphase()
    k.unphase()


def build_l1ffn(hostc, hostw, upto="all"):
    nc = bass.Bass("TRN2", target_bir_lowering=False)
    ctx = ExitStack()
    im = {}
    with ctx:
        k = mk_ker(nc, ctx, im, hostc)
        R = win(nc, "R", [D, TT])
        Rout = nc.dram_tensor("Rout", [D, TT], F32, kind="ExternalOutput").ap()
        emit_l1ffn(k, nc, im, hostc, hostw, R, Rout, upto)
        k.finish()
    return nc, im



def build_fused(hostc, hostw):
    nc = bass.Bass("TRN2", target_bir_lowering=False)
    ctx = ExitStack()
    im = {}
    with ctx:
        k = mk_ker(nc, ctx, im, hostc)
        R = win(nc, "R", [D, TT])
        Rout = nc.dram_tensor("Rout", [D, TT], F32, kind="ExternalOutput").ap()
        Rs = scratch_ap(k, nc, "Rstate", [D, TT], F32)
        emit_l0ffn(k, nc, im, hostc, hostw, R, Rs)
        emit_l0attn(k, nc, im, hostc, hostw, Rs, Rs)
        emit_l0ffn(k, nc, im, hostc, hostw, Rs, Rs)
        emit_l1ffn(k, nc, im, hostc, hostw, Rs, Rs)
        emit_l1attn(k, nc, im, hostc, hostw, Rs, Rs)
        emit_l1ffn(k, nc, im, hostc, hostw, Rs, Rout)
        k.finish()
    return nc, im



def kernel(**inputs):
    hostc = host_consts()
    hostw = host_weights(inputs)
    x = np.ascontiguousarray(np.asarray(inputs["L0_attn_x"], dtype=np.float32))
    Rs = []
    for c in range(NCORE):
        Rs.append(np.ascontiguousarray(
            np.concatenate([x[2 * c].T, x[2 * c + 1].T], axis=1)))
    nc, im = build_fused(hostc, hostw)
    in_maps = [dict(im, R=Rs[c]) for c in range(NCORE)]
    res = run_bass_kernel_spmd(nc, in_maps, core_ids=list(range(NCORE)))
    out = np.empty((16, 2048, 1024), np.float32)
    for c in range(NCORE):
        ro = np.asarray(res.results[c]["Rout"])
        out[2 * c] = ro[:, :T].T
        out[2 * c + 1] = ro[:, T:].T
    return out


# version 04
# speedup vs baseline: 1.1798x; 1.1730x over previous
import numpy as np
import ml_dtypes
from contextlib import ExitStack
import concourse.bass as bass
import concourse.mybir as mybir
from concourse.bass_utils import run_bass_kernel_spmd

F32 = mybir.dt.float32
BF16 = mybir.dt.bfloat16
AF = mybir.ActivationFunctionType
ALU = mybir.AluOpType
AX = mybir.AxisListType
BF = ml_dtypes.bfloat16

P = 128
T = 2048
NB = 2
TT = NB * T
NEG = -1e30
D = 1024
DT = D // P

ENG_SEM = {"tensor": "pe", "vector": "dv", "scalar": "ac",
           "gpsimd": "pl", "sync": "qs"}
QUEUE_ENG = {"qs": "sync", "qg": "gpsimd", "qa": "scalar"}


def cdiv(a, b):
    return (a + b - 1) // b


class Ker:
    def __init__(self, nc, block, ctx):
        self.nc = nc
        self.block = block
        self.ctx = ctx
        E = ctx.enter_context
        self.sems = {}
        self.cnt = {}
        for name in ["pe", "dv", "ac", "pl", "qs", "qg", "qa"]:
            self.sems[name] = E(nc.semaphore(name))
            self.cnt[name] = 0
        self.psbig = [E(nc.psum_tensor(f"psb{i}", [P, 2048], F32))
                      for i in range(2)]
        self.ps = [self.psbig[i // 4][:, (i % 4) * 512:(i % 4) * 512 + 512]
                   for i in range(8)]
        self.sb_off = 16512
        self.sb_peak = self.sb_off
        self._phase_stack = []
        self._names = 0
        self.scratch = self.sb([P, 2048], F32, "kscratch")

    def sb(self, shape, dtype, name=None):
        per_part = int(np.prod(shape[1:])) * mybir.dt.size(dtype)
        off = (self.sb_off + 63) & ~63
        self.sb_off = off + per_part
        self.sb_peak = max(self.sb_peak, self.sb_off)
        assert self.sb_off <= 229370, f"SBUF overflow: {self.sb_off} B/partition"
        self._names += 1
        nm = name or f"sb{self._names}"
        return self.nc.alloc_sbuf_tensor_at(nm, list(shape), dtype, offset=off)

    def sb_tiles(self, n, cols, dtype):
        return [self.sb([P, cols], dtype) for _ in range(n)]

    def phase(self):
        self._phase_stack.append(self.sb_off)

    def unphase(self):
        self.sb_off = self._phase_stack.pop()

    def inc(self, inst, sem, n=1):
        inst.then_inc(self.sems[sem], n)
        self.cnt[sem] += n
        return self.cnt[sem]

    def op(self, fn_map, snap=None):
        if snap is None:
            snap = dict(self.cnt)
        for en in ["sync", "gpsimd", "tensor", "vector", "scalar"]:
            if en not in fn_map:
                continue
            fn = fn_map[en]

            def body(eng, fn=fn, en=en):
                own = ENG_SEM.get(en)
                for s, v in snap.items():
                    if v > 0 and s != own:
                        eng.wait_ge(self.sems[s], v)
                fn(eng)
            getattr(self.block, en)(body)

    def finish(self, out_queues=("qs", "qg", "qa")):
        snap = dict(self.cnt)

        def body(eng):
            for s, v in snap.items():
                if v > 0:
                    eng.wait_ge(self.sems[s], v)
        self.block.sync(body)



def op_load(k, pairs, queue="qs"):
    def body(eng):
        for d, s in pairs:
            k.inc(eng.dma_start(out=d, in_=s), queue, 16)
    k.op({QUEUE_ENG[queue]: body})


def op_store(k, pairs, queue="qs"):
    def body(eng):
        for d, s in pairs:
            k.inc(eng.dma_start(out=d, in_=s), queue, 16)
    k.op({QUEUE_ENG[queue]: body})



def op_linearB(k, xs, w_hbm, din, dout, tt, epi, epi_eng="vector",
               w_chunk=512, epi_incs=1):
    ndi = din // P
    ndo_ch = cdiv(dout, w_chunk)
    ntc = tt // 512
    assert ntc <= 4
    esem = ENG_SEM[epi_eng]
    k.phase()
    wbuf = [k.sb([P, ndi * w_chunk], BF16), k.sb([P, ndi * w_chunk], BF16)]
    pe0, ev0, qg0 = k.cnt["pe"], k.cnt[esem], k.cnt["qg"]

    rounds_per_chunk = [cdiv(min(w_chunk, dout - c * w_chunk), P)
                        for c in range(ndo_ch)]
    pe_after_chunk = np.cumsum(rounds_per_chunk)

    def dma_body(eng):
        for c in range(ndo_ch):
            if c >= 2:
                eng.wait_ge(k.sems["pe"], pe0 + int(pe_after_chunk[c - 2]))
            nds = min(w_chunk, dout - c * w_chunk)
            dst = wbuf[c % 2][:].rearrange("p (di c) -> p di c", c=w_chunk)
            inst = eng.dma_start(
                out=dst[:, :, 0:nds],
                in_=w_hbm[0:din, c * w_chunk: c * w_chunk + nds]
                .rearrange("(di p) c -> p di c", p=P))
            k.inc(inst, "qg", 16)

    def pe_body(eng):
        g = 0
        for c in range(ndo_ch):
            eng.wait_ge(k.sems["qg"], qg0 + 16 * (c + 1))
            nds = min(w_chunk, dout - c * w_chunk)
            for s in range(cdiv(nds, P)):
                m = min(P, nds - s * P)
                big = k.psbig[g % 2]
                if g >= 2:
                    eng.wait_ge(k.sems[esem], ev0 + (g - 1) * epi_incs)
                for di in range(ndi):
                    for tc in range(ntc):
                        inst = eng.matmul(
                            big[0:m, tc * 512:(tc + 1) * 512],
                            wbuf[c % 2][:, di * w_chunk + s * P:
                                        di * w_chunk + s * P + m],
                            xs[di][:, tc * 512:(tc + 1) * 512],
                            start=(di == 0), stop=(di == ndi - 1))
                k.inc(inst, "pe", 1)
                g += 1

    def ev_body(eng):
        g = 0
        for c in range(ndo_ch):
            nds = min(w_chunk, dout - c * w_chunk)
            for s in range(cdiv(nds, P)):
                m = min(P, nds - s * P)
                eng.wait_ge(k.sems["pe"], pe0 + g + 1)
                epi(eng, k.psbig[g % 2][0:m, 0:tt], c * (w_chunk // P) + s, m)
                g += 1

    k.op({"gpsimd": dma_body, "tensor": pe_body, epi_eng: ev_body})
    k.unphase()


def op_linearA(k, xs, w_sb, din, dout, tt, epi, epi_eng="vector",
               epi_incs=1):
    ndi = din // P
    ntl = tt // P
    ndc = cdiv(dout, 512)
    assert ndc <= 4
    esem = ENG_SEM[epi_eng]
    pe0, ev0 = k.cnt["pe"], k.cnt[esem]

    def pe_body(eng):
        for ti in range(ntl):
            big = k.psbig[ti % 2]
            if ti >= 2:
                eng.wait_ge(k.sems[esem], ev0 + (ti - 1) * epi_incs)
            for di in range(ndi):
                for dc in range(ndc):
                    n = min(512, dout - dc * 512)
                    inst = eng.matmul(
                        big[:, dc * 512:dc * 512 + n],
                        xs[di][:, ti * P:(ti + 1) * P],
                        w_sb[di][:, dc * 512: dc * 512 + n],
                        start=(di == 0), stop=(di == ndi - 1))
            k.inc(inst, "pe", 1)

    def ev_body(eng):
        for ti in range(ntl):
            eng.wait_ge(k.sems["pe"], pe0 + ti + 1)
            epi(eng, k.psbig[ti % 2][:, 0:dout], ti, dout)

    k.op({"tensor": pe_body, epi_eng: ev_body})


def op_norm(k, x_tiles, nr, tt, ones_col, ones_row, apply_fn,
            rms=False, eps=1e-5, chunk=2048):
    DIM = nr * P
    ncH = tt // chunk
    n512 = chunk // 512
    k.phase()
    x2 = k.sb_tiles(nr, chunk, F32)
    srow = k.sb([1, chunk], F32)
    s2row = k.sb([1, chunk], F32)
    arow = k.sb([1, chunk], F32)
    brow = k.sb([1, chunk], F32)
    trow = k.sb([1, chunk], F32)

    for ch in range(ncH):
        c0 = ch * chunk

        def ac1(eng, c0=c0):
            for di in range(nr):
                inst = eng.activation(out=x2[di][:],
                                      in_=x_tiles[di][:, c0:c0 + chunk],
                                      func=AF.Square)
            k.inc(inst, "ac", 1)
        k.op({"scalar": ac1})

        def pe1(eng, c0=c0):
            for j in range(n512):
                if not rms:
                    for di in range(nr):
                        inst = eng.matmul(
                            k.ps[2 * j][0:1, :], ones_col[:, 0:1],
                            x_tiles[di][:, c0 + j * 512: c0 + (j + 1) * 512],
                            start=(di == 0), stop=(di == nr - 1))
                for di in range(nr):
                    inst = eng.matmul(
                        k.ps[2 * j + 1][0:1, :], ones_col[:, 0:1],
                        x2[di][:, j * 512:(j + 1) * 512],
                        start=(di == 0), stop=(di == nr - 1))
            k.inc(inst, "pe", 1)
        k.op({"tensor": pe1})

        def dv1(eng):
            for j in range(n512):
                cs = slice(j * 512, (j + 1) * 512)
                if not rms:
                    eng.tensor_scalar(out=srow[0:1, cs], in0=k.ps[2 * j][0:1, :],
                                      scalar1=1.0 / DIM, scalar2=None,
                                      op0=ALU.mult)
                eng.tensor_scalar(out=s2row[0:1, cs], in0=k.ps[2 * j + 1][0:1, :],
                                  scalar1=1.0 / DIM, scalar2=eps,
                                  op0=ALU.mult, op1=ALU.add)
            if not rms:
                inst = eng.tensor_mul(out=trow[0:1, :], in0=srow[0:1, :],
                                      in1=srow[0:1, :])
                eng.wait_ge(k.sems["dv"], k.inc(inst, "dv", 1))
                inst = eng.tensor_sub(out=trow[0:1, :], in0=s2row[0:1, :],
                                      in1=trow[0:1, :])
            else:
                inst = eng.tensor_copy(out=trow[0:1, :], in_=s2row[0:1, :])
            k.inc(inst, "dv", 1)
        k.op({"vector": dv1})

        def ac2(eng):
            inst = eng.activation(out=arow[0:1, :], in_=trow[0:1, :],
                                  func=AF.Ln)
            eng.wait_ge(k.sems["ac"], k.inc(inst, "ac", 1))
            inst = eng.activation(out=arow[0:1, :], in_=arow[0:1, :],
                                  func=AF.Exp, scale=-0.5)
            k.inc(inst, "ac", 1)
        k.op({"scalar": ac2})

        if not rms:
            def dv2(eng):
                inst = eng.scalar_tensor_tensor(
                    out=brow[0:1, :], in0=srow[0:1, :], scalar=-1.0,
                    in1=arow[0:1, :], op0=ALU.mult, op1=ALU.mult)
                k.inc(inst, "dv", 1)
            k.op({"vector": dv2})

        def pe2(eng):
            for j in range(n512):
                inst = eng.matmul(k.ps[j][:, :], ones_row[0:1, :],
                                  arow[0:1, j * 512:(j + 1) * 512],
                                  start=True, stop=True)
                if not rms:
                    inst = eng.matmul(k.ps[4 + j][:, :], ones_row[0:1, :],
                                      brow[0:1, j * 512:(j + 1) * 512],
                                      start=True, stop=True)
            k.inc(inst, "pe", 1)
        k.op({"tensor": pe2})

        a_ps = [k.ps[j] for j in range(n512)]
        b_ps = None if rms else [k.ps[4 + j] for j in range(n512)]

        def dv3(eng, c0=c0, a_ps=a_ps, b_ps=b_ps):
            for di in range(nr):
                apply_fn(eng, di, a_ps, b_ps, c0, chunk)
        k.op({"vector": dv3})
    k.unphase()



class Seq:

    def __init__(self, k):
        self.k = k
        self.records = []
        self.snap0 = dict(k.cnt)
        self.last_ms = {}

    def do(self, engine, emit_fn, waits=(), inc=None, incn=1):
        k = self.k
        waits = list(waits)
        if engine != "tensor" and engine in self.last_ms:
            waits.append(self.last_ms[engine])
        val = None
        if inc is not None:
            k.cnt[inc] += incn
            val = k.cnt[inc]
            self.last_ms[engine] = (inc, val)
        self.records.append((engine, waits, emit_fn, inc, incn))
        return val

    def cur(self, sem):
        return self.k.cnt[sem]

    def run(self):
        k = self.k
        engines = {r[0] for r in self.records}
        fn_map = {}
        for en in engines:
            recs = [r for r in self.records if r[0] == en]

            def body(eng, recs=recs):
                last_wait = {}
                for _, waits, emit_fn, inc, incn in recs:
                    for sem, val in waits:
                        if val > 0 and last_wait.get(sem, -1) < val:
                            eng.wait_ge(k.sems[sem], val)
                            last_wait[sem] = val
                    inst = emit_fn(eng)
                    if inc is not None:
                        inst.then_inc(k.sems[inc], incn)
            fn_map[en] = body
        k.op(fn_map, snap=self.snap0)


def _rows(tiles_row0, dh):
    out = []
    r0 = tiles_row0
    left = dh
    while left:
        ti, ro = divmod(r0, P)
        n = min(left, P - ro)
        out.append((ti, ro, n))
        r0 += n
        left -= n
    return out


def op_attention(k, q_tiles, k_tiles, v_tiles, o_tiles, nh, dh, nkv, Tq,
                 mode, scale, cst, qcol0=0):
    QB = 256 if mode == "win256" else 512
    nqb = Tq // QB
    nqt = QB // P
    ndv = cdiv(dh, P)
    k.phase()
    S_sb = k.sb([P, 2048], F32)
    Pdump = k.sb([P, 2048], BF16)
    mcol = k.sb([P, 16], F32)
    den = k.sb([P, 1], F32)
    brow = k.sb([1, 512], F32)
    b2bc = k.sb([P, 512], F32)
    stf = [k.sb([P, 512], F32) for _ in range(2)]
    Pt = [k.sb([P, 512], BF16) for _ in range(4)]

    seq = Seq(k)
    stats_rd = 0
    stats_evac_ms = []
    stt_ms = []
    exp_ms = []
    pv_ms = []
    st_rd = 0
    st = {"last_X1a": 0, "last_B1": 0, "last_B2": 0, "last_X3": 0,
          "last_X4": 0, "last_Oev": 0, "dv_X2": 0}

    iters = [(h, qb) for h in range(nh) for qb in range(nqb)]

    def key_layout(h, qb):
        q0 = qb * QB
        if mode == "full":
            st_subs = [(k0, None) for k0 in range(0, Tq, P)]
            def stats_chunks(qt):
                return [[(c * 512 + j * P, None) for j in range(4)]
                        for c in range(Tq // 512)]
        elif mode == "causal":
            st_subs = []
            for k0 in range(0, (qb + 1) * 512, P):
                if q0 >= k0 + P - 1:
                    st_subs.append((k0, None))
                else:
                    st_subs.append((k0, [("ge", q0 - k0)]))
            def stats_chunks(qt, q0=q0):
                qa = q0 + qt * P
                ch = []
                nfull = qa // 512
                for c in range(nfull):
                    ch.append([(c * 512 + j * P, None) for j in range(4)])
                off = qa - nfull * 512
                ch.append([(nfull * 512 + j * P, ("c", off - j * P))
                           for j in range(4)])
                return ch
        else:
            lo = max(0, q0 - 256)
            st_subs = []
            for k0 in range(lo, q0 + QB, P):
                conds = []
                if q0 - k0 - (P - 1) < 0:
                    conds.append(("ge", q0 - k0))
                if 256 - ((q0 + QB - 1) - k0) < 0:
                    conds.append(("le256", 256 - (q0 - k0)))
                st_subs.append((k0, conds or None))
            def stats_chunks(qt, q0=q0):
                qa = q0 + qt * P
                segs = []
                for kk in range(max(0, qa - 256), qa + P, P):
                    segs.append((kk, ("w", qa - kk)))
                return [segs]
        return q0, st_subs, stats_chunks

    def rows_of(h):
        return _rows(h * dh, dh), _rows((h * nkv // nh) * dh, dh)

    def emit_A(it, paired=False):
        nonlocal stats_rd
        h, qb = iters[it]
        q0, st_subs, stats_chunks = key_layout(h, qb)
        qrows, krows = rows_of(h)
        A_info = []
        vis_of = {}
        first = True
        for qt in range(nqt):
            chunks = stats_chunks(qt)
            vis = 0
            for ci, segs in enumerate(chunks):
                rd = stats_rd
                bank = k.ps[rd % 2]
                waits = [("dv", stats_evac_ms[rd - 2])] if rd >= 2 else []

                def A(eng, segs=segs, bank=bank, qt=qt, qrows=qrows,
                      krows=krows, q0=q0):
                    col = 0
                    for (k0, _m) in segs:
                        for di in range(len(qrows)):
                            ti, ro, n = qrows[di]
                            kti, kro, kn = krows[di]
                            inst = eng.matmul(
                                bank[:, col:col + P],
                                q_tiles[ti][ro:ro + n,
                                            qcol0 + q0 + qt * P:
                                            qcol0 + q0 + (qt + 1) * P],
                                k_tiles[kti][kro:kro + kn,
                                             qcol0 + k0:qcol0 + k0 + P],
                                start=(di == 0), stop=(di == len(qrows) - 1))
                        col += P
                    return inst
                pe_ms = seq.do("tensor", A, waits=waits, inc="pe")
                if paired:
                    evac_round(qt, ci, segs, rd, pe_ms, vis, first)
                    first = True
                    first = (ci == len(chunks) - 1)
                else:
                    A_info.append((qt, ci, segs, rd, pe_ms, vis))
                stats_rd += 1
                vis += len(segs) * P
            if paired:
                emit_chain(qt, vis, 0)
        return None if paired else A_info

    def evac_round(qt, ci, segs, rd, pe_ms, vis0, first, sbase=0):
        if True:
            bank = k.ps[rd % 2]
            vis0 = sbase + vis0

            def E1a(eng, segs=segs, bank=bank, vis0=vis0):
                col = 0
                for (k0, msk) in segs:
                    dst = S_sb[:, vis0 + col: vis0 + col + P]
                    mask_t = None
                    if msk is not None:
                        if msk[0] == "c":
                            off = msk[1]
                            if off >= P:
                                mask_t = None
                            elif off < 0:
                                mask_t = "memset"
                            else:
                                mask_t = cst["wC"]
                        else:
                            off = msk[1]
                            mask_t = (cst["wA"] if off == 256 else
                                      (cst["wC"] if off == 0 else None))
                    if mask_t is None:
                        inst = eng.tensor_scalar_mul(
                            out=dst, in0=bank[:, col:col + P], scalar1=scale)
                    elif mask_t == "memset":
                        inst = eng.memset(dst, NEG)
                    else:
                        inst = eng.scalar_tensor_tensor(
                            out=dst, in0=bank[:, col:col + P], scalar=scale,
                            in1=mask_t[:, 0:P], op0=ALU.mult, op1=ALU.add)
                    col += P
                return inst
            wts = [("pe", pe_ms)]
            if first and st["last_X1a"] > 0:
                wts.append(("ac", st["last_X1a"]))
            ms = seq.do("vector", E1a, waits=wts, inc="dv")
            stats_evac_ms.append(ms)
            return vis0 - sbase + len(segs) * P

    def emit_E1a(it, A_info):
        vis_of = {}
        first = True
        for (qt, ci, segs, rd, pe_ms, vis0) in A_info:
            vis_of[qt] = evac_round(qt, ci, segs, rd, pe_ms, vis0, first,
                                    sbase=qt * 1024)
            first = False
        return vis_of

    def emit_chain(qt, vis, sbase):
        def E1b(eng, qt=qt, vis=vis, sbase=sbase):
            return eng.reduce_max(out=mcol[:, 4 * qt:4 * qt + 1],
                                  in_=S_sb[:, sbase:sbase + vis], axis=AX.X,
                                  negate=True)
        dv_E1b = seq.do("vector", E1b, inc="dv")

        def X1a(eng, qt=qt, vis=vis, sbase=sbase):
            return eng.activation(out=Pdump[:, sbase:sbase + vis],
                                  in_=S_sb[:, sbase:sbase + vis],
                                  func=AF.Exp,
                                  bias=mcol[:, 4 * qt:4 * qt + 1],
                                  accum_out=den[:, 0:1])
        st["last_X1a"] = seq.do("scalar", X1a, waits=[("dv", dv_E1b)],
                                inc="ac")

        def X1b(eng, qt=qt):
            return eng.activation(out=mcol[:, 4 * qt + 1:4 * qt + 2],
                                  in_=den[:, 0:1], func=AF.Ln)
        ac_x1b = seq.do("scalar", X1b, inc="ac")

        def X2(eng, qt=qt):
            return eng.tensor_sub(
                out=mcol[:, 4 * qt + 2:4 * qt + 3],
                in0=mcol[:, 4 * qt:4 * qt + 1],
                in1=mcol[:, 4 * qt + 1:4 * qt + 2])
        wts = [("ac", ac_x1b)]
        if st["last_B1"] > 0:
            wts.append(("pe", st["last_B1"]))
        st["dv_X2"] = seq.do("vector", X2, waits=wts, inc="dv")

    def emit_BC(it, vis_of, chains_done=False):
        nonlocal st_rd
        h, qb = iters[it]
        q0, st_subs, stats_chunks = key_layout(h, qb)
        qrows, krows = rows_of(h)
        kvh = h * nkv // nh

        if not chains_done:
            for qt in range(nqt):
                emit_chain(qt, vis_of[qt], qt * 1024)

        def B1(eng, nqt=nqt):
            for qt in range(nqt):
                inst = eng.transpose(
                    k.ps[6][0:1, qt * P:(qt + 1) * P],
                    mcol[:, 4 * qt + 2:4 * qt + 3], cst["identity"][:])
            return inst
        wts = [("dv", st["dv_X2"])]
        if st["last_X4"] > 0:
            wts.append(("dv", st["last_X4"]))
        st["last_B1"] = seq.do("tensor", B1, waits=wts, inc="pe")

        def X3(eng, QB=QB):
            return eng.tensor_copy(out=brow[0:1, 0:QB],
                                   in_=k.ps[6][0:1, 0:QB])
        wts = [("pe", st["last_B1"])]
        if st["last_B2"] > 0:
            wts.append(("pe", st["last_B2"]))
        st["last_X3"] = seq.do("vector", X3, waits=wts, inc="dv")

        def B2(eng, QB=QB):
            return eng.matmul(k.ps[6][:, 0:QB], cst["ones_row"][0:1, :],
                              brow[0:1, 0:QB], start=True, stop=True)
        st["last_B2"] = seq.do("tensor", B2, waits=[("dv", st["last_X3"])],
                               inc="pe")

        def X4(eng, QB=QB):
            return eng.tensor_copy(out=b2bc[:, 0:QB], in_=k.ps[6][:, 0:QB])
        st["last_X4"] = seq.do("vector", X4, waits=[("pe", st["last_B2"])],
                               inc="dv")

        nsub = len(st_subs)
        first_pv = True

        def emit_ST(si):
            nonlocal st_rd
            k0, conds = st_subs[si]
            g = st_rd
            bank = k.ps[2 + g % 2]

            def ST(eng, k0=k0, bank=bank, QB=QB, q0=q0, qrows=qrows,
                   krows=krows):
                for di in range(len(qrows)):
                    ti, ro, n = qrows[di]
                    kti, kro, kn = krows[di]
                    inst = eng.matmul(
                        bank[:, 0:QB],
                        k_tiles[kti][kro:kro + kn,
                                     qcol0 + k0:qcol0 + k0 + P],
                        q_tiles[ti][ro:ro + n, qcol0 + q0:qcol0 + q0 + QB],
                        start=(di == 0), stop=(di == len(qrows) - 1))
                return inst
            wts = []
            if g >= 2:
                wts.append(("dv", stt_ms[g - 2]))
            pe_st = seq.do("tensor", ST, waits=wts, inc="pe")

            def STT(eng, bank=bank, g=g, QB=QB):
                return eng.scalar_tensor_tensor(
                    out=stf[g % 2][:, 0:QB], in0=bank[:, 0:QB],
                    scalar=scale, in1=b2bc[:, 0:QB],
                    op0=ALU.mult, op1=ALU.add)
            wts = [("pe", pe_st), ("dv", st["last_X4"])]
            if g >= 2:
                wts.append(("ac", exp_ms[g - 2]))
            stt_ms.append(seq.do("vector", STT, waits=wts, inc="dv"))

            def EXP(eng, g=g, QB=QB):
                return eng.activation(out=Pt[g % 4][:, 0:QB],
                                      in_=stf[g % 2][:, 0:QB], func=AF.Exp)
            wts = [("dv", stt_ms[g])]
            if g >= 4:
                wts.append(("pe", pv_ms[g - 4]))
            ac_exp = seq.do("scalar", EXP, waits=wts, inc="ac")
            exp_ms.append(ac_exp)

            pv_wait = [("ac", ac_exp)]
            if conds:
                def MSK(eng, conds=conds, g=g, QB=QB):
                    for c in conds:
                        if c[0] == "ge":
                            inst = eng.affine_select(
                                out=Pt[g % 4][:, 0:QB],
                                in_=Pt[g % 4][:, 0:QB],
                                pattern=[[1, QB]], compare_op=ALU.is_ge,
                                fill=0.0, base=c[1], channel_multiplier=-1)
                        else:
                            inst = eng.affine_select(
                                out=Pt[g % 4][:, 0:QB],
                                in_=Pt[g % 4][:, 0:QB],
                                pattern=[[-1, QB]], compare_op=ALU.is_ge,
                                fill=0.0, base=c[1], channel_multiplier=1)
                    return inst
                pl_ms = seq.do("gpsimd", MSK, waits=[("ac", ac_exp)],
                               inc="pl")
                pv_wait = [("pl", pl_ms)]
            st_rd += 1
            return g, pv_wait

        def emit_PV(si, g, pv_wait, first, last):
            k0, _ = st_subs[si]

            def PV(eng, k0=k0, g=g, QB=QB, kvh=kvh, first=first, last=last):
                kt = (qcol0 + k0) // P
                for dvt in range(ndv):
                    n = min(P, dh - dvt * P)
                    inst = eng.matmul(
                        k.ps[4 + dvt][0:n, 0:QB],
                        v_tiles[kt][:, kvh * dh + dvt * P:
                                    kvh * dh + dvt * P + n],
                        Pt[g % 4][:, 0:QB],
                        start=first, stop=last)
                return inst
            wts = list(pv_wait)
            if first and st["last_Oev"] > 0:
                wts.append(("dv", st["last_Oev"]))
            pe_pv = seq.do("tensor", PV, waits=wts, inc="pe")
            pv_ms.append(pe_pv)

        pend = []
        for si in range(nsub):
            pend.append((si,) + emit_ST(si))
            if len(pend) > 1:
                si0, g0, w0 = pend.pop(0)
                emit_PV(si0, g0, w0, si0 == 0, si0 == nsub - 1)
        for (si0, g0, w0) in pend:
            emit_PV(si0, g0, w0, si0 == 0, si0 == nsub - 1)

        def OEV(eng, h=h, q0=q0, QB=QB):
            for dvt in range(ndv):
                n = min(P, dh - dvt * P)
                for (ti, ro, nn) in _rows(h * dh + dvt * P, n):
                    inst = eng.tensor_copy(
                        out=o_tiles[ti][ro:ro + nn,
                                        qcol0 + q0:qcol0 + q0 + QB],
                        in_=k.ps[4 + dvt][0:nn, 0:QB])
            return inst
        st["last_Oev"] = seq.do("vector", OEV, waits=[("pe", pv_ms[-1])],
                                inc="dv")

    if mode == "win256":
        A_pend = emit_A(0)
        for it in range(len(iters)):
            vis_of = emit_E1a(it, A_pend)
            A_pend = emit_A(it + 1) if it + 1 < len(iters) else None
            emit_BC(it, vis_of)
    else:
        for it in range(len(iters)):
            emit_A(it, paired=True)
            emit_BC(it, None, chains_done=True)

    seq.run()
    k.unphase()



def op_qk_prep(k, tiles, nh, dh, tt, cst, l2eps=None, scale_col=None,
               rope_tabs=None, Tb=T):
    nt = len(tiles)
    ntc = tt // 512
    k.phase()
    q2 = [k.sb([P, 512], F32) for _ in range(2)]
    rrow = k.sb([min(nh, P), 512], F32)
    tmp = k.sb([P, 512], F32)
    t2 = k.sb([P, 512], F32)
    seq = Seq(k)
    mm_ms = []
    bc_ms = {}
    rot_ms = {}
    stt_ms = {}
    fin_ms = {}

    for tc in range(ntc):
        cs = slice(tc * 512, (tc + 1) * 512)
        tb0 = (tc * 512) % Tb
        tbc = slice(tb0, tb0 + 512)

        if l2eps is not None:
            for j in range(nt):
                g = tc * nt + j

                def SQ(eng, j=j, cs=cs, g=g):
                    return eng.activation(out=q2[g % 2][:],
                                          in_=tiles[j][:, cs], func=AF.Square)
                wts = [("pe", mm_ms[g - 2])] if g >= 2 else []
                ac_sq = seq.do("scalar", SQ, waits=wts, inc="ac")

                def MM(eng, j=j, g=g, first=(j == 0), last=(j == nt - 1)):
                    return eng.matmul(k.ps[0][0:nh, :],
                                      cst["sel_ssq"][:, j * nh:(j + 1) * nh],
                                      q2[g % 2][:], start=first, stop=last)
                mm_ms.append(seq.do("tensor", MM, waits=[("ac", ac_sq)],
                                    inc="pe"))

            def RN1(eng, cs=cs):
                return eng.tensor_copy(out=rrow[:], in_=k.ps[0][0:nh, :])
            dv_rn1 = seq.do("vector", RN1, waits=[("pe", mm_ms[-1])], inc="dv")

            def RN2(eng):
                return eng.activation(out=rrow[:], in_=rrow[:], func=AF.Sqrt)
            ac_rn2 = seq.do("scalar", RN2, waits=[("dv", dv_rn1)], inc="ac")

            def RN3(eng, l2eps=l2eps):
                return eng.tensor_scalar_max(out=rrow[:], in0=rrow[:],
                                             scalar1=l2eps)
            dv_rn3 = seq.do("vector", RN3, waits=[("ac", ac_rn2)], inc="dv")

            def RN4(eng):
                return eng.reciprocal(out=rrow[:], in_=rrow[:])
            dv_rn4 = seq.do("vector", RN4, inc="dv")

        for j in range(nt):
            g = tc * nt + j
            bank_bc = k.ps[1 + g % 2]
            if l2eps is not None:
                def BC(eng, j=j, bank=bank_bc):
                    return eng.matmul(bank[:, :],
                                      cst["sel_bc"][0:nh, j * P:(j + 1) * P],
                                      rrow[:], start=True, stop=True)
                wts = [("dv", dv_rn4)]
                prev = stt_ms.get((tc, j - 2)) or (stt_ms.get((tc - 1, nt - 2 + j))
                                                   if tc else None)
                if (tc * nt + j) >= 2:
                    pg = g - 2
                    pv = stt_ms.get((pg // nt, pg % nt))
                    if pv:
                        wts.append(("dv", pv))
                bc_ms[(tc, j)] = seq.do("tensor", BC, waits=wts, inc="pe")

                def APPLY(eng, j=j, cs=cs, bank=bank_bc):
                    if scale_col is not None:
                        return eng.scalar_tensor_tensor(
                            out=tiles[j][:, cs], in0=tiles[j][:, cs],
                            scalar=scale_col[j][:, 0:1], in1=bank[:, :],
                            op0=ALU.mult, op1=ALU.mult)
                    return eng.tensor_tensor(
                        out=tiles[j][:, cs], in0=tiles[j][:, cs],
                        in1=bank[:, :], op=ALU.mult)
                wts = [("pe", bc_ms[(tc, j)])]
                rp = rot_ms.get((tc - 1, j)) if tc else None
                stt_ms[(tc, j)] = seq.do("vector", APPLY, waits=wts, inc="dv")

            if rope_tabs is not None:
                A_t, B_t = rope_tabs
                bank_rot = k.ps[3 + g % 2]

                def ROT(eng, j=j, cs=cs, bank=bank_rot):
                    return eng.matmul(bank[:, :], cst["pswap"][:],
                                      tiles[j][:, cs], start=True, stop=True)
                wts = []
                if l2eps is not None:
                    wts.append(("dv", stt_ms[(tc, j)]))
                pg = g - 2
                if pg >= 0:
                    pv = fin_ms.get((pg // nt, pg % nt))
                    if pv:
                        wts.append(("dv", pv))
                rot_ms[(tc, j)] = seq.do("tensor", ROT, waits=wts, inc="pe")

                def RP1(eng, j=j, cs=cs, tbc=tbc):
                    return eng.tensor_tensor(out=tmp[:], in0=tiles[j][:, cs],
                                             in1=A_t[:, tbc], op=ALU.mult)
                wts = [("pe", rot_ms[(tc, j)])]
                dv_rp1 = seq.do("vector", RP1, waits=wts, inc="dv")

                def RP2(eng, j=j, tbc=tbc, bank=bank_rot):
                    return eng.tensor_tensor(out=t2[:], in0=bank[:, :],
                                             in1=B_t[:, tbc], op=ALU.mult)
                dv_rp2 = seq.do("vector", RP2, inc="dv")

                def RP3(eng, j=j, cs=cs):
                    return eng.tensor_add(out=tiles[j][:, cs], in0=tmp[:],
                                          in1=t2[:])
                fin_ms[(tc, j)] = seq.do("vector", RP3, inc="dv")

    seq.run()
    k.unphase()



def op_mamba_conv(k, xr_tiles, xs_t, cst, Tb=T):
    nd = 16
    ntc = Tb // 512
    k.phase()
    acc = k.sb([P, 512], F32)

    def conv_body(eng):
        last_copy = None
        for j in range(nd):
            for tc in range(ntc):
                c0 = 3 + tc * 512
                if last_copy is not None:
                    eng.wait_ge(k.sems["dv"], last_copy)
                inst = eng.tensor_scalar(
                    out=acc[:], in0=xr_tiles[j][:, c0 - 3:c0 - 3 + 512],
                    scalar1=cst["conv_w"][:, 4 * j:4 * j + 1],
                    scalar2=cst["conv_b"][:, j:j + 1],
                    op0=ALU.mult, op1=ALU.add)
                for i in (1, 2, 3):
                    eng.wait_ge(k.sems["dv"], k.inc(inst, "dv", 1))
                    inst = eng.scalar_tensor_tensor(
                        out=acc[:], in0=xr_tiles[j][:, c0 - 3 + i:c0 - 3 + i + 512],
                        scalar=cst["conv_w"][:, 4 * j + i:4 * j + i + 1],
                        in1=acc[:], op0=ALU.mult, op1=ALU.add)
                eng.wait_ge(k.sems["dv"], k.inc(inst, "dv", 1))
                inst = eng.tensor_copy(out=xs_t[j][:, tc * 512:(tc + 1) * 512],
                                       in_=acc[:])
                last_copy = k.inc(inst, "dv", 1)
    k.op({"vector": conv_body})

    def silu_body(eng):
        for j in range(nd):
            inst = eng.activation(out=xs_t[j][:], in_=xs_t[j][:], func=AF.Silu)
            k.inc(inst, "ac", 1)
    k.op({"scalar": silu_body})
    k.unphase()


def op_mamba_scan(k, xs_t, z_hbm, w_xproj_hbm, cst, Tb=T, zcol0=0):
    nd = 16
    ntc = Tb // 512
    y_tiles = xs_t
    k.phase()
    ssm = k.sb([80, Tb], F32)
    delta = k.sb([P, Tb], F32)
    u = k.sb([P, Tb], F32)
    dA = k.sb([P, Tb], F32)
    Pin = k.sb([P, Tb], F32)
    hbuf = k.sb([P, Tb], F32)
    yacc = k.sb([P, Tb], F32)
    tmp = k.sb([P, 512], F32)
    zbuf = [k.sb([P, Tb], BF16) for _ in range(2)]

    def epi(eng, ps_ap, ds, m):
        inst = eng.tensor_copy(out=ssm[0:m, 0:Tb], in_=ps_ap)
        k.inc(inst, "dv", 1)
    op_linearB(k, xs_t, w_xproj_hbm, nd * P, 80, Tb, epi, w_chunk=80)

    seq = Seq(k)
    dep_ms = {}
    zld_ms = {}
    for j in range(nd):
        def ZLD(eng, j=j):
            return eng.dma_start(out=zbuf[j % 2][:],
                                 in_=z_hbm[j * P:(j + 1) * P,
                                           zcol0:zcol0 + Tb])
        wz = []
        if (j - 2, "zgate") in dep_ms:
            wz.append(dep_ms[(j - 2, "zgate")])
        zld_ms[j] = seq.do("sync", ZLD, waits=wz, inc="qs", incn=16)
        for tc in range(ntc):
            def OUT(eng, j=j, tc=tc):
                return eng.matmul(
                    k.ps[tc][:, :], cst["dtw_row"][0:1, j * P:(j + 1) * P],
                    ssm[0:1, tc * 512:(tc + 1) * 512],
                    start=True, stop=True)
            wts = []
            if (j - 1, tc, "dpre_free") in dep_ms:
                wts.append(dep_ms[(j - 1, tc, "dpre_free")])
            pe_out = seq.do("tensor", OUT, waits=wts, inc="pe")

            def SP1(eng, tc=tc):
                return eng.activation(out=delta[:, tc * 512:(tc + 1) * 512],
                                      in_=k.ps[tc][:, :], func=AF.Abs,
                                      bias=cst["dtb"][:, j:j + 1] if False
                                      else 0.0)
            def SP1b(eng, tc=tc, j=j):
                return eng.activation(out=delta[:, tc * 512:(tc + 1) * 512],
                                      in_=k.ps[tc][:, :], func=AF.Abs,
                                      bias=cst["dtb"][:, j:j + 1])
            w1 = [("pe", pe_out)]
            if (j - 1, "delta_last_reader") in dep_ms and tc == 0:
                w1.append(dep_ms[(j - 1, "delta_last_reader")])
            ac1 = seq.do("scalar", SP1b, waits=w1, inc="ac")

            def SP2(eng, tc=tc):
                cs = slice(tc * 512, (tc + 1) * 512)
                return eng.activation(out=delta[:, cs], in_=delta[:, cs],
                                      func=AF.Exp, scale=-1.0)
            ac2 = seq.do("scalar", SP2, inc="ac")

            def SP3(eng, tc=tc):
                cs = slice(tc * 512, (tc + 1) * 512)
                return eng.activation(out=delta[:, cs], in_=delta[:, cs],
                                      func=AF.Ln, bias=1.0)
            ac3 = seq.do("scalar", SP3, inc="ac")

            def RL1(eng, tc=tc, j=j):
                return eng.tensor_scalar(
                    out=tmp[:], in0=k.ps[tc][:, :],
                    scalar1=cst["dtb"][:, j:j + 1], scalar2=0.0,
                    op0=ALU.add, op1=ALU.max)
            dv1 = seq.do("vector", RL1, waits=[("ac", ac1)], inc="dv")

            def RL2(eng, tc=tc):
                cs = slice(tc * 512, (tc + 1) * 512)
                return eng.tensor_add(out=delta[:, cs], in0=delta[:, cs],
                                      in1=tmp[:])
            dv2 = seq.do("vector", RL2, waits=[("ac", ac3)], inc="dv")
            dep_ms[(j, tc, "dpre_free")] = ("dv", dv1)

        def U(eng, j=j):
            return eng.tensor_mul(out=u[:], in0=delta[:], in1=xs_t[j][:])
        dv_u = seq.do("vector", U, inc="dv")

        def YZ(eng):
            return eng.memset(yacc[:], 0.0)
        dv_yz = seq.do("vector", YZ, inc="dv")

        prev_ac_da = None
        for n in range(16):
            def DA1(eng, j=j, n=n):
                return eng.tensor_scalar_mul(
                    out=dA[:], in0=delta[:],
                    scalar1=cst["A_cols"][:, j * 16 + n:j * 16 + n + 1])
            wda = [("ac", prev_ac_da)] if prev_ac_da else []
            dv_da1 = seq.do("vector", DA1, waits=wda, inc="dv")
            if n == 15:
                dep_ms[(j, "delta_last_reader")] = ("dv", dv_da1)

            def DA2(eng):
                return eng.activation(out=dA[:], in_=dA[:], func=AF.Exp)
            ac_da = seq.do("scalar", DA2, waits=[("dv", dv_da1)], inc="ac")
            prev_ac_da = ac_da

            bc_ms = []
            for tc in range(ntc):
                g2 = (j * 16 + n) * ntc + tc

                def BB(eng, n=n, tc=tc, g2=g2):
                    return eng.matmul(
                        k.ps[4 + g2 % 2][:, :],
                        cst["selB"][0:80, n * P:(n + 1) * P],
                        ssm[0:80, tc * 512:(tc + 1) * 512],
                        start=True, stop=True)
                wts = []
                if ("pin", g2 - 2) in dep_ms:
                    wts.append(dep_ms[("pin", g2 - 2)])
                pe_b = seq.do("tensor", BB, waits=wts, inc="pe")

                def CC(eng, n=n, tc=tc, g2=g2):
                    return eng.matmul(
                        k.ps[6 + g2 % 2][:, :],
                        cst["selC"][0:80, n * P:(n + 1) * P],
                        ssm[0:80, tc * 512:(tc + 1) * 512],
                        start=True, stop=True)
                wts = []
                if ("ym", g2 - 2) in dep_ms:
                    wts.append(dep_ms[("ym", g2 - 2)])
                pe_c = seq.do("tensor", CC, waits=wts, inc="pe")
                bc_ms.append((pe_b, pe_c, g2))

                def PIN(eng, tc=tc, g2=g2):
                    cs = slice(tc * 512, (tc + 1) * 512)
                    return eng.tensor_tensor(out=Pin[:, cs], in0=u[:, cs],
                                             in1=k.ps[4 + g2 % 2][:, :],
                                             op=ALU.mult)
                dv_pin = seq.do("vector", PIN, waits=[("pe", pe_b)], inc="dv")
                dep_ms[("pin", g2)] = ("dv", dv_pin)

            def SCAN(eng):
                return eng.tensor_tensor_scan(
                    out=hbuf[:], data0=dA[:], data1=Pin[:], initial=0.0,
                    op0=ALU.mult, op1=ALU.add)
            dv_scan = seq.do("vector", SCAN, waits=[("ac", ac_da)], inc="dv")

            for tc in range(ntc):
                g2 = bc_ms[tc][2]

                def YM(eng, tc=tc, g2=g2):
                    cs = slice(tc * 512, (tc + 1) * 512)
                    return eng.tensor_tensor(out=hbuf[:, cs], in0=hbuf[:, cs],
                                             in1=k.ps[6 + g2 % 2][:, :],
                                             op=ALU.mult)
                dv_ym = seq.do("vector", YM, waits=[("pe", bc_ms[tc][1])],
                               inc="dv")
                dep_ms[("ym", g2)] = ("dv", dv_ym)

                def YA(eng, tc=tc):
                    cs = slice(tc * 512, (tc + 1) * 512)
                    return eng.tensor_add(out=yacc[:, cs], in0=yacc[:, cs],
                                          in1=hbuf[:, cs])
                seq.do("vector", YA, inc="dv")

        def FD(eng, j=j):
            return eng.scalar_tensor_tensor(
                out=yacc[:], in0=xs_t[j][:], scalar=cst["D_col"][:, j:j + 1],
                in1=yacc[:], op0=ALU.mult, op1=ALU.add)
        dv_fd = seq.do("vector", FD, inc="dv")

        def SZ(eng, j=j):
            return eng.activation(out=y_tiles[j][:], in_=zbuf[j % 2][:],
                                  func=AF.Silu)
        ac_sz = seq.do("scalar", SZ, waits=[("qs", zld_ms[j]),
                                            ("dv", dv_fd)], inc="ac")

        def YG(eng, j=j):
            return eng.tensor_mul(out=y_tiles[j][:], in0=yacc[:],
                                  in1=y_tiles[j][:])
        dv_yg = seq.do("vector", YG, waits=[("ac", ac_sz), ("dv", dv_fd)],
                       inc="dv")
        dep_ms[(j, "zgate")] = ("ac", ac_sz)

    seq.run()
    k.unphase()


"""Orchestration: host prep + per-sublayer program builders + kernel().

Multi-launch version: one launch per top-level sublayer application.
R state [1024, 4096] fp32 per core flows between launches host-side.
"""
import numpy as np
import ml_dtypes
from contextlib import ExitStack
import concourse.bass as bass
import concourse.mybir as mybir
from concourse.bass_utils import run_bass_kernel_spmd


BFd = ml_dtypes.bfloat16
NCORE = 8



def host_consts():
    c = {}
    c["identity"] = np.eye(P, dtype=np.float32)
    qi, kj = np.arange(P)[:, None], np.arange(P)[None, :]
    c["wC"] = np.where(qi >= kj, 0.0, NEG).astype(np.float32)
    c["wA"] = np.where(kj >= qi, 0.0, NEG).astype(np.float32)
    ssq = np.zeros((P, 8 * 16), np.float32)
    sbc = np.zeros((16, 8 * P), np.float32)
    for j in range(8):
        for hf in range(2):
            h = 2 * j + hf
            ssq[hf * 64:(hf + 1) * 64, j * 16 + h] = 1.0
            sbc[h, j * P + hf * 64: j * P + (hf + 1) * 64] = 1.0
    c["sel_ssq"] = ssq
    c["sel_bc"] = sbc
    ssq4 = np.zeros((P, 8 * 4), np.float32)
    sbc4 = np.zeros((4, 8 * P), np.float32)
    for j in range(8):
        h = j // 2
        ssq4[:, j * 4 + h] = 1.0
        sbc4[h, j * P:(j + 1) * P] = 1.0
    c["sel_ssq4"] = ssq4
    c["sel_bc4"] = sbc4
    psw = np.zeros((P, P), np.float32)
    for blk in range(2):
        for r in range(64):
            psw[blk * 64 + (r + 32) % 64, blk * 64 + r] = 1.0
    c["pswap"] = psw
    DIM_HEAD, WINDOW, n = 64, 256, T
    half = np.arange(0, DIM_HEAD, 2, dtype=np.float32)
    inv_freq = 1.0 / 10000.0 ** (half / DIM_HEAD)
    t = np.arange(n, dtype=np.float32)
    freqs = np.concatenate([t[:, None] * inv_freq[None, :]] * 2, -1)
    base_scale = (half + 0.4 * DIM_HEAD) / (1.4 * DIM_HEAD)
    power = (t - n // 2) / (WINDOW // 2)
    scale = np.concatenate([base_scale[None, :] ** power[:, None]] * 2, -1)
    sgn = np.where(np.arange(DIM_HEAD) < 32, -1.0, 1.0)[None, :]

    def tilep(tab):
        return np.concatenate([tab.T, tab.T], 0).astype(np.float32)
    c["xpos_qA"] = tilep(np.cos(freqs) * scale)
    c["xpos_qB"] = tilep(np.sin(freqs) * scale * sgn)
    c["xpos_kA"] = tilep(np.cos(freqs) / scale)
    c["xpos_kB"] = tilep(np.sin(freqs) / scale * sgn)
    inv2 = 1.0 / 100000.0 ** (np.arange(0, 64, 2, dtype=np.float32) / 64)
    f2 = t[:, None] * inv2[None, :]
    A2 = np.concatenate([np.cos(f2), np.cos(f2)], 1)
    B2 = np.concatenate([-np.sin(f2), np.sin(f2)], 1)
    c["rope1_A"] = tilep(A2 * 1.0)
    c["rope1_B"] = tilep(B2 * 1.0)
    selB = np.zeros((80, 16 * P), np.float32)
    selC = np.zeros((80, 16 * P), np.float32)
    for nn in range(16):
        selB[32 + nn, nn * P:(nn + 1) * P] = 1.0
        selC[64 + nn, nn * P:(nn + 1) * P] = 1.0
    c["selB"] = selB
    c["selC"] = selC
    es = np.zeros((32, 9 * P), np.float32)
    for e in range(9):
        es[e, e * P:(e + 1) * P] = 1.0
    c["exp_sel"] = es
    return c


def col_layout(v):
    n = v.shape[0] // P
    return v.reshape(n, P).T.copy().astype(np.float32)


def pad_cols(w, to):
    out = np.zeros((w.shape[0], to), w.dtype)
    out[:, :w.shape[1]] = w
    return out


def pad_rows(w, to):
    out = np.zeros((to, w.shape[1]), w.dtype)
    out[:w.shape[0]] = w
    return out


def host_weights(inp):
    w = {}
    for i in range(6):
        w[f"f{i}_wqkv"] = np.ascontiguousarray(inp["L0_ffn_Wqkv"][i])
        w[f"f{i}_wo"] = np.ascontiguousarray(inp["L0_ffn_Wo"][i])
        W1 = inp["L0_ffn_W1"][i]
        w[f"f{i}_w1a"] = pad_cols(np.ascontiguousarray(W1[:, :2730]), 2816)
        w[f"f{i}_w1g"] = pad_cols(np.ascontiguousarray(W1[:, 2730:]), 2816)
        b1 = inp["L0_ffn_b1"][i]
        w[f"f{i}_b1a"] = col_layout(pad_rows(b1[:2730].reshape(-1, 1), 2816)[:, 0])
        w[f"f{i}_b1g"] = col_layout(pad_rows(b1[2730:].reshape(-1, 1), 2816)[:, 0])
        w[f"f{i}_w2"] = pad_rows(np.ascontiguousarray(inp["L0_ffn_W2"][i]), 2816)
        w[f"f{i}_b2"] = col_layout(inp["L0_ffn_b2"][i])
        w[f"f{i}_lng"] = col_layout(inp["L0_ffn_ln_g"][i])
        w[f"f{i}_lnb"] = col_layout(inp["L0_ffn_ln_b"][i])
        w[f"f{i}_qs"] = np.concatenate([inp["L0_ffn_q_scale"][i]] * 2).reshape(P, 1).astype(np.float32)
        w[f"f{i}_ks"] = np.concatenate([inp["L0_ffn_k_scale"][i]] * 2).reshape(P, 1).astype(np.float32)
    w["m_norm_m"] = col_layout(inp["L0_attn_norm_m_w"])
    w["m_inproj"] = np.ascontiguousarray(inp["L0_attn_in_proj_w"].T)
    cw = inp["L0_attn_conv_w"][:, 0, :]
    w["m_cw"] = cw.reshape(16, P, 4).transpose(1, 0, 2).reshape(P, 64).copy()
    w["m_cb"] = col_layout(inp["L0_attn_conv_b"])
    xpw = inp["L0_attn_x_proj_w"]
    WXP = np.zeros((2048, 80), np.float32)
    WXP[:, 0] = xpw[32]
    WXP[:, 32:48] = xpw[0:16].T
    WXP[:, 64:80] = xpw[16:32].T
    w["m_wxp"] = WXP
    w["m_dtw"] = inp["L0_attn_dt_w"].reshape(1, 2048).astype(np.float32)
    w["m_dtb"] = col_layout(inp["L0_attn_dt_b"])
    w["m_alog"] = inp["L0_attn_A_log"].reshape(16, P, 16).transpose(1, 0, 2).reshape(P, 256).copy()
    w["m_dc"] = col_layout(inp["L0_attn_D_skip"])
    w["m_outproj"] = np.ascontiguousarray(inp["L0_attn_out_proj_w"].T)
    w["m_norm_attn"] = col_layout(inp["L0_attn_norm_attn_w"])
    w["m_qw"] = np.ascontiguousarray(inp["L0_attn_q_w"].T)
    w["m_kw"] = np.ascontiguousarray(inp["L0_attn_k_w"].T)
    w["m_vw"] = np.ascontiguousarray(inp["L0_attn_v_w"].T)
    w["m_ow"] = np.ascontiguousarray(inp["L0_attn_o_w"].T)
    w["m_norm_ffn"] = col_layout(inp["L0_attn_norm_ffn_w"])
    w["m_gate"] = pad_cols(np.ascontiguousarray(inp["L0_attn_gate_w"].T), 2816)
    w["m_up"] = pad_cols(np.ascontiguousarray(inp["L0_attn_up_w"].T), 2816)
    w["m_down"] = pad_rows(np.ascontiguousarray(inp["L0_attn_down_w"].T), 2816)
    w["a_wq"] = np.ascontiguousarray(inp["L1_attn_Wq"].T)
    w["a_wk"] = np.ascontiguousarray(inp["L1_attn_Wk"].T)
    w["a_wv"] = np.ascontiguousarray(inp["L1_attn_Wv"].T)
    w["a_wo"] = np.ascontiguousarray(inp["L1_attn_Wo"].T)
    w["a_bo"] = col_layout(inp["L1_attn_bo"])
    w["a_qs"] = inp["L1_attn_q_scale"].reshape(1024)
    w["a_qs"] = w["a_qs"].reshape(8, P, 1).transpose(1, 0, 2).reshape(P, 8).copy()
    w["a_ks"] = inp["L1_attn_k_scale"].reshape(1024)
    w["a_ks"] = w["a_ks"].reshape(8, P, 1).transpose(1, 0, 2).reshape(P, 8).copy()
    w["g_ln1"] = col_layout(inp["L1_ffn_ln1_w"])
    w["g_ln2"] = col_layout(inp["L1_ffn_ln2_w"])
    w["g_wq"] = np.ascontiguousarray(inp["L1_ffn_Wq"])
    w["g_wkv"] = np.ascontiguousarray(inp["L1_ffn_Wkv"])
    w["g_wkup"] = np.ascontiguousarray(
        inp["L1_ffn_Wk_up"].transpose(1, 0, 2).reshape(256, 1024))
    w["g_wvup"] = np.ascontiguousarray(
        inp["L1_ffn_Wv_up"].transpose(1, 0, 2).reshape(256, 1024))
    w["g_wo"] = np.ascontiguousarray(inp["L1_ffn_Wo"])
    w["g_wr"] = np.ascontiguousarray(inp["L1_ffn_Wr"])
    w["g_rb"] = inp["L1_ffn_r_bias"].reshape(1, 8).astype(np.float32)
    w["g_sw1"] = np.ascontiguousarray(inp["L1_ffn_sw1"])
    w["g_sw2"] = np.ascontiguousarray(inp["L1_ffn_sw2"])
    w["g_sw3"] = np.ascontiguousarray(inp["L1_ffn_sw3"])
    for e in range(8):
        w[f"g_ew1_{e}"] = np.ascontiguousarray(inp["L1_ffn_ew1"][e])
        w[f"g_ew2_{e}"] = np.ascontiguousarray(inp["L1_ffn_ew2"][e])
        w[f"g_ew3_{e}"] = np.ascontiguousarray(inp["L1_ffn_ew3"][e])
    return w



CONST_SPECS = {
    "identity": ([P, P], F32), "wC": ([P, P], F32), "wA": ([P, P], F32),
    "sel_ssq": ([P, 8 * 16], F32), "sel_bc": ([16, 8 * P], F32),
    "sel_ssq4": ([P, 8 * 4], F32), "sel_bc4": ([4, 8 * P], F32),
    "pswap": ([P, P], BF16),
    "xpos_qA": ([P, T], BF16), "xpos_qB": ([P, T], BF16),
    "xpos_kA": ([P, T], BF16), "xpos_kB": ([P, T], BF16),
    "rope1_A": ([P, T], BF16), "rope1_B": ([P, T], BF16),
    "selB": ([80, 16 * P], F32), "selC": ([80, 16 * P], F32),
    "exp_sel": ([32, 9 * P], BF16),
}


BASE_CONSTS = ["identity", "wC", "wA"]


def mk_ker(nc, ctx, im, hostc):
    block = ctx.enter_context(nc.Block())
    k = Ker(nc, block, ctx)
    k._const_aps = {}
    k._w_aps = {}
    base = {}
    pairs = []
    for nm in BASE_CONSTS:
        shape, dt = CONST_SPECS[nm]
        ap = const_ap(k, nc, im, hostc, nm)
        sb = k.sb(shape, dt, "cst_" + nm)
        base[nm] = sb
        pairs.append((sb[:], ap))
    base["ones_col"] = k.sb([P, 1], F32, "ones_col")
    base["ones_row"] = k.sb([1, P], F32, "ones_row")

    def init(eng):
        eng.memset(base["ones_col"][:], 1.0)
        inst = eng.memset(base["ones_row"][:], 1.0)
        k.inc(inst, "pl", 1)
    k.op({"gpsimd": init})
    op_load(k, pairs, queue="qs")
    k._base_cst = base
    return k


def const_ap(k, nc, im, hostc, nm):
    if nm in k._const_aps:
        return k._const_aps[nm]
    shape, dt = CONST_SPECS[nm]
    ap = nc.dram_tensor("c_" + nm, list(shape), F32,
                        kind="ExternalInput").ap()
    k._const_aps[nm] = ap
    im["c_" + nm] = np.ascontiguousarray(hostc[nm]).astype(np.float32)
    return ap


def sec_consts(k, nc, im, hostc, names):
    cst = dict(k._base_cst)
    pairs_qs, pairs_qg = [], []
    for nm in names:
        if nm in BASE_CONSTS:
            continue
        shape, dt = CONST_SPECS[nm]
        ap = const_ap(k, nc, im, hostc, nm)
        sb = k.sb(shape, dt)
        cst[nm] = sb
        (pairs_qg if dt == BF16 else pairs_qs).append((sb[:], ap))
    if pairs_qs:
        op_load(k, pairs_qs, queue="qs")
    if pairs_qg:
        op_load(k, pairs_qg, queue="qg")
    return cst


def scratch_ap(k, nc, name, shape, dtype):
    if not hasattr(k, "_s_aps"):
        k._s_aps = {}
    if name in k._s_aps:
        return k._s_aps[name]
    ap = nc.dram_tensor(name, list(shape), dtype, kind="Internal").ap()
    k._s_aps[name] = ap
    return ap


def weight_ap(k, nc, im, hostw, name, shape):
    if name in k._w_aps:
        return k._w_aps[name]
    ap = nc.dram_tensor(name, list(shape), F32, kind="ExternalInput").ap()
    k._w_aps[name] = ap
    im[name] = hostw[name]
    return ap


def win(nc, name, shape, dtype=F32):
    return nc.dram_tensor(name, list(shape), dtype, kind="ExternalInput").ap()


def wcol_sb(k, nc, name, arr_shape):
    ap = nc.dram_tensor(name, list(arr_shape), F32, kind="ExternalInput").ap()
    sb = k.sb(list(arr_shape), F32, name + "_sb")
    return sb, ap


def ld_shifted_bf16(k, dst_tiles, X_ap, b, upper_from=4):
    c0 = b * T

    def body(eng):
        for j in range(upper_from, 8):
            inst = eng.memset(dst_tiles[j][:, 0:1], 0.0)
        k.inc(inst, "pl", 1)
        for j in range(8):
            if j < upper_from:
                inst = eng.dma_start(out=dst_tiles[j][:],
                                     in_=X_ap[j * P:(j + 1) * P, c0:c0 + T])
            else:
                inst = eng.dma_start(
                    out=dst_tiles[j][:, 1:T],
                    in_=X_ap[j * P:(j + 1) * P, c0:c0 + T - 1])
            k.inc(inst, "qg", 16)
    k.op({"gpsimd": body})


def ld_tiles(k, dst_tiles, X_ap, b, n=8, queue="qg", Tb=T):
    c0 = b * Tb

    def body(eng):
        for j in range(n):
            inst = eng.dma_start(out=dst_tiles[j][:],
                                 in_=X_ap[j * P:(j + 1) * P, c0:c0 + Tb])
            k.inc(inst, queue, 16)
    k.op({QUEUE_ENG[queue]: body})


def st_tiles(k, src_tiles, X_ap, b, n=8, queue="qg", Tb=T):
    c0 = b * Tb

    def body(eng):
        for j in range(n):
            inst = eng.dma_start(out=X_ap[j * P:(j + 1) * P, c0:c0 + Tb],
                                 in_=src_tiles[j][:])
            k.inc(inst, queue, 16)
    k.op({QUEUE_ENG[queue]: body})




def norm_to_bf16(k, x_tiles, out_tiles, cst, rms=False, eps=1e-5,
                 gain_col=None, bias_col=None, chunk=512, nr=8):
    k.phase()
    tmp = k.sb([P, 512], F32)
    tmp2 = k.sb([P, 512], F32)

    def apply_fn(eng, di, a_ps, b_ps, c0, w):
        for j in range(w // 512):
            cs = slice(c0 + j * 512, c0 + (j + 1) * 512)
            if rms:
                if gain_col is not None:
                    inst = eng.scalar_tensor_tensor(
                        out=out_tiles[di][:, cs], in0=x_tiles[di][:, cs],
                        scalar=gain_col[:, di:di + 1], in1=a_ps[j][:],
                        op0=ALU.mult, op1=ALU.mult)
                else:
                    inst = eng.tensor_tensor(
                        out=out_tiles[di][:, cs], in0=x_tiles[di][:, cs],
                        in1=a_ps[j][:], op=ALU.mult)
                k.inc(inst, "dv", 1)
            else:
                inst = eng.tensor_tensor(out=tmp[:], in0=x_tiles[di][:, cs],
                                         in1=a_ps[j][:], op=ALU.mult)
                eng.wait_ge(k.sems["dv"], k.inc(inst, "dv", 1))
                if gain_col is None:
                    inst = eng.tensor_tensor(out=out_tiles[di][:, cs],
                                             in0=tmp[:], in1=b_ps[j][:],
                                             op=ALU.add)
                    k.inc(inst, "dv", 1)
                else:
                    inst = eng.tensor_tensor(out=tmp2[:], in0=tmp[:],
                                             in1=b_ps[j][:], op=ALU.add)
                    eng.wait_ge(k.sems["dv"], k.inc(inst, "dv", 1))
                    inst = eng.tensor_scalar(
                        out=out_tiles[di][:, cs], in0=tmp2[:],
                        scalar1=gain_col[:, di:di + 1],
                        scalar2=bias_col[:, di:di + 1],
                        op0=ALU.mult, op1=ALU.add)
                    k.inc(inst, "dv", 1)
    op_norm(k, x_tiles, nr, x_tiles[0].shape[1] if hasattr(x_tiles[0], "shape")
            else T, cst["ones_col"], cst["ones_row"], apply_fn, rms=rms,
            eps=eps, chunk=chunk)
    k.unphase()


def op_hmul(k, a_tiles, b_tiles, n):
    def body(eng):
        for i in range(n):
            inst = eng.tensor_mul(out=a_tiles[i][:], in0=a_tiles[i][:],
                                  in1=b_tiles[i][:])
            k.inc(inst, "dv", 1)
    k.op({"vector": body})


def mk_epi_copy(k, out_tiles, row_off=0, col_off=0, eng_sem="dv"):
    def epi(eng, ps_ap, ds, m):
        ti, ro = divmod(row_off + ds * P, P)
        w = ps_ap.shape[1]
        inst = eng.tensor_copy(
            out=out_tiles[ti][ro:ro + m, col_off:col_off + w], in_=ps_ap)
        k.inc(inst, eng_sem, 1)
    return epi


def mk_epiA_copy(k, out_tiles, tile_off=0, col_off=0):
    def epi(eng, ps_ap, ti, wid):
        inst = eng.tensor_copy(
            out=out_tiles[tile_off + ti][:, col_off:col_off + wid], in_=ps_ap)
        k.inc(inst, "dv", 1)
    return epi


def mk_epi_scale(k, out_tiles, mul, col_off=0):
    def epi(eng, ps_ap, ds, m):
        w = ps_ap.shape[1]
        inst = eng.tensor_scalar_mul(
            out=out_tiles[ds][0:m, col_off:col_off + w],
            in0=ps_ap, scalar1=mul)
        k.inc(inst, "dv", 1)
    return epi


def mk_epi_bias(k, out_tiles, bias_col, col_off=0):
    def epi(eng, ps_ap, ds, m):
        w = ps_ap.shape[1]
        inst = eng.tensor_scalar(
            out=out_tiles[ds][0:m, col_off:col_off + w],
            in0=ps_ap, scalar1=bias_col[:, ds:ds + 1], scalar2=None,
            op0=ALU.add)
        k.inc(inst, "dv", 1)
    return epi


def mk_epi_act(k, out_tiles, func, bias_col=None, col_off=0):
    def epi(eng, ps_ap, ds, m):
        w = ps_ap.shape[1]
        kw = {}
        if bias_col is not None:
            kw["bias"] = bias_col[:, ds:ds + 1]
        inst = eng.activation(
            out=out_tiles[ds][0:m, col_off:col_off + w],
            in_=ps_ap, func=func, **kw)
        k.inc(inst, "ac", 1)
    return epi


def mk_epi_residual(k, r_tiles, alpha=1.0, bias_col=None, col_off=0):
    tmp = k.scratch

    def epi(eng, ps_ap, ds, m):
        w = ps_ap.shape[1]
        cs = slice(col_off, col_off + w)
        if bias_col is None:
            inst = eng.scalar_tensor_tensor(
                out=r_tiles[ds][0:m, cs], in0=ps_ap, scalar=alpha,
                in1=r_tiles[ds][0:m, cs], op0=ALU.mult, op1=ALU.add)
            k.inc(inst, "dv", 1)
        else:
            inst = eng.tensor_scalar(out=tmp[0:m, 0:w], in0=ps_ap,
                                     scalar1=bias_col[:, ds:ds + 1],
                                     scalar2=None, op0=ALU.add)
            eng.wait_ge(k.sems["dv"], k.inc(inst, "dv", 1))
            inst = eng.scalar_tensor_tensor(
                out=r_tiles[ds][0:m, cs], in0=tmp[0:m, 0:w], scalar=alpha,
                in1=r_tiles[ds][0:m, cs], op0=ALU.mult, op1=ALU.add)
            k.inc(inst, "dv", 1)
    return epi



def emit_l0ffn(k, nc, im, hostc, hostw, R, Rout, nlayers=6):
    names = ["sel_ssq", "sel_bc", "pswap",
             "xpos_qA", "xpos_qB", "xpos_kA", "xpos_kB"]
    k.phase()
    if True:
        cst = sec_consts(k, nc, im, hostc, names)
        X = scratch_ap(k, nc, "Xscr", [D, TT], F32)
        wq_aps, cols = {}, {}
        colpairs = []
        for i in range(6):
            for nm, shp in [("wqkv", [D, 3072]), ("wo", [D, D]),
                            ("w1a", [D, 2816]), ("w1g", [D, 2816]),
                            ("w2", [2816, D])]:
                wq_aps[(i, nm)] = weight_ap(k, nc, im, hostw,
                                            f"f{i}_{nm}", shp)
            for nm, shp in [("b1a", [P, 22]), ("b1g", [P, 22]),
                            ("b2", [P, 8]), ("lng", [P, 8]), ("lnb", [P, 8]),
                            ("qs", [P, 1]), ("ks", [P, 1])]:
                ap = weight_ap(k, nc, im, hostw, f"f{i}_{nm}", shp)
                sb = k.sb(list(shp), F32)
                cols[(i, nm)] = sb
                colpairs.append((sb[:], ap))
        op_load(k, colpairs, queue="qs")

        for b in range(NB):
            k.phase()
            rb = k.sb_tiles(8, T, F32)
            xf = k.sb_tiles(8, T, F32)
            ld_tiles(k, rb, R, b, queue="qs")
            k.phase()
            tmp = k.sb([P, 512], F32)

            def apply_fn(eng, di, a_ps, b_ps, c0, w):
                for j in range(w // 512):
                    cs = slice(c0 + j * 512, c0 + (j + 1) * 512)
                    inst = eng.tensor_tensor(out=tmp[:], in0=rb[di][:, cs],
                                             in1=a_ps[j][:], op=ALU.mult)
                    eng.wait_ge(k.sems["dv"], k.inc(inst, "dv", 1))
                    inst = eng.tensor_tensor(out=xf[di][:, cs], in0=tmp[:],
                                             in1=b_ps[j][:], op=ALU.add)
                    k.inc(inst, "dv", 1)
            op_norm(k, rb, 8, T, cst["ones_col"], cst["ones_row"], apply_fn,
                    rms=False, chunk=512)
            k.unphase()
            st_tiles(k, xf, X, b, queue="qs")
            k.unphase()

        for i in range(nlayers):
            for b in range(NB):
                k.phase()
                ot = k.sb_tiles(8, T, BF16)
                k.phase()
                vtok = k.sb_tiles(16, 1024, BF16)
                qk = k.sb_tiles(16, T, BF16)
                k.phase()
                xs = k.sb_tiles(8, T, BF16)
                wv_sb = k.sb_tiles(8, 512, BF16)
                ld_shifted_bf16(k, xs, X, b)
                for dc2 in range(2):
                    op_load(k, [(wv_sb[j][:],
                                 wq_aps[(i, "wqkv")][j * P:(j + 1) * P,
                                                     2048 + dc2 * 512:
                                                     2560 + dc2 * 512])
                                for j in range(8)], queue="qg")
                    op_linearA(k, xs, wv_sb, D, 512, T,
                               mk_epiA_copy(k, vtok, col_off=dc2 * 512))
                op_linearB(k, xs, wq_aps[(i, "wqkv")][:, 0:2048], D, 2048, T,
                           mk_epi_copy(k, qk), w_chunk=128)
                k.unphase()
                op_qk_prep(k, qk[0:8], 16, 64, T, cst, l2eps=1e-12,
                           scale_col=[cols[(i, "qs")]] * 8,
                           rope_tabs=(cst["xpos_qA"], cst["xpos_qB"]))
                op_qk_prep(k, qk[8:16], 16, 64, T, cst, l2eps=1e-12,
                           scale_col=[cols[(i, "ks")]] * 8,
                           rope_tabs=(cst["xpos_kA"], cst["xpos_kB"]))
                op_attention(k, qk[0:8], qk[8:16], vtok, ot, 16, 64, 16, T,
                             "win256", 8.0, cst)
                k.unphase()
                k.phase()
                xb = k.sb_tiles(8, T, F32)
                ld_tiles(k, xb, X, b, queue="qs")
                op_linearB(k, ot, wq_aps[(i, "wo")], D, D, T,
                           mk_epi_residual(k, xb))
                st_tiles(k, xb, X, b, queue="qs")
                k.unphase()
                k.unphase()

            for b in range(NB):
                k.phase()
                yout = k.sb_tiles(8, T, BF16)
                k.phase()
                xn = k.sb_tiles(8, T, BF16)
                k.phase()
                xsf = k.sb_tiles(8, T, F32)
                ld_shifted_bf16(k, xsf, X, b)
                norm_to_bf16(k, xsf, xn, cst, rms=False, eps=1e-5,
                             gain_col=cols[(i, "lng")],
                             bias_col=cols[(i, "lnb")], chunk=512)
                k.unphase()
                k.phase()
                gt = [k.sb([P, T], BF16) for _ in range(22)]
                op_linearB(k, xn, wq_aps[(i, "w1g")], D, 2816, T,
                           mk_epi_act(k, gt, AF.Gelu, cols[(i, "b1g")]),
                           epi_eng="scalar")

                def epi_amul(eng, ps_ap, ds, m, i=i):
                    w = ps_ap.shape[1]
                    inst = eng.tensor_scalar(
                        out=k.scratch[0:m, 0:w], in0=ps_ap,
                        scalar1=cols[(i, "b1a")][:, ds:ds + 1], scalar2=None,
                        op0=ALU.add)
                    eng.wait_ge(k.sems["dv"], k.inc(inst, "dv", 1))
                    inst = eng.tensor_mul(
                        out=gt[ds][0:m, 0:w], in0=gt[ds][0:m, 0:w],
                        in1=k.scratch[0:m, 0:w])
                    k.inc(inst, "dv", 1)
                op_linearB(k, xn, wq_aps[(i, "w1a")], D, 2816, T, epi_amul,
                           epi_incs=2)
                op_linearB(k, gt, wq_aps[(i, "w2")], 2816, D, T,
                           mk_epi_bias(k, yout, cols[(i, "b2")]),
                           w_chunk=128)
                k.unphase()
                k.unphase()
                k.phase()
                xb = k.sb_tiles(8, T, F32)
                ld_tiles(k, xb, X, b, queue="qs")

                def addres(eng):
                    for j in range(8):
                        inst = eng.tensor_add(out=xb[j][:], in0=xb[j][:],
                                              in1=yout[j][:])
                        k.inc(inst, "dv", 1)
                k.op({"vector": addres})
                st_tiles(k, xb, X, b, queue="qs")
                k.unphase()
                k.unphase()

        for b in range(NB):
            k.phase()
            rb = k.sb_tiles(8, T, F32)
            xb = k.sb_tiles(8, T, F32)
            ld_tiles(k, rb, R, b, queue="qs")
            ld_tiles(k, xb, X, b, queue="qg")

            def addf(eng):
                for j in range(8):
                    inst = eng.scalar_tensor_tensor(
                        out=rb[j][:], in0=xb[j][:], scalar=0.5, in1=rb[j][:],
                        op0=ALU.mult, op1=ALU.add)
                    k.inc(inst, "dv", 1)
            k.op({"vector": addf})
            st_tiles(k, rb, Rout, b, queue="qs")
            k.unphase()
    k.unphase()


def build_l0ffn(hostc, hostw, nlayers=6):
    nc = bass.Bass("TRN2", target_bir_lowering=False)
    ctx = ExitStack()
    im = {}
    with ctx:
        k = mk_ker(nc, ctx, im, hostc)
        R = win(nc, "R", [D, TT])
        Rout = nc.dram_tensor("Rout", [D, TT], F32, kind="ExternalOutput").ap()
        emit_l0ffn(k, nc, im, hostc, hostw, R, Rout, nlayers)
        k.finish()
    return nc, im



def emit_l0attn(k, nc, im, hostc, hostw, R, Rout):
    names = ["selB", "selC"]
    k.phase()
    if True:
        cst = sec_consts(k, nc, im, hostc, names)
        X = scratch_ap(k, nc, "Xscr", [D, TT], F32)
        Z = scratch_ap(k, nc, "Zscr", [2048, TT], BF16)
        XN = scratch_ap(k, nc, "XNscr", [D, TT], BF16)
        waps = {}
        for nm, shp in [("inproj", [D, 4096]), ("wxp", [2048, 80]),
                        ("outproj", [2048, D]), ("qw", [D, D]),
                        ("kw", [D, 512]), ("vw", [D, 512]), ("ow", [D, D]),
                        ("gate", [D, 2816]), ("up", [D, 2816]),
                        ("down", [2816, D])]:
            waps[nm] = weight_ap(k, nc, im, hostw, "m_" + nm, shp)
        mcols = {}
        colpairs = []
        for nm, shp in [("norm_m", [P, 8]), ("cw", [P, 64]), ("cb", [P, 16]),
                        ("dtw", [1, 2048]), ("dtb", [P, 16]),
                        ("alog", [P, 256]), ("dc", [P, 16]),
                        ("norm_attn", [P, 8]), ("norm_ffn", [P, 8])]:
            ap = weight_ap(k, nc, im, hostw, "m_" + nm, shp)
            sb = k.sb(list(shp), F32)
            mcols[nm] = sb
            colpairs.append((sb[:], ap))
        op_load(k, colpairs, queue="qs")
        A_cols = k.sb([P, 256], F32, "A_cols")

        def acomp(eng):
            inst = eng.activation(out=A_cols[:], in_=mcols["alog"][:],
                                  func=AF.Exp)
            k.inc(inst, "ac", 1)
        k.op({"scalar": acomp})

        def aneg(eng):
            inst = eng.tensor_scalar_mul(out=A_cols[:], in0=A_cols[:],
                                         scalar1=-1.0)
            k.inc(inst, "dv", 1)
        k.op({"vector": aneg})
        mcst = {"conv_w": mcols["cw"], "conv_b": mcols["cb"],
                "dtw_row": mcols["dtw"], "dtb": mcols["dtb"],
                "selB": cst["selB"], "selC": cst["selC"],
                "A_cols": A_cols, "D_col": mcols["dc"]}

        for b in range(NB):
            k.phase()
            rb = k.sb_tiles(8, T, F32)
            xf = k.sb_tiles(8, T, F32)
            ld_tiles(k, rb, R, b, queue="qs")
            k.phase()
            tmp = k.sb([P, 512], F32)

            def apply_fn(eng, di, a_ps, b_ps, c0, w, rb=rb, xf=xf, tmp=tmp):
                for j in range(w // 512):
                    cs = slice(c0 + j * 512, c0 + (j + 1) * 512)
                    inst = eng.tensor_tensor(out=tmp[:], in0=rb[di][:, cs],
                                             in1=a_ps[j][:], op=ALU.mult)
                    eng.wait_ge(k.sems["dv"], k.inc(inst, "dv", 1))
                    inst = eng.tensor_tensor(out=xf[di][:, cs], in0=tmp[:],
                                             in1=b_ps[j][:], op=ALU.add)
                    k.inc(inst, "dv", 1)
            op_norm(k, rb, 8, T, cst["ones_col"], cst["ones_row"], apply_fn,
                    rms=False, chunk=512)
            k.unphase()
            st_tiles(k, xf, X, b, queue="qs")
            k.unphase()

        for b in range(NB):
            k.phase()
            xnm = k.sb_tiles(8, T, BF16)
            k.phase()
            xb = k.sb_tiles(8, T, F32)
            ld_tiles(k, xb, X, b, queue="qs")
            norm_to_bf16(k, xb, xnm, cst, rms=True, eps=1e-6,
                         gain_col=mcols["norm_m"], chunk=512)
            k.unphase()
            st_tiles(k, xnm, XN, b, queue="qg")
            k.unphase()
            k.phase()
            xs_t = k.sb_tiles(16, T, BF16)
            k.phase()
            xr = k.sb_tiles(16, 3 + T, BF16)
            k.phase()
            xnm = k.sb_tiles(8, T, BF16)
            ld_tiles(k, xnm, XN, b, queue="qg")
            op_linearB(k, xnm, waps["inproj"][:, 2048:4096], D, 2048, T,
                       mk_epi_copy(k, xs_t), w_chunk=256)
            st_tiles(k, xs_t, Z, b, n=16, queue="qg", Tb=T)

            def zeropad(eng, xr=xr):
                for j in range(16):
                    inst = eng.memset(xr[j][:, 0:3], 0.0)
                k.inc(inst, "pl", 1)
            k.op({"gpsimd": zeropad})

            def epi_inx(eng, ps_ap, ds, m, xr=xr):
                w = ps_ap.shape[1]
                inst = eng.tensor_copy(out=xr[ds][0:m, 3:3 + w], in_=ps_ap)
                k.inc(inst, "dv", 1)
            op_linearB(k, xnm, waps["inproj"][:, 0:2048], D, 2048, T,
                       epi_inx, w_chunk=256)
            k.unphase()
            op_mamba_conv(k, xr, xs_t, mcst, Tb=T)
            k.unphase()
            op_mamba_scan(k, xs_t, Z, waps["wxp"], mcst, Tb=T, zcol0=b * T)
            k.phase()
            xb = k.sb_tiles(8, T, F32)
            ld_tiles(k, xb, X, b, queue="qs")
            op_linearB(k, xs_t, waps["outproj"], 2048, D, T,
                       mk_epi_residual(k, xb), w_chunk=256)
            st_tiles(k, xb, X, b, queue="qs")
            k.unphase()
            k.unphase()

        for b in range(NB):
            k.phase()
            xnm = k.sb_tiles(8, T, BF16)
            k.phase()
            xb = k.sb_tiles(8, T, F32)
            ld_tiles(k, xb, X, b, queue="qs")
            norm_to_bf16(k, xb, xnm, cst, rms=True, eps=1e-6,
                         gain_col=mcols["norm_attn"], chunk=512)
            k.unphase()
            k.phase()
            ot = k.sb_tiles(8, T, BF16)
            k.phase()
            qt = k.sb_tiles(8, T, BF16)
            kt = k.sb_tiles(4, T, BF16)
            vtok = k.sb_tiles(16, 512, BF16)
            k.phase()
            wv_sb = k.sb_tiles(8, 512, BF16)
            op_load(k, [(wv_sb[j][:], waps["vw"][j * P:(j + 1) * P, :])
                        for j in range(8)], queue="qg")
            op_linearA(k, xnm, wv_sb, D, 512, T, mk_epiA_copy(k, vtok))
            k.unphase()
            op_linearB(k, xnm, waps["qw"], D, D, T, mk_epi_copy(k, qt))
            op_linearB(k, xnm, waps["kw"], D, 512, T, mk_epi_copy(k, kt))
            op_attention(k, qt, kt, vtok, ot, 4, 256, 2, T, "causal",
                         1.0 / 16.0, cst)
            k.unphase()
            k.phase()
            xb = k.sb_tiles(8, T, F32)
            ld_tiles(k, xb, X, b, queue="qs")
            op_linearB(k, ot, waps["ow"], D, D, T, mk_epi_residual(k, xb))
            st_tiles(k, xb, X, b, queue="qs")
            k.unphase()
            k.unphase()
            k.unphase()

        for b in range(NB):
            k.phase()
            yout = k.sb_tiles(8, T, BF16)
            k.phase()
            xn = k.sb_tiles(8, T, BF16)
            k.phase()
            xb = k.sb_tiles(8, T, F32)
            ld_tiles(k, xb, X, b, queue="qs")
            norm_to_bf16(k, xb, xn, cst, rms=True, eps=1e-6,
                         gain_col=mcols["norm_ffn"], chunk=512)
            k.unphase()
            k.phase()
            gt = [k.sb([P, T], BF16) for _ in range(22)]
            op_linearB(k, xn, waps["gate"], D, 2816, T,
                       mk_epi_act(k, gt, AF.Silu), epi_eng="scalar")

            def epi_umul(eng, ps_ap, ds, m):
                w = ps_ap.shape[1]
                inst = eng.tensor_copy(out=k.scratch[0:m, 0:w], in_=ps_ap)
                eng.wait_ge(k.sems["dv"], k.inc(inst, "dv", 1))
                inst = eng.tensor_mul(
                    out=gt[ds][0:m, 0:w], in0=gt[ds][0:m, 0:w],
                    in1=k.scratch[0:m, 0:w])
                k.inc(inst, "dv", 1)
            op_linearB(k, xn, waps["up"], D, 2816, T, epi_umul, epi_incs=2)
            op_linearB(k, gt, waps["down"], 2816, D, T,
                       mk_epi_copy(k, yout), w_chunk=128)
            k.unphase()
            k.unphase()
            k.phase()
            xb = k.sb_tiles(8, T, F32)
            ld_tiles(k, xb, X, b, queue="qs")

            def addres(eng, xb=xb, yout=yout):
                for j in range(8):
                    inst = eng.tensor_add(out=xb[j][:], in0=xb[j][:],
                                          in1=yout[j][:])
                    k.inc(inst, "dv", 1)
            k.op({"vector": addres})
            st_tiles(k, xb, X, b, queue="qs")
            k.unphase()
            k.unphase()

        for b in range(NB):
            k.phase()
            rb = k.sb_tiles(8, T, F32)
            xb = k.sb_tiles(8, T, F32)
            ld_tiles(k, rb, R, b, queue="qs")
            ld_tiles(k, xb, X, b, queue="qg")

            def addf(eng, rb=rb, xb=xb):
                for j in range(8):
                    inst = eng.tensor_add(out=rb[j][:], in0=rb[j][:],
                                          in1=xb[j][:])
                    k.inc(inst, "dv", 1)
            k.op({"vector": addf})
            st_tiles(k, rb, Rout, b, queue="qs")
            k.unphase()
    k.unphase()


def build_l0attn(hostc, hostw):
    nc = bass.Bass("TRN2", target_bir_lowering=False)
    ctx = ExitStack()
    im = {}
    with ctx:
        k = mk_ker(nc, ctx, im, hostc)
        R = win(nc, "R", [D, TT])
        Rout = nc.dram_tensor("Rout", [D, TT], F32, kind="ExternalOutput").ap()
        emit_l0attn(k, nc, im, hostc, hostw, R, Rout)
        k.finish()
    return nc, im



def emit_l1attn(k, nc, im, hostc, hostw, R, Rout):
    names = ["sel_ssq", "sel_bc"]
    k.phase()
    if True:
        cst = sec_consts(k, nc, im, hostc, names)
        waps = {}
        for nm in ["wq", "wk", "wv", "wo"]:
            waps[nm] = weight_ap(k, nc, im, hostw, "a_" + nm, [D, D])
        acols = {}
        colpairs = []
        for nm, shp in [("bo", [P, 8]), ("qs", [P, 8]), ("ks", [P, 8])]:
            ap = weight_ap(k, nc, im, hostw, "a_" + nm, shp)
            sb = k.sb(list(shp), F32)
            acols[nm] = sb
            colpairs.append((sb[:], ap))
        op_load(k, colpairs, queue="qs")

        for b in range(NB):
            k.phase()
            xn = k.sb_tiles(8, T, BF16)
            k.phase()
            rb = k.sb_tiles(8, T, F32)
            ld_tiles(k, rb, R, b, queue="qs")
            k.phase()
            tmp = k.sb([P, 512], F32)

            def apply_fn(eng, di, a_ps, b_ps, c0, w, rb=rb, xn=xn, tmp=tmp):
                for j in range(w // 512):
                    cs = slice(c0 + j * 512, c0 + (j + 1) * 512)
                    inst = eng.tensor_tensor(out=tmp[:], in0=rb[di][:, cs],
                                             in1=a_ps[j][:], op=ALU.mult)
                    eng.wait_ge(k.sems["dv"], k.inc(inst, "dv", 1))
                    inst = eng.tensor_tensor(out=xn[di][:, cs], in0=tmp[:],
                                             in1=b_ps[j][:], op=ALU.add)
                    k.inc(inst, "dv", 1)
            op_norm(k, rb, 8, T, cst["ones_col"], cst["ones_row"], apply_fn,
                    rms=False, chunk=512)
            k.unphase()
            k.unphase()
            k.phase()
            ot = k.sb_tiles(8, T, BF16)
            k.phase()
            qt = k.sb_tiles(8, T, BF16)
            kt = k.sb_tiles(8, T, BF16)
            vtok = k.sb_tiles(16, 1024, BF16)
            k.phase()
            wv_sb = k.sb_tiles(8, 512, BF16)
            for dc2 in range(2):
                op_load(k, [(wv_sb[j][:],
                             waps["wv"][j * P:(j + 1) * P,
                                        dc2 * 512:(dc2 + 1) * 512])
                            for j in range(8)], queue="qg")
                op_linearA(k, xn, wv_sb, D, 512, T,
                           mk_epiA_copy(k, vtok, col_off=dc2 * 512))
            k.unphase()
            op_linearB(k, xn, waps["wq"], D, D, T, mk_epi_copy(k, qt))
            op_linearB(k, xn, waps["wk"], D, D, T, mk_epi_copy(k, kt))
            op_qk_prep(k, qt, 16, 64, T, cst, l2eps=1e-12,
                       scale_col=[acols["qs"][:, j:j + 1] for j in range(8)])
            op_qk_prep(k, kt, 16, 64, T, cst, l2eps=1e-12,
                       scale_col=[acols["ks"][:, j:j + 1] for j in range(8)])
            op_attention(k, qt, kt, vtok, ot, 16, 64, 16, T, "full",
                         0.125, cst)
            k.unphase()
            k.phase()
            rb = k.sb_tiles(8, T, F32)
            ld_tiles(k, rb, R, b, queue="qs")
            op_linearB(k, ot, waps["wo"], D, D, T,
                       mk_epi_residual(k, rb, bias_col=acols["bo"]),
                       epi_incs=2)
            st_tiles(k, rb, Rout, b, queue="qs")
            k.unphase()
            k.unphase()
            k.unphase()
    k.unphase()


def build_l1attn(hostc, hostw):
    nc = bass.Bass("TRN2", target_bir_lowering=False)
    ctx = ExitStack()
    im = {}
    with ctx:
        k = mk_ker(nc, ctx, im, hostc)
        R = win(nc, "R", [D, TT])
        Rout = nc.dram_tensor("Rout", [D, TT], F32, kind="ExternalOutput").ap()
        emit_l1attn(k, nc, im, hostc, hostw, R, Rout)
        k.finish()
    return nc, im



def emit_l1ffn(k, nc, im, hostc, hostw, R, Rout, upto="all"):
    names = ["rope1_A", "rope1_B", "pswap", "exp_sel"]
    k.phase()
    if True:
        cst = sec_consts(k, nc, im, hostc, names)
        X = scratch_ap(k, nc, "Xscr", [D, TT], F32)
        waps = {}
        for nm, shp in [("wq", [D, D]), ("wkv", [D, 256]), ("wkup", [256, D]),
                        ("wvup", [256, D]), ("wo", [D, D]), ("wr", [D, 8]),
                        ("sw1", [D, 2048]), ("sw2", [2048, D]),
                        ("sw3", [D, 2048])]:
            waps[nm] = weight_ap(k, nc, im, hostw, "g_" + nm, shp)
        for e in range(8):
            for nm, shp in [("ew1", [D, 2048]), ("ew2", [2048, D]),
                            ("ew3", [D, 2048])]:
                waps[f"{nm}_{e}"] = weight_ap(k, nc, im, hostw,
                                              f"g_{nm}_{e}", shp)
        gcols = {}
        colpairs = []
        for nm, shp in [("ln1", [P, 8]), ("ln2", [P, 8]), ("rb", [1, 8])]:
            ap = weight_ap(k, nc, im, hostw, "g_" + nm, shp)
            sb = k.sb(list(shp), F32)
            gcols[nm] = sb
            colpairs.append((sb[:], ap))
        op_load(k, colpairs, queue="qs")

        for b in range(NB):
            k.phase()
            rb = k.sb_tiles(8, T, F32)
            xf = k.sb_tiles(8, T, F32)
            ld_tiles(k, rb, R, b, queue="qs")
            k.phase()
            tmp = k.sb([P, 512], F32)

            def apply_fn(eng, di, a_ps, b_ps, c0, w, rb=rb, xf=xf, tmp=tmp):
                for j in range(w // 512):
                    cs = slice(c0 + j * 512, c0 + (j + 1) * 512)
                    inst = eng.tensor_tensor(out=tmp[:], in0=rb[di][:, cs],
                                             in1=a_ps[j][:], op=ALU.mult)
                    eng.wait_ge(k.sems["dv"], k.inc(inst, "dv", 1))
                    inst = eng.tensor_tensor(out=xf[di][:, cs], in0=tmp[:],
                                             in1=b_ps[j][:], op=ALU.add)
                    k.inc(inst, "dv", 1)
            op_norm(k, rb, 8, T, cst["ones_col"], cst["ones_row"], apply_fn,
                    rms=False, chunk=512)
            k.unphase()
            st_tiles(k, xf, X, b, queue="qs")
            k.unphase()

        for b in range(NB):
            k.phase()
            xn = k.sb_tiles(8, T, BF16)
            k.phase()
            xb = k.sb_tiles(8, T, F32)
            ld_tiles(k, xb, X, b, queue="qs")
            norm_to_bf16(k, xb, xn, cst, rms=True, eps=1e-6,
                         gain_col=gcols["ln1"], chunk=512)
            k.unphase()
            k.phase()
            ot = k.sb_tiles(8, T, BF16)
            k.phase()
            qt = k.sb_tiles(8, T, BF16)
            kt = k.sb_tiles(8, T, BF16)
            vtok = k.sb_tiles(16, 1024, BF16)
            k.phase()
            kv = k.sb_tiles(2, T, BF16)
            op_linearB(k, xn, waps["wkv"], D, 256, T,
                       mk_epi_scale(k, kv, 0.25))
            op_linearB(k, kv, waps["wkup"], 256, D, T, mk_epi_copy(k, kt))
            k.phase()
            wv_sb = k.sb_tiles(2, 1024, BF16)
            op_load(k, [(wv_sb[j][:], waps["wvup"][j * P:(j + 1) * P, :])
                        for j in range(2)], queue="qg")
            op_linearA(k, kv, wv_sb, 256, 1024, T, mk_epiA_copy(k, vtok))
            k.unphase()
            k.unphase()
            op_linearB(k, xn, waps["wq"], D, D, T, mk_epi_copy(k, qt))
            op_qk_prep(k, qt, 16, 64, T, cst,
                       rope_tabs=(cst["rope1_A"], cst["rope1_B"]))
            op_qk_prep(k, kt, 16, 64, T, cst,
                       rope_tabs=(cst["rope1_A"], cst["rope1_B"]))
            op_attention(k, qt, kt, vtok, ot, 16, 64, 16, T, "causal",
                         0.125, cst)
            k.unphase()
            k.phase()
            xb = k.sb_tiles(8, T, F32)
            ld_tiles(k, xb, X, b, queue="qs")
            op_linearB(k, ot, waps["wo"], D, D, T, mk_epi_residual(k, xb))
            st_tiles(k, xb, X, b, queue="qs")
            k.unphase()
            k.unphase()
            k.unphase()

        for b in range(NB if upto == "all" else 0):
            k.phase()
            xn = k.sb_tiles(8, T, BF16)
            k.phase()
            xb = k.sb_tiles(8, T, F32)
            ld_tiles(k, xb, X, b, queue="qs")
            norm_to_bf16(k, xb, xn, cst, rms=True, eps=1e-6,
                         gain_col=gcols["ln2"], chunk=512)
            k.unphase()
            k.phase()
            acc = k.sb_tiles(8, T, F32)
            ST = k.sb([32, T], BF16)
            abc = k.sb([P, T], BF16)

            def zacc(eng, acc=acc, ST=ST):
                for j in range(8):
                    inst = eng.memset(acc[j][:], 0.0)
                inst = eng.memset(ST[:], 0.0)
                k.inc(inst, "pl", 1)
            k.op({"gpsimd": zacc})

            k.phase()
            wr_sb = k.sb_tiles(8, 8, BF16)
            rbc = k.sb([P, 8], F32)
            probs = k.sb([P, 16 * 8], F32)
            sca = k.sb([P, 16], F32)
            eqb = k.sb([P, 8], F32)
            cumb = k.sb([P, 8], F32)
            zeros8 = k.sb([P, 8], F32)
            op_load(k, [(wr_sb[j][:], waps["wr"][j * P:(j + 1) * P, :])
                        for j in range(8)], queue="qg")

            def pre(eng):
                inst = eng.memset(zeros8[:], 0.0)
                k.inc(inst, "pl", 1)
            k.op({"gpsimd": pre})

            def rbcast(eng):
                inst = eng.matmul(k.ps[7][:, 0:8], cst["ones_row"][0:1, :],
                                  gcols["rb"][0:1, 0:8], start=True, stop=True)
                k.inc(inst, "pe", 1)
            k.op({"tensor": rbcast})

            def rbc_cp(eng):
                inst = eng.tensor_copy(out=rbc[:], in_=k.ps[7][:, 0:8])
                k.inc(inst, "dv", 1)
            k.op({"vector": rbc_cp})

            def epi_router(eng, ps_ap, ti, wid):
                inst = eng.tensor_tensor(out=probs[:, ti * 8:ti * 8 + 8],
                                         in0=ps_ap, in1=rbc[:], op=ALU.add)
                k.inc(inst, "dv", 1)
            op_linearA(k, xn, wr_sb, D, 8, T, epi_router)

            seq = Seq(k)
            for ti in range(16):
                pr = lambda ti=ti: probs[:, ti * 8:ti * 8 + 8]

                def C1(eng, pr=pr):
                    return eng.tensor_scalar_max(out=pr(), in0=pr(),
                                                 scalar1=-50.0)
                seq.do("vector", C1, inc="dv")

                def C2(eng, pr=pr):
                    return eng.tensor_scalar_min(out=pr(), in0=pr(),
                                                 scalar1=50.0)
                dvc = seq.do("vector", C2, inc="dv")

                def SG(eng, pr=pr):
                    return eng.activation(out=pr(), in_=pr(), func=AF.Sigmoid)
                acs = seq.do("scalar", SG, waits=[("dv", dvc)], inc="ac")

                def PM(eng, pr=pr, ti=ti):
                    return eng.reduce_max(out=sca[:, 15:16], in_=pr(),
                                          axis=AX.X)
                seq.do("vector", PM, waits=[("ac", acs)], inc="dv")

                def EQ(eng, pr=pr):
                    return eng.tensor_tensor(
                        out=eqb[:], in0=pr(),
                        in1=sca[:, 15:16].broadcast_to([P, 8]),
                        op=ALU.is_equal)
                seq.do("vector", EQ, inc="dv")

                def CU(eng):
                    return eng.tensor_tensor_scan(
                        out=cumb[:], data0=eqb[:], data1=zeros8[:],
                        initial=0.0, op0=ALU.add, op1=ALU.add)
                seq.do("vector", CU, inc="dv")

                def FE(eng):
                    return eng.tensor_single_scalar(out=cumb[:], in_=cumb[:],
                                                    scalar=1.0,
                                                    op=ALU.is_equal)
                seq.do("vector", FE, inc="dv")

                def OH(eng):
                    return eng.tensor_mul(out=eqb[:], in0=eqb[:], in1=cumb[:])
                seq.do("vector", OH, inc="dv")

                def PC1(eng):
                    return eng.tensor_scalar_max(out=sca[:, 15:16],
                                                 in0=sca[:, 15:16],
                                                 scalar1=1e-8)
                seq.do("vector", PC1, inc="dv")

                def PC2(eng):
                    return eng.tensor_scalar_min(out=sca[:, 15:16],
                                                 in0=sca[:, 15:16],
                                                 scalar1=1.0 - 1e-8)
                seq.do("vector", PC2, inc="dv")

                def TW(eng):
                    return eng.tensor_scalar(out=sca[:, 14:15],
                                             in0=sca[:, 15:16],
                                             scalar1=0.5 + 1e-8, scalar2=2.0,
                                             op0=ALU.add, op1=ALU.min)
                seq.do("vector", TW, inc="dv")

                def TW2(eng):
                    return eng.tensor_scalar_max(out=sca[:, 14:15],
                                                 in0=sca[:, 14:15],
                                                 scalar1=0.5)
                seq.do("vector", TW2, inc="dv")

                def IT(eng):
                    return eng.reciprocal(out=sca[:, 14:15],
                                          in_=sca[:, 14:15])
                seq.do("vector", IT, inc="dv")

                def SA(eng):
                    return eng.tensor_scalar_mul(out=sca[:, 0:1],
                                                 in0=sca[:, 14:15],
                                                 scalar1=0.5)
                seq.do("vector", SA, inc="dv")

                def PI(eng):
                    return eng.tensor_mul(out=sca[:, 15:16],
                                          in0=sca[:, 15:16],
                                          in1=sca[:, 14:15])
                seq.do("vector", PI, inc="dv")

                def BC(eng):
                    return eng.tensor_tensor(
                        out=sca[:, 1:9], in0=eqb[:],
                        in1=sca[:, 15:16].broadcast_to([P, 8]), op=ALU.mult)
                dvb = seq.do("vector", BC, inc="dv")

                def TR(eng, ti=ti):
                    return eng.transpose(k.ps[6][0:9, 0:P], sca[:, 0:9],
                                         cst["identity"][:])
                pet = seq.do("tensor", TR, waits=[("dv", dvb)], inc="pe")

                def CP(eng, ti=ti):
                    return eng.tensor_copy(out=ST[0:9, ti * P:(ti + 1) * P],
                                           in_=k.ps[6][0:9, 0:P])
                seq.do("vector", CP, waits=[("pe", pet)], inc="dv")
            seq.run()
            k.unphase()

            k.phase()
            ht = [k.sb([P, T], BF16) for _ in range(16)]
            for e in range(-1, 8):
                w1 = waps["sw1"] if e < 0 else waps[f"ew1_{e}"]
                w3 = waps["sw3"] if e < 0 else waps[f"ew3_{e}"]
                w2 = waps["sw2"] if e < 0 else waps[f"ew2_{e}"]
                op_linearB(k, xn, w1, D, 2048, T,
                           mk_epi_act(k, ht, AF.Silu), epi_eng="scalar",
                           w_chunk=256)

                def epi_mul(eng, ps_ap, ds, m):
                    w = ps_ap.shape[1]
                    inst = eng.tensor_copy(out=k.scratch[0:m, 0:w], in_=ps_ap)
                    eng.wait_ge(k.sems["dv"], k.inc(inst, "dv", 1))
                    inst = eng.tensor_mul(
                        out=ht[ds][0:m, 0:w], in0=ht[ds][0:m, 0:w],
                        in1=k.scratch[0:m, 0:w])
                    k.inc(inst, "dv", 1)
                op_linearB(k, xn, w3, D, 2048, T, epi_mul, epi_incs=2,
                           w_chunk=256)

                sl = 0 if e < 0 else 1 + e

                def bca(eng, sl=sl):
                    for tc in range(4):
                        inst = eng.matmul(
                            k.ps[4 + tc][:, :],
                            cst["exp_sel"][0:32, sl * P:(sl + 1) * P],
                            ST[0:32, tc * 512:(tc + 1) * 512],
                            start=True, stop=True)
                        k.inc(inst, "pe", 1)
                k.op({"tensor": bca})

                def bcc(eng):
                    for tc in range(4):
                        inst = eng.tensor_copy(
                            out=abc[:, tc * 512:(tc + 1) * 512],
                            in_=k.ps[4 + tc][:, :])
                        k.inc(inst, "dv", 1)
                k.op({"vector": bcc})

                def epi_acc(eng, ps_ap, ds, m):
                    w = ps_ap.shape[1]
                    inst = eng.tensor_tensor(
                        out=k.scratch[0:m, 0:w], in0=ps_ap,
                        in1=abc[0:m, 0:w], op=ALU.mult)
                    eng.wait_ge(k.sems["dv"], k.inc(inst, "dv", 1))
                    inst = eng.tensor_add(
                        out=acc[ds][0:m, 0:w], in0=acc[ds][0:m, 0:w],
                        in1=k.scratch[0:m, 0:w])
                    k.inc(inst, "dv", 1)
                op_linearB(k, ht, w2, 2048, D, T, epi_acc, epi_incs=2,
                           w_chunk=256)
            k.unphase()

            k.phase()
            xb = k.sb_tiles(8, T, F32)
            ld_tiles(k, xb, X, b, queue="qs")

            def addres(eng, xb=xb, acc=acc):
                for j in range(8):
                    inst = eng.tensor_add(out=xb[j][:], in0=xb[j][:],
                                          in1=acc[j][:])
                    k.inc(inst, "dv", 1)
            k.op({"vector": addres})
            st_tiles(k, xb, X, b, queue="qs")
            k.unphase()
            k.unphase()
            k.unphase()

        for b in range(NB):
            k.phase()
            rb = k.sb_tiles(8, T, F32)
            xb = k.sb_tiles(8, T, F32)
            ld_tiles(k, rb, R, b, queue="qs")
            ld_tiles(k, xb, X, b, queue="qg")

            def addf(eng, rb=rb, xb=xb):
                for j in range(8):
                    inst = eng.scalar_tensor_tensor(
                        out=rb[j][:], in0=xb[j][:], scalar=0.5, in1=rb[j][:],
                        op0=ALU.mult, op1=ALU.add)
                    k.inc(inst, "dv", 1)
            k.op({"vector": addf})
            st_tiles(k, rb, Rout, b, queue="qs")
            k.unphase()
    k.unphase()


def build_l1ffn(hostc, hostw, upto="all"):
    nc = bass.Bass("TRN2", target_bir_lowering=False)
    ctx = ExitStack()
    im = {}
    with ctx:
        k = mk_ker(nc, ctx, im, hostc)
        R = win(nc, "R", [D, TT])
        Rout = nc.dram_tensor("Rout", [D, TT], F32, kind="ExternalOutput").ap()
        emit_l1ffn(k, nc, im, hostc, hostw, R, Rout, upto)
        k.finish()
    return nc, im



def build_fused(hostc, hostw):
    nc = bass.Bass("TRN2", target_bir_lowering=False)
    ctx = ExitStack()
    im = {}
    with ctx:
        k = mk_ker(nc, ctx, im, hostc)
        R = win(nc, "R", [D, TT])
        Rout = nc.dram_tensor("Rout", [D, TT], F32, kind="ExternalOutput").ap()
        Rs = scratch_ap(k, nc, "Rstate", [D, TT], F32)
        emit_l0ffn(k, nc, im, hostc, hostw, R, Rs)
        emit_l0attn(k, nc, im, hostc, hostw, Rs, Rs)
        emit_l0ffn(k, nc, im, hostc, hostw, Rs, Rs)
        emit_l1ffn(k, nc, im, hostc, hostw, Rs, Rs)
        emit_l1attn(k, nc, im, hostc, hostw, Rs, Rs)
        emit_l1ffn(k, nc, im, hostc, hostw, Rs, Rout)
        k.finish()
    return nc, im



def kernel(**inputs):
    hostc = host_consts()
    hostw = host_weights(inputs)
    x = np.ascontiguousarray(np.asarray(inputs["L0_attn_x"], dtype=np.float32))
    Rs = []
    for c in range(NCORE):
        Rs.append(np.ascontiguousarray(
            np.concatenate([x[2 * c].T, x[2 * c + 1].T], axis=1)))
    nc, im = build_fused(hostc, hostw)
    in_maps = [dict(im, R=Rs[c]) for c in range(NCORE)]
    res = run_bass_kernel_spmd(nc, in_maps, core_ids=list(range(NCORE)))
    out = np.empty((16, 2048, 1024), np.float32)
    for c in range(NCORE):
        ro = np.asarray(res.results[c]["Rout"])
        out[2 * c] = ro[:, :T].T
        out[2 * c + 1] = ro[:, T:].T
    return out


# version 06 (final)
# speedup vs baseline: 1.1839x; 1.0035x over previous
import numpy as np
import ml_dtypes
from contextlib import ExitStack
import concourse.bass as bass
import concourse.mybir as mybir
from concourse.bass_utils import run_bass_kernel_spmd

F32 = mybir.dt.float32
BF16 = mybir.dt.bfloat16
AF = mybir.ActivationFunctionType
ALU = mybir.AluOpType
AX = mybir.AxisListType
BF = ml_dtypes.bfloat16

P = 128
T = 2048
NB = 2
TT = NB * T
NEG = -1e30
D = 1024
DT = D // P

ENG_SEM = {"tensor": "pe", "vector": "dv", "scalar": "ac",
           "gpsimd": "pl", "sync": "qs"}
QUEUE_ENG = {"qs": "sync", "qg": "gpsimd", "qa": "scalar"}


def cdiv(a, b):
    return (a + b - 1) // b


class Ker:
    def __init__(self, nc, block, ctx):
        self.nc = nc
        self.block = block
        self.ctx = ctx
        E = ctx.enter_context
        self.sems = {}
        self.cnt = {}
        for name in ["pe", "dv", "ac", "pl", "qs", "qg", "qa"]:
            self.sems[name] = E(nc.semaphore(name))
            self.cnt[name] = 0
        self.psbig = [E(nc.psum_tensor(f"psb{i}", [P, 2048], F32))
                      for i in range(2)]
        self.ps = [self.psbig[i // 4][:, (i % 4) * 512:(i % 4) * 512 + 512]
                   for i in range(8)]
        self.sb_off = 16512
        self.sb_peak = self.sb_off
        self._phase_stack = []
        self._names = 0
        self.scratch = self.sb([P, 2048], F32, "kscratch")

    def sb(self, shape, dtype, name=None):
        per_part = int(np.prod(shape[1:])) * mybir.dt.size(dtype)
        off = (self.sb_off + 63) & ~63
        self.sb_off = off + per_part
        self.sb_peak = max(self.sb_peak, self.sb_off)
        assert self.sb_off <= 229370, f"SBUF overflow: {self.sb_off} B/partition"
        self._names += 1
        nm = name or f"sb{self._names}"
        return self.nc.alloc_sbuf_tensor_at(nm, list(shape), dtype, offset=off)

    def sb_tiles(self, n, cols, dtype):
        return [self.sb([P, cols], dtype) for _ in range(n)]

    def phase(self):
        self._phase_stack.append(self.sb_off)

    def unphase(self):
        self.sb_off = self._phase_stack.pop()

    def inc(self, inst, sem, n=1):
        inst.then_inc(self.sems[sem], n)
        self.cnt[sem] += n
        return self.cnt[sem]

    def op(self, fn_map, snap=None):
        if snap is None:
            snap = dict(self.cnt)
        for en in ["sync", "gpsimd", "tensor", "vector", "scalar"]:
            if en not in fn_map:
                continue
            fn = fn_map[en]

            def body(eng, fn=fn, en=en):
                own = ENG_SEM.get(en)
                for s, v in snap.items():
                    if v > 0 and s != own:
                        eng.wait_ge(self.sems[s], v)
                fn(eng)
            getattr(self.block, en)(body)

    def finish(self, out_queues=("qs", "qg", "qa")):
        snap = dict(self.cnt)

        def body(eng):
            for s, v in snap.items():
                if v > 0:
                    eng.wait_ge(self.sems[s], v)
        self.block.sync(body)



def op_load(k, pairs, queue="qs"):
    def body(eng):
        for d, s in pairs:
            k.inc(eng.dma_start(out=d, in_=s), queue, 16)
    k.op({QUEUE_ENG[queue]: body})


def op_store(k, pairs, queue="qs"):
    def body(eng):
        for d, s in pairs:
            k.inc(eng.dma_start(out=d, in_=s), queue, 16)
    k.op({QUEUE_ENG[queue]: body})



def op_linearB(k, xs, w_hbm, din, dout, tt, epi, epi_eng="vector",
               w_chunk=512, epi_incs=1):
    ndi = din // P
    ndo_ch = cdiv(dout, w_chunk)
    ntc = tt // 512
    assert ntc <= 4
    esem = ENG_SEM[epi_eng]
    k.phase()
    wbuf = [k.sb([P, ndi * w_chunk], BF16), k.sb([P, ndi * w_chunk], BF16)]
    pe0, ev0, qg0 = k.cnt["pe"], k.cnt[esem], k.cnt["qg"]

    rounds_per_chunk = [cdiv(min(w_chunk, dout - c * w_chunk), P)
                        for c in range(ndo_ch)]
    pe_after_chunk = np.cumsum(rounds_per_chunk)

    def dma_body(eng):
        for c in range(ndo_ch):
            if c >= 2:
                eng.wait_ge(k.sems["pe"], pe0 + int(pe_after_chunk[c - 2]))
            nds = min(w_chunk, dout - c * w_chunk)
            dst = wbuf[c % 2][:].rearrange("p (di c) -> p di c", c=w_chunk)
            inst = eng.dma_start(
                out=dst[:, :, 0:nds],
                in_=w_hbm[0:din, c * w_chunk: c * w_chunk + nds]
                .rearrange("(di p) c -> p di c", p=P))
            k.inc(inst, "qg", 16)

    def pe_body(eng):
        g = 0
        for c in range(ndo_ch):
            eng.wait_ge(k.sems["qg"], qg0 + 16 * (c + 1))
            nds = min(w_chunk, dout - c * w_chunk)
            for s in range(cdiv(nds, P)):
                m = min(P, nds - s * P)
                big = k.psbig[g % 2]
                if g >= 2:
                    eng.wait_ge(k.sems[esem], ev0 + (g - 1) * epi_incs)
                for di in range(ndi):
                    for tc in range(ntc):
                        inst = eng.matmul(
                            big[0:m, tc * 512:(tc + 1) * 512],
                            wbuf[c % 2][:, di * w_chunk + s * P:
                                        di * w_chunk + s * P + m],
                            xs[di][:, tc * 512:(tc + 1) * 512],
                            start=(di == 0), stop=(di == ndi - 1))
                k.inc(inst, "pe", 1)
                g += 1

    def ev_body(eng):
        g = 0
        for c in range(ndo_ch):
            nds = min(w_chunk, dout - c * w_chunk)
            for s in range(cdiv(nds, P)):
                m = min(P, nds - s * P)
                eng.wait_ge(k.sems["pe"], pe0 + g + 1)
                epi(eng, k.psbig[g % 2][0:m, 0:tt], c * (w_chunk // P) + s, m)
                g += 1

    k.op({"gpsimd": dma_body, "tensor": pe_body, epi_eng: ev_body})
    k.unphase()


def op_linearA(k, xs, w_sb, din, dout, tt, epi, epi_eng="vector",
               epi_incs=1):
    ndi = din // P
    ntl = tt // P
    ndc = cdiv(dout, 512)
    assert ndc <= 4
    esem = ENG_SEM[epi_eng]
    pe0, ev0 = k.cnt["pe"], k.cnt[esem]

    def pe_body(eng):
        for ti in range(ntl):
            big = k.psbig[ti % 2]
            if ti >= 2:
                eng.wait_ge(k.sems[esem], ev0 + (ti - 1) * epi_incs)
            for di in range(ndi):
                for dc in range(ndc):
                    n = min(512, dout - dc * 512)
                    inst = eng.matmul(
                        big[:, dc * 512:dc * 512 + n],
                        xs[di][:, ti * P:(ti + 1) * P],
                        w_sb[di][:, dc * 512: dc * 512 + n],
                        start=(di == 0), stop=(di == ndi - 1))
            k.inc(inst, "pe", 1)

    def ev_body(eng):
        for ti in range(ntl):
            eng.wait_ge(k.sems["pe"], pe0 + ti + 1)
            epi(eng, k.psbig[ti % 2][:, 0:dout], ti, dout)

    k.op({"tensor": pe_body, epi_eng: ev_body})


def op_norm(k, x_tiles, nr, tt, ones_col, ones_row, apply_fn,
            rms=False, eps=1e-5, chunk=2048):
    DIM = nr * P
    ncH = tt // chunk
    n512 = chunk // 512
    k.phase()
    x2 = k.sb_tiles(nr, chunk, F32)
    srow = k.sb([1, chunk], F32)
    s2row = k.sb([1, chunk], F32)
    arow = k.sb([1, chunk], F32)
    brow = k.sb([1, chunk], F32)
    trow = k.sb([1, chunk], F32)

    for ch in range(ncH):
        c0 = ch * chunk

        def ac1(eng, c0=c0):
            for di in range(nr):
                inst = eng.activation(out=x2[di][:],
                                      in_=x_tiles[di][:, c0:c0 + chunk],
                                      func=AF.Square)
            k.inc(inst, "ac", 1)
        k.op({"scalar": ac1})

        def pe1(eng, c0=c0):
            for j in range(n512):
                if not rms:
                    for di in range(nr):
                        inst = eng.matmul(
                            k.ps[2 * j][0:1, :], ones_col[:, 0:1],
                            x_tiles[di][:, c0 + j * 512: c0 + (j + 1) * 512],
                            start=(di == 0), stop=(di == nr - 1))
                for di in range(nr):
                    inst = eng.matmul(
                        k.ps[2 * j + 1][0:1, :], ones_col[:, 0:1],
                        x2[di][:, j * 512:(j + 1) * 512],
                        start=(di == 0), stop=(di == nr - 1))
            k.inc(inst, "pe", 1)
        k.op({"tensor": pe1})

        def dv1(eng):
            for j in range(n512):
                cs = slice(j * 512, (j + 1) * 512)
                if not rms:
                    eng.tensor_scalar(out=srow[0:1, cs], in0=k.ps[2 * j][0:1, :],
                                      scalar1=1.0 / DIM, scalar2=None,
                                      op0=ALU.mult)
                eng.tensor_scalar(out=s2row[0:1, cs], in0=k.ps[2 * j + 1][0:1, :],
                                  scalar1=1.0 / DIM, scalar2=eps,
                                  op0=ALU.mult, op1=ALU.add)
            if not rms:
                inst = eng.tensor_mul(out=trow[0:1, :], in0=srow[0:1, :],
                                      in1=srow[0:1, :])
                eng.wait_ge(k.sems["dv"], k.inc(inst, "dv", 1))
                inst = eng.tensor_sub(out=trow[0:1, :], in0=s2row[0:1, :],
                                      in1=trow[0:1, :])
            else:
                inst = eng.tensor_copy(out=trow[0:1, :], in_=s2row[0:1, :])
            k.inc(inst, "dv", 1)
        k.op({"vector": dv1})

        def ac2(eng):
            inst = eng.activation(out=arow[0:1, :], in_=trow[0:1, :],
                                  func=AF.Ln)
            eng.wait_ge(k.sems["ac"], k.inc(inst, "ac", 1))
            inst = eng.activation(out=arow[0:1, :], in_=arow[0:1, :],
                                  func=AF.Exp, scale=-0.5)
            k.inc(inst, "ac", 1)
        k.op({"scalar": ac2})

        if not rms:
            def dv2(eng):
                inst = eng.scalar_tensor_tensor(
                    out=brow[0:1, :], in0=srow[0:1, :], scalar=-1.0,
                    in1=arow[0:1, :], op0=ALU.mult, op1=ALU.mult)
                k.inc(inst, "dv", 1)
            k.op({"vector": dv2})

        def pe2(eng):
            for j in range(n512):
                inst = eng.matmul(k.ps[j][:, :], ones_row[0:1, :],
                                  arow[0:1, j * 512:(j + 1) * 512],
                                  start=True, stop=True)
                if not rms:
                    inst = eng.matmul(k.ps[4 + j][:, :], ones_row[0:1, :],
                                      brow[0:1, j * 512:(j + 1) * 512],
                                      start=True, stop=True)
            k.inc(inst, "pe", 1)
        k.op({"tensor": pe2})

        a_ps = [k.ps[j] for j in range(n512)]
        b_ps = None if rms else [k.ps[4 + j] for j in range(n512)]

        def dv3(eng, c0=c0, a_ps=a_ps, b_ps=b_ps):
            for di in range(nr):
                apply_fn(eng, di, a_ps, b_ps, c0, chunk)
        k.op({"vector": dv3})
    k.unphase()



class Seq:

    def __init__(self, k):
        self.k = k
        self.records = []
        self.snap0 = dict(k.cnt)
        self.last_ms = {}

    def do(self, engine, emit_fn, waits=(), inc=None, incn=1):
        k = self.k
        waits = list(waits)
        if engine != "tensor" and engine in self.last_ms:
            waits.append(self.last_ms[engine])
        val = None
        if inc is not None:
            k.cnt[inc] += incn
            val = k.cnt[inc]
            self.last_ms[engine] = (inc, val)
        self.records.append((engine, waits, emit_fn, inc, incn))
        return val

    def cur(self, sem):
        return self.k.cnt[sem]

    def run(self):
        k = self.k
        engines = {r[0] for r in self.records}
        fn_map = {}
        for en in engines:
            recs = [r for r in self.records if r[0] == en]

            def body(eng, recs=recs):
                last_wait = {}
                for _, waits, emit_fn, inc, incn in recs:
                    for sem, val in waits:
                        if val > 0 and last_wait.get(sem, -1) < val:
                            eng.wait_ge(k.sems[sem], val)
                            last_wait[sem] = val
                    inst = emit_fn(eng)
                    if inc is not None:
                        inst.then_inc(k.sems[inc], incn)
            fn_map[en] = body
        k.op(fn_map, snap=self.snap0)


def _rows(tiles_row0, dh):
    out = []
    r0 = tiles_row0
    left = dh
    while left:
        ti, ro = divmod(r0, P)
        n = min(left, P - ro)
        out.append((ti, ro, n))
        r0 += n
        left -= n
    return out


def op_attention(k, q_tiles, k_tiles, v_tiles, o_tiles, nh, dh, nkv, Tq,
                 mode, scale, cst, qcol0=0):
    QB = 256 if mode == "win256" else 512
    nqb = Tq // QB
    nqt = QB // P
    ndv = cdiv(dh, P)
    k.phase()
    S_sb = k.sb([P, 2048], F32)
    Pdump = k.sb([P, 2048], BF16)
    mcol = k.sb([P, 16], F32)
    den = k.sb([P, 1], F32)
    brow = k.sb([1, 512], F32)
    b2bc = k.sb([P, 512], F32)
    stf = [k.sb([P, 512], F32) for _ in range(2)]
    Pt = [k.sb([P, 512], BF16) for _ in range(4)]

    seq = Seq(k)
    stats_rd = 0
    stats_evac_ms = []
    stt_ms = []
    exp_ms = []
    pv_ms = []
    st_rd = 0
    st = {"last_X1a": 0, "last_B1": 0, "last_B2": 0, "last_X3": 0,
          "last_X4": 0, "last_Oev": 0, "dv_X2": 0}

    iters = [(h, qb) for h in range(nh) for qb in range(nqb)]

    def key_layout(h, qb):
        q0 = qb * QB
        if mode == "full":
            st_subs = [(k0, None) for k0 in range(0, Tq, P)]
            def stats_chunks(qt):
                return [[(c * 512 + j * P, None) for j in range(4)]
                        for c in range(Tq // 512)]
        elif mode == "causal":
            st_subs = []
            for k0 in range(0, (qb + 1) * 512, P):
                if q0 >= k0 + P - 1:
                    st_subs.append((k0, None))
                else:
                    st_subs.append((k0, [("ge", q0 - k0)]))
            def stats_chunks(qt, q0=q0):
                qa = q0 + qt * P
                ch = []
                nfull = qa // 512
                for c in range(nfull):
                    ch.append([(c * 512 + j * P, None) for j in range(4)])
                off = qa - nfull * 512
                ch.append([(nfull * 512 + j * P, ("c", off - j * P))
                           for j in range(4)])
                return ch
        else:
            lo = max(0, q0 - 256)
            st_subs = []
            for k0 in range(lo, q0 + QB, P):
                conds = []
                if q0 - k0 - (P - 1) < 0:
                    conds.append(("ge", q0 - k0))
                if 256 - ((q0 + QB - 1) - k0) < 0:
                    conds.append(("le256", 256 - (q0 - k0)))
                st_subs.append((k0, conds or None))
            def stats_chunks(qt, q0=q0):
                qa = q0 + qt * P
                segs = []
                for kk in range(max(0, qa - 256), qa + P, P):
                    segs.append((kk, ("w", qa - kk)))
                return [segs]
        return q0, st_subs, stats_chunks

    def rows_of(h):
        return _rows(h * dh, dh), _rows((h * nkv // nh) * dh, dh)

    def emit_A(it, paired=False):
        nonlocal stats_rd
        h, qb = iters[it]
        q0, st_subs, stats_chunks = key_layout(h, qb)
        qrows, krows = rows_of(h)
        A_info = []
        vis_of = {}
        first = True
        for qt in range(nqt):
            chunks = stats_chunks(qt)
            vis = 0
            for ci, segs in enumerate(chunks):
                rd = stats_rd
                bank = k.ps[rd % 2]
                waits = [("dv", stats_evac_ms[rd - 2])] if rd >= 2 else []

                def A(eng, segs=segs, bank=bank, qt=qt, qrows=qrows,
                      krows=krows, q0=q0):
                    col = 0
                    for (k0, _m) in segs:
                        for di in range(len(qrows)):
                            ti, ro, n = qrows[di]
                            kti, kro, kn = krows[di]
                            inst = eng.matmul(
                                bank[:, col:col + P],
                                q_tiles[ti][ro:ro + n,
                                            qcol0 + q0 + qt * P:
                                            qcol0 + q0 + (qt + 1) * P],
                                k_tiles[kti][kro:kro + kn,
                                             qcol0 + k0:qcol0 + k0 + P],
                                start=(di == 0), stop=(di == len(qrows) - 1))
                        col += P
                    return inst
                pe_ms = seq.do("tensor", A, waits=waits, inc="pe")
                if paired:
                    evac_round(qt, ci, segs, rd, pe_ms, vis, first)
                    first = True
                    first = (ci == len(chunks) - 1)
                else:
                    A_info.append((qt, ci, segs, rd, pe_ms, vis))
                stats_rd += 1
                vis += len(segs) * P
            if paired:
                emit_chain(qt, vis, 0)
        return None if paired else A_info

    def evac_round(qt, ci, segs, rd, pe_ms, vis0, first, sbase=0):
        if True:
            bank = k.ps[rd % 2]
            vis0 = sbase + vis0

            def E1a(eng, segs=segs, bank=bank, vis0=vis0):
                col = 0
                for (k0, msk) in segs:
                    dst = S_sb[:, vis0 + col: vis0 + col + P]
                    mask_t = None
                    if msk is not None:
                        if msk[0] == "c":
                            off = msk[1]
                            if off >= P:
                                mask_t = None
                            elif off < 0:
                                mask_t = "memset"
                            else:
                                mask_t = cst["wC"]
                        else:
                            off = msk[1]
                            mask_t = (cst["wA"] if off == 256 else
                                      (cst["wC"] if off == 0 else None))
                    if mask_t is None:
                        inst = eng.tensor_scalar_mul(
                            out=dst, in0=bank[:, col:col + P], scalar1=scale)
                    elif mask_t == "memset":
                        inst = eng.memset(dst, NEG)
                    else:
                        inst = eng.scalar_tensor_tensor(
                            out=dst, in0=bank[:, col:col + P], scalar=scale,
                            in1=mask_t[:, 0:P], op0=ALU.mult, op1=ALU.add)
                    col += P
                return inst
            wts = [("pe", pe_ms)]
            if first and st["last_X1a"] > 0:
                wts.append(("ac", st["last_X1a"]))
            ms = seq.do("vector", E1a, waits=wts, inc="dv")
            stats_evac_ms.append(ms)
            return vis0 - sbase + len(segs) * P

    def emit_E1a(it, A_info):
        vis_of = {}
        first = True
        for (qt, ci, segs, rd, pe_ms, vis0) in A_info:
            vis_of[qt] = evac_round(qt, ci, segs, rd, pe_ms, vis0, first,
                                    sbase=qt * 1024)
            first = False
        return vis_of

    def emit_chain(qt, vis, sbase):
        def E1b(eng, qt=qt, vis=vis, sbase=sbase):
            return eng.reduce_max(out=mcol[:, 4 * qt:4 * qt + 1],
                                  in_=S_sb[:, sbase:sbase + vis], axis=AX.X,
                                  negate=True)
        dv_E1b = seq.do("vector", E1b, inc="dv")

        def X1a(eng, qt=qt, vis=vis, sbase=sbase):
            return eng.activation(out=Pdump[:, sbase:sbase + vis],
                                  in_=S_sb[:, sbase:sbase + vis],
                                  func=AF.Exp,
                                  bias=mcol[:, 4 * qt:4 * qt + 1],
                                  accum_out=den[:, 0:1])
        st["last_X1a"] = seq.do("scalar", X1a, waits=[("dv", dv_E1b)],
                                inc="ac")

        def X1b(eng, qt=qt):
            return eng.activation(out=mcol[:, 4 * qt + 1:4 * qt + 2],
                                  in_=den[:, 0:1], func=AF.Ln)
        ac_x1b = seq.do("scalar", X1b, inc="ac")

        def X2(eng, qt=qt):
            return eng.tensor_sub(
                out=mcol[:, 4 * qt + 2:4 * qt + 3],
                in0=mcol[:, 4 * qt:4 * qt + 1],
                in1=mcol[:, 4 * qt + 1:4 * qt + 2])
        wts = [("ac", ac_x1b)]
        if st["last_B1"] > 0:
            wts.append(("pe", st["last_B1"]))
        st["dv_X2"] = seq.do("vector", X2, waits=wts, inc="dv")

    def emit_BC(it, vis_of, chains_done=False):
        nonlocal st_rd
        h, qb = iters[it]
        q0, st_subs, stats_chunks = key_layout(h, qb)
        qrows, krows = rows_of(h)
        kvh = h * nkv // nh

        if not chains_done:
            for qt in range(nqt):
                emit_chain(qt, vis_of[qt], qt * 1024)

        def B1(eng, nqt=nqt):
            for qt in range(nqt):
                inst = eng.transpose(
                    k.ps[6][0:1, qt * P:(qt + 1) * P],
                    mcol[:, 4 * qt + 2:4 * qt + 3], cst["identity"][:])
            return inst
        wts = [("dv", st["dv_X2"])]
        if st["last_X4"] > 0:
            wts.append(("dv", st["last_X4"]))
        st["last_B1"] = seq.do("tensor", B1, waits=wts, inc="pe")

        def X3(eng, QB=QB):
            return eng.tensor_copy(out=brow[0:1, 0:QB],
                                   in_=k.ps[6][0:1, 0:QB])
        wts = [("pe", st["last_B1"])]
        if st["last_B2"] > 0:
            wts.append(("pe", st["last_B2"]))
        st["last_X3"] = seq.do("vector", X3, waits=wts, inc="dv")

        def B2(eng, QB=QB):
            return eng.matmul(k.ps[6][:, 0:QB], cst["ones_row"][0:1, :],
                              brow[0:1, 0:QB], start=True, stop=True)
        st["last_B2"] = seq.do("tensor", B2, waits=[("dv", st["last_X3"])],
                               inc="pe")

        def X4(eng, QB=QB):
            return eng.tensor_copy(out=b2bc[:, 0:QB], in_=k.ps[6][:, 0:QB])
        st["last_X4"] = seq.do("vector", X4, waits=[("pe", st["last_B2"])],
                               inc="dv")

        nsub = len(st_subs)
        first_pv = True

        def emit_ST(si):
            nonlocal st_rd
            k0, conds = st_subs[si]
            g = st_rd
            bank = k.ps[2 + g % 2]

            def ST(eng, k0=k0, bank=bank, QB=QB, q0=q0, qrows=qrows,
                   krows=krows):
                for di in range(len(qrows)):
                    ti, ro, n = qrows[di]
                    kti, kro, kn = krows[di]
                    inst = eng.matmul(
                        bank[:, 0:QB],
                        k_tiles[kti][kro:kro + kn,
                                     qcol0 + k0:qcol0 + k0 + P],
                        q_tiles[ti][ro:ro + n, qcol0 + q0:qcol0 + q0 + QB],
                        start=(di == 0), stop=(di == len(qrows) - 1))
                return inst
            wts = []
            if g >= 2:
                wts.append(("dv", stt_ms[g - 2]))
            pe_st = seq.do("tensor", ST, waits=wts, inc="pe")

            def STT(eng, bank=bank, g=g, QB=QB):
                return eng.scalar_tensor_tensor(
                    out=stf[g % 2][:, 0:QB], in0=bank[:, 0:QB],
                    scalar=scale, in1=b2bc[:, 0:QB],
                    op0=ALU.mult, op1=ALU.add)
            wts = [("pe", pe_st), ("dv", st["last_X4"])]
            if g >= 2:
                wts.append(("ac", exp_ms[g - 2]))
            stt_ms.append(seq.do("vector", STT, waits=wts, inc="dv"))

            def EXP(eng, g=g, QB=QB):
                return eng.activation(out=Pt[g % 4][:, 0:QB],
                                      in_=stf[g % 2][:, 0:QB], func=AF.Exp)
            wts = [("dv", stt_ms[g])]
            if g >= 4:
                wts.append(("pe", pv_ms[g - 4]))
            ac_exp = seq.do("scalar", EXP, waits=wts, inc="ac")
            exp_ms.append(ac_exp)

            pv_wait = [("ac", ac_exp)]
            if conds:
                def MSK(eng, conds=conds, g=g, QB=QB):
                    for c in conds:
                        if c[0] == "ge":
                            inst = eng.affine_select(
                                out=Pt[g % 4][:, 0:QB],
                                in_=Pt[g % 4][:, 0:QB],
                                pattern=[[1, QB]], compare_op=ALU.is_ge,
                                fill=0.0, base=c[1], channel_multiplier=-1)
                        else:
                            inst = eng.affine_select(
                                out=Pt[g % 4][:, 0:QB],
                                in_=Pt[g % 4][:, 0:QB],
                                pattern=[[-1, QB]], compare_op=ALU.is_ge,
                                fill=0.0, base=c[1], channel_multiplier=1)
                    return inst
                pl_ms = seq.do("gpsimd", MSK, waits=[("ac", ac_exp)],
                               inc="pl")
                pv_wait = [("pl", pl_ms)]
            st_rd += 1
            return g, pv_wait

        def emit_PV(si, g, pv_wait, first, last):
            k0, _ = st_subs[si]

            def PV(eng, k0=k0, g=g, QB=QB, kvh=kvh, first=first, last=last):
                kt = (qcol0 + k0) // P
                for dvt in range(ndv):
                    n = min(P, dh - dvt * P)
                    inst = eng.matmul(
                        k.ps[4 + dvt][0:n, 0:QB],
                        v_tiles[kt][:, kvh * dh + dvt * P:
                                    kvh * dh + dvt * P + n],
                        Pt[g % 4][:, 0:QB],
                        start=first, stop=last)
                return inst
            wts = list(pv_wait)
            if first and st["last_Oev"] > 0:
                wts.append(("dv", st["last_Oev"]))
            pe_pv = seq.do("tensor", PV, waits=wts, inc="pe")
            pv_ms.append(pe_pv)

        pend = []
        for si in range(nsub):
            pend.append((si,) + emit_ST(si))
            if len(pend) > 1:
                si0, g0, w0 = pend.pop(0)
                emit_PV(si0, g0, w0, si0 == 0, si0 == nsub - 1)
        for (si0, g0, w0) in pend:
            emit_PV(si0, g0, w0, si0 == 0, si0 == nsub - 1)

        def OEV(eng, h=h, q0=q0, QB=QB):
            for dvt in range(ndv):
                n = min(P, dh - dvt * P)
                for (ti, ro, nn) in _rows(h * dh + dvt * P, n):
                    inst = eng.tensor_copy(
                        out=o_tiles[ti][ro:ro + nn,
                                        qcol0 + q0:qcol0 + q0 + QB],
                        in_=k.ps[4 + dvt][0:nn, 0:QB])
            return inst
        st["last_Oev"] = seq.do("vector", OEV, waits=[("pe", pv_ms[-1])],
                                inc="dv")

    if mode == "win256":
        A_pend = emit_A(0)
        for it in range(len(iters)):
            vis_of = emit_E1a(it, A_pend)
            A_pend = emit_A(it + 1) if it + 1 < len(iters) else None
            emit_BC(it, vis_of)
    else:
        for it in range(len(iters)):
            emit_A(it, paired=True)
            emit_BC(it, None, chains_done=True)

    seq.run()
    k.unphase()



def op_qk_prep(k, tiles, nh, dh, tt, cst, l2eps=None, scale_col=None,
               rope_tabs=None, Tb=T):
    nt = len(tiles)
    n512 = tt // 512
    assert tt <= 2048 and tt == Tb
    k.phase()
    if l2eps is not None:
        q2 = k.sb([P, tt], F32)
        rrow = k.sb([min(nh, P), tt], F32)
    if rope_tabs is not None:
        tmp = k.sb([P, 1024], F32)
        t2 = k.sb([P, 1024], F32)
    seq = Seq(k)

    dv_rn = None
    ap_ms = {}
    rp_ms = {}

    if l2eps is not None:
        mm_ms = []
        for j in range(nt):
            def SQ(eng, j=j):
                return eng.activation(out=q2[:], in_=tiles[j][:],
                                      func=AF.Square)
            wts = [("pe", mm_ms[j - 1])] if j >= 1 else []
            ac_sq = seq.do("scalar", SQ, waits=wts, inc="ac")

            def MM(eng, j=j, first=(j == 0), last=(j == nt - 1)):
                for c in range(n512):
                    inst = eng.matmul(
                        k.psbig[0][0:nh, c * 512:(c + 1) * 512],
                        cst["sel_ssq"][:, j * nh:(j + 1) * nh],
                        q2[:, c * 512:(c + 1) * 512],
                        start=first, stop=last)
                return inst
            mm_ms.append(seq.do("tensor", MM, waits=[("ac", ac_sq)],
                                inc="pe"))

        def RN1(eng):
            return eng.tensor_copy(out=rrow[:], in_=k.psbig[0][0:nh, 0:tt])
        dv1 = seq.do("vector", RN1, waits=[("pe", mm_ms[-1])], inc="dv")

        def RN2(eng):
            return eng.activation(out=rrow[:], in_=rrow[:], func=AF.Sqrt)
        ac2 = seq.do("scalar", RN2, waits=[("dv", dv1)], inc="ac")

        def RN3(eng, l2eps=l2eps):
            return eng.tensor_scalar_max(out=rrow[:], in0=rrow[:],
                                         scalar1=l2eps)
        dv3 = seq.do("vector", RN3, waits=[("ac", ac2)], inc="dv")

        def RN4(eng):
            return eng.activation(out=rrow[:], in_=rrow[:], func=AF.Ln)
        seq.do("scalar", RN4, waits=[("dv", dv3)], inc="ac")

        def RN5(eng):
            return eng.activation(out=rrow[:], in_=rrow[:], func=AF.Exp,
                                  scale=-1.0)
        ac5 = seq.do("scalar", RN5, inc="ac")
        dv_rn = ac5

    for j in range(nt):
        if l2eps is not None:
            def BC(eng, j=j):
                for c in range(n512):
                    inst = eng.matmul(
                        k.psbig[0][:, c * 512:(c + 1) * 512],
                        cst["sel_bc"][0:nh, j * P:(j + 1) * P],
                        rrow[:, c * 512:(c + 1) * 512],
                        start=True, stop=True)
                return inst
            wts = [("ac", dv_rn)]
            if (j - 1) in ap_ms:
                wts.append(("dv", ap_ms[j - 1]))
            pe_bc = seq.do("tensor", BC, waits=wts, inc="pe")

            def APPLY(eng, j=j):
                if scale_col is not None:
                    return eng.scalar_tensor_tensor(
                        out=tiles[j][:], in0=tiles[j][:],
                        scalar=scale_col[j][:, 0:1],
                        in1=k.psbig[0][:, 0:tt],
                        op0=ALU.mult, op1=ALU.mult)
                return eng.tensor_tensor(out=tiles[j][:], in0=tiles[j][:],
                                         in1=k.psbig[0][:, 0:tt],
                                         op=ALU.mult)
            ap_ms[j] = seq.do("vector", APPLY, waits=[("pe", pe_bc)],
                              inc="dv")

        if rope_tabs is not None:
            A_t, B_t = rope_tabs

            def ROT(eng, j=j):
                for c in range(n512):
                    inst = eng.matmul(
                        k.psbig[1][:, c * 512:(c + 1) * 512], cst["pswap"][:],
                        tiles[j][:, c * 512:(c + 1) * 512],
                        start=True, stop=True)
                return inst
            wts = []
            if l2eps is not None:
                wts.append(("dv", ap_ms[j]))
            if (j - 1) in rp_ms:
                wts.append(("dv", rp_ms[j - 1]))
            pe_rot = seq.do("tensor", ROT, waits=wts, inc="pe")

            nh2 = max(1, tt // 1024)
            w2 = min(1024, tt)
            for c2 in range(nh2):
                cs = slice(c2 * w2, (c2 + 1) * w2)

                def RP1(eng, j=j, cs=cs):
                    return eng.tensor_tensor(out=tmp[:, 0:w2],
                                             in0=tiles[j][:, cs],
                                             in1=A_t[:, cs], op=ALU.mult)
                seq.do("vector", RP1,
                       waits=[("pe", pe_rot)], inc="dv")

                def RP2(eng, j=j, cs=cs):
                    return eng.tensor_tensor(out=t2[:, 0:w2],
                                             in0=k.psbig[1][:, cs],
                                             in1=B_t[:, cs], op=ALU.mult)
                rp_ms[j] = seq.do("vector", RP2, inc="dv")

                def RP3(eng, j=j, cs=cs):
                    return eng.tensor_add(out=tiles[j][:, cs],
                                          in0=tmp[:, 0:w2], in1=t2[:, 0:w2])
                seq.do("vector", RP3, inc="dv")

    seq.run()
    k.unphase()


def op_mamba_conv(k, xr_tiles, xs_t, cst, Tb=T):
    nd = 16
    ntc = Tb // 512
    k.phase()
    acc = k.sb([P, 512], F32)

    def conv_body(eng):
        last_copy = None
        for j in range(nd):
            for tc in range(ntc):
                c0 = 3 + tc * 512
                if last_copy is not None:
                    eng.wait_ge(k.sems["dv"], last_copy)
                inst = eng.tensor_scalar(
                    out=acc[:], in0=xr_tiles[j][:, c0 - 3:c0 - 3 + 512],
                    scalar1=cst["conv_w"][:, 4 * j:4 * j + 1],
                    scalar2=cst["conv_b"][:, j:j + 1],
                    op0=ALU.mult, op1=ALU.add)
                for i in (1, 2, 3):
                    eng.wait_ge(k.sems["dv"], k.inc(inst, "dv", 1))
                    inst = eng.scalar_tensor_tensor(
                        out=acc[:], in0=xr_tiles[j][:, c0 - 3 + i:c0 - 3 + i + 512],
                        scalar=cst["conv_w"][:, 4 * j + i:4 * j + i + 1],
                        in1=acc[:], op0=ALU.mult, op1=ALU.add)
                eng.wait_ge(k.sems["dv"], k.inc(inst, "dv", 1))
                inst = eng.tensor_copy(out=xs_t[j][:, tc * 512:(tc + 1) * 512],
                                       in_=acc[:])
                last_copy = k.inc(inst, "dv", 1)
    k.op({"vector": conv_body})

    def silu_body(eng):
        for j in range(nd):
            inst = eng.activation(out=xs_t[j][:], in_=xs_t[j][:], func=AF.Silu)
            k.inc(inst, "ac", 1)
    k.op({"scalar": silu_body})
    k.unphase()


def op_mamba_scan(k, xs_t, z_hbm, w_xproj_hbm, cst, Tb=T, zcol0=0):
    nd = 16
    ntc = Tb // 512
    y_tiles = xs_t
    k.phase()
    ssm = k.sb([80, Tb], F32)
    delta = k.sb([P, Tb], F32)
    u = k.sb([P, Tb], F32)
    dA = k.sb([P, Tb], F32)
    Pin = k.sb([P, Tb], F32)
    hbuf = k.sb([P, Tb], F32)
    yacc = k.sb([P, Tb], F32)
    tmp = k.sb([P, 512], F32)
    zbuf = [k.sb([P, Tb], BF16) for _ in range(2)]

    def epi(eng, ps_ap, ds, m):
        inst = eng.tensor_copy(out=ssm[0:m, 0:Tb], in_=ps_ap)
        k.inc(inst, "dv", 1)
    op_linearB(k, xs_t, w_xproj_hbm, nd * P, 80, Tb, epi, w_chunk=80)

    seq = Seq(k)
    dep_ms = {}
    zld_ms = {}
    for j in range(nd):
        def ZLD(eng, j=j):
            return eng.dma_start(out=zbuf[j % 2][:],
                                 in_=z_hbm[j * P:(j + 1) * P,
                                           zcol0:zcol0 + Tb])
        wz = []
        if (j - 2, "zgate") in dep_ms:
            wz.append(dep_ms[(j - 2, "zgate")])
        zld_ms[j] = seq.do("sync", ZLD, waits=wz, inc="qs", incn=16)
        for tc in range(ntc):
            def OUT(eng, j=j, tc=tc):
                return eng.matmul(
                    k.ps[tc][:, :], cst["dtw_row"][0:1, j * P:(j + 1) * P],
                    ssm[0:1, tc * 512:(tc + 1) * 512],
                    start=True, stop=True)
            wts = []
            if (j - 1, tc, "dpre_free") in dep_ms:
                wts.append(dep_ms[(j - 1, tc, "dpre_free")])
            pe_out = seq.do("tensor", OUT, waits=wts, inc="pe")

            def SP1(eng, tc=tc):
                return eng.activation(out=delta[:, tc * 512:(tc + 1) * 512],
                                      in_=k.ps[tc][:, :], func=AF.Abs,
                                      bias=cst["dtb"][:, j:j + 1] if False
                                      else 0.0)
            def SP1b(eng, tc=tc, j=j):
                return eng.activation(out=delta[:, tc * 512:(tc + 1) * 512],
                                      in_=k.ps[tc][:, :], func=AF.Abs,
                                      bias=cst["dtb"][:, j:j + 1])
            w1 = [("pe", pe_out)]
            if (j - 1, "delta_last_reader") in dep_ms and tc == 0:
                w1.append(dep_ms[(j - 1, "delta_last_reader")])
            ac1 = seq.do("scalar", SP1b, waits=w1, inc="ac")

            def SP2(eng, tc=tc):
                cs = slice(tc * 512, (tc + 1) * 512)
                return eng.activation(out=delta[:, cs], in_=delta[:, cs],
                                      func=AF.Exp, scale=-1.0)
            ac2 = seq.do("scalar", SP2, inc="ac")

            def SP3(eng, tc=tc):
                cs = slice(tc * 512, (tc + 1) * 512)
                return eng.activation(out=delta[:, cs], in_=delta[:, cs],
                                      func=AF.Ln, bias=1.0)
            ac3 = seq.do("scalar", SP3, inc="ac")

            def RL1(eng, tc=tc, j=j):
                return eng.tensor_scalar(
                    out=tmp[:], in0=k.ps[tc][:, :],
                    scalar1=cst["dtb"][:, j:j + 1], scalar2=0.0,
                    op0=ALU.add, op1=ALU.max)
            dv1 = seq.do("vector", RL1, waits=[("ac", ac1)], inc="dv")

            def RL2(eng, tc=tc):
                cs = slice(tc * 512, (tc + 1) * 512)
                return eng.tensor_add(out=delta[:, cs], in0=delta[:, cs],
                                      in1=tmp[:])
            dv2 = seq.do("vector", RL2, waits=[("ac", ac3)], inc="dv")
            dep_ms[(j, "rl2_last")] = dv2
            dep_ms[(j, tc, "dpre_free")] = ("dv", dv1)

        def U(eng, j=j):
            return eng.tensor_mul(out=u[:], in0=delta[:], in1=xs_t[j][:])
        dv_u = seq.do("vector", U, inc="dv")

        def YZ(eng):
            return eng.memset(yacc[:], 0.0)
        dv_yz = seq.do("vector", YZ, inc="dv")

        prev_ac_da = None
        for n in range(16):
            def DA1(eng, j=j, n=n):
                return eng.tensor_scalar_mul(
                    out=dA[:], in0=delta[:],
                    scalar1=cst["A_cols"][:, j * 16 + n:j * 16 + n + 1])
            wda = [("dv", dep_ms[(j, "rl2_last")])]
            if prev_ac_da:
                wda.append(("ac", prev_ac_da))
            dv_da1 = seq.do("vector", DA1, waits=wda, inc="dv")
            if n == 15:
                dep_ms[(j, "delta_last_reader")] = ("dv", dv_da1)

            def DA2(eng):
                return eng.activation(out=dA[:], in_=dA[:], func=AF.Exp)
            ac_da = seq.do("scalar", DA2, waits=[("dv", dv_da1)], inc="ac")
            prev_ac_da = ac_da

            bc_ms = []
            for tc in range(ntc):
                g2 = (j * 16 + n) * ntc + tc

                def BB(eng, n=n, tc=tc, g2=g2):
                    return eng.matmul(
                        k.ps[4 + g2 % 2][:, :],
                        cst["selB"][0:80, n * P:(n + 1) * P],
                        ssm[0:80, tc * 512:(tc + 1) * 512],
                        start=True, stop=True)
                wts = []
                if ("pin", g2 - 2) in dep_ms:
                    wts.append(dep_ms[("pin", g2 - 2)])
                pe_b = seq.do("tensor", BB, waits=wts, inc="pe")

                def CC(eng, n=n, tc=tc, g2=g2):
                    return eng.matmul(
                        k.ps[6 + g2 % 2][:, :],
                        cst["selC"][0:80, n * P:(n + 1) * P],
                        ssm[0:80, tc * 512:(tc + 1) * 512],
                        start=True, stop=True)
                wts = []
                if ("ym", g2 - 2) in dep_ms:
                    wts.append(dep_ms[("ym", g2 - 2)])
                pe_c = seq.do("tensor", CC, waits=wts, inc="pe")
                bc_ms.append((pe_b, pe_c, g2))

                def PIN(eng, tc=tc, g2=g2):
                    cs = slice(tc * 512, (tc + 1) * 512)
                    return eng.tensor_tensor(out=Pin[:, cs], in0=u[:, cs],
                                             in1=k.ps[4 + g2 % 2][:, :],
                                             op=ALU.mult)
                dv_pin = seq.do("vector", PIN, waits=[("pe", pe_b)], inc="dv")
                dep_ms[("pin", g2)] = ("dv", dv_pin)

            def SCAN(eng):
                return eng.tensor_tensor_scan(
                    out=hbuf[:], data0=dA[:], data1=Pin[:], initial=0.0,
                    op0=ALU.mult, op1=ALU.add)
            dv_scan = seq.do("vector", SCAN, waits=[("ac", ac_da)],
                             inc="dv")

            for tc in range(ntc):
                g2 = bc_ms[tc][2]

                def YM(eng, tc=tc, g2=g2):
                    cs = slice(tc * 512, (tc + 1) * 512)
                    return eng.tensor_tensor(out=hbuf[:, cs], in0=hbuf[:, cs],
                                             in1=k.ps[6 + g2 % 2][:, :],
                                             op=ALU.mult)
                dv_ym = seq.do("vector", YM, waits=[("pe", bc_ms[tc][1])],
                               inc="dv")
                dep_ms[("ym", g2)] = ("dv", dv_ym)

                def YA(eng, tc=tc):
                    cs = slice(tc * 512, (tc + 1) * 512)
                    return eng.tensor_add(out=yacc[:, cs], in0=yacc[:, cs],
                                          in1=hbuf[:, cs])
                seq.do("vector", YA, inc="dv")

        def FD(eng, j=j):
            return eng.scalar_tensor_tensor(
                out=yacc[:], in0=xs_t[j][:], scalar=cst["D_col"][:, j:j + 1],
                in1=yacc[:], op0=ALU.mult, op1=ALU.add)
        dv_fd = seq.do("vector", FD, inc="dv")

        def SZ(eng, j=j):
            return eng.activation(out=y_tiles[j][:], in_=zbuf[j % 2][:],
                                  func=AF.Silu)
        ac_sz = seq.do("scalar", SZ, waits=[("qs", zld_ms[j]),
                                            ("dv", dv_fd)], inc="ac")

        def YG(eng, j=j):
            return eng.tensor_mul(out=y_tiles[j][:], in0=yacc[:],
                                  in1=y_tiles[j][:])
        dv_yg = seq.do("vector", YG, waits=[("ac", ac_sz), ("dv", dv_fd)],
                       inc="dv")
        dep_ms[(j, "zgate")] = ("ac", ac_sz)

    seq.run()
    k.unphase()


"""Orchestration: host prep + per-sublayer program builders + kernel().

Multi-launch version: one launch per top-level sublayer application.
R state [1024, 4096] fp32 per core flows between launches host-side.
"""
import numpy as np
import ml_dtypes
from contextlib import ExitStack
import concourse.bass as bass
import concourse.mybir as mybir
from concourse.bass_utils import run_bass_kernel_spmd


BFd = ml_dtypes.bfloat16
NCORE = 8



def host_consts():
    c = {}
    c["identity"] = np.eye(P, dtype=np.float32)
    qi, kj = np.arange(P)[:, None], np.arange(P)[None, :]
    c["wC"] = np.where(qi >= kj, 0.0, NEG).astype(np.float32)
    c["wA"] = np.where(kj >= qi, 0.0, NEG).astype(np.float32)
    ssq = np.zeros((P, 8 * 16), np.float32)
    sbc = np.zeros((16, 8 * P), np.float32)
    for j in range(8):
        for hf in range(2):
            h = 2 * j + hf
            ssq[hf * 64:(hf + 1) * 64, j * 16 + h] = 1.0
            sbc[h, j * P + hf * 64: j * P + (hf + 1) * 64] = 1.0
    c["sel_ssq"] = ssq
    c["sel_bc"] = sbc
    ssq4 = np.zeros((P, 8 * 4), np.float32)
    sbc4 = np.zeros((4, 8 * P), np.float32)
    for j in range(8):
        h = j // 2
        ssq4[:, j * 4 + h] = 1.0
        sbc4[h, j * P:(j + 1) * P] = 1.0
    c["sel_ssq4"] = ssq4
    c["sel_bc4"] = sbc4
    psw = np.zeros((P, P), np.float32)
    for blk in range(2):
        for r in range(64):
            psw[blk * 64 + (r + 32) % 64, blk * 64 + r] = 1.0
    c["pswap"] = psw
    DIM_HEAD, WINDOW, n = 64, 256, T
    half = np.arange(0, DIM_HEAD, 2, dtype=np.float32)
    inv_freq = 1.0 / 10000.0 ** (half / DIM_HEAD)
    t = np.arange(n, dtype=np.float32)
    freqs = np.concatenate([t[:, None] * inv_freq[None, :]] * 2, -1)
    base_scale = (half + 0.4 * DIM_HEAD) / (1.4 * DIM_HEAD)
    power = (t - n // 2) / (WINDOW // 2)
    scale = np.concatenate([base_scale[None, :] ** power[:, None]] * 2, -1)
    sgn = np.where(np.arange(DIM_HEAD) < 32, -1.0, 1.0)[None, :]

    def tilep(tab):
        return np.concatenate([tab.T, tab.T], 0).astype(np.float32)
    c["xpos_qA"] = tilep(np.cos(freqs) * scale)
    c["xpos_qB"] = tilep(np.sin(freqs) * scale * sgn)
    c["xpos_kA"] = tilep(np.cos(freqs) / scale)
    c["xpos_kB"] = tilep(np.sin(freqs) / scale * sgn)
    inv2 = 1.0 / 100000.0 ** (np.arange(0, 64, 2, dtype=np.float32) / 64)
    f2 = t[:, None] * inv2[None, :]
    A2 = np.concatenate([np.cos(f2), np.cos(f2)], 1)
    B2 = np.concatenate([-np.sin(f2), np.sin(f2)], 1)
    c["rope1_A"] = tilep(A2 * 1.0)
    c["rope1_B"] = tilep(B2 * 1.0)
    selB = np.zeros((80, 16 * P), np.float32)
    selC = np.zeros((80, 16 * P), np.float32)
    for nn in range(16):
        selB[32 + nn, nn * P:(nn + 1) * P] = 1.0
        selC[64 + nn, nn * P:(nn + 1) * P] = 1.0
    c["selB"] = selB
    c["selC"] = selC
    es = np.zeros((32, 9 * P), np.float32)
    for e in range(9):
        es[e, e * P:(e + 1) * P] = 1.0
    c["exp_sel"] = es
    return c


def col_layout(v):
    n = v.shape[0] // P
    return v.reshape(n, P).T.copy().astype(np.float32)


def pad_cols(w, to):
    out = np.zeros((w.shape[0], to), w.dtype)
    out[:, :w.shape[1]] = w
    return out


def pad_rows(w, to):
    out = np.zeros((to, w.shape[1]), w.dtype)
    out[:w.shape[0]] = w
    return out


def host_weights(inp):
    w = {}
    for i in range(6):
        w[f"f{i}_wqkv"] = np.ascontiguousarray(inp["L0_ffn_Wqkv"][i])
        w[f"f{i}_wo"] = np.ascontiguousarray(inp["L0_ffn_Wo"][i])
        W1 = inp["L0_ffn_W1"][i]
        w[f"f{i}_w1a"] = pad_cols(np.ascontiguousarray(W1[:, :2730]), 2816)
        w[f"f{i}_w1g"] = pad_cols(np.ascontiguousarray(W1[:, 2730:]), 2816)
        b1 = inp["L0_ffn_b1"][i]
        w[f"f{i}_b1a"] = col_layout(pad_rows(b1[:2730].reshape(-1, 1), 2816)[:, 0])
        w[f"f{i}_b1g"] = col_layout(pad_rows(b1[2730:].reshape(-1, 1), 2816)[:, 0])
        w[f"f{i}_w2"] = pad_rows(np.ascontiguousarray(inp["L0_ffn_W2"][i]), 2816)
        w[f"f{i}_b2"] = col_layout(inp["L0_ffn_b2"][i])
        w[f"f{i}_lng"] = col_layout(inp["L0_ffn_ln_g"][i])
        w[f"f{i}_lnb"] = col_layout(inp["L0_ffn_ln_b"][i])
        w[f"f{i}_qs"] = np.concatenate([inp["L0_ffn_q_scale"][i]] * 2).reshape(P, 1).astype(np.float32)
        w[f"f{i}_ks"] = np.concatenate([inp["L0_ffn_k_scale"][i]] * 2).reshape(P, 1).astype(np.float32)
    w["m_norm_m"] = col_layout(inp["L0_attn_norm_m_w"])
    w["m_inproj"] = np.ascontiguousarray(inp["L0_attn_in_proj_w"].T)
    cw = inp["L0_attn_conv_w"][:, 0, :]
    w["m_cw"] = cw.reshape(16, P, 4).transpose(1, 0, 2).reshape(P, 64).copy()
    w["m_cb"] = col_layout(inp["L0_attn_conv_b"])
    xpw = inp["L0_attn_x_proj_w"]
    WXP = np.zeros((2048, 80), np.float32)
    WXP[:, 0] = xpw[32]
    WXP[:, 32:48] = xpw[0:16].T
    WXP[:, 64:80] = xpw[16:32].T
    w["m_wxp"] = WXP
    w["m_dtw"] = inp["L0_attn_dt_w"].reshape(1, 2048).astype(np.float32)
    w["m_dtb"] = col_layout(inp["L0_attn_dt_b"])
    w["m_alog"] = inp["L0_attn_A_log"].reshape(16, P, 16).transpose(1, 0, 2).reshape(P, 256).copy()
    w["m_dc"] = col_layout(inp["L0_attn_D_skip"])
    w["m_outproj"] = np.ascontiguousarray(inp["L0_attn_out_proj_w"].T)
    w["m_norm_attn"] = col_layout(inp["L0_attn_norm_attn_w"])
    w["m_qw"] = np.ascontiguousarray(inp["L0_attn_q_w"].T)
    w["m_kw"] = np.ascontiguousarray(inp["L0_attn_k_w"].T)
    w["m_vw"] = np.ascontiguousarray(inp["L0_attn_v_w"].T)
    w["m_ow"] = np.ascontiguousarray(inp["L0_attn_o_w"].T)
    w["m_norm_ffn"] = col_layout(inp["L0_attn_norm_ffn_w"])
    w["m_gate"] = pad_cols(np.ascontiguousarray(inp["L0_attn_gate_w"].T), 2816)
    w["m_up"] = pad_cols(np.ascontiguousarray(inp["L0_attn_up_w"].T), 2816)
    w["m_down"] = pad_rows(np.ascontiguousarray(inp["L0_attn_down_w"].T), 2816)
    w["a_wq"] = np.ascontiguousarray(inp["L1_attn_Wq"].T)
    w["a_wk"] = np.ascontiguousarray(inp["L1_attn_Wk"].T)
    w["a_wv"] = np.ascontiguousarray(inp["L1_attn_Wv"].T)
    w["a_wo"] = np.ascontiguousarray(inp["L1_attn_Wo"].T)
    w["a_bo"] = col_layout(inp["L1_attn_bo"])
    w["a_qs"] = inp["L1_attn_q_scale"].reshape(1024)
    w["a_qs"] = w["a_qs"].reshape(8, P, 1).transpose(1, 0, 2).reshape(P, 8).copy()
    w["a_ks"] = inp["L1_attn_k_scale"].reshape(1024)
    w["a_ks"] = w["a_ks"].reshape(8, P, 1).transpose(1, 0, 2).reshape(P, 8).copy()
    w["g_ln1"] = col_layout(inp["L1_ffn_ln1_w"])
    w["g_ln2"] = col_layout(inp["L1_ffn_ln2_w"])
    w["g_wq"] = np.ascontiguousarray(inp["L1_ffn_Wq"])
    w["g_wkv"] = np.ascontiguousarray(inp["L1_ffn_Wkv"])
    w["g_wkup"] = np.ascontiguousarray(
        inp["L1_ffn_Wk_up"].transpose(1, 0, 2).reshape(256, 1024))
    w["g_wvup"] = np.ascontiguousarray(
        inp["L1_ffn_Wv_up"].transpose(1, 0, 2).reshape(256, 1024))
    w["g_wo"] = np.ascontiguousarray(inp["L1_ffn_Wo"])
    w["g_wr"] = np.ascontiguousarray(inp["L1_ffn_Wr"])
    w["g_rb"] = inp["L1_ffn_r_bias"].reshape(1, 8).astype(np.float32)
    w["g_sw1"] = np.ascontiguousarray(inp["L1_ffn_sw1"])
    w["g_sw2"] = np.ascontiguousarray(inp["L1_ffn_sw2"])
    w["g_sw3"] = np.ascontiguousarray(inp["L1_ffn_sw3"])
    for e in range(8):
        w[f"g_ew1_{e}"] = np.ascontiguousarray(inp["L1_ffn_ew1"][e])
        w[f"g_ew2_{e}"] = np.ascontiguousarray(inp["L1_ffn_ew2"][e])
        w[f"g_ew3_{e}"] = np.ascontiguousarray(inp["L1_ffn_ew3"][e])
    return w



CONST_SPECS = {
    "identity": ([P, P], F32), "wC": ([P, P], F32), "wA": ([P, P], F32),
    "sel_ssq": ([P, 8 * 16], F32), "sel_bc": ([16, 8 * P], F32),
    "sel_ssq4": ([P, 8 * 4], F32), "sel_bc4": ([4, 8 * P], F32),
    "pswap": ([P, P], BF16),
    "xpos_qA": ([P, T], BF16), "xpos_qB": ([P, T], BF16),
    "xpos_kA": ([P, T], BF16), "xpos_kB": ([P, T], BF16),
    "rope1_A": ([P, T], BF16), "rope1_B": ([P, T], BF16),
    "selB": ([80, 16 * P], F32), "selC": ([80, 16 * P], F32),
    "exp_sel": ([32, 9 * P], BF16),
}


BASE_CONSTS = ["identity", "wC", "wA"]


def mk_ker(nc, ctx, im, hostc):
    block = ctx.enter_context(nc.Block())
    k = Ker(nc, block, ctx)
    k._const_aps = {}
    k._w_aps = {}
    base = {}
    pairs = []
    for nm in BASE_CONSTS:
        shape, dt = CONST_SPECS[nm]
        ap = const_ap(k, nc, im, hostc, nm)
        sb = k.sb(shape, dt, "cst_" + nm)
        base[nm] = sb
        pairs.append((sb[:], ap))
    base["ones_col"] = k.sb([P, 1], F32, "ones_col")
    base["ones_row"] = k.sb([1, P], F32, "ones_row")

    def init(eng):
        eng.memset(base["ones_col"][:], 1.0)
        inst = eng.memset(base["ones_row"][:], 1.0)
        k.inc(inst, "pl", 1)
    k.op({"gpsimd": init})
    op_load(k, pairs, queue="qs")
    k._base_cst = base
    return k


def const_ap(k, nc, im, hostc, nm):
    if nm in k._const_aps:
        return k._const_aps[nm]
    shape, dt = CONST_SPECS[nm]
    ap = nc.dram_tensor("c_" + nm, list(shape), F32,
                        kind="ExternalInput").ap()
    k._const_aps[nm] = ap
    im["c_" + nm] = np.ascontiguousarray(hostc[nm]).astype(np.float32)
    return ap


def sec_consts(k, nc, im, hostc, names):
    cst = dict(k._base_cst)
    pairs_qs, pairs_qg = [], []
    for nm in names:
        if nm in BASE_CONSTS:
            continue
        shape, dt = CONST_SPECS[nm]
        ap = const_ap(k, nc, im, hostc, nm)
        sb = k.sb(shape, dt)
        cst[nm] = sb
        (pairs_qg if dt == BF16 else pairs_qs).append((sb[:], ap))
    if pairs_qs:
        op_load(k, pairs_qs, queue="qs")
    if pairs_qg:
        op_load(k, pairs_qg, queue="qg")
    return cst


def scratch_ap(k, nc, name, shape, dtype):
    if not hasattr(k, "_s_aps"):
        k._s_aps = {}
    if name in k._s_aps:
        return k._s_aps[name]
    ap = nc.dram_tensor(name, list(shape), dtype, kind="Internal").ap()
    k._s_aps[name] = ap
    return ap


def weight_ap(k, nc, im, hostw, name, shape):
    if name in k._w_aps:
        return k._w_aps[name]
    ap = nc.dram_tensor(name, list(shape), F32, kind="ExternalInput").ap()
    k._w_aps[name] = ap
    im[name] = hostw[name]
    return ap


def win(nc, name, shape, dtype=F32):
    return nc.dram_tensor(name, list(shape), dtype, kind="ExternalInput").ap()


def wcol_sb(k, nc, name, arr_shape):
    ap = nc.dram_tensor(name, list(arr_shape), F32, kind="ExternalInput").ap()
    sb = k.sb(list(arr_shape), F32, name + "_sb")
    return sb, ap


def ld_shifted_bf16(k, dst_tiles, X_ap, b, upper_from=4):
    c0 = b * T

    def body(eng):
        for j in range(upper_from, 8):
            inst = eng.memset(dst_tiles[j][:, 0:1], 0.0)
        k.inc(inst, "pl", 1)
        for j in range(8):
            if j < upper_from:
                inst = eng.dma_start(out=dst_tiles[j][:],
                                     in_=X_ap[j * P:(j + 1) * P, c0:c0 + T])
            else:
                inst = eng.dma_start(
                    out=dst_tiles[j][:, 1:T],
                    in_=X_ap[j * P:(j + 1) * P, c0:c0 + T - 1])
            k.inc(inst, "qg", 16)
    k.op({"gpsimd": body})


def ld_tiles(k, dst_tiles, X_ap, b, n=8, queue="qg", Tb=T):
    c0 = b * Tb

    def body(eng):
        for j in range(n):
            inst = eng.dma_start(out=dst_tiles[j][:],
                                 in_=X_ap[j * P:(j + 1) * P, c0:c0 + Tb])
            k.inc(inst, queue, 16)
    k.op({QUEUE_ENG[queue]: body})


def st_tiles(k, src_tiles, X_ap, b, n=8, queue="qg", Tb=T):
    c0 = b * Tb

    def body(eng):
        for j in range(n):
            inst = eng.dma_start(out=X_ap[j * P:(j + 1) * P, c0:c0 + Tb],
                                 in_=src_tiles[j][:])
            k.inc(inst, queue, 16)
    k.op({QUEUE_ENG[queue]: body})




def norm_to_bf16(k, x_tiles, out_tiles, cst, rms=False, eps=1e-5,
                 gain_col=None, bias_col=None, chunk=512, nr=8):
    k.phase()
    tmp = k.sb([P, 512], F32)
    tmp2 = k.sb([P, 512], F32)

    def apply_fn(eng, di, a_ps, b_ps, c0, w):
        for j in range(w // 512):
            cs = slice(c0 + j * 512, c0 + (j + 1) * 512)
            if rms:
                if gain_col is not None:
                    inst = eng.scalar_tensor_tensor(
                        out=out_tiles[di][:, cs], in0=x_tiles[di][:, cs],
                        scalar=gain_col[:, di:di + 1], in1=a_ps[j][:],
                        op0=ALU.mult, op1=ALU.mult)
                else:
                    inst = eng.tensor_tensor(
                        out=out_tiles[di][:, cs], in0=x_tiles[di][:, cs],
                        in1=a_ps[j][:], op=ALU.mult)
                k.inc(inst, "dv", 1)
            else:
                inst = eng.tensor_tensor(out=tmp[:], in0=x_tiles[di][:, cs],
                                         in1=a_ps[j][:], op=ALU.mult)
                eng.wait_ge(k.sems["dv"], k.inc(inst, "dv", 1))
                if gain_col is None:
                    inst = eng.tensor_tensor(out=out_tiles[di][:, cs],
                                             in0=tmp[:], in1=b_ps[j][:],
                                             op=ALU.add)
                    k.inc(inst, "dv", 1)
                else:
                    inst = eng.tensor_tensor(out=tmp2[:], in0=tmp[:],
                                             in1=b_ps[j][:], op=ALU.add)
                    eng.wait_ge(k.sems["dv"], k.inc(inst, "dv", 1))
                    inst = eng.tensor_scalar(
                        out=out_tiles[di][:, cs], in0=tmp2[:],
                        scalar1=gain_col[:, di:di + 1],
                        scalar2=bias_col[:, di:di + 1],
                        op0=ALU.mult, op1=ALU.add)
                    k.inc(inst, "dv", 1)
    op_norm(k, x_tiles, nr, x_tiles[0].shape[1] if hasattr(x_tiles[0], "shape")
            else T, cst["ones_col"], cst["ones_row"], apply_fn, rms=rms,
            eps=eps, chunk=chunk)
    k.unphase()


def op_hmul(k, a_tiles, b_tiles, n):
    def body(eng):
        for i in range(n):
            inst = eng.tensor_mul(out=a_tiles[i][:], in0=a_tiles[i][:],
                                  in1=b_tiles[i][:])
            k.inc(inst, "dv", 1)
    k.op({"vector": body})


def mk_epi_copy(k, out_tiles, row_off=0, col_off=0, eng_sem="dv"):
    def epi(eng, ps_ap, ds, m):
        ti, ro = divmod(row_off + ds * P, P)
        w = ps_ap.shape[1]
        inst = eng.tensor_copy(
            out=out_tiles[ti][ro:ro + m, col_off:col_off + w], in_=ps_ap)
        k.inc(inst, eng_sem, 1)
    return epi


def mk_epiA_copy(k, out_tiles, tile_off=0, col_off=0):
    def epi(eng, ps_ap, ti, wid):
        inst = eng.tensor_copy(
            out=out_tiles[tile_off + ti][:, col_off:col_off + wid], in_=ps_ap)
        k.inc(inst, "dv", 1)
    return epi


def mk_epi_scale(k, out_tiles, mul, col_off=0):
    def epi(eng, ps_ap, ds, m):
        w = ps_ap.shape[1]
        inst = eng.tensor_scalar_mul(
            out=out_tiles[ds][0:m, col_off:col_off + w],
            in0=ps_ap, scalar1=mul)
        k.inc(inst, "dv", 1)
    return epi


def mk_epi_bias(k, out_tiles, bias_col, col_off=0):
    def epi(eng, ps_ap, ds, m):
        w = ps_ap.shape[1]
        inst = eng.tensor_scalar(
            out=out_tiles[ds][0:m, col_off:col_off + w],
            in0=ps_ap, scalar1=bias_col[:, ds:ds + 1], scalar2=None,
            op0=ALU.add)
        k.inc(inst, "dv", 1)
    return epi


def mk_epi_act(k, out_tiles, func, bias_col=None, col_off=0):
    def epi(eng, ps_ap, ds, m):
        w = ps_ap.shape[1]
        kw = {}
        if bias_col is not None:
            kw["bias"] = bias_col[:, ds:ds + 1]
        inst = eng.activation(
            out=out_tiles[ds][0:m, col_off:col_off + w],
            in_=ps_ap, func=func, **kw)
        k.inc(inst, "ac", 1)
    return epi


def mk_epi_residual(k, r_tiles, alpha=1.0, bias_col=None, col_off=0):
    tmp = k.scratch

    def epi(eng, ps_ap, ds, m):
        w = ps_ap.shape[1]
        cs = slice(col_off, col_off + w)
        if bias_col is None:
            inst = eng.scalar_tensor_tensor(
                out=r_tiles[ds][0:m, cs], in0=ps_ap, scalar=alpha,
                in1=r_tiles[ds][0:m, cs], op0=ALU.mult, op1=ALU.add)
            k.inc(inst, "dv", 1)
        else:
            inst = eng.tensor_scalar(out=tmp[0:m, 0:w], in0=ps_ap,
                                     scalar1=bias_col[:, ds:ds + 1],
                                     scalar2=None, op0=ALU.add)
            eng.wait_ge(k.sems["dv"], k.inc(inst, "dv", 1))
            inst = eng.scalar_tensor_tensor(
                out=r_tiles[ds][0:m, cs], in0=tmp[0:m, 0:w], scalar=alpha,
                in1=r_tiles[ds][0:m, cs], op0=ALU.mult, op1=ALU.add)
            k.inc(inst, "dv", 1)
    return epi



def emit_l0ffn(k, nc, im, hostc, hostw, R, Rout, nlayers=6):
    names = ["sel_ssq", "sel_bc", "pswap",
             "xpos_qA", "xpos_qB", "xpos_kA", "xpos_kB"]
    k.phase()
    if True:
        cst = sec_consts(k, nc, im, hostc, names)
        X = scratch_ap(k, nc, "Xscr", [D, TT], F32)
        wq_aps, cols = {}, {}
        colpairs = []
        for i in range(6):
            for nm, shp in [("wqkv", [D, 3072]), ("wo", [D, D]),
                            ("w1a", [D, 2816]), ("w1g", [D, 2816]),
                            ("w2", [2816, D])]:
                wq_aps[(i, nm)] = weight_ap(k, nc, im, hostw,
                                            f"f{i}_{nm}", shp)
            for nm, shp in [("b1a", [P, 22]), ("b1g", [P, 22]),
                            ("b2", [P, 8]), ("lng", [P, 8]), ("lnb", [P, 8]),
                            ("qs", [P, 1]), ("ks", [P, 1])]:
                ap = weight_ap(k, nc, im, hostw, f"f{i}_{nm}", shp)
                sb = k.sb(list(shp), F32)
                cols[(i, nm)] = sb
                colpairs.append((sb[:], ap))
        op_load(k, colpairs, queue="qs")

        for b in range(NB):
            k.phase()
            rb = k.sb_tiles(8, T, F32)
            xf = k.sb_tiles(8, T, F32)
            ld_tiles(k, rb, R, b, queue="qs")
            k.phase()
            tmp = k.sb([P, 512], F32)

            def apply_fn(eng, di, a_ps, b_ps, c0, w):
                for j in range(w // 512):
                    cs = slice(c0 + j * 512, c0 + (j + 1) * 512)
                    inst = eng.tensor_tensor(out=tmp[:], in0=rb[di][:, cs],
                                             in1=a_ps[j][:], op=ALU.mult)
                    eng.wait_ge(k.sems["dv"], k.inc(inst, "dv", 1))
                    inst = eng.tensor_tensor(out=xf[di][:, cs], in0=tmp[:],
                                             in1=b_ps[j][:], op=ALU.add)
                    k.inc(inst, "dv", 1)
            op_norm(k, rb, 8, T, cst["ones_col"], cst["ones_row"], apply_fn,
                    rms=False, chunk=512)
            k.unphase()
            st_tiles(k, xf, X, b, queue="qs")
            k.unphase()

        for i in range(nlayers):
            for b in range(NB):
                k.phase()
                ot = k.sb_tiles(8, T, BF16)
                k.phase()
                vtok = k.sb_tiles(16, 1024, BF16)
                qk = k.sb_tiles(16, T, BF16)
                k.phase()
                xs = k.sb_tiles(8, T, BF16)
                wv_sb = k.sb_tiles(8, 512, BF16)
                ld_shifted_bf16(k, xs, X, b)
                for dc2 in range(2):
                    op_load(k, [(wv_sb[j][:],
                                 wq_aps[(i, "wqkv")][j * P:(j + 1) * P,
                                                     2048 + dc2 * 512:
                                                     2560 + dc2 * 512])
                                for j in range(8)], queue="qg")
                    op_linearA(k, xs, wv_sb, D, 512, T,
                               mk_epiA_copy(k, vtok, col_off=dc2 * 512))
                op_linearB(k, xs, wq_aps[(i, "wqkv")][:, 0:2048], D, 2048, T,
                           mk_epi_copy(k, qk), w_chunk=128)
                k.unphase()
                op_qk_prep(k, qk[0:8], 16, 64, T, cst, l2eps=1e-12,
                           scale_col=[cols[(i, "qs")]] * 8,
                           rope_tabs=(cst["xpos_qA"], cst["xpos_qB"]))
                op_qk_prep(k, qk[8:16], 16, 64, T, cst, l2eps=1e-12,
                           scale_col=[cols[(i, "ks")]] * 8,
                           rope_tabs=(cst["xpos_kA"], cst["xpos_kB"]))
                op_attention(k, qk[0:8], qk[8:16], vtok, ot, 16, 64, 16, T,
                             "win256", 8.0, cst)
                k.unphase()
                k.phase()
                xb = k.sb_tiles(8, T, F32)
                ld_tiles(k, xb, X, b, queue="qs")
                op_linearB(k, ot, wq_aps[(i, "wo")], D, D, T,
                           mk_epi_residual(k, xb))
                st_tiles(k, xb, X, b, queue="qs")
                k.unphase()
                k.unphase()

            for b in range(NB):
                k.phase()
                yout = k.sb_tiles(8, T, BF16)
                k.phase()
                xn = k.sb_tiles(8, T, BF16)
                k.phase()
                xsf = k.sb_tiles(8, T, F32)
                ld_shifted_bf16(k, xsf, X, b)
                norm_to_bf16(k, xsf, xn, cst, rms=False, eps=1e-5,
                             gain_col=cols[(i, "lng")],
                             bias_col=cols[(i, "lnb")], chunk=512)
                k.unphase()
                k.phase()
                gt = [k.sb([P, T], BF16) for _ in range(22)]
                op_linearB(k, xn, wq_aps[(i, "w1g")], D, 2816, T,
                           mk_epi_act(k, gt, AF.Gelu, cols[(i, "b1g")]),
                           epi_eng="scalar")

                def epi_amul(eng, ps_ap, ds, m, i=i):
                    w = ps_ap.shape[1]
                    inst = eng.tensor_scalar(
                        out=k.scratch[0:m, 0:w], in0=ps_ap,
                        scalar1=cols[(i, "b1a")][:, ds:ds + 1], scalar2=None,
                        op0=ALU.add)
                    eng.wait_ge(k.sems["dv"], k.inc(inst, "dv", 1))
                    inst = eng.tensor_mul(
                        out=gt[ds][0:m, 0:w], in0=gt[ds][0:m, 0:w],
                        in1=k.scratch[0:m, 0:w])
                    k.inc(inst, "dv", 1)
                op_linearB(k, xn, wq_aps[(i, "w1a")], D, 2816, T, epi_amul,
                           epi_incs=2)
                op_linearB(k, gt, wq_aps[(i, "w2")], 2816, D, T,
                           mk_epi_bias(k, yout, cols[(i, "b2")]),
                           w_chunk=128)
                k.unphase()
                k.unphase()
                k.phase()
                xb = k.sb_tiles(8, T, F32)
                ld_tiles(k, xb, X, b, queue="qs")

                def addres(eng):
                    for j in range(8):
                        inst = eng.tensor_add(out=xb[j][:], in0=xb[j][:],
                                              in1=yout[j][:])
                        k.inc(inst, "dv", 1)
                k.op({"vector": addres})
                st_tiles(k, xb, X, b, queue="qs")
                k.unphase()
                k.unphase()

        for b in range(NB):
            k.phase()
            rb = k.sb_tiles(8, T, F32)
            xb = k.sb_tiles(8, T, F32)
            ld_tiles(k, rb, R, b, queue="qs")
            ld_tiles(k, xb, X, b, queue="qg")

            def addf(eng):
                for j in range(8):
                    inst = eng.scalar_tensor_tensor(
                        out=rb[j][:], in0=xb[j][:], scalar=0.5, in1=rb[j][:],
                        op0=ALU.mult, op1=ALU.add)
                    k.inc(inst, "dv", 1)
            k.op({"vector": addf})
            st_tiles(k, rb, Rout, b, queue="qs")
            k.unphase()
    k.unphase()


def build_l0ffn(hostc, hostw, nlayers=6):
    nc = bass.Bass("TRN2", target_bir_lowering=False)
    ctx = ExitStack()
    im = {}
    with ctx:
        k = mk_ker(nc, ctx, im, hostc)
        R = win(nc, "R", [D, TT])
        Rout = nc.dram_tensor("Rout", [D, TT], F32, kind="ExternalOutput").ap()
        emit_l0ffn(k, nc, im, hostc, hostw, R, Rout, nlayers)
        k.finish()
    return nc, im



def emit_l0attn(k, nc, im, hostc, hostw, R, Rout):
    names = ["selB", "selC"]
    k.phase()
    if True:
        cst = sec_consts(k, nc, im, hostc, names)
        X = scratch_ap(k, nc, "Xscr", [D, TT], F32)
        Z = scratch_ap(k, nc, "Zscr", [2048, TT], BF16)
        XN = scratch_ap(k, nc, "XNscr", [D, TT], BF16)
        waps = {}
        for nm, shp in [("inproj", [D, 4096]), ("wxp", [2048, 80]),
                        ("outproj", [2048, D]), ("qw", [D, D]),
                        ("kw", [D, 512]), ("vw", [D, 512]), ("ow", [D, D]),
                        ("gate", [D, 2816]), ("up", [D, 2816]),
                        ("down", [2816, D])]:
            waps[nm] = weight_ap(k, nc, im, hostw, "m_" + nm, shp)
        mcols = {}
        colpairs = []
        for nm, shp in [("norm_m", [P, 8]), ("cw", [P, 64]), ("cb", [P, 16]),
                        ("dtw", [1, 2048]), ("dtb", [P, 16]),
                        ("alog", [P, 256]), ("dc", [P, 16]),
                        ("norm_attn", [P, 8]), ("norm_ffn", [P, 8])]:
            ap = weight_ap(k, nc, im, hostw, "m_" + nm, shp)
            sb = k.sb(list(shp), F32)
            mcols[nm] = sb
            colpairs.append((sb[:], ap))
        op_load(k, colpairs, queue="qs")
        A_cols = k.sb([P, 256], F32, "A_cols")

        def acomp(eng):
            inst = eng.activation(out=A_cols[:], in_=mcols["alog"][:],
                                  func=AF.Exp)
            k.inc(inst, "ac", 1)
        k.op({"scalar": acomp})

        def aneg(eng):
            inst = eng.tensor_scalar_mul(out=A_cols[:], in0=A_cols[:],
                                         scalar1=-1.0)
            k.inc(inst, "dv", 1)
        k.op({"vector": aneg})
        mcst = {"conv_w": mcols["cw"], "conv_b": mcols["cb"],
                "dtw_row": mcols["dtw"], "dtb": mcols["dtb"],
                "selB": cst["selB"], "selC": cst["selC"],
                "A_cols": A_cols, "D_col": mcols["dc"]}

        for b in range(NB):
            k.phase()
            rb = k.sb_tiles(8, T, F32)
            xf = k.sb_tiles(8, T, F32)
            ld_tiles(k, rb, R, b, queue="qs")
            k.phase()
            tmp = k.sb([P, 512], F32)

            def apply_fn(eng, di, a_ps, b_ps, c0, w, rb=rb, xf=xf, tmp=tmp):
                for j in range(w // 512):
                    cs = slice(c0 + j * 512, c0 + (j + 1) * 512)
                    inst = eng.tensor_tensor(out=tmp[:], in0=rb[di][:, cs],
                                             in1=a_ps[j][:], op=ALU.mult)
                    eng.wait_ge(k.sems["dv"], k.inc(inst, "dv", 1))
                    inst = eng.tensor_tensor(out=xf[di][:, cs], in0=tmp[:],
                                             in1=b_ps[j][:], op=ALU.add)
                    k.inc(inst, "dv", 1)
            op_norm(k, rb, 8, T, cst["ones_col"], cst["ones_row"], apply_fn,
                    rms=False, chunk=512)
            k.unphase()
            st_tiles(k, xf, X, b, queue="qs")
            k.unphase()

        for b in range(NB):
            k.phase()
            xnm = k.sb_tiles(8, T, BF16)
            k.phase()
            xb = k.sb_tiles(8, T, F32)
            ld_tiles(k, xb, X, b, queue="qs")
            norm_to_bf16(k, xb, xnm, cst, rms=True, eps=1e-6,
                         gain_col=mcols["norm_m"], chunk=512)
            k.unphase()
            st_tiles(k, xnm, XN, b, queue="qg")
            k.unphase()
            k.phase()
            xs_t = k.sb_tiles(16, T, BF16)
            k.phase()
            xr = k.sb_tiles(16, 3 + T, BF16)
            k.phase()
            xnm = k.sb_tiles(8, T, BF16)
            ld_tiles(k, xnm, XN, b, queue="qg")
            op_linearB(k, xnm, waps["inproj"][:, 2048:4096], D, 2048, T,
                       mk_epi_copy(k, xs_t), w_chunk=256)
            st_tiles(k, xs_t, Z, b, n=16, queue="qg", Tb=T)

            def zeropad(eng, xr=xr):
                for j in range(16):
                    inst = eng.memset(xr[j][:, 0:3], 0.0)
                k.inc(inst, "pl", 1)
            k.op({"gpsimd": zeropad})

            def epi_inx(eng, ps_ap, ds, m, xr=xr):
                w = ps_ap.shape[1]
                inst = eng.tensor_copy(out=xr[ds][0:m, 3:3 + w], in_=ps_ap)
                k.inc(inst, "dv", 1)
            op_linearB(k, xnm, waps["inproj"][:, 0:2048], D, 2048, T,
                       epi_inx, w_chunk=256)
            k.unphase()
            op_mamba_conv(k, xr, xs_t, mcst, Tb=T)
            k.unphase()
            op_mamba_scan(k, xs_t, Z, waps["wxp"], mcst, Tb=T, zcol0=b * T)
            k.phase()
            xb = k.sb_tiles(8, T, F32)
            ld_tiles(k, xb, X, b, queue="qs")
            op_linearB(k, xs_t, waps["outproj"], 2048, D, T,
                       mk_epi_residual(k, xb), w_chunk=256)
            st_tiles(k, xb, X, b, queue="qs")
            k.unphase()
            k.unphase()

        for b in range(NB):
            k.phase()
            xnm = k.sb_tiles(8, T, BF16)
            k.phase()
            xb = k.sb_tiles(8, T, F32)
            ld_tiles(k, xb, X, b, queue="qs")
            norm_to_bf16(k, xb, xnm, cst, rms=True, eps=1e-6,
                         gain_col=mcols["norm_attn"], chunk=512)
            k.unphase()
            k.phase()
            ot = k.sb_tiles(8, T, BF16)
            k.phase()
            qt = k.sb_tiles(8, T, BF16)
            kt = k.sb_tiles(4, T, BF16)
            vtok = k.sb_tiles(16, 512, BF16)
            k.phase()
            wv_sb = k.sb_tiles(8, 512, BF16)
            op_load(k, [(wv_sb[j][:], waps["vw"][j * P:(j + 1) * P, :])
                        for j in range(8)], queue="qg")
            op_linearA(k, xnm, wv_sb, D, 512, T, mk_epiA_copy(k, vtok))
            k.unphase()
            op_linearB(k, xnm, waps["qw"], D, D, T, mk_epi_copy(k, qt))
            op_linearB(k, xnm, waps["kw"], D, 512, T, mk_epi_copy(k, kt))
            op_attention(k, qt, kt, vtok, ot, 4, 256, 2, T, "causal",
                         1.0 / 16.0, cst)
            k.unphase()
            k.phase()
            xb = k.sb_tiles(8, T, F32)
            ld_tiles(k, xb, X, b, queue="qs")
            op_linearB(k, ot, waps["ow"], D, D, T, mk_epi_residual(k, xb))
            st_tiles(k, xb, X, b, queue="qs")
            k.unphase()
            k.unphase()
            k.unphase()

        for b in range(NB):
            k.phase()
            yout = k.sb_tiles(8, T, BF16)
            k.phase()
            xn = k.sb_tiles(8, T, BF16)
            k.phase()
            xb = k.sb_tiles(8, T, F32)
            ld_tiles(k, xb, X, b, queue="qs")
            norm_to_bf16(k, xb, xn, cst, rms=True, eps=1e-6,
                         gain_col=mcols["norm_ffn"], chunk=512)
            k.unphase()
            k.phase()
            gt = [k.sb([P, T], BF16) for _ in range(22)]
            op_linearB(k, xn, waps["gate"], D, 2816, T,
                       mk_epi_act(k, gt, AF.Silu), epi_eng="scalar")

            def epi_umul(eng, ps_ap, ds, m):
                w = ps_ap.shape[1]
                inst = eng.tensor_copy(out=k.scratch[0:m, 0:w], in_=ps_ap)
                eng.wait_ge(k.sems["dv"], k.inc(inst, "dv", 1))
                inst = eng.tensor_mul(
                    out=gt[ds][0:m, 0:w], in0=gt[ds][0:m, 0:w],
                    in1=k.scratch[0:m, 0:w])
                k.inc(inst, "dv", 1)
            op_linearB(k, xn, waps["up"], D, 2816, T, epi_umul, epi_incs=2)
            op_linearB(k, gt, waps["down"], 2816, D, T,
                       mk_epi_copy(k, yout), w_chunk=128)
            k.unphase()
            k.unphase()
            k.phase()
            xb = k.sb_tiles(8, T, F32)
            ld_tiles(k, xb, X, b, queue="qs")

            def addres(eng, xb=xb, yout=yout):
                for j in range(8):
                    inst = eng.tensor_add(out=xb[j][:], in0=xb[j][:],
                                          in1=yout[j][:])
                    k.inc(inst, "dv", 1)
            k.op({"vector": addres})
            st_tiles(k, xb, X, b, queue="qs")
            k.unphase()
            k.unphase()

        for b in range(NB):
            k.phase()
            rb = k.sb_tiles(8, T, F32)
            xb = k.sb_tiles(8, T, F32)
            ld_tiles(k, rb, R, b, queue="qs")
            ld_tiles(k, xb, X, b, queue="qg")

            def addf(eng, rb=rb, xb=xb):
                for j in range(8):
                    inst = eng.tensor_add(out=rb[j][:], in0=rb[j][:],
                                          in1=xb[j][:])
                    k.inc(inst, "dv", 1)
            k.op({"vector": addf})
            st_tiles(k, rb, Rout, b, queue="qs")
            k.unphase()
    k.unphase()


def build_l0attn(hostc, hostw):
    nc = bass.Bass("TRN2", target_bir_lowering=False)
    ctx = ExitStack()
    im = {}
    with ctx:
        k = mk_ker(nc, ctx, im, hostc)
        R = win(nc, "R", [D, TT])
        Rout = nc.dram_tensor("Rout", [D, TT], F32, kind="ExternalOutput").ap()
        emit_l0attn(k, nc, im, hostc, hostw, R, Rout)
        k.finish()
    return nc, im



def emit_l1attn(k, nc, im, hostc, hostw, R, Rout):
    names = ["sel_ssq", "sel_bc"]
    k.phase()
    if True:
        cst = sec_consts(k, nc, im, hostc, names)
        waps = {}
        for nm in ["wq", "wk", "wv", "wo"]:
            waps[nm] = weight_ap(k, nc, im, hostw, "a_" + nm, [D, D])
        acols = {}
        colpairs = []
        for nm, shp in [("bo", [P, 8]), ("qs", [P, 8]), ("ks", [P, 8])]:
            ap = weight_ap(k, nc, im, hostw, "a_" + nm, shp)
            sb = k.sb(list(shp), F32)
            acols[nm] = sb
            colpairs.append((sb[:], ap))
        op_load(k, colpairs, queue="qs")

        for b in range(NB):
            k.phase()
            xn = k.sb_tiles(8, T, BF16)
            k.phase()
            rb = k.sb_tiles(8, T, F32)
            ld_tiles(k, rb, R, b, queue="qs")
            k.phase()
            tmp = k.sb([P, 512], F32)

            def apply_fn(eng, di, a_ps, b_ps, c0, w, rb=rb, xn=xn, tmp=tmp):
                for j in range(w // 512):
                    cs = slice(c0 + j * 512, c0 + (j + 1) * 512)
                    inst = eng.tensor_tensor(out=tmp[:], in0=rb[di][:, cs],
                                             in1=a_ps[j][:], op=ALU.mult)
                    eng.wait_ge(k.sems["dv"], k.inc(inst, "dv", 1))
                    inst = eng.tensor_tensor(out=xn[di][:, cs], in0=tmp[:],
                                             in1=b_ps[j][:], op=ALU.add)
                    k.inc(inst, "dv", 1)
            op_norm(k, rb, 8, T, cst["ones_col"], cst["ones_row"], apply_fn,
                    rms=False, chunk=512)
            k.unphase()
            k.unphase()
            k.phase()
            ot = k.sb_tiles(8, T, BF16)
            k.phase()
            qt = k.sb_tiles(8, T, BF16)
            kt = k.sb_tiles(8, T, BF16)
            vtok = k.sb_tiles(16, 1024, BF16)
            k.phase()
            wv_sb = k.sb_tiles(8, 512, BF16)
            for dc2 in range(2):
                op_load(k, [(wv_sb[j][:],
                             waps["wv"][j * P:(j + 1) * P,
                                        dc2 * 512:(dc2 + 1) * 512])
                            for j in range(8)], queue="qg")
                op_linearA(k, xn, wv_sb, D, 512, T,
                           mk_epiA_copy(k, vtok, col_off=dc2 * 512))
            k.unphase()
            op_linearB(k, xn, waps["wq"], D, D, T, mk_epi_copy(k, qt))
            op_linearB(k, xn, waps["wk"], D, D, T, mk_epi_copy(k, kt))
            op_qk_prep(k, qt, 16, 64, T, cst, l2eps=1e-12,
                       scale_col=[acols["qs"][:, j:j + 1] for j in range(8)])
            op_qk_prep(k, kt, 16, 64, T, cst, l2eps=1e-12,
                       scale_col=[acols["ks"][:, j:j + 1] for j in range(8)])
            op_attention(k, qt, kt, vtok, ot, 16, 64, 16, T, "full",
                         0.125, cst)
            k.unphase()
            k.phase()
            rb = k.sb_tiles(8, T, F32)
            ld_tiles(k, rb, R, b, queue="qs")
            op_linearB(k, ot, waps["wo"], D, D, T,
                       mk_epi_residual(k, rb, bias_col=acols["bo"]),
                       epi_incs=2)
            st_tiles(k, rb, Rout, b, queue="qs")
            k.unphase()
            k.unphase()
            k.unphase()
    k.unphase()


def build_l1attn(hostc, hostw):
    nc = bass.Bass("TRN2", target_bir_lowering=False)
    ctx = ExitStack()
    im = {}
    with ctx:
        k = mk_ker(nc, ctx, im, hostc)
        R = win(nc, "R", [D, TT])
        Rout = nc.dram_tensor("Rout", [D, TT], F32, kind="ExternalOutput").ap()
        emit_l1attn(k, nc, im, hostc, hostw, R, Rout)
        k.finish()
    return nc, im



def emit_l1ffn(k, nc, im, hostc, hostw, R, Rout, upto="all"):
    names = ["rope1_A", "rope1_B", "pswap", "exp_sel"]
    k.phase()
    if True:
        cst = sec_consts(k, nc, im, hostc, names)
        X = scratch_ap(k, nc, "Xscr", [D, TT], F32)
        waps = {}
        for nm, shp in [("wq", [D, D]), ("wkv", [D, 256]), ("wkup", [256, D]),
                        ("wvup", [256, D]), ("wo", [D, D]), ("wr", [D, 8]),
                        ("sw1", [D, 2048]), ("sw2", [2048, D]),
                        ("sw3", [D, 2048])]:
            waps[nm] = weight_ap(k, nc, im, hostw, "g_" + nm, shp)
        for e in range(8):
            for nm, shp in [("ew1", [D, 2048]), ("ew2", [2048, D]),
                            ("ew3", [D, 2048])]:
                waps[f"{nm}_{e}"] = weight_ap(k, nc, im, hostw,
                                              f"g_{nm}_{e}", shp)
        gcols = {}
        colpairs = []
        for nm, shp in [("ln1", [P, 8]), ("ln2", [P, 8]), ("rb", [1, 8])]:
            ap = weight_ap(k, nc, im, hostw, "g_" + nm, shp)
            sb = k.sb(list(shp), F32)
            gcols[nm] = sb
            colpairs.append((sb[:], ap))
        op_load(k, colpairs, queue="qs")

        for b in range(NB):
            k.phase()
            rb = k.sb_tiles(8, T, F32)
            xf = k.sb_tiles(8, T, F32)
            ld_tiles(k, rb, R, b, queue="qs")
            k.phase()
            tmp = k.sb([P, 512], F32)

            def apply_fn(eng, di, a_ps, b_ps, c0, w, rb=rb, xf=xf, tmp=tmp):
                for j in range(w // 512):
                    cs = slice(c0 + j * 512, c0 + (j + 1) * 512)
                    inst = eng.tensor_tensor(out=tmp[:], in0=rb[di][:, cs],
                                             in1=a_ps[j][:], op=ALU.mult)
                    eng.wait_ge(k.sems["dv"], k.inc(inst, "dv", 1))
                    inst = eng.tensor_tensor(out=xf[di][:, cs], in0=tmp[:],
                                             in1=b_ps[j][:], op=ALU.add)
                    k.inc(inst, "dv", 1)
            op_norm(k, rb, 8, T, cst["ones_col"], cst["ones_row"], apply_fn,
                    rms=False, chunk=512)
            k.unphase()
            st_tiles(k, xf, X, b, queue="qs")
            k.unphase()

        for b in range(NB):
            k.phase()
            xn = k.sb_tiles(8, T, BF16)
            k.phase()
            xb = k.sb_tiles(8, T, F32)
            ld_tiles(k, xb, X, b, queue="qs")
            norm_to_bf16(k, xb, xn, cst, rms=True, eps=1e-6,
                         gain_col=gcols["ln1"], chunk=512)
            k.unphase()
            k.phase()
            ot = k.sb_tiles(8, T, BF16)
            k.phase()
            qt = k.sb_tiles(8, T, BF16)
            kt = k.sb_tiles(8, T, BF16)
            vtok = k.sb_tiles(16, 1024, BF16)
            k.phase()
            kv = k.sb_tiles(2, T, BF16)
            op_linearB(k, xn, waps["wkv"], D, 256, T,
                       mk_epi_scale(k, kv, 0.25))
            op_linearB(k, kv, waps["wkup"], 256, D, T, mk_epi_copy(k, kt))
            k.phase()
            wv_sb = k.sb_tiles(2, 1024, BF16)
            op_load(k, [(wv_sb[j][:], waps["wvup"][j * P:(j + 1) * P, :])
                        for j in range(2)], queue="qg")
            op_linearA(k, kv, wv_sb, 256, 1024, T, mk_epiA_copy(k, vtok))
            k.unphase()
            k.unphase()
            op_linearB(k, xn, waps["wq"], D, D, T, mk_epi_copy(k, qt))
            op_qk_prep(k, qt, 16, 64, T, cst,
                       rope_tabs=(cst["rope1_A"], cst["rope1_B"]))
            op_qk_prep(k, kt, 16, 64, T, cst,
                       rope_tabs=(cst["rope1_A"], cst["rope1_B"]))
            op_attention(k, qt, kt, vtok, ot, 16, 64, 16, T, "causal",
                         0.125, cst)
            k.unphase()
            k.phase()
            xb = k.sb_tiles(8, T, F32)
            ld_tiles(k, xb, X, b, queue="qs")
            op_linearB(k, ot, waps["wo"], D, D, T, mk_epi_residual(k, xb))
            st_tiles(k, xb, X, b, queue="qs")
            k.unphase()
            k.unphase()
            k.unphase()

        for b in range(NB if upto == "all" else 0):
            k.phase()
            xn = k.sb_tiles(8, T, BF16)
            k.phase()
            xb = k.sb_tiles(8, T, F32)
            ld_tiles(k, xb, X, b, queue="qs")
            norm_to_bf16(k, xb, xn, cst, rms=True, eps=1e-6,
                         gain_col=gcols["ln2"], chunk=512)
            k.unphase()
            k.phase()
            acc = k.sb_tiles(8, T, F32)
            ST = k.sb([32, T], BF16)
            abc = k.sb([P, T], BF16)

            def zacc(eng, acc=acc, ST=ST):
                for j in range(8):
                    inst = eng.memset(acc[j][:], 0.0)
                inst = eng.memset(ST[:], 0.0)
                k.inc(inst, "pl", 1)
            k.op({"gpsimd": zacc})

            k.phase()
            wr_sb = k.sb_tiles(8, 8, BF16)
            rbc = k.sb([P, 8], F32)
            probs = k.sb([P, 16 * 8], F32)
            sca = k.sb([P, 16], F32)
            eqb = k.sb([P, 8], F32)
            cumb = k.sb([P, 8], F32)
            zeros8 = k.sb([P, 8], F32)
            op_load(k, [(wr_sb[j][:], waps["wr"][j * P:(j + 1) * P, :])
                        for j in range(8)], queue="qg")

            def pre(eng):
                inst = eng.memset(zeros8[:], 0.0)
                k.inc(inst, "pl", 1)
            k.op({"gpsimd": pre})

            def rbcast(eng):
                inst = eng.matmul(k.ps[7][:, 0:8], cst["ones_row"][0:1, :],
                                  gcols["rb"][0:1, 0:8], start=True, stop=True)
                k.inc(inst, "pe", 1)
            k.op({"tensor": rbcast})

            def rbc_cp(eng):
                inst = eng.tensor_copy(out=rbc[:], in_=k.ps[7][:, 0:8])
                k.inc(inst, "dv", 1)
            k.op({"vector": rbc_cp})

            def epi_router(eng, ps_ap, ti, wid):
                inst = eng.tensor_tensor(out=probs[:, ti * 8:ti * 8 + 8],
                                         in0=ps_ap, in1=rbc[:], op=ALU.add)
                k.inc(inst, "dv", 1)
            op_linearA(k, xn, wr_sb, D, 8, T, epi_router)

            seq = Seq(k)
            for ti in range(16):
                pr = lambda ti=ti: probs[:, ti * 8:ti * 8 + 8]

                def C1(eng, pr=pr):
                    return eng.tensor_scalar_max(out=pr(), in0=pr(),
                                                 scalar1=-50.0)
                seq.do("vector", C1, inc="dv")

                def C2(eng, pr=pr):
                    return eng.tensor_scalar_min(out=pr(), in0=pr(),
                                                 scalar1=50.0)
                dvc = seq.do("vector", C2, inc="dv")

                def SG(eng, pr=pr):
                    return eng.activation(out=pr(), in_=pr(), func=AF.Sigmoid)
                acs = seq.do("scalar", SG, waits=[("dv", dvc)], inc="ac")

                def PM(eng, pr=pr, ti=ti):
                    return eng.reduce_max(out=sca[:, 15:16], in_=pr(),
                                          axis=AX.X)
                seq.do("vector", PM, waits=[("ac", acs)], inc="dv")

                def EQ(eng, pr=pr):
                    return eng.tensor_tensor(
                        out=eqb[:], in0=pr(),
                        in1=sca[:, 15:16].broadcast_to([P, 8]),
                        op=ALU.is_equal)
                seq.do("vector", EQ, inc="dv")

                def CU(eng):
                    return eng.tensor_tensor_scan(
                        out=cumb[:], data0=eqb[:], data1=zeros8[:],
                        initial=0.0, op0=ALU.add, op1=ALU.add)
                seq.do("vector", CU, inc="dv")

                def FE(eng):
                    return eng.tensor_single_scalar(out=cumb[:], in_=cumb[:],
                                                    scalar=1.0,
                                                    op=ALU.is_equal)
                seq.do("vector", FE, inc="dv")

                def OH(eng):
                    return eng.tensor_mul(out=eqb[:], in0=eqb[:], in1=cumb[:])
                seq.do("vector", OH, inc="dv")

                def PC1(eng):
                    return eng.tensor_scalar_max(out=sca[:, 15:16],
                                                 in0=sca[:, 15:16],
                                                 scalar1=1e-8)
                seq.do("vector", PC1, inc="dv")

                def PC2(eng):
                    return eng.tensor_scalar_min(out=sca[:, 15:16],
                                                 in0=sca[:, 15:16],
                                                 scalar1=1.0 - 1e-8)
                seq.do("vector", PC2, inc="dv")

                def TW(eng):
                    return eng.tensor_scalar(out=sca[:, 14:15],
                                             in0=sca[:, 15:16],
                                             scalar1=0.5 + 1e-8, scalar2=2.0,
                                             op0=ALU.add, op1=ALU.min)
                seq.do("vector", TW, inc="dv")

                def TW2(eng):
                    return eng.tensor_scalar_max(out=sca[:, 14:15],
                                                 in0=sca[:, 14:15],
                                                 scalar1=0.5)
                seq.do("vector", TW2, inc="dv")

                def IT(eng):
                    return eng.reciprocal(out=sca[:, 14:15],
                                          in_=sca[:, 14:15])
                seq.do("vector", IT, inc="dv")

                def SA(eng):
                    return eng.tensor_scalar_mul(out=sca[:, 0:1],
                                                 in0=sca[:, 14:15],
                                                 scalar1=0.5)
                seq.do("vector", SA, inc="dv")

                def PI(eng):
                    return eng.tensor_mul(out=sca[:, 15:16],
                                          in0=sca[:, 15:16],
                                          in1=sca[:, 14:15])
                seq.do("vector", PI, inc="dv")

                def BC(eng):
                    return eng.tensor_tensor(
                        out=sca[:, 1:9], in0=eqb[:],
                        in1=sca[:, 15:16].broadcast_to([P, 8]), op=ALU.mult)
                dvb = seq.do("vector", BC, inc="dv")

                def TR(eng, ti=ti):
                    return eng.transpose(k.ps[6][0:9, 0:P], sca[:, 0:9],
                                         cst["identity"][:])
                pet = seq.do("tensor", TR, waits=[("dv", dvb)], inc="pe")

                def CP(eng, ti=ti):
                    return eng.tensor_copy(out=ST[0:9, ti * P:(ti + 1) * P],
                                           in_=k.ps[6][0:9, 0:P])
                seq.do("vector", CP, waits=[("pe", pet)], inc="dv")
            seq.run()
            k.unphase()

            k.phase()
            ht = [k.sb([P, T], BF16) for _ in range(16)]
            for e in range(-1, 8):
                w1 = waps["sw1"] if e < 0 else waps[f"ew1_{e}"]
                w3 = waps["sw3"] if e < 0 else waps[f"ew3_{e}"]
                w2 = waps["sw2"] if e < 0 else waps[f"ew2_{e}"]
                op_linearB(k, xn, w1, D, 2048, T,
                           mk_epi_act(k, ht, AF.Silu), epi_eng="scalar",
                           w_chunk=256)

                def epi_mul(eng, ps_ap, ds, m):
                    w = ps_ap.shape[1]
                    inst = eng.tensor_copy(out=k.scratch[0:m, 0:w], in_=ps_ap)
                    eng.wait_ge(k.sems["dv"], k.inc(inst, "dv", 1))
                    inst = eng.tensor_mul(
                        out=ht[ds][0:m, 0:w], in0=ht[ds][0:m, 0:w],
                        in1=k.scratch[0:m, 0:w])
                    k.inc(inst, "dv", 1)
                op_linearB(k, xn, w3, D, 2048, T, epi_mul, epi_incs=2,
                           w_chunk=256)

                sl = 0 if e < 0 else 1 + e

                def bca(eng, sl=sl):
                    for tc in range(4):
                        inst = eng.matmul(
                            k.ps[4 + tc][:, :],
                            cst["exp_sel"][0:32, sl * P:(sl + 1) * P],
                            ST[0:32, tc * 512:(tc + 1) * 512],
                            start=True, stop=True)
                        k.inc(inst, "pe", 1)
                k.op({"tensor": bca})

                def bcc(eng):
                    for tc in range(4):
                        inst = eng.tensor_copy(
                            out=abc[:, tc * 512:(tc + 1) * 512],
                            in_=k.ps[4 + tc][:, :])
                        k.inc(inst, "dv", 1)
                k.op({"vector": bcc})

                def epi_acc(eng, ps_ap, ds, m):
                    w = ps_ap.shape[1]
                    inst = eng.tensor_tensor(
                        out=k.scratch[0:m, 0:w], in0=ps_ap,
                        in1=abc[0:m, 0:w], op=ALU.mult)
                    eng.wait_ge(k.sems["dv"], k.inc(inst, "dv", 1))
                    inst = eng.tensor_add(
                        out=acc[ds][0:m, 0:w], in0=acc[ds][0:m, 0:w],
                        in1=k.scratch[0:m, 0:w])
                    k.inc(inst, "dv", 1)
                op_linearB(k, ht, w2, 2048, D, T, epi_acc, epi_incs=2,
                           w_chunk=256)
            k.unphase()

            k.phase()
            xb = k.sb_tiles(8, T, F32)
            ld_tiles(k, xb, X, b, queue="qs")

            def addres(eng, xb=xb, acc=acc):
                for j in range(8):
                    inst = eng.tensor_add(out=xb[j][:], in0=xb[j][:],
                                          in1=acc[j][:])
                    k.inc(inst, "dv", 1)
            k.op({"vector": addres})
            st_tiles(k, xb, X, b, queue="qs")
            k.unphase()
            k.unphase()
            k.unphase()

        for b in range(NB):
            k.phase()
            rb = k.sb_tiles(8, T, F32)
            xb = k.sb_tiles(8, T, F32)
            ld_tiles(k, rb, R, b, queue="qs")
            ld_tiles(k, xb, X, b, queue="qg")

            def addf(eng, rb=rb, xb=xb):
                for j in range(8):
                    inst = eng.scalar_tensor_tensor(
                        out=rb[j][:], in0=xb[j][:], scalar=0.5, in1=rb[j][:],
                        op0=ALU.mult, op1=ALU.add)
                    k.inc(inst, "dv", 1)
            k.op({"vector": addf})
            st_tiles(k, rb, Rout, b, queue="qs")
            k.unphase()
    k.unphase()


def build_l1ffn(hostc, hostw, upto="all"):
    nc = bass.Bass("TRN2", target_bir_lowering=False)
    ctx = ExitStack()
    im = {}
    with ctx:
        k = mk_ker(nc, ctx, im, hostc)
        R = win(nc, "R", [D, TT])
        Rout = nc.dram_tensor("Rout", [D, TT], F32, kind="ExternalOutput").ap()
        emit_l1ffn(k, nc, im, hostc, hostw, R, Rout, upto)
        k.finish()
    return nc, im



def build_fused(hostc, hostw):
    nc = bass.Bass("TRN2", target_bir_lowering=False)
    ctx = ExitStack()
    im = {}
    with ctx:
        k = mk_ker(nc, ctx, im, hostc)
        R = win(nc, "R", [D, TT])
        Rout = nc.dram_tensor("Rout", [D, TT], F32, kind="ExternalOutput").ap()
        Rs = scratch_ap(k, nc, "Rstate", [D, TT], F32)
        emit_l0ffn(k, nc, im, hostc, hostw, R, Rs)
        emit_l0attn(k, nc, im, hostc, hostw, Rs, Rs)
        emit_l0ffn(k, nc, im, hostc, hostw, Rs, Rs)
        emit_l1ffn(k, nc, im, hostc, hostw, Rs, Rs)
        emit_l1attn(k, nc, im, hostc, hostw, Rs, Rs)
        emit_l1ffn(k, nc, im, hostc, hostw, Rs, Rout)
        k.finish()
    return nc, im



def kernel(**inputs):
    hostc = host_consts()
    hostw = host_weights(inputs)
    x = np.ascontiguousarray(np.asarray(inputs["L0_attn_x"], dtype=np.float32))
    Rs = []
    for c in range(NCORE):
        Rs.append(np.ascontiguousarray(
            np.concatenate([x[2 * c].T, x[2 * c + 1].T], axis=1)))
    nc, im = build_fused(hostc, hostw)
    in_maps = [dict(im, R=Rs[c]) for c in range(NCORE)]
    res = run_bass_kernel_spmd(nc, in_maps, core_ids=list(range(NCORE)))
    out = np.empty((16, 2048, 1024), np.float32)
    for c in range(NCORE):
        ro = np.asarray(res.results[c]["Rout"])
        out[2 * c] = ro[:, :T].T
        out[2 * c + 1] = ro[:, T:].T
    return out
